# version 04
# baseline (speedup 1.0000x reference)
import numpy as np
import ml_dtypes
import bass_rust
import concourse.bass as bass
import concourse.mybir as mybir
from concourse.bass_utils import run_bass_kernel_spmd

F32 = mybir.dt.float32
BF16 = mybir.dt.bfloat16
I32 = mybir.dt.int32
U8 = mybir.dt.uint8
AF = mybir.ActivationFunctionType
ALU = mybir.AluOpType
AX = mybir.AxisListType

P = 128
D = 1024
DT = D // P
SBUF_BYTES = 203 * 1024
SZ = {F32: 4, BF16: 2, I32: 4, U8: 1}

ENGINES = ("pe", "act", "dve", "pool", "sp")


class Bld:
    def __init__(self, nc):
        self.nc = nc
        self.stream = {e: [] for e in ENGINES}
        self.cnt = {}
        self.known = {e: {} for e in ENGINES}
        self.base = []

    def barrier(self):
        self.base = [(s, v) for s, v in self.cnt.items() if v > 0]
        return list(self.base)

    def sem(self, name):
        if name not in self.cnt:
            self.cnt[name] = 0
        return name

    def op(self, eng, fn, waits=(), inc=None, n=1):
        elide = eng != "pool"
        ws = []
        seen = {}
        for t in list(self.base) + list(waits):
            if t is None:
                continue
            s, v = t
            if v <= 0 or (elide and v <= self.known[eng].get(s, 0)):
                continue
            if v <= seen.get(s, 0):
                continue
            seen[s] = v
            ws.append((s, v))
            if elide:
                self.known[eng][s] = v
        ws = [(s, v) for s, v in ws if v == seen[s]]
        tok = None
        if inc is None:
            inc = eng
        self.sem(inc)
        self.cnt[inc] += n
        tok = (inc, self.cnt[inc])
        self.stream[eng].append((ws, fn, inc, n))
        return tok

    def nopop(self, eng, waits):
        return self.op(eng, lambda E: E.nop(), waits, inc=eng + "_ck")

    def ckpt(self, eng, sem):
        return self.nopop(eng, [(sem, self.cnt.get(sem, 0))])

    def dma(self, eng, out, in_, waits=(), sem=None, exclusive=True):
        sem = self.sem(sem or ("dq_" + eng))
        ws = list(waits)
        if exclusive and self.cnt.get(sem, 0) > 0:
            ws.append((sem, self.cnt[sem]))
        return self.op(eng, lambda E, o=out, i=in_: E.dma_start(o, i), ws,
                       inc=sem, n=16)

    def build_into(self, block, semobjs):
        decos = {"pe": block.tensor, "act": block.scalar, "dve": block.vector,
                 "pool": block.gpsimd, "sp": block.sync}
        for eng in ENGINES:
            ops = self.stream[eng]

            @decos[eng]
            def _(E, ops=ops):
                for ws, fn, inc, n in ops:
                    for s, v in ws:
                        E.wait_ge(semobjs[s], v)
                    ins = fn(E)
                    ins.then_inc(semobjs[inc], n)


class Arena:
    def __init__(self, size):
        self.size = size
        self.off = 0
        self.hi = 0

    def alloc(self, nbytes, align=64):
        self.off = (self.off + align - 1) // align * align
        r = self.off
        self.off += nbytes
        self.hi = max(self.hi, self.off)
        assert self.off <= self.size, f"arena overflow: {self.off} > {self.size}"
        return r

    def mark(self):
        return self.off

    def reset(self, mark):
        self.off = mark


class Ctx:
    def __init__(self, nc, b, T):
        self.nc = nc
        self.b = b
        self.T = T
        self.C = T // 512
        self.ar = Arena(SBUF_BYTES)
        self.sb = None
        self.ps = None
        self.dram = {}
        self.bankuse = {}
        self.stg_war = {}

    def v(self, off, cols, dtype, p0=0, pn=P):
        return self.sb[p0:p0 + pn, off:off + cols * SZ[dtype]].bitcast(dtype)

    def bank(self, i, cols, dtype, p0=0, pn=P):
        o = 2048 * i
        return self.ps[p0:p0 + pn, o:o + cols * SZ[dtype]].bitcast(dtype)


def row1(dram_1d):
    return dram_1d.rearrange("(o x) -> o x", o=1)


def col1(dram_1d):
    return dram_1d.rearrange("(x o) -> x o", o=1)


def emit_ln(ctx, xsrc, xn_off, xn_stride, g_dram=None, bg_dram=None, shift=False,
            tag="ln", want_xb=False, src_tok=(), fout_dram=None):
    b, T, C = ctx.b, ctx.T, ctx.C
    mark0 = ctx.ar.mark()

    def sl(k, c0, n, tiles):
        off = 0 if (shift and k >= DT // 2) else 1
        return tiles[k][:, off + c0: off + c0 + n]

    oc = ctx.v(ctx.ar.alloc(256), 1, BF16)
    oc_tok = b.op("pool", lambda E, o=oc: E.memset(o, 1.0), list(src_tok))

    g_rows, b_cols, g_tok, b_tok = [], [], [], []
    for k in range(DT):
        gv = ctx.v(ctx.ar.alloc(256), 128, BF16, pn=1)
        g_rows.append(gv)
        if g_dram is not None:
            b.dma("pool", gv, row1(g_dram[k * P:(k + 1) * P]), sem="q_gld",
                  exclusive=False, waits=list(src_tok))
            g_tok.append(None)
        else:
            g_tok.append(b.op("pool", lambda E, o=gv: E.memset(o, 1.0), list(src_tok)))
        if bg_dram is not None:
            bv = ctx.v(ctx.ar.alloc(64), 1, F32)
            b_cols.append(bv)
            b.dma("sp", bv, col1(bg_dram[k * P:(k + 1) * P]), sem="q_sld",
                  exclusive=False, waits=list(src_tok))
            b_tok.append(None)
        else:
            b_cols.append(None)
            b_tok.append(None)
    if g_dram is not None:
        gall = b.ckpt("pool", "q_gld")
        g_tok = [gall if t is None else t for t in g_tok]
    if bg_dram is not None:
        ball = b.ckpt("sp", "q_sld")
        b_tok = [ball for _ in b_tok]

    rstd_bf = ctx.v(ctx.ar.alloc(T * 2), T, BF16, pn=1)
    nmr_bf = ctx.v(ctx.ar.alloc(T * 2), T, BF16, pn=1)
    srows = ctx.ar.alloc(4 * 512 * 4)
    rstd_tok = [None] * C
    nmr_tok = [None] * C

    mark1 = ctx.ar.mark()
    xb_stride = (T + 4) * 2
    xb_off = ctx.ar.alloc(DT * xb_stride)
    xb_tiles = [ctx.v(xb_off + k * xb_stride, T + 1, BF16) for k in range(DT)]
    xb_tok = []
    for k in range(DT):
        z = b.op("pool", lambda E, t=xb_tiles[k]: E.memset(t[:, 0:1], 0.0),
                 list(src_tok))
        tk = b.dma("pool", xb_tiles[k][:, 1:T + 1], xsrc[k * P:(k + 1) * P, :],
                   sem=f"q_gx{k}", waits=list(src_tok))
        xb_tok.append(tk)

    sq_off = ctx.ar.alloc(2 * 512 * 2)
    sq_bufs = [ctx.v(sq_off + i * 1024, 512, BF16) for i in range(2)]
    sq_war = [None, None]

    bank_consumed = ctx.bankuse

    for c in range(C):
        bmu_i, bsq_i = (0, 1) if c % 2 == 0 else (2, 3)
        bmu = ctx.bank(bmu_i, 512, F32, pn=1)
        bsq = ctx.bank(bsq_i, 512, F32, pn=1)
        mu_tok = sq_mm_tok = None
        for k in range(DT):
            w = [xb_tok[k], oc_tok, bank_consumed.get(bmu_i)]
            mu_tok = b.op("pe", lambda E, o=bmu, x=sl(k, c * 512, 512, xb_tiles),
                          k=k: E.matmul(o, oc, x, start=(k == 0), stop=(k == DT - 1)), w)
        for k in range(DT):
            i2 = k % 2
            sq = sq_bufs[i2]
            sqt = b.op("dve", lambda E, s=sq, x=sl(k, c * 512, 512, xb_tiles):
                       E.tensor_tensor(s, x, x, op=ALU.mult),
                       [xb_tok[k], sq_war[i2]])
            w = [sqt, oc_tok, bank_consumed.get(bsq_i)]
            sq_mm_tok = b.op("pe", lambda E, o=bsq, s=sq, k=k:
                             E.matmul(o, oc, s, start=(k == 0), stop=(k == DT - 1)), w)
            sq_war[i2] = sq_mm_tok
        mu_row = ctx.v(srows, 512, F32, pn=1)
        var_row = ctx.v(srows + 2048, 512, F32, pn=1)
        musq = ctx.v(srows + 4096, 512, F32, pn=1)
        rstd_f = ctx.v(srows + 6144, 512, F32, pn=1)
        t1 = b.op("dve", lambda E, o=mu_row, i=bmu: E.tensor_scalar(
            o, i, 1.0 / D, None, op0=ALU.mult), [mu_tok])
        bank_consumed[bmu_i] = t1
        t2 = b.op("dve", lambda E, o=var_row, i=bsq: E.tensor_scalar(
            o, i, 1.0 / D, 1e-5, op0=ALU.mult, op1=ALU.add), [sq_mm_tok])
        bank_consumed[bsq_i] = t2
        t3 = b.op("dve", lambda E, o=musq, i=mu_row: E.tensor_tensor(
            o, i, i, op=ALU.mult), [t1])
        t4 = b.op("dve", lambda E, o=var_row, a=var_row, m=musq: E.tensor_tensor(
            o, a, m, op=ALU.subtract), [t2, t3])
        t5 = b.op("act", lambda E, o=rstd_f, i=var_row: E.activation(o, i, AF.Ln), [t4])
        t6 = b.op("act", lambda E, o=rstd_f, i=rstd_f: E.activation(
            o, i, AF.Exp, scale=-0.5), [t5])
        rstd_tok[c] = b.op("dve", lambda E, o=rstd_bf[:, c * 512:c * 512 + 512],
                           i=rstd_f: E.tensor_copy(o, i), [t6])
        t7 = b.op("dve", lambda E, o=nmr_bf[:, c * 512:c * 512 + 512], m=mu_row,
                  r=rstd_f: E.tensor_tensor(o, m, r, op=ALU.mult), [t6, t3])
        nmr_tok[c] = b.op("dve", lambda E, o=nmr_bf[:, c * 512:c * 512 + 512]:
                          E.tensor_scalar(o, o, -1.0, None, op0=ALU.mult), [t7])

    sq_done = sq_mm_tok

    tmp_off = ctx.ar.alloc(2 * 512 * 4)
    xn_tiles = [ctx.v(xn_off + k * xn_stride, T + 1, BF16) for k in range(DT)]
    if fout_dram is not None:
        fo_off = ctx.ar.alloc(4 * 512 * 4)
        fo_war = {}
    xn_tok = [[None] * C for _ in range(DT)]
    xn_z = [None] * DT
    for k in range(DT):
        for c in range(C):
            kc = k * C + c
            bR_i, bM_i = (4, 5) if kc % 2 == 0 else (6, 7)
            bR = ctx.bank(bR_i, 512, F32)
            bM = ctx.bank(bM_i, 512, F32)
            rtok = b.op("pe", lambda E, o=bR, g=g_rows[k],
                        r=rstd_bf[:, c * 512:c * 512 + 512]: E.matmul(
                o, g, r, start=True, stop=True),
                [rstd_tok[c], g_tok[k], bank_consumed.get(bR_i)])
            mtok = b.op("pe", lambda E, o=bM, g=g_rows[k],
                        m=nmr_bf[:, c * 512:c * 512 + 512]: E.matmul(
                o, g, m, start=True, stop=True),
                [nmr_tok[c], bank_consumed.get(bM_i)])
            xsl = sl(k, c * 512, 512, xb_tiles)
            tmp = ctx.v(tmp_off + (kc % 2) * 2048, 512, F32)
            a1 = b.op("dve", lambda E, o=tmp, x=xsl, R=bR: E.tensor_tensor(
                o, x, R, op=ALU.mult), [xb_tok[k], rtok])
            bank_consumed[bR_i] = a1
            if fout_dram is not None:
                i4 = kc % 4
                fo = ctx.v(fo_off + i4 * 2048, 512, F32)
                a2 = b.op("dve", lambda E, o=fo, t=tmp, M=bM: E.tensor_tensor(
                    o, t, M, op=ALU.add), [a1, mtok, fo_war.get(i4)])
                d = b.dma("sp", fout_dram[k * P:(k + 1) * P, c * 512:c * 512 + 512],
                          fo, waits=[a2], sem=f"q_sxo{i4}")
                fo_war[i4] = d
            elif b_cols[k] is not None:
                a2 = b.op("dve", lambda E, o=xn_tiles[k][:, 1 + c * 512:1 + c * 512 + 512],
                          t=tmp, bb=b_cols[k], M=bM: E.scalar_tensor_tensor(
                    o, t, bb, M, op0=ALU.add, op1=ALU.add), [a1, mtok, b_tok[k]])
            else:
                a2 = b.op("dve", lambda E, o=xn_tiles[k][:, 1 + c * 512:1 + c * 512 + 512],
                          t=tmp, M=bM: E.tensor_tensor(o, t, M, op=ALU.add),
                          [a1, mtok])
            bank_consumed[bM_i] = a2
            xn_tok[k][c] = a2

    store_ck = None
    if fout_dram is not None:
        store_ck = b.nopop("sp", [(f"q_sxo{i}", b.cnt.get(f"q_sxo{i}", 0)) for i in range(4)])
    ctx.ar.reset(mark0)
    return dict(xn_tiles=xn_tiles, xn_tok=xn_tok, xn_z=xn_z, store_ck=store_ck)


def emit_prelude(ctx):
    b = ctx.b
    idb = ctx.v(ctx.ar.alloc(128 * 2), 128, BF16)
    t = b.dma("pool", idb, ctx.dram["c_eye"], sem="q_pre")
    ctx.id_bf = idb
    ctx.id_tok = ("q_pre", b.cnt["q_pre"])


def tlin_stage(ctx, w_dram, KT, MT, NIN, rhs_fn, out_fn, rhs_deps, tag,
               tr_banks=(0, 1), mm_banks=(2, 3), war0=(), scratch_off=None):
    b, C = ctx.b, ctx.C
    G = 8
    ngrp = (KT + G - 1) // G
    nwt = 2 if KT <= 8 else 1
    mark = ctx.ar.mark()
    wraw_sz = NIN * 2
    wT_sz = KT * 128 * 2
    if scratch_off is not None:
        wraw_off = scratch_off
        wT_off = scratch_off + 2 * wraw_sz
    else:
        wraw_off = ctx.ar.alloc(2 * wraw_sz)
        wT_off = ctx.ar.alloc(nwt * wT_sz)
    wload_tok = [None, None]
    wT_tok = [[None] * ngrp, [None] * ngrp]
    slot_last = [None, None]
    mm_last = None
    for j in range(MT):
        sl = j % 2
        wraw = ctx.v(wraw_off + sl * wraw_sz, NIN, BF16)
        wT = ctx.v(wT_off + (sl % nwt) * wT_sz, KT * 128, BF16)
        war = [slot_last[sl]] if slot_last[sl] is not None else list(war0)
        wload_tok[sl] = b.dma("pool", wraw, w_dram[j * P:(j + 1) * P, :],
                              waits=war, sem=f"q_gw{sl}")
        for g in range(ngrp):
            tb_i = tr_banks[(j * ngrp + g) % 2]
            tb = ctx.bank(tb_i, 1024, BF16)
            n_in_g = min(G, KT - g * G)
            ttok = None
            for ki in range(n_in_g):
                k = g * G + ki
                w = [wload_tok[sl], ctx.id_tok, ctx.bankuse.get(tb_i)]
                ttok = b.op("pe", lambda E, o=tb[:, ki * 128:(ki + 1) * 128],
                            i=wraw[:, k * 128:(k + 1) * 128]:
                            E.transpose(o, i, ctx.id_bf), w)
            ev = b.op("act", lambda E, o=wT[:, g * G * 128:(g * G + n_in_g) * 128],
                      i=tb[:, 0:n_in_g * 128]: E.copy(o, i), [ttok])
            ctx.bankuse[tb_i] = ev
            wT_tok[sl][g] = ev
        for c in range(C):
            mb_i = mm_banks[(j * C + c) % 2]
            mb = ctx.bank(mb_i, 512, F32)
            mm = None
            for k in range(KT):
                w = [wT_tok[sl][k // G], ctx.bankuse.get(mb_i)] + (rhs_deps if k == 0 and c == 0 else [])
                mm = b.op("pe", lambda E, o=mb, l=wT[:, k * 128:(k + 1) * 128],
                          r=rhs_fn(k, c), k=k:
                          E.matmul(o, l, r, start=(k == 0), stop=(k == KT - 1)), w)
            mm_last = mm
            ctx.bankuse[mb_i] = out_fn(j, c, mb, mm)
        slot_last[sl] = mm_last
    ctx.ar.reset(mark)
    return mm_last


def emit_scale_cols(ctx, s_dram, ntiles, half=False, tag="sc"):
    b = ctx.b
    views = []
    for j in range(ntiles):
        v = ctx.v(ctx.ar.alloc(64), 1, F32)
        views.append(v)
        b.dma("sp", v, col1(s_dram[j * P:(j + 1) * P]), sem="q_sld", exclusive=False)
    tok = b.ckpt("sp", "q_sld")
    toks = [tok] * ntiles
    if half:
        newtok = []
        for j in range(ntiles):
            t = b.op("dve", lambda E, o=views[j]: E.tensor_scalar(
                o, o, 0.5, None, op0=ALU.mult), [tok])
            newtok.append(t)
        toks = newtok
    return views, toks


def emit_l1ffn(ctx, x_in, x_out, W, scr_a, scr_b, tag="f"):
    b, T, C = ctx.b, ctx.T, ctx.C
    HT = 32
    mark0 = ctx.ar.mark()

    s1_v, s1_t = emit_scale_cols(ctx, W["w1_s"], HT)
    s2_v, s2_t = emit_scale_cols(ctx, W["w2_s"], HT)
    h1_v, h1_t = emit_scale_cols(ctx, W["h1_s"], HT)
    h2_v, h2_t = emit_scale_cols(ctx, W["h2_s"], HT)
    w3_v, w3_t = emit_scale_cols(ctx, W["w3_s"], DT, half=True)

    tl_off = ctx.ar.alloc(2 * 4096 * 2 + 32 * 128 * 2)

    xn_stride = (T + 4) * 2
    xn_off = ctx.ar.alloc(DT * xn_stride)
    ln = emit_ln(ctx, x_in, xn_off, xn_stride, tag=tag + "l")
    xn, xn_tok, xn_z = ln["xn_tiles"], ln["xn_tok"], ln["xn_z"]
    xn_all = [xn_tok[k][c] for k in range(DT) for c in range(ctx.C)]
    mark_ln = ctx.ar.mark()

    big_sz = T * 2
    big_off = ctx.ar.alloc(HT * big_sz)
    big = [ctx.v(big_off + j * big_sz, T, BF16) for j in range(HT)]

    def rhs_xn(k, c):
        return xn[k][:, 1 + c * 512: 1 + c * 512 + 512]

    out_last = [None]

    def out_t1(j, c, ps, mm):
        t = b.op("act", lambda E, o=big[j][:, c * 512:c * 512 + 512], i=ps:
                 E.activation(o, i, AF.Silu, scale=s1_v[j]), [mm, s1_t[j]])
        out_last[0] = t
        return t

    st1_mm = tlin_stage(ctx, W["w1_w"], DT, HT, D, rhs_xn, out_t1, xn_all, tag + "1",
                        war0=[xn_all[-1]], scratch_off=tl_off)
    st1_out = out_last[0]

    def out_h(j, c, ps, mm):
        t = b.op("dve", lambda E, o=big[j][:, c * 512:c * 512 + 512], i=ps,
                 s=s2_v[j], t1s=big[j][:, c * 512:c * 512 + 512]:
                 E.scalar_tensor_tensor(o, i, s, t1s, op0=ALU.mult, op1=ALU.mult),
                 [mm, s2_t[j], out_last[0] if out_last[0] is not None else None])
        out_last[0] = t
        return t

    st2_mm = tlin_stage(ctx, W["w2_w"], DT, HT, D, rhs_xn, out_h, xn_all, tag + "2",
                        tr_banks=(4, 5), mm_banks=(6, 7), war0=[st1_mm],
                        scratch_off=tl_off)
    st2_out = out_last[0]


    if DT * xn_stride >= 20480:
        st_off = xn_off
        xi_off = xn_off + 4 * 1024
        xo_off = xn_off + 4 * 1024 + 4 * 2048
    else:
        st_off = ctx.ar.alloc(4 * 1024)
        xi_off = ctx.ar.alloc(4 * 2048)
        xo_off = ctx.ar.alloc(4 * 2048)

    def mk_store_out(scale_v, scale_t, scr, qsem):
        def out(j, c, ps, mm):
            i = (j * C + c) % 4
            stg = ctx.v(st_off + i * 1024, 512, BF16)
            t = b.op("act", lambda E, o=stg, i=ps: E.activation(
                o, i, AF.Gelu, scale=scale_v[j]),
                [mm, scale_t[j], ctx.stg_war.get(i)])
            d = b.dma("sp", scr[j * P:(j + 1) * P, c * 512:c * 512 + 512], stg,
                      waits=[t], sem=f"{qsem}{i}")
            ctx.stg_war[i] = d
            return t
        return out

    def rhs_big(k, c):
        return big[k][:, c * 512:c * 512 + 512]

    last = tlin_stage(ctx, W["h1_w"], HT, HT, 4096, rhs_big,
                      mk_store_out(h1_v, h1_t, scr_a, "q_s3"), [st2_out], tag + "3",
                      war0=[st2_mm], scratch_off=tl_off)

    g1_tok = []
    for j in range(HT):
        g1_tok.append(b.dma("sp", big[j], scr_a[j * P:(j + 1) * P, :],
                            waits=[last] + [ctx.stg_war.get(i) for i in range(4)],
                            sem="q_sg", exclusive=False))
    g1_all = b.ckpt("sp", "q_sg")
    last4 = tlin_stage(ctx, W["h2_w"], HT, HT, 4096, rhs_big,
                       mk_store_out(h2_v, h2_t, scr_b, "q_s4"), [g1_all], tag + "4",
                       tr_banks=(4, 5), mm_banks=(6, 7), war0=[last],
                       scratch_off=tl_off)

    g2_tok = []
    for j in range(HT):
        g2_tok.append(b.dma("sp", big[j], scr_b[j * P:(j + 1) * P, :],
                            waits=[last4] + [ctx.stg_war.get(i) for i in range(4)],
                            sem="q_sg2", exclusive=False))
    g2_all = b.ckpt("sp", "q_sg2")

    xi_war = {}
    xo_war = {}

    def out_res(j, c, ps, mm):
        i = (j * C + c) % 4
        xi = ctx.v(xi_off + i * 2048, 512, F32)
        xo = ctx.v(xo_off + i * 2048, 512, F32)
        ld = b.dma("sp", xi, x_in[j * P:(j + 1) * P, c * 512:c * 512 + 512],
                   waits=[xi_war.get(i) or g2_all], sem=f"q_sxi{i}")
        t = b.op("dve", lambda E, o=xo, i=ps, s=w3_v[j], x=xi:
                 E.scalar_tensor_tensor(o, i, s, x, op0=ALU.mult, op1=ALU.add),
                 [mm, ld, w3_t[j], xo_war.get(i)])
        xi_war[i] = t
        d = b.dma("sp", x_out[j * P:(j + 1) * P, c * 512:c * 512 + 512], xo,
                  waits=[t], sem=f"q_sxo{i}")
        xo_war[i] = d
        return t

    tlin_stage(ctx, W["w3_w"], HT, DT, 4096, rhs_big, out_res, [g2_all], tag + "5",
               war0=[last4], scratch_off=tl_off)
    out_done = [("q_sxo%d" % i, b.cnt.get("q_sxo%d" % i, 0)) for i in range(4)]
    ctx.ar.reset(mark0)
    return out_done


def wslice_ap(w_dram, ncols_total, col0, ncols, ktiles=DT):
    c = w_dram.copy()
    c.ap = bass_rust.VecI64Pair([[ncols_total, P], [ncols_total * P, ktiles], [1, ncols]])
    c.offset = w_dram.offset + col0
    return c


def emit_l1attn(ctx, x_in, x_out, W_qkv, W_out, rel_emb, bias_hbm, tag="a"):
    b, T, C = ctx.b, ctx.T, ctx.C
    TT_ = T // 128
    NG = (TT_ + 7) // 8
    mark0 = ctx.ar.mark()
    RLEN = 2 * T - 1
    WT = 2 * T - 128
    SKB = T - 128
    bias2d = bias_hbm.rearrange("(a b) -> a b", b=RLEN)
    bias_fl = bias_hbm.rearrange("(x o) -> x o", o=1)

    idx = ctx.v(ctx.ar.alloc(64), 16, I32)
    idx_t = b.dma("sp", idx, ctx.dram["c_t5idx"], sem="q_sld")

    mark_b = ctx.ar.mark()
    oh = ctx.v(ctx.ar.alloc(RLEN * 2 + 64), RLEN, BF16, pn=32)
    b.dma("pool", oh, ctx.dram["c_t5oh"], sem="q_gld", exclusive=False)
    rel = ctx.v(ctx.ar.alloc(64), 16, BF16, pn=32)
    b.dma("pool", rel, rel_emb, sem="q_gld", exclusive=False)
    oh_t = rel_t = b.ckpt("pool", "q_gld")
    brow = ctx.v(ctx.ar.alloc(RLEN * 2 + 64), RLEN, BF16, pn=16)
    ev_last = None
    for c in range((RLEN + 511) // 512):
        n = min(512, RLEN - c * 512)
        bk = ctx.bank(c % 2, n, F32, pn=16)
        mm = b.op("pe", lambda E, o=bk, r=rel, h=oh[:, c * 512:c * 512 + n]:
                  E.matmul(o, r, h, start=True, stop=True),
                  [oh_t, rel_t, ctx.bankuse.get(c % 2)])
        ev_last = b.op("dve", lambda E, o=brow[:, c * 512:c * 512 + n], i=bk:
                       E.tensor_copy(o, i), [mm])
        ctx.bankuse[c % 2] = ev_last
    b.dma("sp", bias2d, brow, waits=[ev_last], sem="q_sbias")
    bias_ck = b.ckpt("sp", "q_sbias")
    ctx.ar.reset(mark_b)

    xn_stride = (T + 4) * 2
    xn_off = ctx.ar.alloc(DT * xn_stride)
    ln = emit_ln(ctx, x_in, xn_off, xn_stride, tag=tag + "l", src_tok=[bias_ck])
    xn, xn_tok = ln["xn_tiles"], ln["xn_tok"]

    q_off = ctx.ar.alloc(DT * T * 2)
    k_off = ctx.ar.alloc(DT * T * 2)
    v_off = ctx.ar.alloc(TT_ * 1024 * 2)
    o_off = ctx.ar.alloc(DT * T * 2)
    q_sb = [ctx.v(q_off + j * T * 2, T, BF16) for j in range(DT)]
    k_sb = [ctx.v(k_off + j * T * 2, T, BF16) for j in range(DT)]
    v_sb = [ctx.v(v_off + j * 1024 * 2, 1024, BF16) for j in range(TT_)]
    o_sb = [ctx.v(o_off + j * T * 2, T, BF16) for j in range(DT)]
    mark1 = ctx.ar.mark()

    ws_off = ctx.ar.alloc(2 * DT * 128 * 2)
    vs_off = ctx.ar.alloc(DT * 512 * 2)
    q_tok = [[None] * C for _ in range(DT)]
    k_tok = [[None] * C for _ in range(DT)]
    v_tok = [None] * TT_
    ln_done = xn_tok[DT - 1][C - 1]
    ws_war = [ln_done, ln_done]
    mmb = 0
    for dst, base, toks in ((q_sb, 0, q_tok), (k_sb, 1024, k_tok)):
        for j in range(DT):
            sl = j % 2
            ws = ctx.v(ws_off + sl * DT * 128 * 2, DT * 128, BF16)
            wt = b.dma("pool", ws, wslice_ap(W_qkv, 3072, base + j * 128, 128),
                       waits=[ws_war[sl]], sem=f"q_gw{sl}")
            mm = None
            for c in range(C):
                bi = 2 + (mmb % 2)
                mmb += 1
                bk = ctx.bank(bi, 512, F32)
                for k in range(DT):
                    w = [wt, xn_tok[k][c], ctx.bankuse.get(bi)]
                    mm = b.op("pe", lambda E, o=bk, l=ws[:, k * 128:(k + 1) * 128],
                              r=xn[k][:, 1 + c * 512:1 + c * 512 + 512], k=k:
                              E.matmul(o, l, r, start=(k == 0), stop=(k == DT - 1)), w)
                ev = b.op("act" if (j + c) % 2 else "dve",
                          lambda E, o=dst[j][:, c * 512:c * 512 + 512], i=bk:
                          (E.copy(o, i) if hasattr(E, "copy") else E.tensor_copy(o, i)),
                          [mm])
                toks[j][c] = ev
                ctx.bankuse[bi] = ev
            ws_war[sl] = mm
    vs_war = ln_done
    for vc in range(2):
        vs = ctx.v(vs_off, DT * 512, BF16)
        wt = b.dma("pool", vs, wslice_ap(W_qkv, 3072, 2048 + vc * 512, 512),
                   waits=[vs_war], sem="q_gw0")
        for tt in range(TT_):
            bi = 4 + (tt % 2)
            bk = ctx.bank(bi, 512, F32)
            mm = None
            for k in range(DT):
                w = [wt, xn_tok[k][(tt * 128) // 512], ctx.bankuse.get(bi)]
                mm = b.op("pe", lambda E, o=bk, l=xn[k][:, 1 + tt * 128:1 + (tt + 1) * 128],
                          r=vs[:, k * 512:(k + 1) * 512], k=k:
                          E.matmul(o, l, r, start=(k == 0), stop=(k == DT - 1)), w)
            ev = b.op("act" if tt % 2 else "dve",
                      lambda E, o=v_sb[tt][:, vc * 512:(vc + 1) * 512], i=bk:
                      (E.copy(o, i) if hasattr(E, "copy") else E.tensor_copy(o, i)),
                      [mm])
            v_tok[tt] = ev
            ctx.bankuse[bi] = ev
        vs_war = mm

    ctx.ar.reset(mark1)
    bsk_off = ctx.ar.alloc(2 * WT * 2)
    P_off = ctx.ar.alloc(2 * T * 2)
    Pf_off = ctx.ar.alloc(T * 4)
    PT_off = ctx.ar.alloc(2 * TT_ * 128 * 2)
    sm_off = ctx.ar.alloc(2 * 192)
    P_bf2 = [ctx.v(P_off + i * T * 2, T, BF16) for i in range(2)]
    P_f2 = [ctx.v(Pf_off, T, F32), ctx.v(Pf_off, T, F32)]
    PT_sb2 = [ctx.v(PT_off + i * TT_ * 128 * 2, TT_ * 128, BF16) for i in range(2)]
    negmax2 = [ctx.v(sm_off + i * 192, 1, F32) for i in range(2)]
    sums2 = [ctx.v(sm_off + i * 192 + 64, 1, F32) for i in range(2)]
    recip2 = [ctx.v(sm_off + i * 192 + 128, 1, F32) for i in range(2)]
    bsk_tok = [None, None]
    bsk_war = [None, None]
    tr_tok = [None, None]
    sc_tok_prev = [None, None]
    mx_prev = [None, None]
    qq = 0
    cmm = None
    oev_tok = None
    for h in range(16):
        bsk = ctx.v(bsk_off + (h % 2) * WT * 2, WT, BF16)
        war = [bsk_war[h % 2] or vs_war, bias_ck, idx_t]
        bsk_tok[h % 2] = b.op(
            "pool",
            lambda E, o=bsk, i=bias_fl, ix=idx[:, h:h + 1]: E.indirect_dma_start(
                o, None, i, bass.IndirectOffsetOnAxis(ap=ix, axis=0)),
            war, inc=f"q_gbsk{h % 2}", n=16)
        jt2, pb2 = h // 2, 64 * (h % 2)
        for qt in range(TT_):
            sl2 = qq % 2
            qq += 1
            P_bf, P_f, PT_sb = P_bf2[sl2], P_f2[sl2], PT_sb2[sl2]
            negmax, sums, recip = negmax2[sl2], sums2[sl2], recip2[sl2]
            smm = None
            for c in range(C):
                bk = ctx.bank(c, 512, F32)
                w = [q_tok[jt2][(qt * 128) // 512], k_tok[jt2][c],
                     ctx.bankuse.get(c), bsk_tok[h % 2]]
                smm = b.op("pe", lambda E, o=bk,
                           l=q_sb[jt2][pb2:pb2 + 64, qt * 128:(qt + 1) * 128],
                           r=k_sb[jt2][pb2:pb2 + 64, c * 512:(c + 1) * 512]:
                           E.matmul(o, l, r, start=True, stop=False), w)
                smm = b.op("pe", lambda E, o=bk, l=ctx.id_bf,
                           r=bsk[:, SKB - 128 * qt + c * 512:SKB - 128 * qt + c * 512 + 512]:
                           E.matmul(o, l, r, start=False, stop=True), [smm, ctx.id_tok])
            spsum = ctx.ps[:, 0:C * 2048].bitcast(F32)
            mx = b.op("dve", lambda E, o=negmax, i=spsum: E.reduce_max(
                o, i, axis=AX.X, negate=True), [smm, mx_prev[sl2]])
            et = b.op("act", lambda E, o=P_f, i=spsum, bb=negmax, s=sums:
                      E.activation(o, i, AF.Exp, bias=bb, accum_out=s),
                      [mx, sc_tok_prev[0], sc_tok_prev[1]])
            for c in range(C):
                ctx.bankuse[c] = et
            rc = b.op("dve", lambda E, o=recip, i=sums: E.reciprocal(o, i), [et])
            sc_tok = b.op("dve", lambda E, o=P_bf, i=P_f, r=recip: E.tensor_scalar(
                o, i, r, None, op0=ALU.mult), [rc, tr_tok[sl2]])
            sc_tok_prev[sl2] = sc_tok
            mx_prev[sl2] = rc
            for g in range(NG):
                tb_i = 6 + (g % 2)
                tb = ctx.bank(tb_i, 1024, BF16)
                nin = min(8, TT_ - g * 8)
                ttok = None
                for ki in range(nin):
                    jt = g * 8 + ki
                    w = [sc_tok, ctx.id_tok, ctx.bankuse.get(tb_i)]
                    ttok = b.op("pe", lambda E, o=tb[:, ki * 128:(ki + 1) * 128],
                                i=P_bf[:, jt * 128:(jt + 1) * 128]:
                                E.transpose(o, i, ctx.id_bf), w)
                g_ev = b.op("dve", lambda E, o=PT_sb[:, g * 1024:g * 1024 + nin * 128],
                            i=tb[:, 0:nin * 128]: E.tensor_copy(o, i), [ttok])
                ctx.bankuse[tb_i] = g_ev
                tr_tok[sl2] = ttok
            pt_ck = g_ev
            for jt in range(TT_):
                bk = ctx.bank(4, 128, F32, pn=64)
                w = [pt_ck, v_tok[jt], ctx.bankuse.get(4)]
                cmm = b.op("pe", lambda E, o=bk, l=v_sb[jt][:, 64 * h:64 * h + 64],
                           r=PT_sb[:, jt * 128:(jt + 1) * 128], jt=jt:
                           E.matmul(o, l, r, start=(jt == 0), stop=(jt == TT_ - 1)), w)
            oev_tok = b.op("act", lambda E, o=o_sb[jt2][pb2:pb2 + 64, qt * 128:(qt + 1) * 128],
                           i=ctx.bank(4, 128, F32, pn=64): E.copy(o, i), [cmm])
            ctx.bankuse[4] = oev_tok
        bsk_war[h % 2] = smm

    o_done = oev_tok
    ctx.ar.reset(mark1)

    ws2_off = ctx.ar.alloc(2 * DT * 128 * 2)
    xi_off = ctx.ar.alloc(4 * 512 * 4)
    xo2_off = ctx.ar.alloc(4 * 512 * 4)
    xi_war, xo_war = {}, {}
    ws2_war = [None, None]
    out_done = []
    for j in range(DT):
        sl = j % 2
        ws = ctx.v(ws2_off + sl * DT * 128 * 2, DT * 128, BF16)
        wt = b.dma("pool", ws, wslice_ap(W_out, 1024, j * 128, 128),
                   waits=[ws2_war[sl] or cmm], sem=f"q_gw{sl}")
        for c in range(C):
            bi = (j * C + c) % 4
            bk = ctx.bank(bi, 512, F32)
            mm = None
            for k in range(DT):
                w = [wt, o_done, ctx.bankuse.get(bi)]
                mm = b.op("pe", lambda E, o=bk, l=ws[:, k * 128:(k + 1) * 128],
                          r=o_sb[k][:, c * 512:(c + 1) * 512], k=k:
                          E.matmul(o, l, r, start=(k == 0), stop=(k == DT - 1)), w)
            i4 = (j * C + c) % 4
            xi = ctx.v(xi_off + i4 * 2048, 512, F32)
            xo = ctx.v(xo2_off + i4 * 2048, 512, F32)
            ld = b.dma("sp", xi, x_in[j * P:(j + 1) * P, c * 512:c * 512 + 512],
                       waits=[xi_war.get(i4) or cmm], sem=f"q_sxi{i4}")
            t = b.op("dve", lambda E, o=xo, i=bk, x=xi: E.tensor_tensor(
                o, i, x, op=ALU.add), [mm, ld, xo_war.get(i4)])
            xi_war[i4] = t
            ctx.bankuse[bi] = t
            d = b.dma("sp", x_out[j * P:(j + 1) * P, c * 512:c * 512 + 512], xo,
                      waits=[t], sem=f"q_sxo{i4}")
            xo_war[i4] = d
            out_done.append(d)
        ws2_war[sl] = mm
    ctx.ar.reset(mark0)
    return out_done[-4:]


def emit_l0attn(ctx, x_in, x_out, W, tag="m"):
    b, T, C = ctx.b, ctx.T, ctx.C
    TT_ = T // 128
    mark0 = ctx.ar.mark()

    qup_off = ctx.ar.alloc(DT * T * 2)
    y_off = ctx.ar.alloc(DT * T * 2)
    ckvT_off = ctx.ar.alloc(T * 2)
    ckv_off = ctx.ar.alloc(TT_ * 128 * 2)
    qup_sb = [ctx.v(qup_off + j * T * 2, T, BF16) for j in range(DT)]
    y_sb = [ctx.v(y_off + j * T * 2, T, BF16) for j in range(DT)]
    ckvT = ctx.v(ckvT_off, T, BF16)
    ckv_sb = ctx.v(ckv_off, TT_ * 128, BF16)

    mark1 = ctx.ar.mark()

    keff = ctx.v(ctx.ar.alloc(8 * 128 * 2), 8 * 128, BF16)
    veff = ctx.v(ctx.ar.alloc(1024 * 2), 1024, BF16)
    dqT = ctx.v(ctx.ar.alloc(8 * 128 * 2), 8 * 128, BF16)
    dkvT = ctx.v(ctx.ar.alloc(8 * 128 * 2), 8 * 128, BF16)
    uqT = ctx.v(ctx.ar.alloc(8 * 128 * 2), 8 * 128, BF16)
    msk = ctx.v(ctx.ar.alloc(4 * 512 * 2), 4 * 512, BF16)
    mark_prep = ctx.ar.mark()

    dq_raw = ctx.v(ctx.ar.alloc(1024 * 2), 1024, BF16)
    dkv_raw = ctx.v(ctx.ar.alloc(1024 * 2), 1024, BF16)
    uq_raw = ctx.v(ctx.ar.alloc(8 * 128 * 2), 8 * 128, BF16)
    uk_raw = ctx.v(ctx.ar.alloc(8 * 128 * 2), 8 * 128, BF16)
    uv_raw = ctx.v(ctx.ar.alloc(8 * 128 * 2), 8 * 128, BF16)
    b.dma("pool", dq_raw, W["dq"], sem="q_gld", exclusive=False)
    b.dma("pool", dkv_raw, W["dkv"], sem="q_gld", exclusive=False)
    b.dma("pool", uq_raw, wslice_ap(W["uq"], 128, 0, 128), sem="q_gld", exclusive=False)
    b.dma("pool", uk_raw, wslice_ap(W["uk"], 128, 0, 128), sem="q_gld", exclusive=False)
    b.dma("pool", uv_raw, wslice_ap(W["uv"], 128, 0, 128), sem="q_gld", exclusive=False)
    raw_ck = b.ckpt("pool", "q_gld")

    tr_ev = {}
    for name, srcv, dstv in (("dq", dq_raw, dqT), ("dkv", dkv_raw, dkvT)):
        for g in range(1):
            tb = ctx.bank(6, 1024, BF16)
            tt = None
            for k in range(DT):
                tt = b.op("pe", lambda E, o=tb[:, k * 128:(k + 1) * 128],
                          i=srcv[:, k * 128:(k + 1) * 128]:
                          E.transpose(o, i, ctx.id_bf), [raw_ck, ctx.id_tok, ctx.bankuse.get(6)])
            ev = b.op("dve", lambda E, o=dstv, i=tb: E.tensor_copy(o, i), [tt])
            ctx.bankuse[6] = ev
            tr_ev[name] = ev
    tb = ctx.bank(7, 1024, BF16)
    tt = None
    for k in range(DT):
        tt = b.op("pe", lambda E, o=tb[:, k * 128:(k + 1) * 128],
                  i=uq_raw[:, k * 128:(k + 1) * 128]:
                  E.transpose(o, i, ctx.id_bf), [raw_ck, ctx.id_tok, ctx.bankuse.get(7)])
    uqT_ev = b.op("dve", lambda E, o=uqT, i=tb: E.tensor_copy(o, i), [tt])
    ctx.bankuse[7] = uqT_ev

    t1_sb = ctx.v(ctx.ar.alloc(128 * 2), 128, BF16)
    bk5 = ctx.bank(5, 128, F32)
    mm = None
    for k in range(DT):
        mm = b.op("pe", lambda E, o=bk5, l=uq_raw[:, k * 128:(k + 1) * 128],
                  r=uk_raw[:, k * 128:(k + 1) * 128], k=k:
                  E.matmul(o, l, r, start=(k == 0), stop=(k == DT - 1)),
                  [raw_ck, ctx.bankuse.get(5)])
    t1_ev = b.op("dve", lambda E, o=t1_sb, i=bk5: E.tensor_copy(o, i), [mm])
    ctx.bankuse[5] = t1_ev
    keff_ev = None
    for k in range(DT):
        bk5b = ctx.bank(5, 128, F32)
        mm = b.op("pe", lambda E, o=bk5b, l=dq_raw[:, k * 128:(k + 1) * 128], r=t1_sb:
                  E.matmul(o, l, r, start=True, stop=True),
                  [t1_ev, ctx.bankuse.get(5)])
        keff_ev = b.op("act", lambda E, o=keff[:, k * 128:(k + 1) * 128], i=bk5b:
                       E.activation(o, i, AF.Copy, scale=0.125), [mm])
        ctx.bankuse[5] = keff_ev

    wo_raw = ctx.v(ctx.ar.alloc(1024 * 2), 1024, BF16)
    woT = ctx.v(ctx.ar.alloc(DT * 1024 * 2), DT * 1024, BF16)
    wo_war = None
    for k in range(DT):
        ld = b.dma("pool", wo_raw, W["o"][k * P:(k + 1) * P, :], waits=[wo_war],
                   sem="q_gw0")
        for g in range(2):
            tb = ctx.bank(6 + g % 2, 1024, BF16)
            tt = None
            for ki in range(4):
                cc = g * 4 + ki
                tt = b.op("pe", lambda E, o=tb[:, ki * 128:(ki + 1) * 128],
                          i=wo_raw[:, cc * 128:(cc + 1) * 128]:
                          E.transpose(o, i, ctx.id_bf),
                          [ld, ctx.id_tok, ctx.bankuse.get(6 + g % 2)])
            ev = None
            for ki in range(4):
                cc = g * 4 + ki
                ev = b.op("dve", lambda E, o=woT[:, cc * 1024 + k * 128:cc * 1024 + (k + 1) * 128],
                          i=tb[:, ki * 128:(ki + 1) * 128]: E.tensor_copy(o, i), [tt])
            ctx.bankuse[6 + g % 2] = ev
            wo_war = ev
    woT_ck = wo_war
    for half in range(2):
        bkv = ctx.bank(5, 512, F32)
        mm = None
        for k in range(DT):
            mm = b.op("pe", lambda E, o=bkv, l=uv_raw[:, k * 128:(k + 1) * 128],
                      r=woT[:, k * 1024 + half * 512:k * 1024 + half * 512 + 512], k=k:
                      E.matmul(o, l, r, start=(k == 0), stop=(k == DT - 1)),
                      [raw_ck, woT_ck, ctx.bankuse.get(5)])
        ev = b.op("dve", lambda E, o=veff[:, half * 512:(half + 1) * 512], i=bkv:
                  E.tensor_copy(o, i), [mm])
        ctx.bankuse[5] = ev
        veff_ck = ev

    b.dma("pool", msk, ctx.dram["c_l0am"], sem="q_gld", exclusive=False)
    msk_ck = b.ckpt("pool", "q_gld")
    ctx.ar.reset(mark_prep)

    xn_stride = (T + 4) * 2
    xn_off = ctx.ar.alloc(DT * xn_stride)
    ln = emit_ln(ctx, x_in, xn_off, xn_stride, tag=tag + "l", src_tok=[veff_ck])
    xn, xn_tok = ln["xn_tiles"], ln["xn_tok"]

    qlow = ctx.v(ctx.ar.alloc(T * 2), T, BF16)
    for c in range(C):
        bk = ctx.bank(c % 2, 512, F32)
        mm = None
        for k in range(DT):
            mm = b.op("pe", lambda E, o=bk, l=dkvT[:, k * 128:(k + 1) * 128],
                      r=xn[k][:, 1 + c * 512:1 + c * 512 + 512], k=k:
                      E.matmul(o, l, r, start=(k == 0), stop=(k == DT - 1)),
                      [tr_ev["dkv"], xn_tok[k][c], ctx.bankuse.get(c % 2)])
        ev = b.op("dve", lambda E, o=ckvT[:, c * 512:(c + 1) * 512], i=bk:
                  E.tensor_copy(o, i), [mm])
        ctx.bankuse[c % 2] = ev
        ckvT_tok = ev
    for c in range(C):
        bk = ctx.bank(2 + c % 2, 512, F32)
        mm = None
        for k in range(DT):
            mm = b.op("pe", lambda E, o=bk, l=dqT[:, k * 128:(k + 1) * 128],
                      r=xn[k][:, 1 + c * 512:1 + c * 512 + 512], k=k:
                      E.matmul(o, l, r, start=(k == 0), stop=(k == DT - 1)),
                      [tr_ev["dq"], xn_tok[k][c], ctx.bankuse.get(2 + c % 2)])
        ev = b.op("dve", lambda E, o=qlow[:, c * 512:(c + 1) * 512], i=bk:
                  E.tensor_copy(o, i), [mm])
        ctx.bankuse[2 + c % 2] = ev
        qlow_tok = ev
    ckv_tok = None
    for g in range((TT_ + 7) // 8):
        tb = ctx.bank(6 + g % 2, 1024, BF16)
        nin = min(8, TT_ - g * 8)
        tt = None
        for ki in range(nin):
            jt = g * 8 + ki
            tt = b.op("pe", lambda E, o=tb[:, ki * 128:(ki + 1) * 128],
                      i=ckvT[:, jt * 128:(jt + 1) * 128]:
                      E.transpose(o, i, ctx.id_bf),
                      [ckvT_tok, ctx.id_tok, ctx.bankuse.get(6 + g % 2)])
        ckv_tok = b.op("dve", lambda E, o=ckv_sb[:, g * 1024:g * 1024 + nin * 128],
                       i=tb[:, 0:nin * 128]: E.tensor_copy(o, i), [tt])
        ctx.bankuse[6 + g % 2] = ckv_tok
    qup_tok = [[None] * C for _ in range(DT)]
    for j in range(DT):
        for c in range(C):
            bk = ctx.bank((j * C + c) % 2, 512, F32)
            mm = b.op("pe", lambda E, o=bk, l=uqT[:, j * 128:(j + 1) * 128],
                      r=qlow[:, c * 512:(c + 1) * 512]:
                      E.matmul(o, l, r, start=True, stop=True),
                      [uqT_ev, qlow_tok, ctx.bankuse.get((j * C + c) % 2)])
            ev = b.op("act" if (j + c) % 2 else "dve",
                      lambda E, o=qup_sb[j][:, c * 512:(c + 1) * 512], i=bk:
                      (E.copy(o, i) if hasattr(E, "copy") else E.tensor_copy(o, i)),
                      [mm])
            qup_tok[j][c] = ev
            ctx.bankuse[(j * C + c) % 2] = ev

    ql_off = ctx.ar.alloc(T * 2)
    P_off = ctx.ar.alloc(2 * T * 2)
    Pf_off = ctx.ar.alloc(T * 4)
    PT_off = ctx.ar.alloc(2 * TT_ * 128 * 2)
    cx_off = ctx.ar.alloc(2 * 128 * 2)
    sm_off = ctx.ar.alloc(2 * 192)
    ql_sb = ctx.v(ql_off, T, BF16)
    P_bf2 = [ctx.v(P_off + i * T * 2, T, BF16) for i in range(2)]
    P_f2 = [ctx.v(Pf_off, T, F32), ctx.v(Pf_off, T, F32)]
    PT_sb2 = [ctx.v(PT_off + i * TT_ * 128 * 2, TT_ * 128, BF16) for i in range(2)]
    cx_sb2 = [ctx.v(cx_off + i * 128 * 2, 128, BF16) for i in range(2)]
    negmax2 = [ctx.v(sm_off + i * 192, 1, F32) for i in range(2)]
    sums2 = [ctx.v(sm_off + i * 192 + 64, 1, F32) for i in range(2)]
    recip2 = [ctx.v(sm_off + i * 192 + 128, 1, F32) for i in range(2)]
    tr_tok = [None, None]
    sc_prev = [None, None]
    mx_prev = [None, None]
    cx_war = [None, None]
    qq = 0
    ql_war = None
    yev = None
    for h in range(16):
        jt2, pb2 = h // 2, 64 * (h % 2)
        ql_toks = []
        for c in range(C):
            bki = c % 2
            bk = ctx.bank(bki, 512, F32)
            mm = b.op("pe", lambda E, o=bk, l=keff[pb2:pb2 + 64, jt2 * 128:(jt2 + 1) * 128],
                      r=qup_sb[jt2][pb2:pb2 + 64, c * 512:(c + 1) * 512]:
                      E.matmul(o, l, r, start=True, stop=True),
                      [keff_ev, qup_tok[jt2][c], ctx.bankuse.get(bki), ql_war])
            ev = b.op("dve", lambda E, o=ql_sb[:, c * 512:(c + 1) * 512], i=bk:
                      E.tensor_copy(o, i), [mm])
            ctx.bankuse[bki] = ev
            ql_toks.append(ev)
        for qt in range(TT_):
            sl2 = qq % 2
            qq += 1
            P_bf, P_f, PT_sb, cx_sb = P_bf2[sl2], P_f2[sl2], PT_sb2[sl2], cx_sb2[sl2]
            negmax, sums, recip = negmax2[sl2], sums2[sl2], recip2[sl2]
            nch = qt // 4 + 1
            ncols = nch * 512
            nt = qt + 1
            smm = None
            for c in range(nch):
                bk = ctx.bank(c, 512, F32)
                w = [ql_toks[(qt * 128) // 512], ckvT_tok, ctx.bankuse.get(c)]
                smm = b.op("pe", lambda E, o=bk, l=ql_sb[:, qt * 128:(qt + 1) * 128],
                           r=ckvT[:, c * 512:(c + 1) * 512], sp=(c != nch - 1):
                           E.matmul(o, l, r, start=True, stop=sp), w)
                if c == nch - 1:
                    smm = b.op("pe", lambda E, o=bk, l=ctx.id_bf,
                               r=msk[:, (qt % 4) * 512:(qt % 4 + 1) * 512]:
                               E.matmul(o, l, r, start=False, stop=True),
                               [smm, msk_ck, ctx.id_tok])
            spsum = ctx.ps[:, 0:ncols * 4].bitcast(F32)
            mx = b.op("dve", lambda E, o=negmax, i=spsum: E.reduce_max(
                o, i, axis=AX.X, negate=True), [smm, mx_prev[sl2]])
            et = b.op("act", lambda E, o=P_f[:, 0:ncols], i=spsum, bb=negmax, s=sums:
                      E.activation(o, i, AF.Exp, bias=bb, accum_out=s),
                      [mx, sc_prev[0], sc_prev[1]])
            for c in range(nch):
                ctx.bankuse[c] = et
            rc = b.op("dve", lambda E, o=recip, i=sums: E.reciprocal(o, i), [et])
            sc_tok = b.op("dve", lambda E, o=P_bf[:, 0:ncols], i=P_f[:, 0:ncols],
                          r=recip: E.tensor_scalar(o, i, r, None, op0=ALU.mult),
                          [rc, tr_tok[sl2]])
            sc_prev[sl2] = sc_tok
            mx_prev[sl2] = rc
            g_ev = None
            for g in range((nt + 7) // 8):
                tb_i = 6 + (g % 2)
                tb = ctx.bank(tb_i, 1024, BF16)
                nin = min(8, nt - g * 8)
                ttok = None
                for ki in range(nin):
                    jt = g * 8 + ki
                    ttok = b.op("pe", lambda E, o=tb[:, ki * 128:(ki + 1) * 128],
                                i=P_bf[:, jt * 128:(jt + 1) * 128]:
                                E.transpose(o, i, ctx.id_bf),
                                [sc_tok, ctx.id_tok, ctx.bankuse.get(tb_i)])
                g_ev = b.op("dve", lambda E, o=PT_sb[:, g * 1024:g * 1024 + nin * 128],
                            i=tb[:, 0:nin * 128]: E.tensor_copy(o, i), [ttok])
                ctx.bankuse[tb_i] = g_ev
                tr_tok[sl2] = ttok
            cmm = None
            for jt in range(nt):
                bk = ctx.bank(4, 128, F32)
                w = [g_ev, ckv_tok, ctx.bankuse.get(4)]
                cmm = b.op("pe", lambda E, o=bk, l=ckv_sb[:, jt * 128:(jt + 1) * 128],
                           r=PT_sb[:, jt * 128:(jt + 1) * 128], st=(jt == 0),
                           sp=(jt == nt - 1): E.matmul(o, l, r, start=st, stop=sp), w)
            cx_ev = b.op("dve", lambda E, o=cx_sb, i=ctx.bank(4, 128, F32):
                         E.tensor_copy(o, i), [cmm, cx_war[sl2]])
            ctx.bankuse[4] = cx_ev
            ymm = b.op("pe", lambda E, o=ctx.bank(5, 128, F32, pn=64),
                       l=veff[:, 64 * h:64 * h + 64], r=cx_sb:
                       E.matmul(o, l, r, start=True, stop=True),
                       [cx_ev, veff_ck, ctx.bankuse.get(5)])
            yev = b.op("act", lambda E, o=y_sb[jt2][pb2:pb2 + 64, qt * 128:(qt + 1) * 128],
                       i=ctx.bank(5, 128, F32, pn=64): E.copy(o, i), [ymm])
            ctx.bankuse[5] = yev
            cx_war[sl2] = ymm
        ql_war = smm

    y_done = yev
    ctx.ar.reset(mark1)

    xi_off = ctx.ar.alloc(4 * 512 * 4)
    xo2_off = ctx.ar.alloc(4 * 512 * 4)
    xi_war, xo_war = {}, {}
    out_done = []
    for j in range(DT):
        for c in range(C):
            i4 = (j * C + c) % 4
            xi = ctx.v(xi_off + i4 * 2048, 512, F32)
            xo = ctx.v(xo2_off + i4 * 2048, 512, F32)
            ld = b.dma("sp", xi, x_in[j * P:(j + 1) * P, c * 512:c * 512 + 512],
                       waits=[xi_war.get(i4) or y_done], sem=f"q_sxi{i4}")
            t = b.op("dve", lambda E, o=xo, y=y_sb[j][:, c * 512:(c + 1) * 512], x=xi:
                     E.tensor_tensor(o, y, x, op=ALU.add),
                     [y_done, ld, xo_war.get(i4)])
            xi_war[i4] = t
            d = b.dma("sp", x_out[j * P:(j + 1) * P, c * 512:c * 512 + 512], xo,
                      waits=[t], sem=f"q_sxo{i4}")
            xo_war[i4] = d
            out_done.append(d)
    ctx.ar.reset(mark0)
    return out_done[-4:]


def emit_l0ffn(ctx, x_in, x_out, W, xA, xB, tag="s"):
    b, T, C = ctx.b, ctx.T, ctx.C
    NW = T // 256
    NV = T // 128 + 2
    mark_call = ctx.ar.mark()

    msk = ctx.v(ctx.ar.alloc(4 * 512 * 2), 4 * 512, BF16)
    b.dma("pool", msk, ctx.dram["c_l0fm"], sem="q_gld", exclusive=False)
    blk2 = ctx.v(ctx.ar.alloc(2 * 2), 2, BF16)
    b.dma("pool", blk2, ctx.dram["c_blk2"], sem="q_gld", exclusive=False)
    sel2 = ctx.v(ctx.ar.alloc(256), 128, BF16, pn=2)
    b.dma("pool", sel2, ctx.dram["c_sel2"], sem="q_gld", exclusive=False)
    cst_ck = b.ckpt("pool", "q_gld")
    ctx.l0 = dict(msk=msk, blk2=blk2, sel2=sel2, cst_ck=cst_ck)

    xn_stride = (T + 4) * 2
    mk = ctx.ar.mark()
    xn_off0 = ctx.ar.alloc(DT * xn_stride)
    r0 = emit_ln(ctx, x_in, xn_off0, xn_stride, tag=tag + "0", fout_dram=xA)
    src_ck = r0["store_ck"]
    ctx.ar.reset(mk)

    cur, nxt = xA, xB
    for dep in range(6):
        src_ck = _l0_attn_sub(ctx, W, dep, cur, nxt, src_ck, tag)
        cur, nxt = nxt, cur
        src_ck = _l0_ffn_sub(ctx, W, dep, cur, nxt, src_ck, tag)
        cur, nxt = nxt, cur

    xi_off = ctx.ar.alloc(4 * 512 * 4)
    xf_off = ctx.ar.alloc(4 * 512 * 4)
    xo_off = ctx.ar.alloc(4 * 512 * 4)
    xi_war, xf_war, xo_war = {}, {}, {}
    for j in range(DT):
        for c in range(C):
            i4 = (j * C + c) % 4
            xi = ctx.v(xi_off + i4 * 2048, 512, F32)
            xf = ctx.v(xf_off + i4 * 2048, 512, F32)
            xo = ctx.v(xo_off + i4 * 2048, 512, F32)
            l1 = b.dma("sp", xi, x_in[j * P:(j + 1) * P, c * 512:c * 512 + 512],
                       waits=[xi_war.get(i4) or src_ck], sem=f"q_sxi{i4}")
            l2 = b.dma("sp", xf, cur[j * P:(j + 1) * P, c * 512:c * 512 + 512],
                       waits=[xf_war.get(i4) or src_ck], sem=f"q_sxf{i4 % 2}")
            t = b.op("dve", lambda E, o=xo, a=xf, x=xi: E.scalar_tensor_tensor(
                o, a, 0.5, x, op0=ALU.mult, op1=ALU.add),
                [l1, l2, xo_war.get(i4)])
            xi_war[i4] = t
            xf_war[i4] = t
            d = b.dma("sp", x_out[j * P:(j + 1) * P, c * 512:c * 512 + 512], xo,
                      waits=[t], sem=f"q_sxo{i4}")
            xo_war[i4] = d
    done = b.nopop("sp", [(f"q_sxo{i}", b.cnt.get(f"q_sxo{i}", 0)) for i in range(4)])
    ctx.ar.reset(mark_call)
    return done


def _l0_attn_sub(ctx, W, dep, cur, nxt, src_ck, tag):
    b, T, C = ctx.b, ctx.T, ctx.C
    NW = T // 256
    NV = T // 128 + 2
    mark = ctx.ar.mark()
    qkv_d = W["qkv"][dep]
    wo_d = W["wo"][dep]
    msk, blk2, sel2, cst_ck = (ctx.l0[k] for k in ("msk", "blk2", "sel2", "cst_ck"))

    xb_stride = (T + 4) * 2
    xb_off = ctx.ar.alloc(DT * xb_stride)
    xb = [ctx.v(xb_off + k * xb_stride, T + 1, BF16) for k in range(DT)]
    xb_tok = []
    for k in range(DT):
        z = b.op("pool", lambda E, t=xb[k]: E.memset(t[:, 0:1], 0.0), [src_ck])
        tk = b.dma("pool", xb[k][:, 1:T + 1], cur[k * P:(k + 1) * P, :],
                   sem=f"q_gx{k}", waits=[src_ck, z])
        xb_tok.append(tk)

    def xs(k, c0, n):
        off = 0 if k >= DT // 2 else 1
        return xb[k][:, off + c0: off + c0 + n]

    qs_col = ctx.v(ctx.ar.alloc(64), 1, F32)
    qss_col = ctx.v(ctx.ar.alloc(64), 1, F32)
    ks_col = ctx.v(ctx.ar.alloc(64), 1, F32)
    kss_col = ctx.v(ctx.ar.alloc(64), 1, F32)
    for dst, srcv, sw in ((qs_col, W["qs"][dep], 0), (qss_col, W["qs"][dep], 1),
                          (ks_col, W["ks"][dep], 0), (kss_col, W["ks"][dep], 1)):
        for half in range(2):
            p0 = half * 64
            if sw:
                b.dma("sp", dst[p0:p0 + 32, :], col1(srcv[32:64]), sem="q_sld", exclusive=False)
                b.dma("sp", dst[p0 + 32:p0 + 64, :], col1(srcv[0:32]), sem="q_sld", exclusive=False)
            else:
                b.dma("sp", dst[p0:p0 + 64, :], col1(srcv[0:64]), sem="q_sld", exclusive=False)
    scol_ck = b.ckpt("sp", "q_sld")
    csq = ctx.v(ctx.ar.alloc(T * 2), T, BF16)
    ssq = ctx.v(ctx.ar.alloc(T * 2), T, BF16)
    csk = ctx.v(ctx.ar.alloc(T * 2), T, BF16)
    ssk = ctx.v(ctx.ar.alloc(T * 2), T, BF16)
    mk_rot = ctx.ar.mark()
    rotq_raw = ctx.v(ctx.ar.alloc(2 * T * 4), 2 * T, F32)
    rotk_raw = ctx.v(ctx.ar.alloc(2 * T * 4), 2 * T, F32)
    b.dma("pool", rotq_raw, ctx.dram["c_rotq"], sem="q_gld", exclusive=False,
          waits=[src_ck])
    b.dma("pool", rotk_raw, ctx.dram["c_rotk"], sem="q_gld", exclusive=False,
          waits=[src_ck])
    rot_ck = b.ckpt("pool", "q_gld")
    tab_tok = []
    for dst, raw, c0, scal, s2 in ((csq, rotq_raw, 0, qs_col, 8.0),
                                   (ssq, rotq_raw, T, qss_col, 8.0),
                                   (csk, rotk_raw, 0, ks_col, 1.0),
                                   (ssk, rotk_raw, T, kss_col, 1.0)):
        tab_tok.append(b.op("dve", lambda E, o=dst, i=raw[:, c0:c0 + T], s=scal, s2=s2:
                            E.tensor_scalar(o, i, s, s2, op0=ALU.mult, op1=ALU.mult),
                            [rot_ck, scol_ck]))
    tab_ck = tab_tok[-1]
    ctx.ar.reset(mk_rot)

    v2_off = ctx.ar.alloc(NV * 1024 * 2)
    v2 = ctx.v(v2_off, NV * 1024, BF16)
    vz = b.op("pool", lambda E, o=v2[:, 0:2048]: E.memset(o, 0.0), [src_ck, tab_ck])
    vs_off = ctx.ar.alloc(DT * 512 * 2)
    v_tok = None
    vload_war = None
    for vc in range(2):
        vs = ctx.v(vs_off, DT * 512, BF16)
        wt = b.dma("pool", vs, wslice_ap(qkv_d, 3072, 2048 + vc * 512, 512),
                   waits=[vload_war or src_ck, tab_ck], sem="q_gw0")
        mm = None
        for tt in range(T // 128):
            bi = (tt + vc) % 2
            bk = ctx.bank(bi, 512, F32)
            for k in range(DT):
                w = [wt, xb_tok[k], ctx.bankuse.get(bi)]
                mm = b.op("pe", lambda E, o=bk, l=xs(k, tt * 128, 128),
                          r=vs[:, k * 512:(k + 1) * 512], k=k:
                          E.matmul(o, l, r, start=(k == 0), stop=(k == DT - 1)), w)
            ev = b.op("act" if tt % 2 else "dve",
                      lambda E, o=v2[:, (tt + 2) * 1024 + vc * 512:(tt + 2) * 1024 + vc * 512 + 512],
                      i=bk: (E.copy(o, i) if hasattr(E, "copy") else E.tensor_copy(o, i)),
                      [mm, vz, tab_ck])
            v_tok = ev
            ctx.bankuse[bi] = ev
        vload_war = mm

    oT_off = ctx.ar.alloc(DT * T * 2)
    oT = [ctx.v(oT_off + j * T * 2, T, BF16) for j in range(DT)]
    mark_heads = ctx.ar.mark()
    qraw = ctx.v(ctx.ar.alloc(T * 2), T, BF16)
    kraw = ctx.v(ctx.ar.alloc(T * 2), T, BF16)
    swp = ctx.v(ctx.ar.alloc(T * 2), T, BF16)
    t1b = ctx.v(ctx.ar.alloc(T * 2), T, BF16)
    sqb = ctx.v(ctx.ar.alloc(T * 2), T, BF16)
    qfin = ctx.v(ctx.ar.alloc(T * 2), T, BF16)
    kfin = ctx.v(ctx.ar.alloc((T + 256) * 2), T + 256, BF16)
    kz0 = b.op("pool", lambda E, o=kfin[:, 0:256]: E.memset(o, 0.0), [src_ck])
    nrow = ctx.v(ctx.ar.alloc(T * 4), T, F32, pn=2)
    nrow_bf = ctx.v(ctx.ar.alloc(T * 2), T, BF16, pn=2)
    Pf_off = ctx.ar.alloc(2 * 512 * 4)
    P2_off = ctx.ar.alloc(2 * 1024 * 2)
    PTo_off = ctx.ar.alloc(2 * 1024 * 2)
    sm_off = ctx.ar.alloc(2 * 192)
    P_f2 = [ctx.v(Pf_off + i * 2048, 512, F32) for i in range(2)]
    P22 = [ctx.v(P2_off + i * 2048, 1024, BF16) for i in range(2)]
    PT2 = [ctx.v(PTo_off + i * 2048, 1024, BF16) for i in range(2)]
    negmax2 = [ctx.v(sm_off + i * 192, 1, F32) for i in range(2)]
    sums2 = [ctx.v(sm_off + i * 192 + 64, 1, F32) for i in range(2)]
    recip2 = [ctx.v(sm_off + i * 192 + 128, 1, F32) for i in range(2)]
    qg_off = ctx.ar.alloc(DT * 512 * 2)
    kg_off = ctx.ar.alloc(DT * 512 * 2)
    qgv = ctx.v(qg_off, DT * 512, BF16)
    kgv = ctx.v(kg_off, DT * 512, BF16)
    grp_war = {"q": src_ck, "k": src_ck}
    grp_tok = {}

    ws_war = [None, None]
    fin_war = {"q": None, "k": None}
    oev = None
    et_war = [None, None]
    trl = [None, None]
    ptev_war = [None, None]
    hw = 0
    for hp in range(DT):
        sl = hp % 2
        if hp % 4 == 0:
            grp = hp // 4
            grp_tok["q"] = b.dma("pool", qgv, wslice_ap(qkv_d, 3072, grp * 512, 512),
                                 waits=[grp_war["q"]], sem="q_gw0")
            grp_tok["k"] = b.dma("pool", kgv, wslice_ap(qkv_d, 3072, 1024 + grp * 512, 512),
                                 waits=[grp_war["k"]], sem="q_gw1")
        po = (hp % 4) * 128
        for side, raw, fin, cst, sst, sc_col in (
                ("q", qraw, qfin, csq, ssq, None), ("k", kraw, kfin, csk, ssk, None)):
            gv = qgv if side == "q" else kgv
            wtok = grp_tok[side]
            ev_toks = []
            for c in range(C):
                bi = 2 + c % 2
                bk = ctx.bank(bi, 512, F32)
                mm = None
                for k in range(DT):
                    w = [wtok, xb_tok[k], ctx.bankuse.get(bi)]
                    mm = b.op("pe", lambda E, o=bk, l=gv[:, k * 512 + po:k * 512 + po + 128],
                              r=xs(k, c * 512, 512), k=k:
                              E.matmul(o, l, r, start=(k == 0), stop=(k == DT - 1)), w)
                ev = b.op("dve", lambda E, o=raw[:, c * 512:(c + 1) * 512], i=bk:
                          E.tensor_copy(o, i), [mm, fin_war[side]])
                ev_toks.append(ev)
                ctx.bankuse[bi] = ev
            raw_ck = ev_toks[-1]
            sq_t = b.op("dve", lambda E, o=sqb, i=raw[:, 0:T]: E.tensor_tensor(
                o, i, i, op=ALU.mult), [raw_ck])
            nr_toks = []
            for c in range(C):
                bk4 = ctx.bank(4, 512, F32, pn=2)
                mm = b.op("pe", lambda E, o=bk4, l=blk2, r=sqb[:, c * 512:(c + 1) * 512]:
                          E.matmul(o, l, r, start=True, stop=True),
                          [sq_t, cst_ck, ctx.bankuse.get(4)])
                ev = b.op("dve", lambda E, o=nrow[:, c * 512:(c + 1) * 512], i=bk4:
                          E.tensor_copy(o, i), [mm])
                nr_toks.append(ev)
                ctx.bankuse[4] = ev
            t_cl = b.op("dve", lambda E, o=nrow: E.tensor_scalar(
                o, o, 1e-24, None, op0=ALU.max), [nr_toks[-1]])
            t_ln = b.op("act", lambda E, o=nrow: E.activation(o, o, AF.Ln), [t_cl])
            t_ex = b.op("act", lambda E, o=nrow: E.activation(o, o, AF.Exp, scale=-0.5), [t_ln])
            t_bf = b.op("dve", lambda E, o=nrow_bf, i=nrow: E.tensor_copy(o, i), [t_ex])
            sw_t = []
            for g4 in range(4):
                p0 = g4 * 32
                p1 = (g4 * 32 + 32) % 64 + (64 if g4 >= 2 else 0)
                sw_t.append(b.op("act", lambda E, o=swp[p0:p0 + 32, :], i=raw[p1:p1 + 32, 0:T]:
                                 E.copy(o, i), [raw_ck, fin_war[side]]))
            tt1 = b.op("dve", lambda E, o=t1b, i=raw[:, 0:T], cc=cst: E.tensor_tensor(
                o, i, cc, op=ALU.mult), [raw_ck, tab_ck])
            tt2 = b.op("dve", lambda E, o=swp, i=swp, ss=sst: E.tensor_tensor(
                o, i, ss, op=ALU.mult), sw_t + [tab_ck])
            tt3 = b.op("dve", lambda E, o=t1b, a=t1b, bb=swp: E.tensor_tensor(
                o, a, bb, op=ALU.add), [tt1, tt2])
            fin0 = 0 if side == "q" else 256
            f_toks = []
            for c in range(C):
                bk5 = ctx.bank(5, 512, F32)
                mmb = b.op("pe", lambda E, o=bk5, l=sel2, r=nrow_bf[:, c * 512:(c + 1) * 512]:
                           E.matmul(o, l, r, start=True, stop=True),
                           [t_bf, cst_ck, ctx.bankuse.get(5)])
                fv = b.op("dve", lambda E, o=fin[:, fin0 + c * 512:fin0 + c * 512 + 512],
                          t=t1b[:, c * 512:(c + 1) * 512], m=bk5: E.tensor_tensor(
                    o, t, m, op=ALU.mult), [tt3, mmb, kz0 if side == "k" else None])
                f_toks.append(fv)
                ctx.bankuse[5] = fv
            fin_war[side] = f_toks[-1]
            if hp % 4 == 3:
                grp_war[side] = mm
        qfin_ck = fin_war["q"]
        kfin_ck = fin_war["k"]
        for hh in range(2):
            h = hp * 2 + hh
            pb = 64 * hh
            for w_ in range(NW):
                sl2 = hw % 2
                hw += 1
                P_f, P2, PT = P_f2[sl2], P22[sl2], PT2[sl2]
                negmax, sums, recip = negmax2[sl2], sums2[sl2], recip2[sl2]
                smm = {}
                for qsub in range(2):
                    bi = (w_ * 2 + qsub) % 4
                    bk = ctx.bank(bi, 512, F32)
                    mv = 2 * qsub + (2 if w_ == 0 else 0)
                    mv = qsub + (2 if w_ == 0 else 0)
                    m1 = b.op("pe", lambda E, o=bk, l=qfin[pb:pb + 64, (w_ * 256 + qsub * 128):(w_ * 256 + qsub * 128 + 128)],
                              r=kfin[pb:pb + 64, w_ * 256:w_ * 256 + 512]:
                              E.matmul(o, l, r, start=True, stop=False),
                              [qfin_ck, kfin_ck, ctx.bankuse.get(bi)])
                    m2 = b.op("pe", lambda E, o=bk, l=ctx.id_bf,
                              r=msk[:, mv * 512:(mv + 1) * 512]:
                              E.matmul(o, l, r, start=False, stop=True),
                              [m1, cst_ck, ctx.id_tok])
                    smm[qsub] = m2
                for qsub in range(2):
                    bi = (w_ * 2 + qsub) % 4
                    bk = ctx.bank(bi, 512, F32)
                    mx = b.op("dve", lambda E, o=negmax, i=bk: E.reduce_max(
                        o, i, axis=AX.X, negate=True), [smm[qsub]])
                    et = b.op("act", lambda E, o=P_f, i=bk, bb=negmax, s=sums:
                              E.activation(o, i, AF.Exp, bias=bb, accum_out=s),
                              [mx, et_war[sl2]])
                    ctx.bankuse[bi] = et
                    rc = b.op("dve", lambda E, o=recip, i=sums: E.reciprocal(o, i), [et])
                    sc = b.op("dve", lambda E, o=P2[:, qsub * 512:(qsub + 1) * 512],
                              i=P_f, r=recip: E.tensor_scalar(o, i, r, None, op0=ALU.mult),
                              [rc, trl[sl2]])
                    et_war[sl2] = sc
                    smm[qsub] = sc
                tb = ctx.bank(6 + w_ % 2, 1024, BF16)
                ttok = None
                for jt in range(4):
                    for qsub in range(2):
                        ttok = b.op("pe", lambda E, o=tb[:, (jt * 2 + qsub) * 128:(jt * 2 + qsub + 1) * 128],
                                    i=P2[:, qsub * 512 + jt * 128:qsub * 512 + (jt + 1) * 128]:
                                    E.transpose(o, i, ctx.id_bf),
                                    [smm[0], smm[1], ctx.id_tok, ctx.bankuse.get(6 + w_ % 2)])
                trl[sl2] = ttok
                ptev = b.op("dve", lambda E, o=PT, i=tb: E.tensor_copy(o, i),
                            [ttok, ptev_war[sl2]])
                ctx.bankuse[6 + w_ % 2] = ptev
                cmm = None
                for jt in range(4):
                    bk = ctx.bank(4, 256, F32, pn=64)
                    w = [ptev, v_tok, ctx.bankuse.get(4)]
                    cmm = b.op("pe", lambda E, o=bk,
                               l=v2[:, (2 * w_ + jt) * 1024 + 64 * h:(2 * w_ + jt) * 1024 + 64 * h + 64],
                               r=PT[:, jt * 256:(jt + 1) * 256], st=(jt == 0), sp=(jt == 3):
                               E.matmul(o, l, r, start=st, stop=sp), w)
                oev = b.op("act", lambda E, o=oT[hp][pb:pb + 64, w_ * 256:(w_ + 1) * 256],
                           i=ctx.bank(4, 256, F32, pn=64): E.copy(o, i), [cmm])
                ctx.bankuse[4] = oev
                ptev_war[sl2] = cmm

    o_done = oev
    ctx.ar.reset(mark_heads)
    ws2_off = ctx.ar.alloc(2 * DT * 128 * 2)
    xi_off = ctx.ar.alloc(4 * 512 * 4)
    xo_off = ctx.ar.alloc(4 * 512 * 4)
    xi_war, xo_war = {}, {}
    ws2_war = [None, None]
    for j in range(DT):
        sl = j % 2
        ws = ctx.v(ws2_off + sl * DT * 128 * 2, DT * 128, BF16)
        wt = b.dma("pool", ws, wslice_ap(wo_d, 1024, j * 128, 128),
                   waits=[ws2_war[sl] or o_done], sem=f"q_gw{sl}")
        for c in range(C):
            bi = (j * C + c) % 4
            bk = ctx.bank(bi, 512, F32)
            mm = None
            for k in range(DT):
                w = [wt, o_done, ctx.bankuse.get(bi)]
                mm = b.op("pe", lambda E, o=bk, l=ws[:, k * 128:(k + 1) * 128],
                          r=oT[k][:, c * 512:(c + 1) * 512], k=k:
                          E.matmul(o, l, r, start=(k == 0), stop=(k == DT - 1)), w)
            i4 = (j * C + c) % 4
            xi = ctx.v(xi_off + i4 * 2048, 512, F32)
            xo = ctx.v(xo_off + i4 * 2048, 512, F32)
            ld = b.dma("sp", xi, cur[j * P:(j + 1) * P, c * 512:c * 512 + 512],
                       waits=[xi_war.get(i4) or o_done], sem=f"q_sxi{i4}")
            t = b.op("dve", lambda E, o=xo, i=bk, x=xi: E.tensor_tensor(
                o, i, x, op=ALU.add), [mm, ld, xo_war.get(i4)])
            xi_war[i4] = t
            ctx.bankuse[bi] = t
            d = b.dma("sp", nxt[j * P:(j + 1) * P, c * 512:c * 512 + 512], xo,
                      waits=[t], sem=f"q_sxo{i4}")
            xo_war[i4] = d
        ws2_war[sl] = mm
    ck = b.nopop("sp", [(f"q_sxo{i}", b.cnt.get(f"q_sxo{i}", 0)) for i in range(4)])
    ctx.ar.reset(mark)
    return ck


def _l0_ffn_sub(ctx, W, dep, cur, nxt, src_ck, tag):
    b, T, C = ctx.b, ctx.T, ctx.C
    mark = ctx.ar.mark()
    w1_d, w2_d = W["w1"][dep], W["w2"][dep]
    NP = 22

    Pst_off = ctx.ar.alloc(NP * T * 2)
    Pst = [ctx.v(Pst_off + j * T * 2, T, BF16) for j in range(NP)]

    b1a, b1g, b2c = [], [], []
    for j in range(NP):
        na = 128 if j < NP - 1 else 42
        va = ctx.v(ctx.ar.alloc(64), 1, F32, pn=na)
        vg = ctx.v(ctx.ar.alloc(64), 1, F32, pn=na)
        b.dma("sp", va, col1(W["b1"][dep][j * 128:j * 128 + na]), sem="q_sld", exclusive=False)
        b.dma("sp", vg, col1(W["b1"][dep][2730 + j * 128:2730 + j * 128 + na]), sem="q_sld", exclusive=False)
        b1a.append(va)
        b1g.append(vg)
    for j in range(DT):
        v = ctx.v(ctx.ar.alloc(64), 1, F32)
        b.dma("sp", v, col1(W["b2"][dep][j * 128:(j + 1) * 128]), sem="q_sld", exclusive=False)
        b2c.append(v)
    col_ck = b.ckpt("sp", "q_sld")

    mk_ln = ctx.ar.mark()
    xn_stride = (T + 4) * 2
    xn_off = ctx.ar.alloc(DT * xn_stride)
    ln = emit_ln(ctx, cur, xn_off, xn_stride, g_dram=W["ln_g"][dep],
                 bg_dram=W["ln_b"][dep], shift=True, tag=tag + f"n{dep}",
                 src_tok=[src_ck])
    xn, xn_tok = ln["xn_tiles"], ln["xn_tok"]

    gbuf = ctx.v(ctx.ar.alloc(T * 2), T, BF16)
    wbig_off = ctx.ar.alloc(DT * 2730 * 2)
    wbig = ctx.v(wbig_off, DT * 2730, BF16)
    ln_done = ln["xn_tok"][DT - 1][ctx.C - 1]
    half_war = ln_done
    gw_tok = None
    gel_toks = []
    for half in (0, 1):
        wt = b.dma("pool", wbig, wslice_ap(w1_d, 5460, half * 2730, 2730),
                   waits=[half_war], sem="q_gw0")
        for j in range(NP):
            na = 128 if j < NP - 1 else 42
            bias_c = b1a[j] if half == 0 else b1g[j]
            for c in range(C):
                bi = 2 + (j + c) % 2
                bk = ctx.bank(bi, 512, F32, pn=na)
                mm = None
                for k in range(DT):
                    w = [wt, xn_tok[k][c], ctx.bankuse.get(bi)]
                    mm = b.op("pe", lambda E, o=bk,
                              l=wbig[:, k * 2730 + j * 128:k * 2730 + j * 128 + na],
                              r=xn[k][:, 1 + c * 512:1 + c * 512 + 512], k=k:
                              E.matmul(o, l, r, start=(k == 0), stop=(k == DT - 1)), w)
                if half == 0:
                    ev = b.op("dve", lambda E, o=Pst[j][0:na, c * 512:(c + 1) * 512],
                              i=bk, bc=bias_c: E.tensor_scalar(
                        o, i, bc, None, op0=ALU.add), [mm, col_ck])
                else:
                    ev = b.op("act", lambda E, o=gbuf[0:na, c * 512:(c + 1) * 512],
                              i=bk, bc=bias_c: E.activation(o, i, AF.Gelu, bias=bc),
                              [mm, col_ck, gw_tok])
                ctx.bankuse[bi] = ev
            if half == 1:
                gw_tok = b.op("dve", lambda E, o=Pst[j][0:na, 0:T], a=Pst[j][0:na, 0:T],
                              g=gbuf[0:na, 0:T]: E.tensor_tensor(o, a, g, op=ALU.mult),
                              [ev])
                gel_toks.append(gw_tok)
        half_war = mm
    P_ck = gw_tok

    ctx.ar.reset(mk_ln)
    w2_off = ctx.ar.alloc(NP * 1024 * 2)
    w2t = [ctx.v(w2_off + k * 1024 * 2, 1024, BF16, pn=(128 if k < NP - 1 else 42))
           for k in range(NP)]
    w2_tok = []
    for k in range(NP):
        na = 128 if k < NP - 1 else 42
        w2_tok.append(b.dma("pool", w2t[k], w2_d[k * 128:k * 128 + na, :],
                            waits=[P_ck if k < 2 else w2_tok[k - 2]], sem=f"q_gw{k % 2}"))
    xi_off = ctx.ar.alloc(4 * 512 * 4)
    xo_off = ctx.ar.alloc(4 * 512 * 4)
    xi_war, xo_war = {}, {}
    for j in range(DT):
        for c in range(C):
            bi = (j * C + c) % 2
            bk = ctx.bank(bi, 512, F32)
            mm = None
            for k in range(NP):
                na = 128 if k < NP - 1 else 42
                w = [w2_tok[k], P_ck, ctx.bankuse.get(bi)]
                mm = b.op("pe", lambda E, o=bk, l=w2t[k][:, j * 128:(j + 1) * 128],
                          r=Pst[k][0:na, c * 512:(c + 1) * 512], k=k:
                          E.matmul(o, l, r, start=(k == 0), stop=(k == NP - 1)), w)
            i4 = (j * C + c) % 4
            xi = ctx.v(xi_off + i4 * 2048, 512, F32)
            xo = ctx.v(xo_off + i4 * 2048, 512, F32)
            ld = b.dma("sp", xi, cur[j * P:(j + 1) * P, c * 512:c * 512 + 512],
                       waits=[xi_war.get(i4) or P_ck], sem=f"q_sxi{i4}")
            t = b.op("dve", lambda E, o=xo, i=bk, bc=b2c[j], x=xi:
                     E.scalar_tensor_tensor(o, i, bc, x, op0=ALU.add, op1=ALU.add),
                     [mm, ld, col_ck, xo_war.get(i4)])
            xi_war[i4] = t
            ctx.bankuse[bi] = t
            d = b.dma("sp", nxt[j * P:(j + 1) * P, c * 512:c * 512 + 512], xo,
                      waits=[t], sem=f"q_sxo{i4}")
            xo_war[i4] = d
    ck = b.nopop("sp", [(f"q_sxo{i}", b.cnt.get(f"q_sxo{i}", 0)) for i in range(4)])
    ctx.ar.reset(mark)
    return ck


def phase_barrier(b):
    sp_d = [(s, v) for s, v in b.cnt.items() if s.startswith("q_s") and v > 0]
    pl_d = [(s, v) for s, v in b.cnt.items() if (s.startswith("q_g") or s == "q_pre") and v > 0]
    t1 = b.nopop("sp", sp_d)
    t2 = b.nopop("pool", pl_d)
    base = [t1, t2]
    for e in ENGINES:
        if b.cnt.get(e, 0) > 0:
            base.append((e, b.cnt[e]))
    b.base = []
    bb = []
    for t in base:
        bb.append(t)
    b.base = bb
    return bb


def build_module(T):
    nc = bass.Bass("TRN2", target_bir_lowering=False)
    b = Bld(nc)
    ctx = Ctx(nc, b, T)
    D_ = D

    def di(name, shape):
        ap = nc.dram_tensor(name, list(shape), F32, kind="ExternalInput").ap()
        ctx.dram[name] = ap
        return ap

    x = di("x_fm", (D_, T))
    W0a = {
        "dq": di("L0_attn_W_dq", (128, D_)),
        "uq": di("L0_attn_W_uq", (D_, 128)),
        "dkv": di("L0_attn_W_dkv", (128, D_)),
        "uk": di("L0_attn_W_uk", (D_, 128)),
        "uv": di("L0_attn_W_uv", (D_, 128)),
        "o": di("L0_attn_W_o", (D_, D_)),
    }
    W0f = {
        "qkv": di("L0_ffn_Wqkv", (6, D_, 3072)),
        "qs": di("L0_ffn_q_scale", (6, 64)),
        "ks": di("L0_ffn_k_scale", (6, 64)),
        "wo": di("L0_ffn_Wo", (6, 1024, D_)),
        "ln_g": di("L0_ffn_ln_g", (6, D_)),
        "ln_b": di("L0_ffn_ln_b", (6, D_)),
        "w1": di("L0_ffn_W1", (6, D_, 5460)),
        "b1": di("L0_ffn_b1", (6, 5460)),
        "w2": di("L0_ffn_W2", (6, 2730, D_)),
        "b2": di("L0_ffn_b2", (6, D_)),
    }
    W1a_qkv = di("L1_attn_W_qkv", (D_, 3 * D_))
    W1a_out = di("L1_attn_W_out", (D_, D_))
    W1a_rel = di("L1_attn_rel_emb", (32, 16))
    W1f = {}
    for n, shp in [("w1_w", (4096, D_)), ("w2_w", (4096, D_)), ("h1_w", (4096, 4096)),
                   ("h2_w", (4096, 4096)), ("w3_w", (D_, 4096)),
                   ("w1_s", (4096,)), ("w2_s", (4096,)), ("h1_s", (4096,)),
                   ("h2_s", (4096,)), ("w3_s", (D_,))]:
        W1f[n] = di("L1_ffn_" + n, shp)
    for n, shp in [("c_eye", (P, P)), ("c_l0fm", (P, 4 * 512)), ("c_l0am", (P, 4 * 512)),
                   ("c_rotq", (P, 2 * T)), ("c_rotk", (P, 2 * T)),
                   ("c_blk2", (P, 2)), ("c_sel2", (2, P))]:
        di(n, shp)
    oht = nc.dram_tensor("c_t5oh", [32, 2 * T - 1], F32, kind="ExternalInput").ap()
    ctx.dram["c_t5oh"] = oht
    idxt = nc.dram_tensor("c_t5idx", [P, 16], I32, kind="ExternalInput").ap()
    ctx.dram["c_t5idx"] = idxt

    xo = nc.dram_tensor("xo_fm", [D_, T], F32, kind="ExternalOutput").ap()
    ya = nc.dram_tensor("ya", [D_, T], F32, kind="Internal").ap()
    yb = nc.dram_tensor("yb", [D_, T], F32, kind="Internal").ap()
    xA = nc.dram_tensor("xA", [D_, T], F32, kind="Internal").ap()
    xB = nc.dram_tensor("xB", [D_, T], F32, kind="Internal").ap()
    scr_a = nc.dram_tensor("scr_a", [4096, T], BF16, kind="Internal").ap()
    scr_b = nc.dram_tensor("scr_b", [4096, T], BF16, kind="Internal").ap()
    bias_hbm = nc.dram_tensor("bias_hbm", [16 * (2 * T - 1)], BF16, kind="Internal").ap()

    from contextlib import ExitStack
    es = ExitStack()
    ctx.sb = es.enter_context(nc.sbuf_tensor("SB", [P, SBUF_BYTES], U8))
    ctx.ps = es.enter_context(nc.psum_tensor("PS", [P, 16384], U8))
    emit_prelude(ctx)
    emit_l0ffn(ctx, x, ya, W0f, xA, xB, tag="s0")
    phase_barrier(b)
    emit_l0attn(ctx, ya, yb, W0a, tag="m0")
    phase_barrier(b)
    emit_l0ffn(ctx, yb, ya, W0f, xA, xB, tag="s1")
    phase_barrier(b)
    emit_l1ffn(ctx, ya, yb, W1f, scr_a, scr_b, tag="f0")
    phase_barrier(b)
    emit_l1attn(ctx, yb, ya, W1a_qkv, W1a_out, W1a_rel, bias_hbm, tag="a0")
    phase_barrier(b)
    done = emit_l1ffn(ctx, ya, xo, W1f, scr_a, scr_b, tag="f1")
    b.nopop("sp", done)

    sems = {}
    for name in b.cnt:
        sems[name] = es.enter_context(nc.semaphore(name))
    blk = es.enter_context(nc.Block())
    b.build_into(blk, sems)
    es.close()
    return nc, b


def host_consts(T):
    HS, WIN = 64, 256
    half = np.arange(0, HS, 2, dtype=np.float64)
    invf = 1.0 / 10000.0 ** (half / HS)
    t = np.arange(T, dtype=np.float64)
    fr = np.concatenate([t[:, None] * invf] * 2, -1)
    base = (half + 0.4 * HS) / (1.4 * HS)
    sc = base[None, :] ** ((t - T // 2) / (WIN // 2))[:, None]
    sc = np.concatenate([sc, sc], -1)
    cos, sin = np.cos(fr), np.sin(fr)
    sign = np.where(np.arange(64) < 32, -1.0, 1.0)

    def fm(a):
        m = a.T
        return np.concatenate([m, m], 0).astype(np.float32)

    rotq = np.concatenate([fm(cos * sc), fm(sin * sc * sign[None, :])], 1)
    rotk = np.concatenate([fm(cos / sc), fm(sin / sc * sign[None, :])], 1)

    l0fm = np.zeros((P, 4 * 512), np.float32)
    for v in range(4):
        qsub, w0 = v % 2, v >= 2
        for p in range(P):
            i = p + 128 * qsub
            ok = np.zeros(512, bool)
            ok[i:i + 257] = True
            if w0:
                ok[:256] = False
            l0fm[p, v * 512:(v + 1) * 512] = np.where(ok, 0.0, -3e38)

    l0am = np.zeros((P, 4 * 512), np.float32)
    for v in range(4):
        for p in range(P):
            jmax = v * 128 + p
            l0am[p, v * 512 + jmax + 1:(v + 1) * 512] = -3e38

    RLEN = 2 * T - 1
    NB, MD = 32, 128
    r = np.arange(-(T - 1), T)
    nb = NB // 2
    rb = (r > 0).astype(np.int64) * nb
    ra = np.abs(r)
    mx = nb // 2
    small = ra < mx
    large = mx + (np.log(np.maximum(ra, 1) / mx) / np.log(MD / mx) * (nb - mx)).astype(np.int64)
    large = np.minimum(large, nb - 1)
    buck = rb + np.where(small, ra, large)
    oh = np.zeros((NB, RLEN), np.float32)
    oh[buck, np.arange(RLEN)] = 1.0
    idx = np.zeros((P, 16), np.int32)
    for h in range(16):
        idx[:, h] = h * RLEN + 127 - np.arange(P)

    blk2 = np.zeros((P, 2), np.float32)
    blk2[:64, 0] = 1
    blk2[64:, 1] = 1

    return dict(c_eye=np.eye(P, dtype=np.float32), c_l0fm=l0fm, c_l0am=l0am,
                c_rotq=rotq, c_rotk=rotk, c_blk2=blk2, c_sel2=blk2.T.copy(),
                c_t5oh=oh, c_t5idx=idx)


_CACHED = {}


def _get_program():
    if "nc" not in _CACHED:
        nc, b = build_module(2048)
        _CACHED["nc"] = nc
    return _CACHED["nc"]


def kernel(**inputs):
    T = 2048
    nc = _get_program()
    consts = host_consts(T)
    x = inputs["L0_attn_x"]
    names = [
        "L0_attn_W_dq", "L0_attn_W_uq", "L0_attn_W_dkv", "L0_attn_W_uk",
        "L0_attn_W_uv", "L0_attn_W_o", "L0_ffn_Wqkv", "L0_ffn_q_scale",
        "L0_ffn_k_scale", "L0_ffn_Wo", "L0_ffn_ln_g", "L0_ffn_ln_b",
        "L0_ffn_W1", "L0_ffn_b1", "L0_ffn_W2", "L0_ffn_b2",
        "L1_attn_W_qkv", "L1_attn_W_out", "L1_attn_rel_emb",
        "L1_ffn_w1_w", "L1_ffn_w1_s", "L1_ffn_w2_w", "L1_ffn_w2_s",
        "L1_ffn_h1_w", "L1_ffn_h1_s", "L1_ffn_h2_w", "L1_ffn_h2_s",
        "L1_ffn_w3_w", "L1_ffn_w3_s",
    ]
    shared = {n: np.ascontiguousarray(np.asarray(inputs[n], dtype=np.float32))
              for n in names}
    shared.update(consts)
    in_maps = []
    for i in range(8):
        m = dict(shared)
        m["x_fm"] = np.ascontiguousarray(np.asarray(x[i], dtype=np.float32).T)
        in_maps.append(m)
    res = run_bass_kernel_spmd(nc, in_maps, core_ids=list(range(8)))
    out = np.stack([np.ascontiguousarray(r["xo_fm"].T) for r in res.results])
    return out.astype(np.float32)


# version 06
# speedup vs baseline: 1.0020x; 1.0020x over previous
import numpy as np
import ml_dtypes
import bass_rust
import concourse.bass as bass
import concourse.mybir as mybir
from concourse.bass_utils import run_bass_kernel_spmd

F32 = mybir.dt.float32
BF16 = mybir.dt.bfloat16
I32 = mybir.dt.int32
U8 = mybir.dt.uint8
AF = mybir.ActivationFunctionType
ALU = mybir.AluOpType
AX = mybir.AxisListType

P = 128
D = 1024
DT = D // P
SBUF_BYTES = 203 * 1024
SZ = {F32: 4, BF16: 2, I32: 4, U8: 1}

ENGINES = ("pe", "act", "dve", "pool", "sp")


class Bld:
    def __init__(self, nc):
        self.nc = nc
        self.stream = {e: [] for e in ENGINES}
        self.cnt = {}
        self.known = {e: {} for e in ENGINES}
        self.base = []

    def barrier(self):
        self.base = [(s, v) for s, v in self.cnt.items() if v > 0]
        return list(self.base)

    def sem(self, name):
        if name not in self.cnt:
            self.cnt[name] = 0
        return name

    def op(self, eng, fn, waits=(), inc=None, n=1):
        elide = eng != "pool"
        ws = []
        seen = {}
        for t in list(self.base) + list(waits):
            if t is None:
                continue
            s, v = t
            if v <= 0 or (elide and v <= self.known[eng].get(s, 0)):
                continue
            if v <= seen.get(s, 0):
                continue
            seen[s] = v
            ws.append((s, v))
            if elide:
                self.known[eng][s] = v
        ws = [(s, v) for s, v in ws if v == seen[s]]
        tok = None
        if inc is None:
            inc = eng
        self.sem(inc)
        self.cnt[inc] += n
        tok = (inc, self.cnt[inc])
        self.stream[eng].append((ws, fn, inc, n))
        return tok

    def nopop(self, eng, waits):
        return self.op(eng, lambda E: E.nop(), waits, inc=eng + "_ck")

    def ckpt(self, eng, sem):
        return self.nopop(eng, [(sem, self.cnt.get(sem, 0))])

    def dma(self, eng, out, in_, waits=(), sem=None, exclusive=True):
        sem = self.sem(sem or ("dq_" + eng))
        ws = list(waits)
        if exclusive and self.cnt.get(sem, 0) > 0:
            ws.append((sem, self.cnt[sem]))
        return self.op(eng, lambda E, o=out, i=in_: E.dma_start(o, i), ws,
                       inc=sem, n=16)

    def build_into(self, block, semobjs):
        decos = {"pe": block.tensor, "act": block.scalar, "dve": block.vector,
                 "pool": block.gpsimd, "sp": block.sync}
        for eng in ENGINES:
            ops = self.stream[eng]

            @decos[eng]
            def _(E, ops=ops):
                for ws, fn, inc, n in ops:
                    for s, v in ws:
                        E.wait_ge(semobjs[s], v)
                    ins = fn(E)
                    ins.then_inc(semobjs[inc], n)


class Arena:
    def __init__(self, size):
        self.size = size
        self.off = 0
        self.hi = 0

    def alloc(self, nbytes, align=64):
        self.off = (self.off + align - 1) // align * align
        r = self.off
        self.off += nbytes
        self.hi = max(self.hi, self.off)
        assert self.off <= self.size, f"arena overflow: {self.off} > {self.size}"
        return r

    def mark(self):
        return self.off

    def reset(self, mark):
        self.off = mark


class Ctx:
    def __init__(self, nc, b, T):
        self.nc = nc
        self.b = b
        self.T = T
        self.C = T // 512
        self.ar = Arena(SBUF_BYTES)
        self.sb = None
        self.ps = None
        self.dram = {}
        self.bankuse = {}
        self.stg_war = {}

    def v(self, off, cols, dtype, p0=0, pn=P):
        return self.sb[p0:p0 + pn, off:off + cols * SZ[dtype]].bitcast(dtype)

    def bank(self, i, cols, dtype, p0=0, pn=P):
        o = 2048 * i
        return self.ps[p0:p0 + pn, o:o + cols * SZ[dtype]].bitcast(dtype)


def row1(dram_1d):
    return dram_1d.rearrange("(o x) -> o x", o=1)


def col1(dram_1d):
    return dram_1d.rearrange("(x o) -> x o", o=1)


def emit_ln(ctx, xsrc, xn_off, xn_stride, g_dram=None, bg_dram=None, shift=False,
            tag="ln", want_xb=False, src_tok=(), fout_dram=None):
    b, T, C = ctx.b, ctx.T, ctx.C
    mark0 = ctx.ar.mark()

    def sl(k, c0, n, tiles):
        off = 0 if (shift and k >= DT // 2) else 1
        return tiles[k][:, off + c0: off + c0 + n]

    oc = ctx.v(ctx.ar.alloc(256), 1, BF16)
    oc_tok = b.op("pool", lambda E, o=oc: E.memset(o, 1.0), list(src_tok))

    g_rows, b_cols, g_tok, b_tok = [], [], [], []
    for k in range(DT):
        gv = ctx.v(ctx.ar.alloc(256), 128, BF16, pn=1)
        g_rows.append(gv)
        if g_dram is not None:
            b.dma("pool", gv, row1(g_dram[k * P:(k + 1) * P]), sem="q_gld",
                  exclusive=False, waits=list(src_tok))
            g_tok.append(None)
        else:
            g_tok.append(b.op("pool", lambda E, o=gv: E.memset(o, 1.0), list(src_tok)))
        if bg_dram is not None:
            bv = ctx.v(ctx.ar.alloc(64), 1, F32)
            b_cols.append(bv)
            b.dma("sp", bv, col1(bg_dram[k * P:(k + 1) * P]), sem="q_sld",
                  exclusive=False, waits=list(src_tok))
            b_tok.append(None)
        else:
            b_cols.append(None)
            b_tok.append(None)
    if g_dram is not None:
        gall = b.ckpt("pool", "q_gld")
        g_tok = [gall if t is None else t for t in g_tok]
    if bg_dram is not None:
        ball = b.ckpt("sp", "q_sld")
        b_tok = [ball for _ in b_tok]

    rstd_bf = ctx.v(ctx.ar.alloc(T * 2), T, BF16, pn=1)
    nmr_bf = ctx.v(ctx.ar.alloc(T * 2), T, BF16, pn=1)
    srows = ctx.ar.alloc(4 * 512 * 4)
    rstd_tok = [None] * C
    nmr_tok = [None] * C

    mark1 = ctx.ar.mark()
    xb_stride = (T + 4) * 2
    xb_off = ctx.ar.alloc(DT * xb_stride)
    xb_tiles = [ctx.v(xb_off + k * xb_stride, T + 1, BF16) for k in range(DT)]
    xb_tok = []
    for k in range(DT):
        z = b.op("pool", lambda E, t=xb_tiles[k]: E.memset(t[:, 0:1], 0.0),
                 list(src_tok))
        tk = b.dma("pool", xb_tiles[k][:, 1:T + 1], xsrc[k * P:(k + 1) * P, :],
                   sem=f"q_gx{k}", waits=list(src_tok))
        xb_tok.append(tk)

    sq_off = ctx.ar.alloc(2 * 512 * 2)
    sq_bufs = [ctx.v(sq_off + i * 1024, 512, BF16) for i in range(2)]
    sq_war = [None, None]

    bank_consumed = ctx.bankuse

    for c in range(C):
        bmu_i, bsq_i = (0, 1) if c % 2 == 0 else (2, 3)
        bmu = ctx.bank(bmu_i, 512, F32, pn=1)
        bsq = ctx.bank(bsq_i, 512, F32, pn=1)
        mu_tok = sq_mm_tok = None
        for k in range(DT):
            w = [xb_tok[k], oc_tok, bank_consumed.get(bmu_i)]
            mu_tok = b.op("pe", lambda E, o=bmu, x=sl(k, c * 512, 512, xb_tiles),
                          k=k: E.matmul(o, oc, x, start=(k == 0), stop=(k == DT - 1)), w)
        for k in range(DT):
            i2 = k % 2
            sq = sq_bufs[i2]
            sqt = b.op("dve", lambda E, s=sq, x=sl(k, c * 512, 512, xb_tiles):
                       E.tensor_tensor(s, x, x, op=ALU.mult),
                       [xb_tok[k], sq_war[i2]])
            w = [sqt, oc_tok, bank_consumed.get(bsq_i)]
            sq_mm_tok = b.op("pe", lambda E, o=bsq, s=sq, k=k:
                             E.matmul(o, oc, s, start=(k == 0), stop=(k == DT - 1)), w)
            sq_war[i2] = sq_mm_tok
        mu_row = ctx.v(srows, 512, F32, pn=1)
        var_row = ctx.v(srows + 2048, 512, F32, pn=1)
        musq = ctx.v(srows + 4096, 512, F32, pn=1)
        rstd_f = ctx.v(srows + 6144, 512, F32, pn=1)
        t1 = b.op("dve", lambda E, o=mu_row, i=bmu: E.tensor_scalar(
            o, i, 1.0 / D, None, op0=ALU.mult), [mu_tok])
        bank_consumed[bmu_i] = t1
        t2 = b.op("dve", lambda E, o=var_row, i=bsq: E.tensor_scalar(
            o, i, 1.0 / D, 1e-5, op0=ALU.mult, op1=ALU.add), [sq_mm_tok])
        bank_consumed[bsq_i] = t2
        t3 = b.op("dve", lambda E, o=musq, i=mu_row: E.tensor_tensor(
            o, i, i, op=ALU.mult), [t1])
        t4 = b.op("dve", lambda E, o=var_row, a=var_row, m=musq: E.tensor_tensor(
            o, a, m, op=ALU.subtract), [t2, t3])
        t5 = b.op("act", lambda E, o=rstd_f, i=var_row: E.activation(o, i, AF.Ln), [t4])
        t6 = b.op("act", lambda E, o=rstd_f, i=rstd_f: E.activation(
            o, i, AF.Exp, scale=-0.5), [t5])
        rstd_tok[c] = b.op("dve", lambda E, o=rstd_bf[:, c * 512:c * 512 + 512],
                           i=rstd_f: E.tensor_copy(o, i), [t6])
        t7 = b.op("dve", lambda E, o=nmr_bf[:, c * 512:c * 512 + 512], m=mu_row,
                  r=rstd_f: E.tensor_tensor(o, m, r, op=ALU.mult), [t6, t3])
        nmr_tok[c] = b.op("dve", lambda E, o=nmr_bf[:, c * 512:c * 512 + 512]:
                          E.tensor_scalar(o, o, -1.0, None, op0=ALU.mult), [t7])

    sq_done = sq_mm_tok

    tmp_off = ctx.ar.alloc(2 * 512 * 4)
    xn_tiles = [ctx.v(xn_off + k * xn_stride, T + 1, BF16) for k in range(DT)]
    if fout_dram is not None:
        fo_off = ctx.ar.alloc(4 * 512 * 4)
        fo_war = {}
    xn_tok = [[None] * C for _ in range(DT)]
    xn_z = [None] * DT
    for k in range(DT):
        for c in range(C):
            kc = k * C + c
            bR_i, bM_i = (4, 5) if kc % 2 == 0 else (6, 7)
            bR = ctx.bank(bR_i, 512, F32)
            bM = ctx.bank(bM_i, 512, F32)
            rtok = b.op("pe", lambda E, o=bR, g=g_rows[k],
                        r=rstd_bf[:, c * 512:c * 512 + 512]: E.matmul(
                o, g, r, start=True, stop=True),
                [rstd_tok[c], g_tok[k], bank_consumed.get(bR_i)])
            mtok = b.op("pe", lambda E, o=bM, g=g_rows[k],
                        m=nmr_bf[:, c * 512:c * 512 + 512]: E.matmul(
                o, g, m, start=True, stop=True),
                [nmr_tok[c], bank_consumed.get(bM_i)])
            xsl = sl(k, c * 512, 512, xb_tiles)
            tmp = ctx.v(tmp_off + (kc % 2) * 2048, 512, F32)
            a1 = b.op("dve", lambda E, o=tmp, x=xsl, R=bR: E.tensor_tensor(
                o, x, R, op=ALU.mult), [xb_tok[k], rtok])
            bank_consumed[bR_i] = a1
            if fout_dram is not None:
                i4 = kc % 4
                fo = ctx.v(fo_off + i4 * 2048, 512, F32)
                a2 = b.op("dve", lambda E, o=fo, t=tmp, M=bM: E.tensor_tensor(
                    o, t, M, op=ALU.add), [a1, mtok, fo_war.get(i4)])
                d = b.dma("sp", fout_dram[k * P:(k + 1) * P, c * 512:c * 512 + 512],
                          fo, waits=[a2], sem=f"q_sxo{i4}")
                fo_war[i4] = d
            elif b_cols[k] is not None:
                a2 = b.op("dve", lambda E, o=xn_tiles[k][:, 1 + c * 512:1 + c * 512 + 512],
                          t=tmp, bb=b_cols[k], M=bM: E.scalar_tensor_tensor(
                    o, t, bb, M, op0=ALU.add, op1=ALU.add), [a1, mtok, b_tok[k]])
            else:
                a2 = b.op("dve", lambda E, o=xn_tiles[k][:, 1 + c * 512:1 + c * 512 + 512],
                          t=tmp, M=bM: E.tensor_tensor(o, t, M, op=ALU.add),
                          [a1, mtok])
            bank_consumed[bM_i] = a2
            xn_tok[k][c] = a2

    store_ck = None
    if fout_dram is not None:
        store_ck = b.nopop("sp", [(f"q_sxo{i}", b.cnt.get(f"q_sxo{i}", 0)) for i in range(4)])
    ctx.ar.reset(mark0)
    return dict(xn_tiles=xn_tiles, xn_tok=xn_tok, xn_z=xn_z, store_ck=store_ck)


def emit_prelude(ctx):
    b = ctx.b
    idb = ctx.v(ctx.ar.alloc(128 * 2), 128, BF16)
    t = b.dma("pool", idb, ctx.dram["c_eye"], sem="q_pre")
    ctx.id_bf = idb
    ctx.id_tok = ("q_pre", b.cnt["q_pre"])


def tlin_stage(ctx, w_dram, KT, MT, NIN, rhs_fn, out_fn, rhs_deps, tag,
               tr_banks=(0, 1), mm_banks=(2, 3), war0=(), scratch_off=None):
    b, C = ctx.b, ctx.C
    G = 8
    ngrp = (KT + G - 1) // G
    nwt = 2 if KT <= 8 else 1
    mark = ctx.ar.mark()
    wraw_sz = NIN * 2
    wT_sz = KT * 128 * 2
    if scratch_off is not None:
        wraw_off = scratch_off
        wT_off = scratch_off + 2 * wraw_sz
    else:
        wraw_off = ctx.ar.alloc(2 * wraw_sz)
        wT_off = ctx.ar.alloc(nwt * wT_sz)
    wload_tok = [None, None]
    wT_tok = [[None] * ngrp, [None] * ngrp]
    slot_last = [None, None]
    mm_last = None
    for j in range(MT):
        sl = j % 2
        wraw = ctx.v(wraw_off + sl * wraw_sz, NIN, BF16)
        wT = ctx.v(wT_off + (sl % nwt) * wT_sz, KT * 128, BF16)
        war = [slot_last[sl]] if slot_last[sl] is not None else list(war0)
        wload_tok[sl] = b.dma("pool", wraw, w_dram[j * P:(j + 1) * P, :],
                              waits=war, sem=f"q_gw{sl}")
        for g in range(ngrp):
            tb_i = tr_banks[(j * ngrp + g) % 2]
            tb = ctx.bank(tb_i, 1024, BF16)
            n_in_g = min(G, KT - g * G)
            ttok = None
            for ki in range(n_in_g):
                k = g * G + ki
                w = [wload_tok[sl], ctx.id_tok, ctx.bankuse.get(tb_i)]
                ttok = b.op("pe", lambda E, o=tb[:, ki * 128:(ki + 1) * 128],
                            i=wraw[:, k * 128:(k + 1) * 128]:
                            E.transpose(o, i, ctx.id_bf), w)
            ev = b.op("act", lambda E, o=wT[:, g * G * 128:(g * G + n_in_g) * 128],
                      i=tb[:, 0:n_in_g * 128]: E.copy(o, i), [ttok])
            ctx.bankuse[tb_i] = ev
            wT_tok[sl][g] = ev
        for c in range(C):
            mb_i = mm_banks[(j * C + c) % 2]
            mb = ctx.bank(mb_i, 512, F32)
            mm = None
            for k in range(KT):
                w = [wT_tok[sl][k // G], ctx.bankuse.get(mb_i)] + (rhs_deps if k == 0 and c == 0 else [])
                mm = b.op("pe", lambda E, o=mb, l=wT[:, k * 128:(k + 1) * 128],
                          r=rhs_fn(k, c), k=k:
                          E.matmul(o, l, r, start=(k == 0), stop=(k == KT - 1)), w)
            mm_last = mm
            ctx.bankuse[mb_i] = out_fn(j, c, mb, mm)
        slot_last[sl] = mm_last
    ctx.ar.reset(mark)
    return mm_last


def emit_scale_cols(ctx, s_dram, ntiles, half=False, tag="sc"):
    b = ctx.b
    views = []
    for j in range(ntiles):
        v = ctx.v(ctx.ar.alloc(64), 1, F32)
        views.append(v)
        b.dma("sp", v, col1(s_dram[j * P:(j + 1) * P]), sem="q_sld", exclusive=False)
    tok = b.ckpt("sp", "q_sld")
    toks = [tok] * ntiles
    if half:
        newtok = []
        for j in range(ntiles):
            t = b.op("dve", lambda E, o=views[j]: E.tensor_scalar(
                o, o, 0.5, None, op0=ALU.mult), [tok])
            newtok.append(t)
        toks = newtok
    return views, toks


def emit_l1ffn(ctx, x_in, x_out, W, scr_a, scr_b, tag="f"):
    b, T, C = ctx.b, ctx.T, ctx.C
    HT = 32
    mark0 = ctx.ar.mark()

    s1_v, s1_t = emit_scale_cols(ctx, W["w1_s"], HT)
    s2_v, s2_t = emit_scale_cols(ctx, W["w2_s"], HT)
    h1_v, h1_t = emit_scale_cols(ctx, W["h1_s"], HT)
    h2_v, h2_t = emit_scale_cols(ctx, W["h2_s"], HT)
    w3_v, w3_t = emit_scale_cols(ctx, W["w3_s"], DT, half=True)

    tl_off = ctx.ar.alloc(2 * 4096 * 2 + 32 * 128 * 2)

    xn_stride = (T + 4) * 2
    xn_off = ctx.ar.alloc(DT * xn_stride)
    ln = emit_ln(ctx, x_in, xn_off, xn_stride, tag=tag + "l")
    xn, xn_tok, xn_z = ln["xn_tiles"], ln["xn_tok"], ln["xn_z"]
    xn_all = [xn_tok[k][c] for k in range(DT) for c in range(ctx.C)]
    mark_ln = ctx.ar.mark()

    big_sz = T * 2
    big_off = ctx.ar.alloc(HT * big_sz)
    big = [ctx.v(big_off + j * big_sz, T, BF16) for j in range(HT)]

    def rhs_xn(k, c):
        return xn[k][:, 1 + c * 512: 1 + c * 512 + 512]

    out_last = [None]

    def out_t1(j, c, ps, mm):
        t = b.op("act", lambda E, o=big[j][:, c * 512:c * 512 + 512], i=ps:
                 E.activation(o, i, AF.Silu, scale=s1_v[j]), [mm, s1_t[j]])
        out_last[0] = t
        return t

    st1_mm = tlin_stage(ctx, W["w1_w"], DT, HT, D, rhs_xn, out_t1, xn_all, tag + "1",
                        war0=[xn_all[-1]], scratch_off=tl_off)
    st1_out = out_last[0]

    def out_h(j, c, ps, mm):
        t = b.op("dve", lambda E, o=big[j][:, c * 512:c * 512 + 512], i=ps,
                 s=s2_v[j], t1s=big[j][:, c * 512:c * 512 + 512]:
                 E.scalar_tensor_tensor(o, i, s, t1s, op0=ALU.mult, op1=ALU.mult),
                 [mm, s2_t[j], out_last[0] if out_last[0] is not None else None])
        out_last[0] = t
        return t

    st2_mm = tlin_stage(ctx, W["w2_w"], DT, HT, D, rhs_xn, out_h, xn_all, tag + "2",
                        tr_banks=(4, 5), mm_banks=(6, 7), war0=[st1_mm],
                        scratch_off=tl_off)
    st2_out = out_last[0]


    if DT * xn_stride >= 20480:
        st_off = xn_off
        xi_off = xn_off + 4 * 1024
        xo_off = xn_off + 4 * 1024 + 4 * 2048
    else:
        st_off = ctx.ar.alloc(4 * 1024)
        xi_off = ctx.ar.alloc(4 * 2048)
        xo_off = ctx.ar.alloc(4 * 2048)

    def mk_store_out(scale_v, scale_t, scr, qsem):
        def out(j, c, ps, mm):
            i = (j * C + c) % 4
            stg = ctx.v(st_off + i * 1024, 512, BF16)
            t = b.op("act", lambda E, o=stg, i=ps: E.activation(
                o, i, AF.Gelu, scale=scale_v[j]),
                [mm, scale_t[j], ctx.stg_war.get(i)])
            d = b.dma("sp", scr[j * P:(j + 1) * P, c * 512:c * 512 + 512], stg,
                      waits=[t], sem=f"{qsem}{i}")
            ctx.stg_war[i] = d
            return t
        return out

    def rhs_big(k, c):
        return big[k][:, c * 512:c * 512 + 512]

    last = tlin_stage(ctx, W["h1_w"], HT, HT, 4096, rhs_big,
                      mk_store_out(h1_v, h1_t, scr_a, "q_s3"), [st2_out], tag + "3",
                      war0=[st2_mm], scratch_off=tl_off)

    g1_tok = []
    for j in range(HT):
        g1_tok.append(b.dma("sp", big[j], scr_a[j * P:(j + 1) * P, :],
                            waits=[last] + [ctx.stg_war.get(i) for i in range(4)],
                            sem="q_sg", exclusive=False))
    g1_all = b.ckpt("sp", "q_sg")
    last4 = tlin_stage(ctx, W["h2_w"], HT, HT, 4096, rhs_big,
                       mk_store_out(h2_v, h2_t, scr_b, "q_s4"), [g1_all], tag + "4",
                       tr_banks=(4, 5), mm_banks=(6, 7), war0=[last],
                       scratch_off=tl_off)

    g2_tok = []
    for j in range(HT):
        g2_tok.append(b.dma("sp", big[j], scr_b[j * P:(j + 1) * P, :],
                            waits=[last4] + [ctx.stg_war.get(i) for i in range(4)],
                            sem="q_sg2", exclusive=False))
    g2_all = b.ckpt("sp", "q_sg2")

    xi_war = {}
    xo_war = {}

    def out_res(j, c, ps, mm):
        i = (j * C + c) % 4
        xi = ctx.v(xi_off + i * 2048, 512, F32)
        xo = ctx.v(xo_off + i * 2048, 512, F32)
        ld = b.dma("sp", xi, x_in[j * P:(j + 1) * P, c * 512:c * 512 + 512],
                   waits=[xi_war.get(i) or g2_all], sem=f"q_sxi{i}")
        t = b.op("dve", lambda E, o=xo, i=ps, s=w3_v[j], x=xi:
                 E.scalar_tensor_tensor(o, i, s, x, op0=ALU.mult, op1=ALU.add),
                 [mm, ld, w3_t[j], xo_war.get(i)])
        xi_war[i] = t
        d = b.dma("sp", x_out[j * P:(j + 1) * P, c * 512:c * 512 + 512], xo,
                  waits=[t], sem=f"q_sxo{i}")
        xo_war[i] = d
        return t

    tlin_stage(ctx, W["w3_w"], HT, DT, 4096, rhs_big, out_res, [g2_all], tag + "5",
               war0=[last4], scratch_off=tl_off)
    out_done = [("q_sxo%d" % i, b.cnt.get("q_sxo%d" % i, 0)) for i in range(4)]
    ctx.ar.reset(mark0)
    return out_done


def wslice_ap(w_dram, ncols_total, col0, ncols, ktiles=DT):
    c = w_dram.copy()
    c.ap = bass_rust.VecI64Pair([[ncols_total, P], [ncols_total * P, ktiles], [1, ncols]])
    c.offset = w_dram.offset + col0
    return c


def emit_l1attn(ctx, x_in, x_out, W_qkv, W_out, rel_emb, bias_hbm, tag="a"):
    b, T, C = ctx.b, ctx.T, ctx.C
    TT_ = T // 128
    NG = (TT_ + 7) // 8
    mark0 = ctx.ar.mark()
    RLEN = 2 * T - 1
    WT = 2 * T - 128
    SKB = T - 128
    bias2d = bias_hbm.rearrange("(a b) -> a b", b=RLEN)
    bias_fl = bias_hbm.rearrange("(x o) -> x o", o=1)

    idx = ctx.v(ctx.ar.alloc(64), 16, I32)
    idx_t = b.dma("sp", idx, ctx.dram["c_t5idx"], sem="q_sld")

    mark_b = ctx.ar.mark()
    oh = ctx.v(ctx.ar.alloc(RLEN * 2 + 64), RLEN, BF16, pn=32)
    b.dma("pool", oh, ctx.dram["c_t5oh"], sem="q_gld", exclusive=False)
    rel = ctx.v(ctx.ar.alloc(64), 16, BF16, pn=32)
    b.dma("pool", rel, rel_emb, sem="q_gld", exclusive=False)
    oh_t = rel_t = b.ckpt("pool", "q_gld")
    brow = ctx.v(ctx.ar.alloc(RLEN * 2 + 64), RLEN, BF16, pn=16)
    ev_last = None
    for c in range((RLEN + 511) // 512):
        n = min(512, RLEN - c * 512)
        bk = ctx.bank(c % 2, n, F32, pn=16)
        mm = b.op("pe", lambda E, o=bk, r=rel, h=oh[:, c * 512:c * 512 + n]:
                  E.matmul(o, r, h, start=True, stop=True),
                  [oh_t, rel_t, ctx.bankuse.get(c % 2)])
        ev_last = b.op("dve", lambda E, o=brow[:, c * 512:c * 512 + n], i=bk:
                       E.tensor_copy(o, i), [mm])
        ctx.bankuse[c % 2] = ev_last
    b.dma("sp", bias2d, brow, waits=[ev_last], sem="q_sbias")
    bias_ck = b.ckpt("sp", "q_sbias")
    ctx.ar.reset(mark_b)

    xn_stride = (T + 4) * 2
    xn_off = ctx.ar.alloc(DT * xn_stride)
    ln = emit_ln(ctx, x_in, xn_off, xn_stride, tag=tag + "l", src_tok=[bias_ck])
    xn, xn_tok = ln["xn_tiles"], ln["xn_tok"]

    q_off = ctx.ar.alloc(DT * T * 2)
    k_off = ctx.ar.alloc(DT * T * 2)
    v_off = ctx.ar.alloc(TT_ * 1024 * 2)
    o_off = ctx.ar.alloc(DT * T * 2)
    q_sb = [ctx.v(q_off + j * T * 2, T, BF16) for j in range(DT)]
    k_sb = [ctx.v(k_off + j * T * 2, T, BF16) for j in range(DT)]
    v_sb = [ctx.v(v_off + j * 1024 * 2, 1024, BF16) for j in range(TT_)]
    o_sb = [ctx.v(o_off + j * T * 2, T, BF16) for j in range(DT)]
    mark1 = ctx.ar.mark()

    ws_off = ctx.ar.alloc(2 * DT * 128 * 2)
    vs_off = ctx.ar.alloc(DT * 512 * 2)
    q_tok = [[None] * C for _ in range(DT)]
    k_tok = [[None] * C for _ in range(DT)]
    v_tok = [None] * TT_
    ln_done = xn_tok[DT - 1][C - 1]
    ws_war = [ln_done, ln_done]
    mmb = 0
    for dst, base, toks in ((q_sb, 0, q_tok), (k_sb, 1024, k_tok)):
        for j in range(DT):
            sl = j % 2
            ws = ctx.v(ws_off + sl * DT * 128 * 2, DT * 128, BF16)
            wt = b.dma("pool", ws, wslice_ap(W_qkv, 3072, base + j * 128, 128),
                       waits=[ws_war[sl]], sem=f"q_gw{sl}")
            mm = None
            for c in range(C):
                bi = 2 + (mmb % 2)
                mmb += 1
                bk = ctx.bank(bi, 512, F32)
                for k in range(DT):
                    w = [wt, xn_tok[k][c], ctx.bankuse.get(bi)]
                    mm = b.op("pe", lambda E, o=bk, l=ws[:, k * 128:(k + 1) * 128],
                              r=xn[k][:, 1 + c * 512:1 + c * 512 + 512], k=k:
                              E.matmul(o, l, r, start=(k == 0), stop=(k == DT - 1)), w)
                ev = b.op("act" if (j + c) % 2 else "dve",
                          lambda E, o=dst[j][:, c * 512:c * 512 + 512], i=bk:
                          (E.copy(o, i) if hasattr(E, "copy") else E.tensor_copy(o, i)),
                          [mm])
                toks[j][c] = ev
                ctx.bankuse[bi] = ev
            ws_war[sl] = mm
    vs_war = ln_done
    for vc in range(2):
        vs = ctx.v(vs_off, DT * 512, BF16)
        wt = b.dma("pool", vs, wslice_ap(W_qkv, 3072, 2048 + vc * 512, 512),
                   waits=[vs_war], sem="q_gw0")
        for tt in range(TT_):
            bi = 4 + (tt % 2)
            bk = ctx.bank(bi, 512, F32)
            mm = None
            for k in range(DT):
                w = [wt, xn_tok[k][(tt * 128) // 512], ctx.bankuse.get(bi)]
                mm = b.op("pe", lambda E, o=bk, l=xn[k][:, 1 + tt * 128:1 + (tt + 1) * 128],
                          r=vs[:, k * 512:(k + 1) * 512], k=k:
                          E.matmul(o, l, r, start=(k == 0), stop=(k == DT - 1)), w)
            ev = b.op("act" if tt % 2 else "dve",
                      lambda E, o=v_sb[tt][:, vc * 512:(vc + 1) * 512], i=bk:
                      (E.copy(o, i) if hasattr(E, "copy") else E.tensor_copy(o, i)),
                      [mm])
            v_tok[tt] = ev
            ctx.bankuse[bi] = ev
        vs_war = mm

    ctx.ar.reset(mark1)
    bsk_off = ctx.ar.alloc(2 * WT * 2)
    P_off = ctx.ar.alloc(2 * T * 2)
    Pf_off = ctx.ar.alloc(T * 4)
    PT_off = ctx.ar.alloc(2 * TT_ * 128 * 2)
    sm_off = ctx.ar.alloc(2 * 192)
    P_bf2 = [ctx.v(P_off + i * T * 2, T, BF16) for i in range(2)]
    P_f2 = [ctx.v(Pf_off, T, F32), ctx.v(Pf_off, T, F32)]
    PT_sb2 = [ctx.v(PT_off + i * TT_ * 128 * 2, TT_ * 128, BF16) for i in range(2)]
    negmax2 = [ctx.v(sm_off + i * 192, 1, F32) for i in range(2)]
    sums2 = [ctx.v(sm_off + i * 192 + 64, 1, F32) for i in range(2)]
    recip2 = [ctx.v(sm_off + i * 192 + 128, 1, F32) for i in range(2)]
    bsk_tok = [None, None]
    bsk_war = [None, None]
    tr_tok = [None, None]
    sc_tok_prev = [None, None]
    mx_prev = [None, None]
    qq = 0
    cmm = None
    oev_tok = None
    for h in range(16):
        bsk = ctx.v(bsk_off + (h % 2) * WT * 2, WT, BF16)
        war = [bsk_war[h % 2] or vs_war, bias_ck, idx_t]
        bsk_tok[h % 2] = b.op(
            "pool",
            lambda E, o=bsk, i=bias_fl, ix=idx[:, h:h + 1]: E.indirect_dma_start(
                o, None, i, bass.IndirectOffsetOnAxis(ap=ix, axis=0)),
            war, inc=f"q_gbsk{h % 2}", n=16)
        jt2, pb2 = h // 2, 64 * (h % 2)
        for qt in range(TT_):
            sl2 = qq % 2
            qq += 1
            P_bf, P_f, PT_sb = P_bf2[sl2], P_f2[sl2], PT_sb2[sl2]
            negmax, sums, recip = negmax2[sl2], sums2[sl2], recip2[sl2]
            smm = None
            for c in range(C):
                bk = ctx.bank(c, 512, F32)
                w = [q_tok[jt2][(qt * 128) // 512], k_tok[jt2][c],
                     ctx.bankuse.get(c), bsk_tok[h % 2]]
                smm = b.op("pe", lambda E, o=bk,
                           l=q_sb[jt2][pb2:pb2 + 64, qt * 128:(qt + 1) * 128],
                           r=k_sb[jt2][pb2:pb2 + 64, c * 512:(c + 1) * 512]:
                           E.matmul(o, l, r, start=True, stop=False), w)
                smm = b.op("pe", lambda E, o=bk, l=ctx.id_bf,
                           r=bsk[:, SKB - 128 * qt + c * 512:SKB - 128 * qt + c * 512 + 512]:
                           E.matmul(o, l, r, start=False, stop=True), [smm, ctx.id_tok])
            spsum = ctx.ps[:, 0:C * 2048].bitcast(F32)
            mx = b.op("dve", lambda E, o=negmax, i=spsum: E.reduce_max(
                o, i, axis=AX.X, negate=True), [smm, mx_prev[sl2]])
            et = b.op("act", lambda E, o=P_f, i=spsum, bb=negmax, s=sums:
                      E.activation(o, i, AF.Exp, bias=bb, accum_out=s),
                      [mx, sc_tok_prev[0], sc_tok_prev[1]])
            for c in range(C):
                ctx.bankuse[c] = et
            rc = b.op("dve", lambda E, o=recip, i=sums: E.reciprocal(o, i), [et])
            sc_tok = b.op("dve", lambda E, o=P_bf, i=P_f, r=recip: E.tensor_scalar(
                o, i, r, None, op0=ALU.mult), [rc, tr_tok[sl2]])
            sc_tok_prev[sl2] = sc_tok
            mx_prev[sl2] = rc
            for g in range(NG):
                tb_i = 6 + (g % 2)
                tb = ctx.bank(tb_i, 1024, BF16)
                nin = min(8, TT_ - g * 8)
                ttok = None
                for ki in range(nin):
                    jt = g * 8 + ki
                    w = [sc_tok, ctx.id_tok, ctx.bankuse.get(tb_i)]
                    ttok = b.op("pe", lambda E, o=tb[:, ki * 128:(ki + 1) * 128],
                                i=P_bf[:, jt * 128:(jt + 1) * 128]:
                                E.transpose(o, i, ctx.id_bf), w)
                g_ev = b.op("dve", lambda E, o=PT_sb[:, g * 1024:g * 1024 + nin * 128],
                            i=tb[:, 0:nin * 128]: E.tensor_copy(o, i), [ttok])
                ctx.bankuse[tb_i] = g_ev
                tr_tok[sl2] = ttok
            pt_ck = g_ev
            for jt in range(TT_):
                bk = ctx.bank(4, 128, F32, pn=64)
                w = [pt_ck, v_tok[jt], ctx.bankuse.get(4)]
                cmm = b.op("pe", lambda E, o=bk, l=v_sb[jt][:, 64 * h:64 * h + 64],
                           r=PT_sb[:, jt * 128:(jt + 1) * 128], jt=jt:
                           E.matmul(o, l, r, start=(jt == 0), stop=(jt == TT_ - 1)), w)
            oev_tok = b.op("act", lambda E, o=o_sb[jt2][pb2:pb2 + 64, qt * 128:(qt + 1) * 128],
                           i=ctx.bank(4, 128, F32, pn=64): E.copy(o, i), [cmm])
            ctx.bankuse[4] = oev_tok
        bsk_war[h % 2] = smm

    o_done = oev_tok
    ctx.ar.reset(mark1)

    ws2_off = ctx.ar.alloc(2 * DT * 128 * 2)
    xi_off = ctx.ar.alloc(4 * 512 * 4)
    xo2_off = ctx.ar.alloc(4 * 512 * 4)
    xi_war, xo_war = {}, {}
    ws2_war = [None, None]
    out_done = []
    for j in range(DT):
        sl = j % 2
        ws = ctx.v(ws2_off + sl * DT * 128 * 2, DT * 128, BF16)
        wt = b.dma("pool", ws, wslice_ap(W_out, 1024, j * 128, 128),
                   waits=[ws2_war[sl] or cmm], sem=f"q_gw{sl}")
        for c in range(C):
            bi = (j * C + c) % 4
            bk = ctx.bank(bi, 512, F32)
            mm = None
            for k in range(DT):
                w = [wt, o_done, ctx.bankuse.get(bi)]
                mm = b.op("pe", lambda E, o=bk, l=ws[:, k * 128:(k + 1) * 128],
                          r=o_sb[k][:, c * 512:(c + 1) * 512], k=k:
                          E.matmul(o, l, r, start=(k == 0), stop=(k == DT - 1)), w)
            i4 = (j * C + c) % 4
            xi = ctx.v(xi_off + i4 * 2048, 512, F32)
            xo = ctx.v(xo2_off + i4 * 2048, 512, F32)
            ld = b.dma("sp", xi, x_in[j * P:(j + 1) * P, c * 512:c * 512 + 512],
                       waits=[xi_war.get(i4) or cmm], sem=f"q_sxi{i4}")
            t = b.op("dve", lambda E, o=xo, i=bk, x=xi: E.tensor_tensor(
                o, i, x, op=ALU.add), [mm, ld, xo_war.get(i4)])
            xi_war[i4] = t
            ctx.bankuse[bi] = t
            d = b.dma("sp", x_out[j * P:(j + 1) * P, c * 512:c * 512 + 512], xo,
                      waits=[t], sem=f"q_sxo{i4}")
            xo_war[i4] = d
            out_done.append(d)
        ws2_war[sl] = mm
    ctx.ar.reset(mark0)
    return out_done[-4:]


def emit_l0attn(ctx, x_in, x_out, W, tag="m"):
    b, T, C = ctx.b, ctx.T, ctx.C
    TT_ = T // 128
    mark0 = ctx.ar.mark()

    qup_off = ctx.ar.alloc(DT * T * 2)
    y_off = ctx.ar.alloc(DT * T * 2)
    ckvT_off = ctx.ar.alloc(T * 2)
    ckv_off = ctx.ar.alloc(TT_ * 128 * 2)
    qup_sb = [ctx.v(qup_off + j * T * 2, T, BF16) for j in range(DT)]
    y_sb = [ctx.v(y_off + j * T * 2, T, BF16) for j in range(DT)]
    ckvT = ctx.v(ckvT_off, T, BF16)
    ckv_sb = ctx.v(ckv_off, TT_ * 128, BF16)

    mark1 = ctx.ar.mark()

    keff = ctx.v(ctx.ar.alloc(8 * 128 * 2), 8 * 128, BF16)
    veff = ctx.v(ctx.ar.alloc(1024 * 2), 1024, BF16)
    dqT = ctx.v(ctx.ar.alloc(8 * 128 * 2), 8 * 128, BF16)
    dkvT = ctx.v(ctx.ar.alloc(8 * 128 * 2), 8 * 128, BF16)
    uqT = ctx.v(ctx.ar.alloc(8 * 128 * 2), 8 * 128, BF16)
    msk = ctx.v(ctx.ar.alloc(4 * 512 * 2), 4 * 512, BF16)
    mark_prep = ctx.ar.mark()

    dq_raw = ctx.v(ctx.ar.alloc(1024 * 2), 1024, BF16)
    dkv_raw = ctx.v(ctx.ar.alloc(1024 * 2), 1024, BF16)
    uq_raw = ctx.v(ctx.ar.alloc(8 * 128 * 2), 8 * 128, BF16)
    uk_raw = ctx.v(ctx.ar.alloc(8 * 128 * 2), 8 * 128, BF16)
    uv_raw = ctx.v(ctx.ar.alloc(8 * 128 * 2), 8 * 128, BF16)
    b.dma("pool", dq_raw, W["dq"], sem="q_gld", exclusive=False)
    b.dma("pool", dkv_raw, W["dkv"], sem="q_gld", exclusive=False)
    b.dma("pool", uq_raw, wslice_ap(W["uq"], 128, 0, 128), sem="q_gld", exclusive=False)
    b.dma("pool", uk_raw, wslice_ap(W["uk"], 128, 0, 128), sem="q_gld", exclusive=False)
    b.dma("pool", uv_raw, wslice_ap(W["uv"], 128, 0, 128), sem="q_gld", exclusive=False)
    raw_ck = b.ckpt("pool", "q_gld")

    tr_ev = {}
    for name, srcv, dstv in (("dq", dq_raw, dqT), ("dkv", dkv_raw, dkvT)):
        for g in range(1):
            tb = ctx.bank(6, 1024, BF16)
            tt = None
            for k in range(DT):
                tt = b.op("pe", lambda E, o=tb[:, k * 128:(k + 1) * 128],
                          i=srcv[:, k * 128:(k + 1) * 128]:
                          E.transpose(o, i, ctx.id_bf), [raw_ck, ctx.id_tok, ctx.bankuse.get(6)])
            ev = b.op("dve", lambda E, o=dstv, i=tb: E.tensor_copy(o, i), [tt])
            ctx.bankuse[6] = ev
            tr_ev[name] = ev
    tb = ctx.bank(7, 1024, BF16)
    tt = None
    for k in range(DT):
        tt = b.op("pe", lambda E, o=tb[:, k * 128:(k + 1) * 128],
                  i=uq_raw[:, k * 128:(k + 1) * 128]:
                  E.transpose(o, i, ctx.id_bf), [raw_ck, ctx.id_tok, ctx.bankuse.get(7)])
    uqT_ev = b.op("dve", lambda E, o=uqT, i=tb: E.tensor_copy(o, i), [tt])
    ctx.bankuse[7] = uqT_ev

    t1_sb = ctx.v(ctx.ar.alloc(128 * 2), 128, BF16)
    bk5 = ctx.bank(5, 128, F32)
    mm = None
    for k in range(DT):
        mm = b.op("pe", lambda E, o=bk5, l=uq_raw[:, k * 128:(k + 1) * 128],
                  r=uk_raw[:, k * 128:(k + 1) * 128], k=k:
                  E.matmul(o, l, r, start=(k == 0), stop=(k == DT - 1)),
                  [raw_ck, ctx.bankuse.get(5)])
    t1_ev = b.op("dve", lambda E, o=t1_sb, i=bk5: E.tensor_copy(o, i), [mm])
    ctx.bankuse[5] = t1_ev
    keff_ev = None
    for k in range(DT):
        bk5b = ctx.bank(5, 128, F32)
        mm = b.op("pe", lambda E, o=bk5b, l=dq_raw[:, k * 128:(k + 1) * 128], r=t1_sb:
                  E.matmul(o, l, r, start=True, stop=True),
                  [t1_ev, ctx.bankuse.get(5)])
        keff_ev = b.op("act", lambda E, o=keff[:, k * 128:(k + 1) * 128], i=bk5b:
                       E.activation(o, i, AF.Copy, scale=0.125), [mm])
        ctx.bankuse[5] = keff_ev

    wo_raw = ctx.v(ctx.ar.alloc(1024 * 2), 1024, BF16)
    woT = ctx.v(ctx.ar.alloc(DT * 1024 * 2), DT * 1024, BF16)
    wo_war = None
    for k in range(DT):
        ld = b.dma("pool", wo_raw, W["o"][k * P:(k + 1) * P, :], waits=[wo_war],
                   sem="q_gw0")
        for g in range(2):
            tb = ctx.bank(6 + g % 2, 1024, BF16)
            tt = None
            for ki in range(4):
                cc = g * 4 + ki
                tt = b.op("pe", lambda E, o=tb[:, ki * 128:(ki + 1) * 128],
                          i=wo_raw[:, cc * 128:(cc + 1) * 128]:
                          E.transpose(o, i, ctx.id_bf),
                          [ld, ctx.id_tok, ctx.bankuse.get(6 + g % 2)])
            ev = None
            for ki in range(4):
                cc = g * 4 + ki
                ev = b.op("dve", lambda E, o=woT[:, cc * 1024 + k * 128:cc * 1024 + (k + 1) * 128],
                          i=tb[:, ki * 128:(ki + 1) * 128]: E.tensor_copy(o, i), [tt])
            ctx.bankuse[6 + g % 2] = ev
            wo_war = ev
    woT_ck = wo_war
    for half in range(2):
        bkv = ctx.bank(5, 512, F32)
        mm = None
        for k in range(DT):
            mm = b.op("pe", lambda E, o=bkv, l=uv_raw[:, k * 128:(k + 1) * 128],
                      r=woT[:, k * 1024 + half * 512:k * 1024 + half * 512 + 512], k=k:
                      E.matmul(o, l, r, start=(k == 0), stop=(k == DT - 1)),
                      [raw_ck, woT_ck, ctx.bankuse.get(5)])
        ev = b.op("dve", lambda E, o=veff[:, half * 512:(half + 1) * 512], i=bkv:
                  E.tensor_copy(o, i), [mm])
        ctx.bankuse[5] = ev
        veff_ck = ev

    b.dma("pool", msk, ctx.dram["c_l0am"], sem="q_gld", exclusive=False)
    msk_ck = b.ckpt("pool", "q_gld")
    ctx.ar.reset(mark_prep)

    xn_stride = (T + 4) * 2
    xn_off = ctx.ar.alloc(DT * xn_stride)
    ln = emit_ln(ctx, x_in, xn_off, xn_stride, tag=tag + "l", src_tok=[veff_ck])
    xn, xn_tok = ln["xn_tiles"], ln["xn_tok"]

    qlow = ctx.v(ctx.ar.alloc(T * 2), T, BF16)
    for c in range(C):
        bk = ctx.bank(c % 2, 512, F32)
        mm = None
        for k in range(DT):
            mm = b.op("pe", lambda E, o=bk, l=dkvT[:, k * 128:(k + 1) * 128],
                      r=xn[k][:, 1 + c * 512:1 + c * 512 + 512], k=k:
                      E.matmul(o, l, r, start=(k == 0), stop=(k == DT - 1)),
                      [tr_ev["dkv"], xn_tok[k][c], ctx.bankuse.get(c % 2)])
        ev = b.op("dve", lambda E, o=ckvT[:, c * 512:(c + 1) * 512], i=bk:
                  E.tensor_copy(o, i), [mm])
        ctx.bankuse[c % 2] = ev
        ckvT_tok = ev
    for c in range(C):
        bk = ctx.bank(2 + c % 2, 512, F32)
        mm = None
        for k in range(DT):
            mm = b.op("pe", lambda E, o=bk, l=dqT[:, k * 128:(k + 1) * 128],
                      r=xn[k][:, 1 + c * 512:1 + c * 512 + 512], k=k:
                      E.matmul(o, l, r, start=(k == 0), stop=(k == DT - 1)),
                      [tr_ev["dq"], xn_tok[k][c], ctx.bankuse.get(2 + c % 2)])
        ev = b.op("dve", lambda E, o=qlow[:, c * 512:(c + 1) * 512], i=bk:
                  E.tensor_copy(o, i), [mm])
        ctx.bankuse[2 + c % 2] = ev
        qlow_tok = ev
    ckv_tok = None
    for g in range((TT_ + 7) // 8):
        tb = ctx.bank(6 + g % 2, 1024, BF16)
        nin = min(8, TT_ - g * 8)
        tt = None
        for ki in range(nin):
            jt = g * 8 + ki
            tt = b.op("pe", lambda E, o=tb[:, ki * 128:(ki + 1) * 128],
                      i=ckvT[:, jt * 128:(jt + 1) * 128]:
                      E.transpose(o, i, ctx.id_bf),
                      [ckvT_tok, ctx.id_tok, ctx.bankuse.get(6 + g % 2)])
        ckv_tok = b.op("dve", lambda E, o=ckv_sb[:, g * 1024:g * 1024 + nin * 128],
                       i=tb[:, 0:nin * 128]: E.tensor_copy(o, i), [tt])
        ctx.bankuse[6 + g % 2] = ckv_tok
    qup_tok = [[None] * C for _ in range(DT)]
    for j in range(DT):
        for c in range(C):
            bk = ctx.bank((j * C + c) % 2, 512, F32)
            mm = b.op("pe", lambda E, o=bk, l=uqT[:, j * 128:(j + 1) * 128],
                      r=qlow[:, c * 512:(c + 1) * 512]:
                      E.matmul(o, l, r, start=True, stop=True),
                      [uqT_ev, qlow_tok, ctx.bankuse.get((j * C + c) % 2)])
            ev = b.op("act" if (j + c) % 2 else "dve",
                      lambda E, o=qup_sb[j][:, c * 512:(c + 1) * 512], i=bk:
                      (E.copy(o, i) if hasattr(E, "copy") else E.tensor_copy(o, i)),
                      [mm])
            qup_tok[j][c] = ev
            ctx.bankuse[(j * C + c) % 2] = ev

    ql_off = ctx.ar.alloc(T * 2)
    P_off = ctx.ar.alloc(2 * T * 2)
    Pf_off = ctx.ar.alloc(T * 4)
    PT_off = ctx.ar.alloc(2 * TT_ * 128 * 2)
    cx_off = ctx.ar.alloc(2 * 128 * 2)
    sm_off = ctx.ar.alloc(2 * 192)
    ql_sb = ctx.v(ql_off, T, BF16)
    P_bf2 = [ctx.v(P_off + i * T * 2, T, BF16) for i in range(2)]
    P_f2 = [ctx.v(Pf_off, T, F32), ctx.v(Pf_off, T, F32)]
    PT_sb2 = [ctx.v(PT_off + i * TT_ * 128 * 2, TT_ * 128, BF16) for i in range(2)]
    cx_sb2 = [ctx.v(cx_off + i * 128 * 2, 128, BF16) for i in range(2)]
    negmax2 = [ctx.v(sm_off + i * 192, 1, F32) for i in range(2)]
    sums2 = [ctx.v(sm_off + i * 192 + 64, 1, F32) for i in range(2)]
    recip2 = [ctx.v(sm_off + i * 192 + 128, 1, F32) for i in range(2)]
    tr_tok = [None, None]
    sc_prev = [None, None]
    mx_prev = [None, None]
    cx_war = [None, None]
    qq = 0
    ql_war = None
    yev = None
    for h in range(16):
        jt2, pb2 = h // 2, 64 * (h % 2)
        ql_toks = []
        for c in range(C):
            bki = c % 2
            bk = ctx.bank(bki, 512, F32)
            mm = b.op("pe", lambda E, o=bk, l=keff[pb2:pb2 + 64, jt2 * 128:(jt2 + 1) * 128],
                      r=qup_sb[jt2][pb2:pb2 + 64, c * 512:(c + 1) * 512]:
                      E.matmul(o, l, r, start=True, stop=True),
                      [keff_ev, qup_tok[jt2][c], ctx.bankuse.get(bki), ql_war])
            ev = b.op("dve", lambda E, o=ql_sb[:, c * 512:(c + 1) * 512], i=bk:
                      E.tensor_copy(o, i), [mm])
            ctx.bankuse[bki] = ev
            ql_toks.append(ev)
        for qt in range(TT_):
            sl2 = qq % 2
            qq += 1
            P_bf, P_f, PT_sb, cx_sb = P_bf2[sl2], P_f2[sl2], PT_sb2[sl2], cx_sb2[sl2]
            negmax, sums, recip = negmax2[sl2], sums2[sl2], recip2[sl2]
            nch = qt // 4 + 1
            ncols = nch * 512
            nt = qt + 1
            smm = None
            for c in range(nch):
                bk = ctx.bank(c, 512, F32)
                w = [ql_toks[(qt * 128) // 512], ckvT_tok, ctx.bankuse.get(c)]
                smm = b.op("pe", lambda E, o=bk, l=ql_sb[:, qt * 128:(qt + 1) * 128],
                           r=ckvT[:, c * 512:(c + 1) * 512], sp=(c != nch - 1):
                           E.matmul(o, l, r, start=True, stop=sp), w)
                if c == nch - 1:
                    smm = b.op("pe", lambda E, o=bk, l=ctx.id_bf,
                               r=msk[:, (qt % 4) * 512:(qt % 4 + 1) * 512]:
                               E.matmul(o, l, r, start=False, stop=True),
                               [smm, msk_ck, ctx.id_tok])
            spsum = ctx.ps[:, 0:ncols * 4].bitcast(F32)
            mx = b.op("dve", lambda E, o=negmax, i=spsum: E.reduce_max(
                o, i, axis=AX.X, negate=True), [smm, mx_prev[sl2]])
            et = b.op("act", lambda E, o=P_f[:, 0:ncols], i=spsum, bb=negmax, s=sums:
                      E.activation(o, i, AF.Exp, bias=bb, accum_out=s),
                      [mx, sc_prev[0], sc_prev[1]])
            for c in range(nch):
                ctx.bankuse[c] = et
            rc = b.op("dve", lambda E, o=recip, i=sums: E.reciprocal(o, i), [et])
            sc_tok = b.op("dve", lambda E, o=P_bf[:, 0:ncols], i=P_f[:, 0:ncols],
                          r=recip: E.tensor_scalar(o, i, r, None, op0=ALU.mult),
                          [rc, tr_tok[sl2]])
            sc_prev[sl2] = sc_tok
            mx_prev[sl2] = rc
            g_ev = None
            for g in range((nt + 7) // 8):
                tb_i = 6 + (g % 2)
                tb = ctx.bank(tb_i, 1024, BF16)
                nin = min(8, nt - g * 8)
                ttok = None
                for ki in range(nin):
                    jt = g * 8 + ki
                    ttok = b.op("pe", lambda E, o=tb[:, ki * 128:(ki + 1) * 128],
                                i=P_bf[:, jt * 128:(jt + 1) * 128]:
                                E.transpose(o, i, ctx.id_bf),
                                [sc_tok, ctx.id_tok, ctx.bankuse.get(tb_i)])
                g_ev = b.op("dve", lambda E, o=PT_sb[:, g * 1024:g * 1024 + nin * 128],
                            i=tb[:, 0:nin * 128]: E.tensor_copy(o, i), [ttok])
                ctx.bankuse[tb_i] = g_ev
                tr_tok[sl2] = ttok
            cmm = None
            for jt in range(nt):
                bk = ctx.bank(4, 128, F32)
                w = [g_ev, ckv_tok, ctx.bankuse.get(4)]
                cmm = b.op("pe", lambda E, o=bk, l=ckv_sb[:, jt * 128:(jt + 1) * 128],
                           r=PT_sb[:, jt * 128:(jt + 1) * 128], st=(jt == 0),
                           sp=(jt == nt - 1): E.matmul(o, l, r, start=st, stop=sp), w)
            cx_ev = b.op("dve", lambda E, o=cx_sb, i=ctx.bank(4, 128, F32):
                         E.tensor_copy(o, i), [cmm, cx_war[sl2]])
            ctx.bankuse[4] = cx_ev
            ymm = b.op("pe", lambda E, o=ctx.bank(5, 128, F32, pn=64),
                       l=veff[:, 64 * h:64 * h + 64], r=cx_sb:
                       E.matmul(o, l, r, start=True, stop=True),
                       [cx_ev, veff_ck, ctx.bankuse.get(5)])
            yev = b.op("act", lambda E, o=y_sb[jt2][pb2:pb2 + 64, qt * 128:(qt + 1) * 128],
                       i=ctx.bank(5, 128, F32, pn=64): E.copy(o, i), [ymm])
            ctx.bankuse[5] = yev
            cx_war[sl2] = ymm
        ql_war = smm

    y_done = yev
    ctx.ar.reset(mark1)

    xi_off = ctx.ar.alloc(4 * 512 * 4)
    xo2_off = ctx.ar.alloc(4 * 512 * 4)
    xi_war, xo_war = {}, {}
    out_done = []
    for j in range(DT):
        for c in range(C):
            i4 = (j * C + c) % 4
            xi = ctx.v(xi_off + i4 * 2048, 512, F32)
            xo = ctx.v(xo2_off + i4 * 2048, 512, F32)
            ld = b.dma("sp", xi, x_in[j * P:(j + 1) * P, c * 512:c * 512 + 512],
                       waits=[xi_war.get(i4) or y_done], sem=f"q_sxi{i4}")
            t = b.op("dve", lambda E, o=xo, y=y_sb[j][:, c * 512:(c + 1) * 512], x=xi:
                     E.tensor_tensor(o, y, x, op=ALU.add),
                     [y_done, ld, xo_war.get(i4)])
            xi_war[i4] = t
            d = b.dma("sp", x_out[j * P:(j + 1) * P, c * 512:c * 512 + 512], xo,
                      waits=[t], sem=f"q_sxo{i4}")
            xo_war[i4] = d
            out_done.append(d)
    ctx.ar.reset(mark0)
    return out_done[-4:]


def emit_l0ffn(ctx, x_in, x_out, W, xA, xB, tag="s"):
    b, T, C = ctx.b, ctx.T, ctx.C
    NW = T // 256
    NV = T // 128 + 2
    mark_call = ctx.ar.mark()

    msk = ctx.v(ctx.ar.alloc(4 * 512 * 2), 4 * 512, BF16)
    b.dma("pool", msk, ctx.dram["c_l0fm"], sem="q_gld", exclusive=False)
    blk2 = ctx.v(ctx.ar.alloc(2 * 2), 2, BF16)
    b.dma("pool", blk2, ctx.dram["c_blk2"], sem="q_gld", exclusive=False)
    sel2 = ctx.v(ctx.ar.alloc(256), 128, BF16, pn=2)
    b.dma("pool", sel2, ctx.dram["c_sel2"], sem="q_gld", exclusive=False)
    cst_ck = b.ckpt("pool", "q_gld")
    ctx.l0 = dict(msk=msk, blk2=blk2, sel2=sel2, cst_ck=cst_ck)

    xn_stride = (T + 4) * 2
    mk = ctx.ar.mark()
    xn_off0 = ctx.ar.alloc(DT * xn_stride)
    r0 = emit_ln(ctx, x_in, xn_off0, xn_stride, tag=tag + "0", fout_dram=xA)
    src_ck = r0["store_ck"]
    ctx.ar.reset(mk)

    cur, nxt = xA, xB
    for dep in range(6):
        src_ck = _l0_attn_sub(ctx, W, dep, cur, nxt, src_ck, tag)
        cur, nxt = nxt, cur
        src_ck = _l0_ffn_sub(ctx, W, dep, cur, nxt, src_ck, tag)
        cur, nxt = nxt, cur

    xi_off = ctx.ar.alloc(4 * 512 * 4)
    xf_off = ctx.ar.alloc(4 * 512 * 4)
    xo_off = ctx.ar.alloc(4 * 512 * 4)
    xi_war, xf_war, xo_war = {}, {}, {}
    for j in range(DT):
        for c in range(C):
            i4 = (j * C + c) % 4
            xi = ctx.v(xi_off + i4 * 2048, 512, F32)
            xf = ctx.v(xf_off + i4 * 2048, 512, F32)
            xo = ctx.v(xo_off + i4 * 2048, 512, F32)
            l1 = b.dma("sp", xi, x_in[j * P:(j + 1) * P, c * 512:c * 512 + 512],
                       waits=[xi_war.get(i4) or src_ck], sem=f"q_sxi{i4}")
            l2 = b.dma("sp", xf, cur[j * P:(j + 1) * P, c * 512:c * 512 + 512],
                       waits=[xf_war.get(i4) or src_ck], sem=f"q_sxf{i4 % 2}")
            t = b.op("dve", lambda E, o=xo, a=xf, x=xi: E.scalar_tensor_tensor(
                o, a, 0.5, x, op0=ALU.mult, op1=ALU.add),
                [l1, l2, xo_war.get(i4)])
            xi_war[i4] = t
            xf_war[i4] = t
            d = b.dma("sp", x_out[j * P:(j + 1) * P, c * 512:c * 512 + 512], xo,
                      waits=[t], sem=f"q_sxo{i4}")
            xo_war[i4] = d
    done = b.nopop("sp", [(f"q_sxo{i}", b.cnt.get(f"q_sxo{i}", 0)) for i in range(4)])
    ctx.ar.reset(mark_call)
    return done


def _l0_attn_sub(ctx, W, dep, cur, nxt, src_ck, tag):
    b, T, C = ctx.b, ctx.T, ctx.C
    NW = T // 256
    NV = T // 128 + 2
    mark = ctx.ar.mark()
    qkv_d = W["qkv"][dep]
    wo_d = W["wo"][dep]
    msk, blk2, sel2, cst_ck = (ctx.l0[k] for k in ("msk", "blk2", "sel2", "cst_ck"))

    xb_stride = (T + 4) * 2
    xb_off = ctx.ar.alloc(DT * xb_stride)
    xb = [ctx.v(xb_off + k * xb_stride, T + 1, BF16) for k in range(DT)]
    xb_tok = []
    for k in range(DT):
        z = b.op("pool", lambda E, t=xb[k]: E.memset(t[:, 0:1], 0.0), [src_ck])
        tk = b.dma("pool", xb[k][:, 1:T + 1], cur[k * P:(k + 1) * P, :],
                   sem=f"q_gx{k}", waits=[src_ck, z])
        xb_tok.append(tk)

    def xs(k, c0, n):
        off = 0 if k >= DT // 2 else 1
        return xb[k][:, off + c0: off + c0 + n]

    qs_col = ctx.v(ctx.ar.alloc(64), 1, F32)
    qss_col = ctx.v(ctx.ar.alloc(64), 1, F32)
    ks_col = ctx.v(ctx.ar.alloc(64), 1, F32)
    kss_col = ctx.v(ctx.ar.alloc(64), 1, F32)
    for dst, srcv, sw in ((qs_col, W["qs"][dep], 0), (qss_col, W["qs"][dep], 1),
                          (ks_col, W["ks"][dep], 0), (kss_col, W["ks"][dep], 1)):
        for half in range(2):
            p0 = half * 64
            if sw:
                b.dma("sp", dst[p0:p0 + 32, :], col1(srcv[32:64]), sem="q_sld", exclusive=False)
                b.dma("sp", dst[p0 + 32:p0 + 64, :], col1(srcv[0:32]), sem="q_sld", exclusive=False)
            else:
                b.dma("sp", dst[p0:p0 + 64, :], col1(srcv[0:64]), sem="q_sld", exclusive=False)
    scol_ck = b.ckpt("sp", "q_sld")
    csq = ctx.v(ctx.ar.alloc(T * 2), T, BF16)
    ssq = ctx.v(ctx.ar.alloc(T * 2), T, BF16)
    csk = ctx.v(ctx.ar.alloc(T * 2), T, BF16)
    ssk = ctx.v(ctx.ar.alloc(T * 2), T, BF16)
    mk_rot = ctx.ar.mark()
    rotq_raw = ctx.v(ctx.ar.alloc(2 * T * 4), 2 * T, F32)
    rotk_raw = ctx.v(ctx.ar.alloc(2 * T * 4), 2 * T, F32)
    b.dma("pool", rotq_raw, ctx.dram["c_rotq"], sem="q_gld", exclusive=False,
          waits=[src_ck])
    b.dma("pool", rotk_raw, ctx.dram["c_rotk"], sem="q_gld", exclusive=False,
          waits=[src_ck])
    rot_ck = b.ckpt("pool", "q_gld")
    tab_tok = []
    for dst, raw, c0, scal, s2 in ((csq, rotq_raw, 0, qs_col, 8.0),
                                   (ssq, rotq_raw, T, qss_col, 8.0),
                                   (csk, rotk_raw, 0, ks_col, 1.0),
                                   (ssk, rotk_raw, T, kss_col, 1.0)):
        tab_tok.append(b.op("dve", lambda E, o=dst, i=raw[:, c0:c0 + T], s=scal, s2=s2:
                            E.tensor_scalar(o, i, s, s2, op0=ALU.mult, op1=ALU.mult),
                            [rot_ck, scol_ck]))
    tab_ck = tab_tok[-1]
    ctx.ar.reset(mk_rot)

    v2_off = ctx.ar.alloc(NV * 1024 * 2)
    v2 = ctx.v(v2_off, NV * 1024, BF16)
    vz = b.op("pool", lambda E, o=v2[:, 0:2048]: E.memset(o, 0.0), [src_ck, tab_ck])
    vs_off = ctx.ar.alloc(DT * 512 * 2)
    v_tok = None
    vload_war = None
    for vc in range(2):
        vs = ctx.v(vs_off, DT * 512, BF16)
        wt = b.dma("pool", vs, wslice_ap(qkv_d, 3072, 2048 + vc * 512, 512),
                   waits=[vload_war or src_ck, tab_ck], sem="q_gw0")
        mm = None
        for tt in range(T // 128):
            bi = (tt + vc) % 2
            bk = ctx.bank(bi, 512, F32)
            for k in range(DT):
                w = [wt, xb_tok[k], ctx.bankuse.get(bi)]
                mm = b.op("pe", lambda E, o=bk, l=xs(k, tt * 128, 128),
                          r=vs[:, k * 512:(k + 1) * 512], k=k:
                          E.matmul(o, l, r, start=(k == 0), stop=(k == DT - 1)), w)
            ev = b.op("act" if tt % 2 else "dve",
                      lambda E, o=v2[:, (tt + 2) * 1024 + vc * 512:(tt + 2) * 1024 + vc * 512 + 512],
                      i=bk: (E.copy(o, i) if hasattr(E, "copy") else E.tensor_copy(o, i)),
                      [mm, vz, tab_ck])
            v_tok = ev
            ctx.bankuse[bi] = ev
        vload_war = mm

    oT_off = ctx.ar.alloc(DT * T * 2)
    oT = [ctx.v(oT_off + j * T * 2, T, BF16) for j in range(DT)]
    mark_heads = ctx.ar.mark()
    qraw = ctx.v(ctx.ar.alloc(T * 2), T, BF16)
    kraw = qraw
    swp = ctx.v(ctx.ar.alloc(T * 2), T, BF16)
    t1b = ctx.v(ctx.ar.alloc(T * 2), T, BF16)
    sq_off2 = ctx.ar.alloc(2 * 512 * 2)
    qf_off = ctx.ar.alloc(2 * T * 2)
    kf_off = ctx.ar.alloc(2 * (T + 256) * 2)
    qfin2 = [ctx.v(qf_off + i * T * 2, T, BF16) for i in range(2)]
    kfin2 = [ctx.v(kf_off + i * (T + 256) * 2, T + 256, BF16) for i in range(2)]
    kz0s = [b.op("pool", lambda E, o=kfin2[0][:, 0:256]: E.memset(o, 0.0), [src_ck]),
            b.op("pool", lambda E, o=kfin2[1][:, 0:256]: E.memset(o, 0.0), [src_ck])]
    nrow = ctx.v(ctx.ar.alloc(T * 4), T, F32, pn=2)
    nrow_bf = ctx.v(ctx.ar.alloc(T * 2), T, BF16, pn=2)
    Pf_off = ctx.ar.alloc(2 * 512 * 4)
    P2_off = ctx.ar.alloc(2 * 1024 * 2)
    PTo_off = ctx.ar.alloc(2 * 1024 * 2)
    sm_off = ctx.ar.alloc(2 * 192)
    P_f2 = [ctx.v(Pf_off + i * 2048, 512, F32) for i in range(2)]
    P22 = [ctx.v(P2_off + i * 2048, 1024, BF16) for i in range(2)]
    PT2 = [ctx.v(PTo_off + i * 2048, 1024, BF16) for i in range(2)]
    negmax2 = [ctx.v(sm_off + i * 192, 1, F32) for i in range(2)]
    sums2 = [ctx.v(sm_off + i * 192 + 64, 1, F32) for i in range(2)]
    recip2 = [ctx.v(sm_off + i * 192 + 128, 1, F32) for i in range(2)]
    qg_off = ctx.ar.alloc(DT * 512 * 2)
    kg_off = ctx.ar.alloc(DT * 512 * 2)
    qgv = ctx.v(qg_off, DT * 512, BF16)
    kgv = ctx.v(kg_off, DT * 512, BF16)
    grp_war = {"q": src_ck, "k": src_ck}
    grp_tok = {}

    fin_prev = {}
    sim_last = [None, None]
    prod_prev = [None]
    oev = None
    et_war = [None, None]
    trl = [None, None]
    ptev_war = [None, None]
    hw_c = [0]

    def emit_production(hp):
        sl = hp % 2
        if hp % 4 == 0:
            grp = hp // 4
            grp_tok["q"] = b.dma("pool", qgv, wslice_ap(qkv_d, 3072, grp * 512, 512),
                                 waits=[grp_war["q"]], sem="q_gw0")
            grp_tok["k"] = b.dma("pool", kgv, wslice_ap(qkv_d, 3072, 1024 + grp * 512, 512),
                                 waits=[grp_war["k"]], sem="q_gw1")
        po = (hp % 4) * 128
        cks = {}
        mm = None
        side_prev = [None]
        for side, raw, fin, cst, sst in (
                ("q", qraw, qfin2[sl], csq, ssq), ("k", kraw, kfin2[sl], csk, ssk)):
            gv = qgv if side == "q" else kgv
            wtok = grp_tok[side]
            ev_toks = []
            for c in range(C):
                bi = 2 + c % 2
                bk = ctx.bank(bi, 512, F32)
                mm = None
                for k in range(DT):
                    w = [wtok, xb_tok[k], ctx.bankuse.get(bi)]
                    mm = b.op("pe", lambda E, o=bk, l=gv[:, k * 512 + po:k * 512 + po + 128],
                              r=xs(k, c * 512, 512), k=k:
                              E.matmul(o, l, r, start=(k == 0), stop=(k == DT - 1)), w)
                ev = b.op("dve", lambda E, o=raw[:, c * 512:(c + 1) * 512], i=bk:
                          E.tensor_copy(o, i), [mm, prod_prev[0], side_prev[0]])
                ev_toks.append(ev)
                ctx.bankuse[bi] = ev
            raw_ck = ev_toks[-1]
            nr_toks = []
            sq_war = [None, None]
            for c in range(C):
                sq = ctx.v(sq_off2 + (c % 2) * 1024, 512, BF16)
                sq_t = b.op("dve", lambda E, o=sq, i=raw[:, c * 512:(c + 1) * 512]:
                            E.tensor_tensor(o, i, i, op=ALU.mult),
                            [ev_toks[c], sq_war[c % 2]])
                bk4 = ctx.bank(4, 512, F32, pn=2)
                mm2 = b.op("pe", lambda E, o=bk4, l=blk2, r=sq:
                           E.matmul(o, l, r, start=True, stop=True),
                           [sq_t, cst_ck, ctx.bankuse.get(4)])
                sq_war[c % 2] = mm2
                ev = b.op("dve", lambda E, o=nrow[:, c * 512:(c + 1) * 512], i=bk4:
                          E.tensor_copy(o, i), [mm2])
                nr_toks.append(ev)
                ctx.bankuse[4] = ev
            t_cl = b.op("dve", lambda E, o=nrow: E.tensor_scalar(
                o, o, 1e-24, None, op0=ALU.max), [nr_toks[-1]])
            t_ln = b.op("act", lambda E, o=nrow: E.activation(o, o, AF.Ln), [t_cl])
            t_ex = b.op("act", lambda E, o=nrow: E.activation(o, o, AF.Exp, scale=-0.5), [t_ln])
            t_bf = b.op("dve", lambda E, o=nrow_bf, i=nrow: E.tensor_copy(o, i), [t_ex])
            sw_t = []
            for g4 in range(4):
                p0 = g4 * 32
                p1 = (g4 * 32 + 32) % 64 + (64 if g4 >= 2 else 0)
                sw_t.append(b.op("act", lambda E, o=swp[p0:p0 + 32, :], i=raw[p1:p1 + 32, 0:T]:
                                 E.copy(o, i), [raw_ck, prod_prev[0]]))
            tt1 = b.op("dve", lambda E, o=t1b, i=raw[:, 0:T], cc=cst: E.tensor_tensor(
                o, i, cc, op=ALU.mult), [raw_ck, tab_ck])
            tt2 = b.op("dve", lambda E, o=swp, i=swp, ss=sst: E.tensor_tensor(
                o, i, ss, op=ALU.mult), sw_t + [tab_ck])
            tt3 = b.op("dve", lambda E, o=t1b, a=t1b, bb=swp: E.tensor_tensor(
                o, a, bb, op=ALU.add), [tt1, tt2])
            fin0 = 0 if side == "q" else 256
            f_toks = []
            for c in range(C):
                bk4b = ctx.bank(4, 512, F32)
                mmb = b.op("pe", lambda E, o=bk4b, l=sel2, r=nrow_bf[:, c * 512:(c + 1) * 512]:
                           E.matmul(o, l, r, start=True, stop=True),
                           [t_bf, cst_ck, ctx.bankuse.get(4)])
                fv = b.op("dve", lambda E, o=fin[:, fin0 + c * 512:fin0 + c * 512 + 512],
                          t=t1b[:, c * 512:(c + 1) * 512], m=bk4b: E.tensor_tensor(
                    o, t, m, op=ALU.mult),
                    [tt3, mmb, kz0s[sl] if side == "k" else None,
                     fin_prev.get((side, sl)), sim_last[sl]])
                f_toks.append(fv)
                ctx.bankuse[4] = fv
            fin_prev[(side, sl)] = f_toks[-1]
            cks[side] = f_toks[-1]
            side_prev[0] = tt2
            if hp % 4 == 3:
                grp_war[side] = mm
        prod_prev[0] = cks["k"]
        return cks

    prod_cks = {0: emit_production(0)}
    for hp in range(DT):
        sl = hp % 2
        cks = prod_cks.pop(sl)
        qfin, kfin = qfin2[sl], kfin2[sl]
        qfin_ck, kfin_ck = cks["q"], cks["k"]
        if hp + 1 < DT:
            prod_cks[(hp + 1) % 2] = emit_production(hp + 1)
        last_sim = None
        for hh in range(2):
            h = hp * 2 + hh
            pb = 64 * hh
            for w_ in range(NW):
                sl2 = hw_c[0] % 2
                hw_c[0] += 1
                P_f, P2, PT = P_f2[sl2], P22[sl2], PT2[sl2]
                negmax, sums, recip = negmax2[sl2], sums2[sl2], recip2[sl2]
                smm = {}
                for qsub in range(2):
                    bi = (w_ * 2 + qsub) % 2
                    bk = ctx.bank(bi, 512, F32)
                    mv = qsub + (2 if w_ == 0 else 0)
                    m1 = b.op("pe", lambda E, o=bk, l=qfin[pb:pb + 64, (w_ * 256 + qsub * 128):(w_ * 256 + qsub * 128 + 128)],
                              r=kfin[pb:pb + 64, w_ * 256:w_ * 256 + 512]:
                              E.matmul(o, l, r, start=True, stop=False),
                              [qfin_ck, kfin_ck, ctx.bankuse.get(bi)])
                    m2 = b.op("pe", lambda E, o=bk, l=ctx.id_bf,
                              r=msk[:, mv * 512:(mv + 1) * 512]:
                              E.matmul(o, l, r, start=False, stop=True),
                              [m1, cst_ck, ctx.id_tok])
                    smm[qsub] = m2
                    last_sim = m2
                for qsub in range(2):
                    bi = (w_ * 2 + qsub) % 2
                    bk = ctx.bank(bi, 512, F32)
                    mx = b.op("dve", lambda E, o=negmax, i=bk: E.reduce_max(
                        o, i, axis=AX.X, negate=True), [smm[qsub]])
                    et = b.op("act", lambda E, o=P_f, i=bk, bb=negmax, s=sums:
                              E.activation(o, i, AF.Exp, bias=bb, accum_out=s),
                              [mx, et_war[sl2]])
                    ctx.bankuse[bi] = et
                    rc = b.op("dve", lambda E, o=recip, i=sums: E.reciprocal(o, i), [et])
                    sc = b.op("dve", lambda E, o=P2[:, qsub * 512:(qsub + 1) * 512],
                              i=P_f, r=recip: E.tensor_scalar(o, i, r, None, op0=ALU.mult),
                              [rc, trl[sl2]])
                    et_war[sl2] = sc
                    smm[qsub] = sc
                tb = ctx.bank(6 + w_ % 2, 1024, BF16)
                ttok = None
                for jt in range(4):
                    for qsub in range(2):
                        ttok = b.op("pe", lambda E, o=tb[:, (jt * 2 + qsub) * 128:(jt * 2 + qsub + 1) * 128],
                                    i=P2[:, qsub * 512 + jt * 128:qsub * 512 + (jt + 1) * 128]:
                                    E.transpose(o, i, ctx.id_bf),
                                    [smm[0], smm[1], ctx.id_tok, ctx.bankuse.get(6 + w_ % 2)])
                trl[sl2] = ttok
                ptev = b.op("dve", lambda E, o=PT, i=tb: E.tensor_copy(o, i),
                            [ttok, ptev_war[sl2]])
                ctx.bankuse[6 + w_ % 2] = ptev
                cmm = None
                for jt in range(4):
                    bk = ctx.bank(5, 256, F32, pn=64)
                    w = [ptev, v_tok, ctx.bankuse.get(5)]
                    cmm = b.op("pe", lambda E, o=bk,
                               l=v2[:, (2 * w_ + jt) * 1024 + 64 * h:(2 * w_ + jt) * 1024 + 64 * h + 64],
                               r=PT[:, jt * 256:(jt + 1) * 256], st=(jt == 0), sp=(jt == 3):
                               E.matmul(o, l, r, start=st, stop=sp), w)
                oev = b.op("act", lambda E, o=oT[hp][pb:pb + 64, w_ * 256:(w_ + 1) * 256],
                           i=ctx.bank(5, 256, F32, pn=64): E.copy(o, i), [cmm])
                ctx.bankuse[5] = oev
                ptev_war[sl2] = cmm
        sim_last[sl] = last_sim

    o_done = oev
    ctx.ar.reset(mark_heads)
    ws2_off = ctx.ar.alloc(2 * DT * 128 * 2)
    xi_off = ctx.ar.alloc(4 * 512 * 4)
    xo_off = ctx.ar.alloc(4 * 512 * 4)
    xi_war, xo_war = {}, {}
    ws2_war = [None, None]
    for j in range(DT):
        sl = j % 2
        ws = ctx.v(ws2_off + sl * DT * 128 * 2, DT * 128, BF16)
        wt = b.dma("pool", ws, wslice_ap(wo_d, 1024, j * 128, 128),
                   waits=[ws2_war[sl] or o_done], sem=f"q_gw{sl}")
        for c in range(C):
            bi = (j * C + c) % 4
            bk = ctx.bank(bi, 512, F32)
            mm = None
            for k in range(DT):
                w = [wt, o_done, ctx.bankuse.get(bi)]
                mm = b.op("pe", lambda E, o=bk, l=ws[:, k * 128:(k + 1) * 128],
                          r=oT[k][:, c * 512:(c + 1) * 512], k=k:
                          E.matmul(o, l, r, start=(k == 0), stop=(k == DT - 1)), w)
            i4 = (j * C + c) % 4
            xi = ctx.v(xi_off + i4 * 2048, 512, F32)
            xo = ctx.v(xo_off + i4 * 2048, 512, F32)
            ld = b.dma("sp", xi, cur[j * P:(j + 1) * P, c * 512:c * 512 + 512],
                       waits=[xi_war.get(i4) or o_done], sem=f"q_sxi{i4}")
            t = b.op("dve", lambda E, o=xo, i=bk, x=xi: E.tensor_tensor(
                o, i, x, op=ALU.add), [mm, ld, xo_war.get(i4)])
            xi_war[i4] = t
            ctx.bankuse[bi] = t
            d = b.dma("sp", nxt[j * P:(j + 1) * P, c * 512:c * 512 + 512], xo,
                      waits=[t], sem=f"q_sxo{i4}")
            xo_war[i4] = d
        ws2_war[sl] = mm
    ck = b.nopop("sp", [(f"q_sxo{i}", b.cnt.get(f"q_sxo{i}", 0)) for i in range(4)])
    ctx.ar.reset(mark)
    return ck


def _l0_ffn_sub(ctx, W, dep, cur, nxt, src_ck, tag):
    b, T, C = ctx.b, ctx.T, ctx.C
    mark = ctx.ar.mark()
    w1_d, w2_d = W["w1"][dep], W["w2"][dep]
    NP = 22

    Pst_off = ctx.ar.alloc(NP * T * 2)
    Pst = [ctx.v(Pst_off + j * T * 2, T, BF16) for j in range(NP)]

    b1a, b1g, b2c = [], [], []
    for j in range(NP):
        na = 128 if j < NP - 1 else 42
        va = ctx.v(ctx.ar.alloc(64), 1, F32, pn=na)
        vg = ctx.v(ctx.ar.alloc(64), 1, F32, pn=na)
        b.dma("sp", va, col1(W["b1"][dep][j * 128:j * 128 + na]), sem="q_sld", exclusive=False)
        b.dma("sp", vg, col1(W["b1"][dep][2730 + j * 128:2730 + j * 128 + na]), sem="q_sld", exclusive=False)
        b1a.append(va)
        b1g.append(vg)
    for j in range(DT):
        v = ctx.v(ctx.ar.alloc(64), 1, F32)
        b.dma("sp", v, col1(W["b2"][dep][j * 128:(j + 1) * 128]), sem="q_sld", exclusive=False)
        b2c.append(v)
    col_ck = b.ckpt("sp", "q_sld")

    mk_ln = ctx.ar.mark()
    xn_stride = (T + 4) * 2
    xn_off = ctx.ar.alloc(DT * xn_stride)
    ln = emit_ln(ctx, cur, xn_off, xn_stride, g_dram=W["ln_g"][dep],
                 bg_dram=W["ln_b"][dep], shift=True, tag=tag + f"n{dep}",
                 src_tok=[src_ck])
    xn, xn_tok = ln["xn_tiles"], ln["xn_tok"]

    gbuf = ctx.v(ctx.ar.alloc(T * 2), T, BF16)
    wbig_off = ctx.ar.alloc(DT * 2730 * 2)
    wbig = ctx.v(wbig_off, DT * 2730, BF16)
    ln_done = ln["xn_tok"][DT - 1][ctx.C - 1]
    half_war = ln_done
    gw_tok = None
    gel_toks = []
    for half in (0, 1):
        wt = b.dma("pool", wbig, wslice_ap(w1_d, 5460, half * 2730, 2730),
                   waits=[half_war], sem="q_gw0")
        for j in range(NP):
            na = 128 if j < NP - 1 else 42
            bias_c = b1a[j] if half == 0 else b1g[j]
            for c in range(C):
                bi = 2 + (j + c) % 2
                bk = ctx.bank(bi, 512, F32, pn=na)
                mm = None
                for k in range(DT):
                    w = [wt, xn_tok[k][c], ctx.bankuse.get(bi)]
                    mm = b.op("pe", lambda E, o=bk,
                              l=wbig[:, k * 2730 + j * 128:k * 2730 + j * 128 + na],
                              r=xn[k][:, 1 + c * 512:1 + c * 512 + 512], k=k:
                              E.matmul(o, l, r, start=(k == 0), stop=(k == DT - 1)), w)
                if half == 0:
                    ev = b.op("dve", lambda E, o=Pst[j][0:na, c * 512:(c + 1) * 512],
                              i=bk, bc=bias_c: E.tensor_scalar(
                        o, i, bc, None, op0=ALU.add), [mm, col_ck])
                else:
                    ev = b.op("act", lambda E, o=gbuf[0:na, c * 512:(c + 1) * 512],
                              i=bk, bc=bias_c: E.activation(o, i, AF.Gelu, bias=bc),
                              [mm, col_ck, gw_tok])
                ctx.bankuse[bi] = ev
            if half == 1:
                gw_tok = b.op("dve", lambda E, o=Pst[j][0:na, 0:T], a=Pst[j][0:na, 0:T],
                              g=gbuf[0:na, 0:T]: E.tensor_tensor(o, a, g, op=ALU.mult),
                              [ev])
                gel_toks.append(gw_tok)
        half_war = mm
    P_ck = gw_tok

    ctx.ar.reset(mk_ln)
    w2_off = ctx.ar.alloc(NP * 1024 * 2)
    w2t = [ctx.v(w2_off + k * 1024 * 2, 1024, BF16, pn=(128 if k < NP - 1 else 42))
           for k in range(NP)]
    w2_tok = []
    for k in range(NP):
        na = 128 if k < NP - 1 else 42
        w2_tok.append(b.dma("pool", w2t[k], w2_d[k * 128:k * 128 + na, :],
                            waits=[P_ck if k < 2 else w2_tok[k - 2]], sem=f"q_gw{k % 2}"))
    xi_off = ctx.ar.alloc(4 * 512 * 4)
    xo_off = ctx.ar.alloc(4 * 512 * 4)
    xi_war, xo_war = {}, {}
    for j in range(DT):
        for c in range(C):
            bi = (j * C + c) % 2
            bk = ctx.bank(bi, 512, F32)
            mm = None
            for k in range(NP):
                na = 128 if k < NP - 1 else 42
                w = [w2_tok[k], P_ck, ctx.bankuse.get(bi)]
                mm = b.op("pe", lambda E, o=bk, l=w2t[k][:, j * 128:(j + 1) * 128],
                          r=Pst[k][0:na, c * 512:(c + 1) * 512], k=k:
                          E.matmul(o, l, r, start=(k == 0), stop=(k == NP - 1)), w)
            i4 = (j * C + c) % 4
            xi = ctx.v(xi_off + i4 * 2048, 512, F32)
            xo = ctx.v(xo_off + i4 * 2048, 512, F32)
            ld = b.dma("sp", xi, cur[j * P:(j + 1) * P, c * 512:c * 512 + 512],
                       waits=[xi_war.get(i4) or P_ck], sem=f"q_sxi{i4}")
            t = b.op("dve", lambda E, o=xo, i=bk, bc=b2c[j], x=xi:
                     E.scalar_tensor_tensor(o, i, bc, x, op0=ALU.add, op1=ALU.add),
                     [mm, ld, col_ck, xo_war.get(i4)])
            xi_war[i4] = t
            ctx.bankuse[bi] = t
            d = b.dma("sp", nxt[j * P:(j + 1) * P, c * 512:c * 512 + 512], xo,
                      waits=[t], sem=f"q_sxo{i4}")
            xo_war[i4] = d
    ck = b.nopop("sp", [(f"q_sxo{i}", b.cnt.get(f"q_sxo{i}", 0)) for i in range(4)])
    ctx.ar.reset(mark)
    return ck


def phase_barrier(b):
    sp_d = [(s, v) for s, v in b.cnt.items() if s.startswith("q_s") and v > 0]
    pl_d = [(s, v) for s, v in b.cnt.items() if (s.startswith("q_g") or s == "q_pre") and v > 0]
    t1 = b.nopop("sp", sp_d)
    t2 = b.nopop("pool", pl_d)
    base = [t1, t2]
    for e in ENGINES:
        if b.cnt.get(e, 0) > 0:
            base.append((e, b.cnt[e]))
    b.base = []
    bb = []
    for t in base:
        bb.append(t)
    b.base = bb
    return bb


def build_module(T):
    nc = bass.Bass("TRN2", target_bir_lowering=False)
    b = Bld(nc)
    ctx = Ctx(nc, b, T)
    D_ = D

    def di(name, shape):
        ap = nc.dram_tensor(name, list(shape), F32, kind="ExternalInput").ap()
        ctx.dram[name] = ap
        return ap

    x = di("x_fm", (D_, T))
    W0a = {
        "dq": di("L0_attn_W_dq", (128, D_)),
        "uq": di("L0_attn_W_uq", (D_, 128)),
        "dkv": di("L0_attn_W_dkv", (128, D_)),
        "uk": di("L0_attn_W_uk", (D_, 128)),
        "uv": di("L0_attn_W_uv", (D_, 128)),
        "o": di("L0_attn_W_o", (D_, D_)),
    }
    W0f = {
        "qkv": di("L0_ffn_Wqkv", (6, D_, 3072)),
        "qs": di("L0_ffn_q_scale", (6, 64)),
        "ks": di("L0_ffn_k_scale", (6, 64)),
        "wo": di("L0_ffn_Wo", (6, 1024, D_)),
        "ln_g": di("L0_ffn_ln_g", (6, D_)),
        "ln_b": di("L0_ffn_ln_b", (6, D_)),
        "w1": di("L0_ffn_W1", (6, D_, 5460)),
        "b1": di("L0_ffn_b1", (6, 5460)),
        "w2": di("L0_ffn_W2", (6, 2730, D_)),
        "b2": di("L0_ffn_b2", (6, D_)),
    }
    W1a_qkv = di("L1_attn_W_qkv", (D_, 3 * D_))
    W1a_out = di("L1_attn_W_out", (D_, D_))
    W1a_rel = di("L1_attn_rel_emb", (32, 16))
    W1f = {}
    for n, shp in [("w1_w", (4096, D_)), ("w2_w", (4096, D_)), ("h1_w", (4096, 4096)),
                   ("h2_w", (4096, 4096)), ("w3_w", (D_, 4096)),
                   ("w1_s", (4096,)), ("w2_s", (4096,)), ("h1_s", (4096,)),
                   ("h2_s", (4096,)), ("w3_s", (D_,))]:
        W1f[n] = di("L1_ffn_" + n, shp)
    for n, shp in [("c_eye", (P, P)), ("c_l0fm", (P, 4 * 512)), ("c_l0am", (P, 4 * 512)),
                   ("c_rotq", (P, 2 * T)), ("c_rotk", (P, 2 * T)),
                   ("c_blk2", (P, 2)), ("c_sel2", (2, P))]:
        di(n, shp)
    oht = nc.dram_tensor("c_t5oh", [32, 2 * T - 1], F32, kind="ExternalInput").ap()
    ctx.dram["c_t5oh"] = oht
    idxt = nc.dram_tensor("c_t5idx", [P, 16], I32, kind="ExternalInput").ap()
    ctx.dram["c_t5idx"] = idxt

    xo = nc.dram_tensor("xo_fm", [D_, T], F32, kind="ExternalOutput").ap()
    ya = nc.dram_tensor("ya", [D_, T], F32, kind="Internal").ap()
    yb = nc.dram_tensor("yb", [D_, T], F32, kind="Internal").ap()
    xA = nc.dram_tensor("xA", [D_, T], F32, kind="Internal").ap()
    xB = nc.dram_tensor("xB", [D_, T], F32, kind="Internal").ap()
    scr_a = nc.dram_tensor("scr_a", [4096, T], BF16, kind="Internal").ap()
    scr_b = nc.dram_tensor("scr_b", [4096, T], BF16, kind="Internal").ap()
    bias_hbm = nc.dram_tensor("bias_hbm", [16 * (2 * T - 1)], BF16, kind="Internal").ap()

    from contextlib import ExitStack
    es = ExitStack()
    ctx.sb = es.enter_context(nc.sbuf_tensor("SB", [P, SBUF_BYTES], U8))
    ctx.ps = es.enter_context(nc.psum_tensor("PS", [P, 16384], U8))
    emit_prelude(ctx)
    emit_l0ffn(ctx, x, ya, W0f, xA, xB, tag="s0")
    phase_barrier(b)
    emit_l0attn(ctx, ya, yb, W0a, tag="m0")
    phase_barrier(b)
    emit_l0ffn(ctx, yb, ya, W0f, xA, xB, tag="s1")
    phase_barrier(b)
    emit_l1ffn(ctx, ya, yb, W1f, scr_a, scr_b, tag="f0")
    phase_barrier(b)
    emit_l1attn(ctx, yb, ya, W1a_qkv, W1a_out, W1a_rel, bias_hbm, tag="a0")
    phase_barrier(b)
    done = emit_l1ffn(ctx, ya, xo, W1f, scr_a, scr_b, tag="f1")
    b.nopop("sp", done)

    sems = {}
    for name in b.cnt:
        sems[name] = es.enter_context(nc.semaphore(name))
    blk = es.enter_context(nc.Block())
    b.build_into(blk, sems)
    es.close()
    return nc, b


def host_consts(T):
    HS, WIN = 64, 256
    half = np.arange(0, HS, 2, dtype=np.float64)
    invf = 1.0 / 10000.0 ** (half / HS)
    t = np.arange(T, dtype=np.float64)
    fr = np.concatenate([t[:, None] * invf] * 2, -1)
    base = (half + 0.4 * HS) / (1.4 * HS)
    sc = base[None, :] ** ((t - T // 2) / (WIN // 2))[:, None]
    sc = np.concatenate([sc, sc], -1)
    cos, sin = np.cos(fr), np.sin(fr)
    sign = np.where(np.arange(64) < 32, -1.0, 1.0)

    def fm(a):
        m = a.T
        return np.concatenate([m, m], 0).astype(np.float32)

    rotq = np.concatenate([fm(cos * sc), fm(sin * sc * sign[None, :])], 1)
    rotk = np.concatenate([fm(cos / sc), fm(sin / sc * sign[None, :])], 1)

    l0fm = np.zeros((P, 4 * 512), np.float32)
    for v in range(4):
        qsub, w0 = v % 2, v >= 2
        for p in range(P):
            i = p + 128 * qsub
            ok = np.zeros(512, bool)
            ok[i:i + 257] = True
            if w0:
                ok[:256] = False
            l0fm[p, v * 512:(v + 1) * 512] = np.where(ok, 0.0, -3e38)

    l0am = np.zeros((P, 4 * 512), np.float32)
    for v in range(4):
        for p in range(P):
            jmax = v * 128 + p
            l0am[p, v * 512 + jmax + 1:(v + 1) * 512] = -3e38

    RLEN = 2 * T - 1
    NB, MD = 32, 128
    r = np.arange(-(T - 1), T)
    nb = NB // 2
    rb = (r > 0).astype(np.int64) * nb
    ra = np.abs(r)
    mx = nb // 2
    small = ra < mx
    large = mx + (np.log(np.maximum(ra, 1) / mx) / np.log(MD / mx) * (nb - mx)).astype(np.int64)
    large = np.minimum(large, nb - 1)
    buck = rb + np.where(small, ra, large)
    oh = np.zeros((NB, RLEN), np.float32)
    oh[buck, np.arange(RLEN)] = 1.0
    idx = np.zeros((P, 16), np.int32)
    for h in range(16):
        idx[:, h] = h * RLEN + 127 - np.arange(P)

    blk2 = np.zeros((P, 2), np.float32)
    blk2[:64, 0] = 1
    blk2[64:, 1] = 1

    return dict(c_eye=np.eye(P, dtype=np.float32), c_l0fm=l0fm, c_l0am=l0am,
                c_rotq=rotq, c_rotk=rotk, c_blk2=blk2, c_sel2=blk2.T.copy(),
                c_t5oh=oh, c_t5idx=idx)


_CACHED = {}


def _get_program():
    if "nc" not in _CACHED:
        nc, b = build_module(2048)
        _CACHED["nc"] = nc
    return _CACHED["nc"]


def kernel(**inputs):
    T = 2048
    nc = _get_program()
    consts = host_consts(T)
    x = inputs["L0_attn_x"]
    names = [
        "L0_attn_W_dq", "L0_attn_W_uq", "L0_attn_W_dkv", "L0_attn_W_uk",
        "L0_attn_W_uv", "L0_attn_W_o", "L0_ffn_Wqkv", "L0_ffn_q_scale",
        "L0_ffn_k_scale", "L0_ffn_Wo", "L0_ffn_ln_g", "L0_ffn_ln_b",
        "L0_ffn_W1", "L0_ffn_b1", "L0_ffn_W2", "L0_ffn_b2",
        "L1_attn_W_qkv", "L1_attn_W_out", "L1_attn_rel_emb",
        "L1_ffn_w1_w", "L1_ffn_w1_s", "L1_ffn_w2_w", "L1_ffn_w2_s",
        "L1_ffn_h1_w", "L1_ffn_h1_s", "L1_ffn_h2_w", "L1_ffn_h2_s",
        "L1_ffn_w3_w", "L1_ffn_w3_s",
    ]
    shared = {n: np.ascontiguousarray(np.asarray(inputs[n], dtype=np.float32))
              for n in names}
    shared.update(consts)
    in_maps = []
    for i in range(8):
        m = dict(shared)
        m["x_fm"] = np.ascontiguousarray(np.asarray(x[i], dtype=np.float32).T)
        in_maps.append(m)
    res = run_bass_kernel_spmd(nc, in_maps, core_ids=list(range(8)))
    out = np.stack([np.ascontiguousarray(r["xo_fm"].T) for r in res.results])
    return out.astype(np.float32)


# version 07
# speedup vs baseline: 1.0208x; 1.0187x over previous
import numpy as np
import ml_dtypes
import bass_rust
import concourse.bass as bass
import concourse.mybir as mybir
from concourse.bass_utils import run_bass_kernel_spmd

F32 = mybir.dt.float32
BF16 = mybir.dt.bfloat16
I32 = mybir.dt.int32
U8 = mybir.dt.uint8
AF = mybir.ActivationFunctionType
ALU = mybir.AluOpType
AX = mybir.AxisListType

P = 128
D = 1024
DT = D // P
SBUF_BYTES = 203 * 1024
SZ = {F32: 4, BF16: 2, I32: 4, U8: 1}

ENGINES = ("pe", "act", "dve", "pool", "sp")


class Bld:
    def __init__(self, nc):
        self.nc = nc
        self.stream = {e: [] for e in ENGINES}
        self.cnt = {}
        self.known = {e: {} for e in ENGINES}
        self.base = []

    def barrier(self):
        self.base = [(s, v) for s, v in self.cnt.items() if v > 0]
        return list(self.base)

    def sem(self, name):
        if name not in self.cnt:
            self.cnt[name] = 0
        return name

    def op(self, eng, fn, waits=(), inc=None, n=1):
        elide = eng != "pool"
        ws = []
        seen = {}
        for t in list(self.base) + list(waits):
            if t is None:
                continue
            s, v = t
            if v <= 0 or (elide and v <= self.known[eng].get(s, 0)):
                continue
            if v <= seen.get(s, 0):
                continue
            seen[s] = v
            ws.append((s, v))
            if elide:
                self.known[eng][s] = v
        ws = [(s, v) for s, v in ws if v == seen[s]]
        tok = None
        if inc is None:
            inc = eng
        self.sem(inc)
        self.cnt[inc] += n
        tok = (inc, self.cnt[inc])
        self.stream[eng].append((ws, fn, inc, n))
        return tok

    def nopop(self, eng, waits):
        return self.op(eng, lambda E: E.nop(), waits, inc=eng + "_ck")

    def ckpt(self, eng, sem):
        return self.nopop(eng, [(sem, self.cnt.get(sem, 0))])

    def dma(self, eng, out, in_, waits=(), sem=None, exclusive=True):
        sem = self.sem(sem or ("dq_" + eng))
        ws = list(waits)
        if exclusive and self.cnt.get(sem, 0) > 0:
            ws.append((sem, self.cnt[sem]))
        return self.op(eng, lambda E, o=out, i=in_: E.dma_start(o, i), ws,
                       inc=sem, n=16)

    def build_into(self, block, semobjs):
        decos = {"pe": block.tensor, "act": block.scalar, "dve": block.vector,
                 "pool": block.gpsimd, "sp": block.sync}
        for eng in ENGINES:
            ops = self.stream[eng]

            @decos[eng]
            def _(E, ops=ops):
                for ws, fn, inc, n in ops:
                    for s, v in ws:
                        E.wait_ge(semobjs[s], v)
                    ins = fn(E)
                    ins.then_inc(semobjs[inc], n)


class Arena:
    def __init__(self, size):
        self.size = size
        self.off = 0
        self.hi = 0

    def alloc(self, nbytes, align=64):
        self.off = (self.off + align - 1) // align * align
        r = self.off
        self.off += nbytes
        self.hi = max(self.hi, self.off)
        assert self.off <= self.size, f"arena overflow: {self.off} > {self.size}"
        return r

    def mark(self):
        return self.off

    def reset(self, mark):
        self.off = mark


class Ctx:
    def __init__(self, nc, b, T):
        self.nc = nc
        self.b = b
        self.T = T
        self.C = T // 512
        self.ar = Arena(SBUF_BYTES)
        self.sb = None
        self.ps = None
        self.dram = {}
        self.bankuse = {}
        self.stg_war = {}

    def v(self, off, cols, dtype, p0=0, pn=P):
        return self.sb[p0:p0 + pn, off:off + cols * SZ[dtype]].bitcast(dtype)

    def bank(self, i, cols, dtype, p0=0, pn=P):
        o = 2048 * i
        return self.ps[p0:p0 + pn, o:o + cols * SZ[dtype]].bitcast(dtype)


def row1(dram_1d):
    return dram_1d.rearrange("(o x) -> o x", o=1)


def col1(dram_1d):
    return dram_1d.rearrange("(x o) -> x o", o=1)


def emit_ln(ctx, xsrc, xn_off, xn_stride, g_dram=None, bg_dram=None, shift=False,
            tag="ln", want_xb=False, src_tok=(), fout_dram=None):
    b, T, C = ctx.b, ctx.T, ctx.C
    mark0 = ctx.ar.mark()

    def sl(k, c0, n, tiles):
        off = 0 if (shift and k >= DT // 2) else 1
        return tiles[k][:, off + c0: off + c0 + n]

    oc = ctx.v(ctx.ar.alloc(256), 1, BF16)
    oc_tok = b.op("pool", lambda E, o=oc: E.memset(o, 1.0), list(src_tok))

    g_rows, b_cols, g_tok, b_tok = [], [], [], []
    for k in range(DT):
        gv = ctx.v(ctx.ar.alloc(256), 128, BF16, pn=1)
        g_rows.append(gv)
        if g_dram is not None:
            b.dma("pool", gv, row1(g_dram[k * P:(k + 1) * P]), sem="q_gld",
                  exclusive=False, waits=list(src_tok))
            g_tok.append(None)
        else:
            g_tok.append(b.op("pool", lambda E, o=gv: E.memset(o, 1.0), list(src_tok)))
        if bg_dram is not None:
            bv = ctx.v(ctx.ar.alloc(64), 1, F32)
            b_cols.append(bv)
            b.dma("sp", bv, col1(bg_dram[k * P:(k + 1) * P]), sem="q_sld",
                  exclusive=False, waits=list(src_tok))
            b_tok.append(None)
        else:
            b_cols.append(None)
            b_tok.append(None)
    if g_dram is not None:
        gall = b.ckpt("pool", "q_gld")
        g_tok = [gall if t is None else t for t in g_tok]
    if bg_dram is not None:
        ball = b.ckpt("sp", "q_sld")
        b_tok = [ball for _ in b_tok]

    rstd_bf = ctx.v(ctx.ar.alloc(T * 2), T, BF16, pn=1)
    nmr_bf = ctx.v(ctx.ar.alloc(T * 2), T, BF16, pn=1)
    srows = ctx.ar.alloc(4 * 512 * 4)
    rstd_tok = [None] * C
    nmr_tok = [None] * C

    mark1 = ctx.ar.mark()
    xb_stride = (T + 4) * 2
    xb_off = ctx.ar.alloc(DT * xb_stride)
    xb_tiles = [ctx.v(xb_off + k * xb_stride, T + 1, BF16) for k in range(DT)]
    xb_tok = []
    for k in range(DT):
        z = b.op("pool", lambda E, t=xb_tiles[k]: E.memset(t[:, 0:1], 0.0),
                 list(src_tok))
        tk = b.dma("pool", xb_tiles[k][:, 1:T + 1], xsrc[k * P:(k + 1) * P, :],
                   sem=f"q_gx{k}", waits=list(src_tok))
        xb_tok.append(tk)

    sq_off = ctx.ar.alloc(2 * 512 * 2)
    sq_bufs = [ctx.v(sq_off + i * 1024, 512, BF16) for i in range(2)]
    sq_war = [None, None]

    bank_consumed = ctx.bankuse

    for c in range(C):
        bmu_i, bsq_i = (0, 1) if c % 2 == 0 else (2, 3)
        bmu = ctx.bank(bmu_i, 512, F32, pn=1)
        bsq = ctx.bank(bsq_i, 512, F32, pn=1)
        mu_tok = sq_mm_tok = None
        for k in range(DT):
            w = [xb_tok[k], oc_tok, bank_consumed.get(bmu_i)]
            mu_tok = b.op("pe", lambda E, o=bmu, x=sl(k, c * 512, 512, xb_tiles),
                          k=k: E.matmul(o, oc, x, start=(k == 0), stop=(k == DT - 1)), w)
        for k in range(DT):
            i2 = k % 2
            sq = sq_bufs[i2]
            sqt = b.op("dve", lambda E, s=sq, x=sl(k, c * 512, 512, xb_tiles):
                       E.tensor_tensor(s, x, x, op=ALU.mult),
                       [xb_tok[k], sq_war[i2]])
            w = [sqt, oc_tok, bank_consumed.get(bsq_i)]
            sq_mm_tok = b.op("pe", lambda E, o=bsq, s=sq, k=k:
                             E.matmul(o, oc, s, start=(k == 0), stop=(k == DT - 1)), w)
            sq_war[i2] = sq_mm_tok
        mu_row = ctx.v(srows, 512, F32, pn=1)
        var_row = ctx.v(srows + 2048, 512, F32, pn=1)
        musq = ctx.v(srows + 4096, 512, F32, pn=1)
        rstd_f = ctx.v(srows + 6144, 512, F32, pn=1)
        t1 = b.op("dve", lambda E, o=mu_row, i=bmu: E.tensor_scalar(
            o, i, 1.0 / D, None, op0=ALU.mult), [mu_tok])
        bank_consumed[bmu_i] = t1
        t2 = b.op("dve", lambda E, o=var_row, i=bsq: E.tensor_scalar(
            o, i, 1.0 / D, 1e-5, op0=ALU.mult, op1=ALU.add), [sq_mm_tok])
        bank_consumed[bsq_i] = t2
        t3 = b.op("dve", lambda E, o=musq, i=mu_row: E.tensor_tensor(
            o, i, i, op=ALU.mult), [t1])
        t4 = b.op("dve", lambda E, o=var_row, a=var_row, m=musq: E.tensor_tensor(
            o, a, m, op=ALU.subtract), [t2, t3])
        t5 = b.op("act", lambda E, o=rstd_f, i=var_row: E.activation(o, i, AF.Ln), [t4])
        t6 = b.op("act", lambda E, o=rstd_f, i=rstd_f: E.activation(
            o, i, AF.Exp, scale=-0.5), [t5])
        rstd_tok[c] = b.op("dve", lambda E, o=rstd_bf[:, c * 512:c * 512 + 512],
                           i=rstd_f: E.tensor_copy(o, i), [t6])
        t7 = b.op("dve", lambda E, o=nmr_bf[:, c * 512:c * 512 + 512], m=mu_row,
                  r=rstd_f: E.tensor_tensor(o, m, r, op=ALU.mult), [t6, t3])
        nmr_tok[c] = b.op("dve", lambda E, o=nmr_bf[:, c * 512:c * 512 + 512]:
                          E.tensor_scalar(o, o, -1.0, None, op0=ALU.mult), [t7])

    sq_done = sq_mm_tok

    tmp_off = ctx.ar.alloc(2 * 512 * 4)
    xn_tiles = [ctx.v(xn_off + k * xn_stride, T + 1, BF16) for k in range(DT)]
    if fout_dram is not None:
        fo_off = ctx.ar.alloc(4 * 512 * 4)
        fo_war = {}
    xn_tok = [[None] * C for _ in range(DT)]
    xn_z = [None] * DT
    for k in range(DT):
        for c in range(C):
            kc = k * C + c
            bR_i, bM_i = (4, 5) if kc % 2 == 0 else (6, 7)
            bR = ctx.bank(bR_i, 512, F32)
            bM = ctx.bank(bM_i, 512, F32)
            rtok = b.op("pe", lambda E, o=bR, g=g_rows[k],
                        r=rstd_bf[:, c * 512:c * 512 + 512]: E.matmul(
                o, g, r, start=True, stop=True),
                [rstd_tok[c], g_tok[k], bank_consumed.get(bR_i)])
            mtok = b.op("pe", lambda E, o=bM, g=g_rows[k],
                        m=nmr_bf[:, c * 512:c * 512 + 512]: E.matmul(
                o, g, m, start=True, stop=True),
                [nmr_tok[c], bank_consumed.get(bM_i)])
            xsl = sl(k, c * 512, 512, xb_tiles)
            tmp = ctx.v(tmp_off + (kc % 2) * 2048, 512, F32)
            a1 = b.op("dve", lambda E, o=tmp, x=xsl, R=bR: E.tensor_tensor(
                o, x, R, op=ALU.mult), [xb_tok[k], rtok])
            bank_consumed[bR_i] = a1
            if fout_dram is not None:
                i4 = kc % 4
                fo = ctx.v(fo_off + i4 * 2048, 512, F32)
                a2 = b.op("dve", lambda E, o=fo, t=tmp, M=bM: E.tensor_tensor(
                    o, t, M, op=ALU.add), [a1, mtok, fo_war.get(i4)])
                d = b.dma("sp", fout_dram[k * P:(k + 1) * P, c * 512:c * 512 + 512],
                          fo, waits=[a2], sem=f"q_sxo{i4}")
                fo_war[i4] = d
            elif b_cols[k] is not None:
                a2 = b.op("dve", lambda E, o=xn_tiles[k][:, 1 + c * 512:1 + c * 512 + 512],
                          t=tmp, bb=b_cols[k], M=bM: E.scalar_tensor_tensor(
                    o, t, bb, M, op0=ALU.add, op1=ALU.add), [a1, mtok, b_tok[k]])
            else:
                a2 = b.op("dve", lambda E, o=xn_tiles[k][:, 1 + c * 512:1 + c * 512 + 512],
                          t=tmp, M=bM: E.tensor_tensor(o, t, M, op=ALU.add),
                          [a1, mtok])
            bank_consumed[bM_i] = a2
            xn_tok[k][c] = a2

    store_ck = None
    if fout_dram is not None:
        store_ck = b.nopop("sp", [(f"q_sxo{i}", b.cnt.get(f"q_sxo{i}", 0)) for i in range(4)])
    ctx.ar.reset(mark0)
    return dict(xn_tiles=xn_tiles, xn_tok=xn_tok, xn_z=xn_z, store_ck=store_ck)


def emit_prelude(ctx):
    b = ctx.b
    idb = ctx.v(ctx.ar.alloc(128 * 2), 128, BF16)
    t = b.dma("pool", idb, ctx.dram["c_eye"], sem="q_pre")
    ctx.id_bf = idb
    ctx.id_tok = ("q_pre", b.cnt["q_pre"])


def tlin_stage(ctx, w_dram, KT, MT, NIN, rhs_fn, out_fn, rhs_deps, tag,
               tr_banks=(0, 1), mm_banks=(2, 3), war0=(), scratch_off=None):
    b, C = ctx.b, ctx.C
    G = 8
    ngrp = (KT + G - 1) // G
    nwt = 2 if KT <= 8 else 1
    mark = ctx.ar.mark()
    wraw_sz = NIN * 2
    wT_sz = KT * 128 * 2
    if scratch_off is not None:
        wraw_off = scratch_off
        wT_off = scratch_off + 2 * wraw_sz
    else:
        wraw_off = ctx.ar.alloc(2 * wraw_sz)
        wT_off = ctx.ar.alloc(nwt * wT_sz)
    wload_tok = [None, None]
    wT_tok = [[None] * ngrp, [None] * ngrp]
    slot_last = [None, None]
    mm_last = None
    for j in range(MT):
        sl = j % 2
        wraw = ctx.v(wraw_off + sl * wraw_sz, NIN, BF16)
        wT = ctx.v(wT_off + (sl % nwt) * wT_sz, KT * 128, BF16)
        war = [slot_last[sl]] if slot_last[sl] is not None else list(war0)
        wload_tok[sl] = b.dma("pool", wraw, w_dram[j * P:(j + 1) * P, :],
                              waits=war, sem=f"q_gw{sl}")
        for g in range(ngrp):
            tb_i = tr_banks[(j * ngrp + g) % 2]
            tb = ctx.bank(tb_i, 1024, BF16)
            n_in_g = min(G, KT - g * G)
            ttok = None
            for ki in range(n_in_g):
                k = g * G + ki
                w = [wload_tok[sl], ctx.id_tok, ctx.bankuse.get(tb_i)]
                ttok = b.op("pe", lambda E, o=tb[:, ki * 128:(ki + 1) * 128],
                            i=wraw[:, k * 128:(k + 1) * 128]:
                            E.transpose(o, i, ctx.id_bf), w)
            ev = b.op("act", lambda E, o=wT[:, g * G * 128:(g * G + n_in_g) * 128],
                      i=tb[:, 0:n_in_g * 128]: E.copy(o, i), [ttok])
            ctx.bankuse[tb_i] = ev
            wT_tok[sl][g] = ev
        for c in range(C):
            mb_i = mm_banks[(j * C + c) % 2]
            mb = ctx.bank(mb_i, 512, F32)
            mm = None
            for k in range(KT):
                w = [wT_tok[sl][k // G], ctx.bankuse.get(mb_i)] + (rhs_deps if k == 0 and c == 0 else [])
                mm = b.op("pe", lambda E, o=mb, l=wT[:, k * 128:(k + 1) * 128],
                          r=rhs_fn(k, c), k=k:
                          E.matmul(o, l, r, start=(k == 0), stop=(k == KT - 1)), w)
            mm_last = mm
            ctx.bankuse[mb_i] = out_fn(j, c, mb, mm)
        slot_last[sl] = mm_last
    ctx.ar.reset(mark)
    return mm_last


def emit_scale_cols(ctx, s_dram, ntiles, half=False, tag="sc"):
    b = ctx.b
    views = []
    for j in range(ntiles):
        v = ctx.v(ctx.ar.alloc(64), 1, F32)
        views.append(v)
        b.dma("sp", v, col1(s_dram[j * P:(j + 1) * P]), sem="q_sld", exclusive=False)
    tok = b.ckpt("sp", "q_sld")
    toks = [tok] * ntiles
    if half:
        newtok = []
        for j in range(ntiles):
            t = b.op("dve", lambda E, o=views[j]: E.tensor_scalar(
                o, o, 0.5, None, op0=ALU.mult), [tok])
            newtok.append(t)
        toks = newtok
    return views, toks


def emit_l1ffn(ctx, x_in, x_out, W, scr_a, scr_b, tag="f"):
    b, T, C = ctx.b, ctx.T, ctx.C
    HT = 32
    mark0 = ctx.ar.mark()

    s1_v, s1_t = emit_scale_cols(ctx, W["w1_s"], HT)
    s2_v, s2_t = emit_scale_cols(ctx, W["w2_s"], HT)
    h1_v, h1_t = emit_scale_cols(ctx, W["h1_s"], HT)
    h2_v, h2_t = emit_scale_cols(ctx, W["h2_s"], HT)
    w3_v, w3_t = emit_scale_cols(ctx, W["w3_s"], DT, half=True)

    tl_off = ctx.ar.alloc(2 * 4096 * 2 + 32 * 128 * 2)

    xn_stride = (T + 4) * 2
    xn_off = ctx.ar.alloc(DT * xn_stride)
    ln = emit_ln(ctx, x_in, xn_off, xn_stride, tag=tag + "l")
    xn, xn_tok, xn_z = ln["xn_tiles"], ln["xn_tok"], ln["xn_z"]
    xn_all = [xn_tok[k][c] for k in range(DT) for c in range(ctx.C)]
    mark_ln = ctx.ar.mark()

    big_sz = T * 2
    big_off = ctx.ar.alloc(HT * big_sz)
    big = [ctx.v(big_off + j * big_sz, T, BF16) for j in range(HT)]

    def rhs_xn(k, c):
        return xn[k][:, 1 + c * 512: 1 + c * 512 + 512]

    out_last = [None]

    def out_t1(j, c, ps, mm):
        t = b.op("act", lambda E, o=big[j][:, c * 512:c * 512 + 512], i=ps:
                 E.activation(o, i, AF.Silu, scale=s1_v[j]), [mm, s1_t[j]])
        out_last[0] = t
        return t

    st1_mm = tlin_stage(ctx, W["w1_w"], DT, HT, D, rhs_xn, out_t1, xn_all, tag + "1",
                        war0=[xn_all[-1]], scratch_off=tl_off)
    st1_out = out_last[0]

    def out_h(j, c, ps, mm):
        t = b.op("dve", lambda E, o=big[j][:, c * 512:c * 512 + 512], i=ps,
                 s=s2_v[j], t1s=big[j][:, c * 512:c * 512 + 512]:
                 E.scalar_tensor_tensor(o, i, s, t1s, op0=ALU.mult, op1=ALU.mult),
                 [mm, s2_t[j], out_last[0] if out_last[0] is not None else None])
        out_last[0] = t
        return t

    st2_mm = tlin_stage(ctx, W["w2_w"], DT, HT, D, rhs_xn, out_h, xn_all, tag + "2",
                        tr_banks=(4, 5), mm_banks=(6, 7), war0=[st1_mm],
                        scratch_off=tl_off)
    st2_out = out_last[0]


    if DT * xn_stride >= 20480:
        st_off = xn_off
        xi_off = xn_off + 4 * 1024
        xo_off = xn_off + 4 * 1024 + 4 * 2048
    else:
        st_off = ctx.ar.alloc(4 * 1024)
        xi_off = ctx.ar.alloc(4 * 2048)
        xo_off = ctx.ar.alloc(4 * 2048)

    def mk_store_out(scale_v, scale_t, scr, qsem):
        def out(j, c, ps, mm):
            i = (j * C + c) % 4
            stg = ctx.v(st_off + i * 1024, 512, BF16)
            t = b.op("act", lambda E, o=stg, i=ps: E.activation(
                o, i, AF.Gelu, scale=scale_v[j]),
                [mm, scale_t[j], ctx.stg_war.get(i)])
            d = b.dma("sp", scr[j * P:(j + 1) * P, c * 512:c * 512 + 512], stg,
                      waits=[t], sem=f"{qsem}{i}")
            ctx.stg_war[i] = d
            return t
        return out

    def rhs_big(k, c):
        return big[k][:, c * 512:c * 512 + 512]

    last = tlin_stage(ctx, W["h1_w"], HT, HT, 4096, rhs_big,
                      mk_store_out(h1_v, h1_t, scr_a, "q_s3"), [st2_out], tag + "3",
                      war0=[st2_mm], scratch_off=tl_off)

    g1_tok = []
    for j in range(HT):
        g1_tok.append(b.dma("sp", big[j], scr_a[j * P:(j + 1) * P, :],
                            waits=[last] + [ctx.stg_war.get(i) for i in range(4)],
                            sem="q_sg", exclusive=False))
    g1_all = b.ckpt("sp", "q_sg")
    last4 = tlin_stage(ctx, W["h2_w"], HT, HT, 4096, rhs_big,
                       mk_store_out(h2_v, h2_t, scr_b, "q_s4"), [g1_all], tag + "4",
                       tr_banks=(4, 5), mm_banks=(6, 7), war0=[last],
                       scratch_off=tl_off)

    g2_tok = []
    for j in range(HT):
        g2_tok.append(b.dma("sp", big[j], scr_b[j * P:(j + 1) * P, :],
                            waits=[last4] + [ctx.stg_war.get(i) for i in range(4)],
                            sem="q_sg2", exclusive=False))
    g2_all = b.ckpt("sp", "q_sg2")

    xi_war = {}
    xo_war = {}

    def out_res(j, c, ps, mm):
        i = (j * C + c) % 4
        xi = ctx.v(xi_off + i * 2048, 512, F32)
        xo = ctx.v(xo_off + i * 2048, 512, F32)
        ld = b.dma("sp", xi, x_in[j * P:(j + 1) * P, c * 512:c * 512 + 512],
                   waits=[xi_war.get(i) or g2_all], sem=f"q_sxi{i}")
        t = b.op("dve", lambda E, o=xo, i=ps, s=w3_v[j], x=xi:
                 E.scalar_tensor_tensor(o, i, s, x, op0=ALU.mult, op1=ALU.add),
                 [mm, ld, w3_t[j], xo_war.get(i)])
        xi_war[i] = t
        d = b.dma("sp", x_out[j * P:(j + 1) * P, c * 512:c * 512 + 512], xo,
                  waits=[t], sem=f"q_sxo{i}")
        xo_war[i] = d
        return t

    tlin_stage(ctx, W["w3_w"], HT, DT, 4096, rhs_big, out_res, [g2_all], tag + "5",
               war0=[last4], scratch_off=tl_off)
    out_done = [("q_sxo%d" % i, b.cnt.get("q_sxo%d" % i, 0)) for i in range(4)]
    ctx.ar.reset(mark0)
    return out_done


def wslice_ap(w_dram, ncols_total, col0, ncols, ktiles=DT):
    c = w_dram.copy()
    c.ap = bass_rust.VecI64Pair([[ncols_total, P], [ncols_total * P, ktiles], [1, ncols]])
    c.offset = w_dram.offset + col0
    return c


def emit_l1attn(ctx, x_in, x_out, W_qkv, W_out, rel_emb, bias_hbm, tag="a"):
    b, T, C = ctx.b, ctx.T, ctx.C
    TT_ = T // 128
    NG = (TT_ + 7) // 8
    mark0 = ctx.ar.mark()
    RLEN = 2 * T - 1
    WT = 2 * T - 128
    SKB = T - 128
    bias2d = bias_hbm.rearrange("(a b) -> a b", b=RLEN)
    bias_fl = bias_hbm.rearrange("(x o) -> x o", o=1)

    idx = ctx.v(ctx.ar.alloc(64), 16, I32)
    idx_t = b.dma("sp", idx, ctx.dram["c_t5idx"], sem="q_sld")

    mark_b = ctx.ar.mark()
    oh = ctx.v(ctx.ar.alloc(RLEN * 2 + 64), RLEN, BF16, pn=32)
    b.dma("pool", oh, ctx.dram["c_t5oh"], sem="q_gld", exclusive=False)
    rel = ctx.v(ctx.ar.alloc(64), 16, BF16, pn=32)
    b.dma("pool", rel, rel_emb, sem="q_gld", exclusive=False)
    oh_t = rel_t = b.ckpt("pool", "q_gld")
    brow = ctx.v(ctx.ar.alloc(RLEN * 2 + 64), RLEN, BF16, pn=16)
    ev_last = None
    for c in range((RLEN + 511) // 512):
        n = min(512, RLEN - c * 512)
        bk = ctx.bank(c % 2, n, F32, pn=16)
        mm = b.op("pe", lambda E, o=bk, r=rel, h=oh[:, c * 512:c * 512 + n]:
                  E.matmul(o, r, h, start=True, stop=True),
                  [oh_t, rel_t, ctx.bankuse.get(c % 2)])
        ev_last = b.op("dve", lambda E, o=brow[:, c * 512:c * 512 + n], i=bk:
                       E.tensor_copy(o, i), [mm])
        ctx.bankuse[c % 2] = ev_last
    b.dma("sp", bias2d, brow, waits=[ev_last], sem="q_sbias")
    bias_ck = b.ckpt("sp", "q_sbias")
    ctx.ar.reset(mark_b)

    xn_stride = (T + 4) * 2
    xn_off = ctx.ar.alloc(DT * xn_stride)
    ln = emit_ln(ctx, x_in, xn_off, xn_stride, tag=tag + "l", src_tok=[bias_ck])
    xn, xn_tok = ln["xn_tiles"], ln["xn_tok"]

    q_off = ctx.ar.alloc(DT * T * 2)
    k_off = ctx.ar.alloc(DT * T * 2)
    v_off = ctx.ar.alloc(TT_ * 1024 * 2)
    o_off = ctx.ar.alloc(DT * T * 2)
    q_sb = [ctx.v(q_off + j * T * 2, T, BF16) for j in range(DT)]
    k_sb = [ctx.v(k_off + j * T * 2, T, BF16) for j in range(DT)]
    v_sb = [ctx.v(v_off + j * 1024 * 2, 1024, BF16) for j in range(TT_)]
    o_sb = [ctx.v(o_off + j * T * 2, T, BF16) for j in range(DT)]
    mark1 = ctx.ar.mark()

    ws_off = ctx.ar.alloc(2 * DT * 128 * 2)
    vs_off = ctx.ar.alloc(DT * 512 * 2)
    q_tok = [[None] * C for _ in range(DT)]
    k_tok = [[None] * C for _ in range(DT)]
    v_tok = [None] * TT_
    ln_done = xn_tok[DT - 1][C - 1]
    ws_war = [ln_done, ln_done]
    mmb = 0
    for dst, base, toks in ((q_sb, 0, q_tok), (k_sb, 1024, k_tok)):
        for j in range(DT):
            sl = j % 2
            ws = ctx.v(ws_off + sl * DT * 128 * 2, DT * 128, BF16)
            wt = b.dma("pool", ws, wslice_ap(W_qkv, 3072, base + j * 128, 128),
                       waits=[ws_war[sl]], sem=f"q_gw{sl}")
            mm = None
            for c in range(C):
                bi = 2 + (mmb % 2)
                mmb += 1
                bk = ctx.bank(bi, 512, F32)
                for k in range(DT):
                    w = [wt, xn_tok[k][c], ctx.bankuse.get(bi)]
                    mm = b.op("pe", lambda E, o=bk, l=ws[:, k * 128:(k + 1) * 128],
                              r=xn[k][:, 1 + c * 512:1 + c * 512 + 512], k=k:
                              E.matmul(o, l, r, start=(k == 0), stop=(k == DT - 1)), w)
                ev = b.op("act" if (j + c) % 2 else "dve",
                          lambda E, o=dst[j][:, c * 512:c * 512 + 512], i=bk:
                          (E.copy(o, i) if hasattr(E, "copy") else E.tensor_copy(o, i)),
                          [mm])
                toks[j][c] = ev
                ctx.bankuse[bi] = ev
            ws_war[sl] = mm
    vs_war = ln_done
    for vc in range(2):
        vs = ctx.v(vs_off, DT * 512, BF16)
        wt = b.dma("pool", vs, wslice_ap(W_qkv, 3072, 2048 + vc * 512, 512),
                   waits=[vs_war], sem="q_gw0")
        for tt in range(TT_):
            bi = 4 + (tt % 2)
            bk = ctx.bank(bi, 512, F32)
            mm = None
            for k in range(DT):
                w = [wt, xn_tok[k][(tt * 128) // 512], ctx.bankuse.get(bi)]
                mm = b.op("pe", lambda E, o=bk, l=xn[k][:, 1 + tt * 128:1 + (tt + 1) * 128],
                          r=vs[:, k * 512:(k + 1) * 512], k=k:
                          E.matmul(o, l, r, start=(k == 0), stop=(k == DT - 1)), w)
            ev = b.op("act" if tt % 2 else "dve",
                      lambda E, o=v_sb[tt][:, vc * 512:(vc + 1) * 512], i=bk:
                      (E.copy(o, i) if hasattr(E, "copy") else E.tensor_copy(o, i)),
                      [mm])
            v_tok[tt] = ev
            ctx.bankuse[bi] = ev
        vs_war = mm

    ctx.ar.reset(mark1)
    bsk_off = ctx.ar.alloc(2 * WT * 2)
    P_off = ctx.ar.alloc(2 * T * 2)
    Pf_off = ctx.ar.alloc(T * 4)
    PT_off = ctx.ar.alloc(2 * TT_ * 128 * 2)
    sm_off = ctx.ar.alloc(2 * 192)
    P_bf2 = [ctx.v(P_off + i * T * 2, T, BF16) for i in range(2)]
    P_f2 = [ctx.v(Pf_off, T, F32), ctx.v(Pf_off, T, F32)]
    PT_sb2 = [ctx.v(PT_off + i * TT_ * 128 * 2, TT_ * 128, BF16) for i in range(2)]
    negmax2 = [ctx.v(sm_off + i * 192, 1, F32) for i in range(2)]
    sums2 = [ctx.v(sm_off + i * 192 + 64, 1, F32) for i in range(2)]
    recip2 = [ctx.v(sm_off + i * 192 + 128, 1, F32) for i in range(2)]
    bsk_tok = [None, None]
    bsk_war = [None, None]
    tr_tok = [None, None]
    sc_tok_prev = [None, None]
    mx_prev = [None, None]
    qq = 0
    cmm = None
    oev_tok = None
    for h in range(16):
        bsk = ctx.v(bsk_off + (h % 2) * WT * 2, WT, BF16)
        war = [bsk_war[h % 2] or vs_war, bias_ck, idx_t]
        bsk_tok[h % 2] = b.op(
            "pool",
            lambda E, o=bsk, i=bias_fl, ix=idx[:, h:h + 1]: E.indirect_dma_start(
                o, None, i, bass.IndirectOffsetOnAxis(ap=ix, axis=0)),
            war, inc=f"q_gbsk{h % 2}", n=16)
        jt2, pb2 = h // 2, 64 * (h % 2)
        for qt in range(TT_):
            sl2 = qq % 2
            qq += 1
            P_bf, P_f, PT_sb = P_bf2[sl2], P_f2[sl2], PT_sb2[sl2]
            negmax, sums, recip = negmax2[sl2], sums2[sl2], recip2[sl2]
            smm = None
            for c in range(C):
                bk = ctx.bank(c, 512, F32)
                w = [q_tok[jt2][(qt * 128) // 512], k_tok[jt2][c],
                     ctx.bankuse.get(c), bsk_tok[h % 2]]
                smm = b.op("pe", lambda E, o=bk,
                           l=q_sb[jt2][pb2:pb2 + 64, qt * 128:(qt + 1) * 128],
                           r=k_sb[jt2][pb2:pb2 + 64, c * 512:(c + 1) * 512]:
                           E.matmul(o, l, r, start=True, stop=False), w)
                smm = b.op("pe", lambda E, o=bk, l=ctx.id_bf,
                           r=bsk[:, SKB - 128 * qt + c * 512:SKB - 128 * qt + c * 512 + 512]:
                           E.matmul(o, l, r, start=False, stop=True), [smm, ctx.id_tok])
            spsum = ctx.ps[:, 0:C * 2048].bitcast(F32)
            mx = b.op("dve", lambda E, o=negmax, i=spsum: E.reduce_max(
                o, i, axis=AX.X, negate=True), [smm, mx_prev[sl2]])
            et = b.op("act", lambda E, o=P_f, i=spsum, bb=negmax, s=sums:
                      E.activation(o, i, AF.Exp, bias=bb, accum_out=s),
                      [mx, sc_tok_prev[0], sc_tok_prev[1]])
            for c in range(C):
                ctx.bankuse[c] = et
            rc = b.op("dve", lambda E, o=recip, i=sums: E.reciprocal(o, i), [et])
            sc_tok = b.op("dve", lambda E, o=P_bf, i=P_f, r=recip: E.tensor_scalar(
                o, i, r, None, op0=ALU.mult), [rc, tr_tok[sl2]])
            sc_tok_prev[sl2] = sc_tok
            mx_prev[sl2] = rc
            for g in range(NG):
                tb_i = 6 + (g % 2)
                tb = ctx.bank(tb_i, 1024, BF16)
                nin = min(8, TT_ - g * 8)
                ttok = None
                for ki in range(nin):
                    jt = g * 8 + ki
                    w = [sc_tok, ctx.id_tok, ctx.bankuse.get(tb_i)]
                    ttok = b.op("pe", lambda E, o=tb[:, ki * 128:(ki + 1) * 128],
                                i=P_bf[:, jt * 128:(jt + 1) * 128]:
                                E.transpose(o, i, ctx.id_bf), w)
                g_ev = b.op("dve", lambda E, o=PT_sb[:, g * 1024:g * 1024 + nin * 128],
                            i=tb[:, 0:nin * 128]: E.tensor_copy(o, i), [ttok])
                ctx.bankuse[tb_i] = g_ev
                tr_tok[sl2] = ttok
            pt_ck = g_ev
            for jt in range(TT_):
                bk = ctx.bank(4, 128, F32, pn=64)
                w = [pt_ck, v_tok[jt], ctx.bankuse.get(4)]
                cmm = b.op("pe", lambda E, o=bk, l=v_sb[jt][:, 64 * h:64 * h + 64],
                           r=PT_sb[:, jt * 128:(jt + 1) * 128], jt=jt:
                           E.matmul(o, l, r, start=(jt == 0), stop=(jt == TT_ - 1)), w)
            oev_tok = b.op("act", lambda E, o=o_sb[jt2][pb2:pb2 + 64, qt * 128:(qt + 1) * 128],
                           i=ctx.bank(4, 128, F32, pn=64): E.copy(o, i), [cmm])
            ctx.bankuse[4] = oev_tok
        bsk_war[h % 2] = smm

    o_done = oev_tok
    ctx.ar.reset(mark1)

    ws2_off = ctx.ar.alloc(2 * DT * 128 * 2)
    xi_off = ctx.ar.alloc(4 * 512 * 4)
    xo2_off = ctx.ar.alloc(4 * 512 * 4)
    xi_war, xo_war = {}, {}
    ws2_war = [None, None]
    out_done = []
    for j in range(DT):
        sl = j % 2
        ws = ctx.v(ws2_off + sl * DT * 128 * 2, DT * 128, BF16)
        wt = b.dma("pool", ws, wslice_ap(W_out, 1024, j * 128, 128),
                   waits=[ws2_war[sl] or cmm], sem=f"q_gw{sl}")
        for c in range(C):
            bi = (j * C + c) % 4
            bk = ctx.bank(bi, 512, F32)
            mm = None
            for k in range(DT):
                w = [wt, o_done, ctx.bankuse.get(bi)]
                mm = b.op("pe", lambda E, o=bk, l=ws[:, k * 128:(k + 1) * 128],
                          r=o_sb[k][:, c * 512:(c + 1) * 512], k=k:
                          E.matmul(o, l, r, start=(k == 0), stop=(k == DT - 1)), w)
            i4 = (j * C + c) % 4
            xi = ctx.v(xi_off + i4 * 2048, 512, F32)
            xo = ctx.v(xo2_off + i4 * 2048, 512, F32)
            ld = b.dma("sp", xi, x_in[j * P:(j + 1) * P, c * 512:c * 512 + 512],
                       waits=[xi_war.get(i4) or cmm], sem=f"q_sxi{i4}")
            t = b.op("dve", lambda E, o=xo, i=bk, x=xi: E.tensor_tensor(
                o, i, x, op=ALU.add), [mm, ld, xo_war.get(i4)])
            xi_war[i4] = t
            ctx.bankuse[bi] = t
            d = b.dma("sp", x_out[j * P:(j + 1) * P, c * 512:c * 512 + 512], xo,
                      waits=[t], sem=f"q_sxo{i4}")
            xo_war[i4] = d
            out_done.append(d)
        ws2_war[sl] = mm
    ctx.ar.reset(mark0)
    return out_done[-4:]


def emit_l0attn(ctx, x_in, x_out, W, tag="m"):
    b, T, C = ctx.b, ctx.T, ctx.C
    TT_ = T // 128
    mark0 = ctx.ar.mark()

    qup_off = ctx.ar.alloc(DT * T * 2)
    y_off = ctx.ar.alloc(DT * T * 2)
    ckvT_off = ctx.ar.alloc(T * 2)
    ckv_off = ctx.ar.alloc(TT_ * 128 * 2)
    qup_sb = [ctx.v(qup_off + j * T * 2, T, BF16) for j in range(DT)]
    y_sb = [ctx.v(y_off + j * T * 2, T, BF16) for j in range(DT)]
    ckvT = ctx.v(ckvT_off, T, BF16)
    ckv_sb = ctx.v(ckv_off, TT_ * 128, BF16)

    mark1 = ctx.ar.mark()

    keff = ctx.v(ctx.ar.alloc(8 * 128 * 2), 8 * 128, BF16)
    veff = ctx.v(ctx.ar.alloc(1024 * 2), 1024, BF16)
    dqT = ctx.v(ctx.ar.alloc(8 * 128 * 2), 8 * 128, BF16)
    dkvT = ctx.v(ctx.ar.alloc(8 * 128 * 2), 8 * 128, BF16)
    uqT = ctx.v(ctx.ar.alloc(8 * 128 * 2), 8 * 128, BF16)
    msk = ctx.v(ctx.ar.alloc(4 * 512 * 2), 4 * 512, BF16)
    mark_prep = ctx.ar.mark()

    dq_raw = ctx.v(ctx.ar.alloc(1024 * 2), 1024, BF16)
    dkv_raw = ctx.v(ctx.ar.alloc(1024 * 2), 1024, BF16)
    uq_raw = ctx.v(ctx.ar.alloc(8 * 128 * 2), 8 * 128, BF16)
    uk_raw = ctx.v(ctx.ar.alloc(8 * 128 * 2), 8 * 128, BF16)
    uv_raw = ctx.v(ctx.ar.alloc(8 * 128 * 2), 8 * 128, BF16)
    b.dma("pool", dq_raw, W["dq"], sem="q_gld", exclusive=False)
    b.dma("pool", dkv_raw, W["dkv"], sem="q_gld", exclusive=False)
    b.dma("pool", uq_raw, wslice_ap(W["uq"], 128, 0, 128), sem="q_gld", exclusive=False)
    b.dma("pool", uk_raw, wslice_ap(W["uk"], 128, 0, 128), sem="q_gld", exclusive=False)
    b.dma("pool", uv_raw, wslice_ap(W["uv"], 128, 0, 128), sem="q_gld", exclusive=False)
    raw_ck = b.ckpt("pool", "q_gld")

    tr_ev = {}
    for name, srcv, dstv in (("dq", dq_raw, dqT), ("dkv", dkv_raw, dkvT)):
        for g in range(1):
            tb = ctx.bank(6, 1024, BF16)
            tt = None
            for k in range(DT):
                tt = b.op("pe", lambda E, o=tb[:, k * 128:(k + 1) * 128],
                          i=srcv[:, k * 128:(k + 1) * 128]:
                          E.transpose(o, i, ctx.id_bf), [raw_ck, ctx.id_tok, ctx.bankuse.get(6)])
            ev = b.op("dve", lambda E, o=dstv, i=tb: E.tensor_copy(o, i), [tt])
            ctx.bankuse[6] = ev
            tr_ev[name] = ev
    tb = ctx.bank(7, 1024, BF16)
    tt = None
    for k in range(DT):
        tt = b.op("pe", lambda E, o=tb[:, k * 128:(k + 1) * 128],
                  i=uq_raw[:, k * 128:(k + 1) * 128]:
                  E.transpose(o, i, ctx.id_bf), [raw_ck, ctx.id_tok, ctx.bankuse.get(7)])
    uqT_ev = b.op("dve", lambda E, o=uqT, i=tb: E.tensor_copy(o, i), [tt])
    ctx.bankuse[7] = uqT_ev

    t1_sb = ctx.v(ctx.ar.alloc(128 * 2), 128, BF16)
    bk5 = ctx.bank(5, 128, F32)
    mm = None
    for k in range(DT):
        mm = b.op("pe", lambda E, o=bk5, l=uq_raw[:, k * 128:(k + 1) * 128],
                  r=uk_raw[:, k * 128:(k + 1) * 128], k=k:
                  E.matmul(o, l, r, start=(k == 0), stop=(k == DT - 1)),
                  [raw_ck, ctx.bankuse.get(5)])
    t1_ev = b.op("dve", lambda E, o=t1_sb, i=bk5: E.tensor_copy(o, i), [mm])
    ctx.bankuse[5] = t1_ev
    keff_ev = None
    for k in range(DT):
        bk5b = ctx.bank(5, 128, F32)
        mm = b.op("pe", lambda E, o=bk5b, l=dq_raw[:, k * 128:(k + 1) * 128], r=t1_sb:
                  E.matmul(o, l, r, start=True, stop=True),
                  [t1_ev, ctx.bankuse.get(5)])
        keff_ev = b.op("act", lambda E, o=keff[:, k * 128:(k + 1) * 128], i=bk5b:
                       E.activation(o, i, AF.Copy, scale=0.125), [mm])
        ctx.bankuse[5] = keff_ev

    wo_raw = ctx.v(ctx.ar.alloc(1024 * 2), 1024, BF16)
    woT = ctx.v(ctx.ar.alloc(DT * 1024 * 2), DT * 1024, BF16)
    wo_war = None
    for k in range(DT):
        ld = b.dma("pool", wo_raw, W["o"][k * P:(k + 1) * P, :], waits=[wo_war],
                   sem="q_gw0")
        for g in range(2):
            tb = ctx.bank(6 + g % 2, 1024, BF16)
            tt = None
            for ki in range(4):
                cc = g * 4 + ki
                tt = b.op("pe", lambda E, o=tb[:, ki * 128:(ki + 1) * 128],
                          i=wo_raw[:, cc * 128:(cc + 1) * 128]:
                          E.transpose(o, i, ctx.id_bf),
                          [ld, ctx.id_tok, ctx.bankuse.get(6 + g % 2)])
            ev = None
            for ki in range(4):
                cc = g * 4 + ki
                ev = b.op("dve", lambda E, o=woT[:, cc * 1024 + k * 128:cc * 1024 + (k + 1) * 128],
                          i=tb[:, ki * 128:(ki + 1) * 128]: E.tensor_copy(o, i), [tt])
            ctx.bankuse[6 + g % 2] = ev
            wo_war = ev
    woT_ck = wo_war
    for half in range(2):
        bkv = ctx.bank(5, 512, F32)
        mm = None
        for k in range(DT):
            mm = b.op("pe", lambda E, o=bkv, l=uv_raw[:, k * 128:(k + 1) * 128],
                      r=woT[:, k * 1024 + half * 512:k * 1024 + half * 512 + 512], k=k:
                      E.matmul(o, l, r, start=(k == 0), stop=(k == DT - 1)),
                      [raw_ck, woT_ck, ctx.bankuse.get(5)])
        ev = b.op("dve", lambda E, o=veff[:, half * 512:(half + 1) * 512], i=bkv:
                  E.tensor_copy(o, i), [mm])
        ctx.bankuse[5] = ev
        veff_ck = ev

    b.dma("pool", msk, ctx.dram["c_l0am"], sem="q_gld", exclusive=False)
    msk_ck = b.ckpt("pool", "q_gld")
    ctx.ar.reset(mark_prep)

    xn_stride = (T + 4) * 2
    xn_off = ctx.ar.alloc(DT * xn_stride)
    ln = emit_ln(ctx, x_in, xn_off, xn_stride, tag=tag + "l", src_tok=[veff_ck])
    xn, xn_tok = ln["xn_tiles"], ln["xn_tok"]

    qlow = ctx.v(ctx.ar.alloc(T * 2), T, BF16)
    for c in range(C):
        bk = ctx.bank(c % 2, 512, F32)
        mm = None
        for k in range(DT):
            mm = b.op("pe", lambda E, o=bk, l=dkvT[:, k * 128:(k + 1) * 128],
                      r=xn[k][:, 1 + c * 512:1 + c * 512 + 512], k=k:
                      E.matmul(o, l, r, start=(k == 0), stop=(k == DT - 1)),
                      [tr_ev["dkv"], xn_tok[k][c], ctx.bankuse.get(c % 2)])
        ev = b.op("dve", lambda E, o=ckvT[:, c * 512:(c + 1) * 512], i=bk:
                  E.tensor_copy(o, i), [mm])
        ctx.bankuse[c % 2] = ev
        ckvT_tok = ev
    for c in range(C):
        bk = ctx.bank(2 + c % 2, 512, F32)
        mm = None
        for k in range(DT):
            mm = b.op("pe", lambda E, o=bk, l=dqT[:, k * 128:(k + 1) * 128],
                      r=xn[k][:, 1 + c * 512:1 + c * 512 + 512], k=k:
                      E.matmul(o, l, r, start=(k == 0), stop=(k == DT - 1)),
                      [tr_ev["dq"], xn_tok[k][c], ctx.bankuse.get(2 + c % 2)])
        ev = b.op("dve", lambda E, o=qlow[:, c * 512:(c + 1) * 512], i=bk:
                  E.tensor_copy(o, i), [mm])
        ctx.bankuse[2 + c % 2] = ev
        qlow_tok = ev
    ckv_tok = None
    for g in range((TT_ + 7) // 8):
        tb = ctx.bank(6 + g % 2, 1024, BF16)
        nin = min(8, TT_ - g * 8)
        tt = None
        for ki in range(nin):
            jt = g * 8 + ki
            tt = b.op("pe", lambda E, o=tb[:, ki * 128:(ki + 1) * 128],
                      i=ckvT[:, jt * 128:(jt + 1) * 128]:
                      E.transpose(o, i, ctx.id_bf),
                      [ckvT_tok, ctx.id_tok, ctx.bankuse.get(6 + g % 2)])
        ckv_tok = b.op("dve", lambda E, o=ckv_sb[:, g * 1024:g * 1024 + nin * 128],
                       i=tb[:, 0:nin * 128]: E.tensor_copy(o, i), [tt])
        ctx.bankuse[6 + g % 2] = ckv_tok
    qup_tok = [[None] * C for _ in range(DT)]
    for j in range(DT):
        for c in range(C):
            bk = ctx.bank((j * C + c) % 2, 512, F32)
            mm = b.op("pe", lambda E, o=bk, l=uqT[:, j * 128:(j + 1) * 128],
                      r=qlow[:, c * 512:(c + 1) * 512]:
                      E.matmul(o, l, r, start=True, stop=True),
                      [uqT_ev, qlow_tok, ctx.bankuse.get((j * C + c) % 2)])
            ev = b.op("act" if (j + c) % 2 else "dve",
                      lambda E, o=qup_sb[j][:, c * 512:(c + 1) * 512], i=bk:
                      (E.copy(o, i) if hasattr(E, "copy") else E.tensor_copy(o, i)),
                      [mm])
            qup_tok[j][c] = ev
            ctx.bankuse[(j * C + c) % 2] = ev

    ql_off = ctx.ar.alloc(T * 2)
    P_off = ctx.ar.alloc(2 * T * 2)
    Pf_off = ctx.ar.alloc(T * 4)
    PT_off = ctx.ar.alloc(2 * TT_ * 128 * 2)
    cx_off = ctx.ar.alloc(2 * 128 * 2)
    sm_off = ctx.ar.alloc(2 * 192)
    ql_sb = ctx.v(ql_off, T, BF16)
    P_bf2 = [ctx.v(P_off + i * T * 2, T, BF16) for i in range(2)]
    P_f2 = [ctx.v(Pf_off, T, F32), ctx.v(Pf_off, T, F32)]
    PT_sb2 = [ctx.v(PT_off + i * TT_ * 128 * 2, TT_ * 128, BF16) for i in range(2)]
    cx_sb2 = [ctx.v(cx_off + i * 128 * 2, 128, BF16) for i in range(2)]
    negmax2 = [ctx.v(sm_off + i * 192, 1, F32) for i in range(2)]
    sums2 = [ctx.v(sm_off + i * 192 + 64, 1, F32) for i in range(2)]
    recip2 = [ctx.v(sm_off + i * 192 + 128, 1, F32) for i in range(2)]
    tr_tok = [None, None]
    sc_prev = [None, None]
    mx_prev = [None, None]
    cx_war = [None, None]
    qq = 0
    ql_war = None
    yev = None
    for h in range(16):
        jt2, pb2 = h // 2, 64 * (h % 2)
        ql_toks = []
        for c in range(C):
            bki = c % 2
            bk = ctx.bank(bki, 512, F32)
            mm = b.op("pe", lambda E, o=bk, l=keff[pb2:pb2 + 64, jt2 * 128:(jt2 + 1) * 128],
                      r=qup_sb[jt2][pb2:pb2 + 64, c * 512:(c + 1) * 512]:
                      E.matmul(o, l, r, start=True, stop=True),
                      [keff_ev, qup_tok[jt2][c], ctx.bankuse.get(bki), ql_war])
            ev = b.op("dve", lambda E, o=ql_sb[:, c * 512:(c + 1) * 512], i=bk:
                      E.tensor_copy(o, i), [mm])
            ctx.bankuse[bki] = ev
            ql_toks.append(ev)
        for qt in range(TT_):
            sl2 = qq % 2
            qq += 1
            P_bf, P_f, PT_sb, cx_sb = P_bf2[sl2], P_f2[sl2], PT_sb2[sl2], cx_sb2[sl2]
            negmax, sums, recip = negmax2[sl2], sums2[sl2], recip2[sl2]
            nch = qt // 4 + 1
            ncols = nch * 512
            nt = qt + 1
            smm = None
            for c in range(nch):
                bk = ctx.bank(c, 512, F32)
                w = [ql_toks[(qt * 128) // 512], ckvT_tok, ctx.bankuse.get(c)]
                smm = b.op("pe", lambda E, o=bk, l=ql_sb[:, qt * 128:(qt + 1) * 128],
                           r=ckvT[:, c * 512:(c + 1) * 512], sp=(c != nch - 1):
                           E.matmul(o, l, r, start=True, stop=sp), w)
                if c == nch - 1:
                    smm = b.op("pe", lambda E, o=bk, l=ctx.id_bf,
                               r=msk[:, (qt % 4) * 512:(qt % 4 + 1) * 512]:
                               E.matmul(o, l, r, start=False, stop=True),
                               [smm, msk_ck, ctx.id_tok])
            spsum = ctx.ps[:, 0:ncols * 4].bitcast(F32)
            mx = b.op("dve", lambda E, o=negmax, i=spsum: E.reduce_max(
                o, i, axis=AX.X, negate=True), [smm, mx_prev[sl2]])
            et = b.op("act", lambda E, o=P_f[:, 0:ncols], i=spsum, bb=negmax, s=sums:
                      E.activation(o, i, AF.Exp, bias=bb, accum_out=s),
                      [mx, sc_prev[0], sc_prev[1]])
            for c in range(nch):
                ctx.bankuse[c] = et
            rc = b.op("dve", lambda E, o=recip, i=sums: E.reciprocal(o, i), [et])
            sc_tok = b.op("dve", lambda E, o=P_bf[:, 0:ncols], i=P_f[:, 0:ncols],
                          r=recip: E.tensor_scalar(o, i, r, None, op0=ALU.mult),
                          [rc, tr_tok[sl2]])
            sc_prev[sl2] = sc_tok
            mx_prev[sl2] = rc
            g_ev = None
            for g in range((nt + 7) // 8):
                tb_i = 6 + (g % 2)
                tb = ctx.bank(tb_i, 1024, BF16)
                nin = min(8, nt - g * 8)
                ttok = None
                for ki in range(nin):
                    jt = g * 8 + ki
                    ttok = b.op("pe", lambda E, o=tb[:, ki * 128:(ki + 1) * 128],
                                i=P_bf[:, jt * 128:(jt + 1) * 128]:
                                E.transpose(o, i, ctx.id_bf),
                                [sc_tok, ctx.id_tok, ctx.bankuse.get(tb_i)])
                g_ev = b.op("dve", lambda E, o=PT_sb[:, g * 1024:g * 1024 + nin * 128],
                            i=tb[:, 0:nin * 128]: E.tensor_copy(o, i), [ttok])
                ctx.bankuse[tb_i] = g_ev
                tr_tok[sl2] = ttok
            cmm = None
            for jt in range(nt):
                bk = ctx.bank(4, 128, F32)
                w = [g_ev, ckv_tok, ctx.bankuse.get(4)]
                cmm = b.op("pe", lambda E, o=bk, l=ckv_sb[:, jt * 128:(jt + 1) * 128],
                           r=PT_sb[:, jt * 128:(jt + 1) * 128], st=(jt == 0),
                           sp=(jt == nt - 1): E.matmul(o, l, r, start=st, stop=sp), w)
            cx_ev = b.op("dve", lambda E, o=cx_sb, i=ctx.bank(4, 128, F32):
                         E.tensor_copy(o, i), [cmm, cx_war[sl2]])
            ctx.bankuse[4] = cx_ev
            ymm = b.op("pe", lambda E, o=ctx.bank(5, 128, F32, pn=64),
                       l=veff[:, 64 * h:64 * h + 64], r=cx_sb:
                       E.matmul(o, l, r, start=True, stop=True),
                       [cx_ev, veff_ck, ctx.bankuse.get(5)])
            yev = b.op("act", lambda E, o=y_sb[jt2][pb2:pb2 + 64, qt * 128:(qt + 1) * 128],
                       i=ctx.bank(5, 128, F32, pn=64): E.copy(o, i), [ymm])
            ctx.bankuse[5] = yev
            cx_war[sl2] = ymm
        ql_war = smm

    y_done = yev
    ctx.ar.reset(mark1)

    xi_off = ctx.ar.alloc(4 * 512 * 4)
    xo2_off = ctx.ar.alloc(4 * 512 * 4)
    xi_war, xo_war = {}, {}
    out_done = []
    for j in range(DT):
        for c in range(C):
            i4 = (j * C + c) % 4
            xi = ctx.v(xi_off + i4 * 2048, 512, F32)
            xo = ctx.v(xo2_off + i4 * 2048, 512, F32)
            ld = b.dma("sp", xi, x_in[j * P:(j + 1) * P, c * 512:c * 512 + 512],
                       waits=[xi_war.get(i4) or y_done], sem=f"q_sxi{i4}")
            t = b.op("dve", lambda E, o=xo, y=y_sb[j][:, c * 512:(c + 1) * 512], x=xi:
                     E.tensor_tensor(o, y, x, op=ALU.add),
                     [y_done, ld, xo_war.get(i4)])
            xi_war[i4] = t
            d = b.dma("sp", x_out[j * P:(j + 1) * P, c * 512:c * 512 + 512], xo,
                      waits=[t], sem=f"q_sxo{i4}")
            xo_war[i4] = d
            out_done.append(d)
    ctx.ar.reset(mark0)
    return out_done[-4:]


def emit_l0ffn(ctx, x_in, x_out, W, xA, xB, tag="s"):
    b, T, C = ctx.b, ctx.T, ctx.C
    NW = T // 256
    NV = T // 128 + 2
    mark_call = ctx.ar.mark()

    msk = ctx.v(ctx.ar.alloc(4 * 512 * 2), 4 * 512, BF16)
    b.dma("pool", msk, ctx.dram["c_l0fm"], sem="q_gld", exclusive=False)
    blk2 = ctx.v(ctx.ar.alloc(2 * 2), 2, BF16)
    b.dma("pool", blk2, ctx.dram["c_blk2"], sem="q_gld", exclusive=False)
    sel2 = ctx.v(ctx.ar.alloc(256), 128, BF16, pn=2)
    b.dma("pool", sel2, ctx.dram["c_sel2"], sem="q_gld", exclusive=False)
    cst_ck = b.ckpt("pool", "q_gld")
    ctx.l0 = dict(msk=msk, blk2=blk2, sel2=sel2, cst_ck=cst_ck)

    xn_stride = (T + 4) * 2
    mk = ctx.ar.mark()
    xn_off0 = ctx.ar.alloc(DT * xn_stride)
    r0 = emit_ln(ctx, x_in, xn_off0, xn_stride, tag=tag + "0", fout_dram=xA)
    src_ck = r0["store_ck"]
    ctx.ar.reset(mk)

    cur, nxt = xA, xB
    for dep in range(6):
        src_ck = _l0_attn_sub(ctx, W, dep, cur, nxt, src_ck, tag)
        cur, nxt = nxt, cur
        src_ck = _l0_ffn_sub(ctx, W, dep, cur, nxt, src_ck, tag)
        cur, nxt = nxt, cur

    xi_off = ctx.ar.alloc(4 * 512 * 4)
    xf_off = ctx.ar.alloc(4 * 512 * 4)
    xo_off = ctx.ar.alloc(4 * 512 * 4)
    xi_war, xf_war, xo_war = {}, {}, {}
    for j in range(DT):
        for c in range(C):
            i4 = (j * C + c) % 4
            xi = ctx.v(xi_off + i4 * 2048, 512, F32)
            xf = ctx.v(xf_off + i4 * 2048, 512, F32)
            xo = ctx.v(xo_off + i4 * 2048, 512, F32)
            l1 = b.dma("sp", xi, x_in[j * P:(j + 1) * P, c * 512:c * 512 + 512],
                       waits=[xi_war.get(i4) or src_ck], sem=f"q_sxi{i4}")
            l2 = b.dma("sp", xf, cur[j * P:(j + 1) * P, c * 512:c * 512 + 512],
                       waits=[xf_war.get(i4) or src_ck], sem=f"q_sxf{i4 % 2}")
            t = b.op("dve", lambda E, o=xo, a=xf, x=xi: E.scalar_tensor_tensor(
                o, a, 0.5, x, op0=ALU.mult, op1=ALU.add),
                [l1, l2, xo_war.get(i4)])
            xi_war[i4] = t
            xf_war[i4] = t
            d = b.dma("sp", x_out[j * P:(j + 1) * P, c * 512:c * 512 + 512], xo,
                      waits=[t], sem=f"q_sxo{i4}")
            xo_war[i4] = d
    done = b.nopop("sp", [(f"q_sxo{i}", b.cnt.get(f"q_sxo{i}", 0)) for i in range(4)])
    ctx.ar.reset(mark_call)
    return done


def _l0_attn_sub(ctx, W, dep, cur, nxt, src_ck, tag):
    b, T, C = ctx.b, ctx.T, ctx.C
    NW = T // 256
    NV = T // 128 + 2
    mark = ctx.ar.mark()
    qkv_d = W["qkv"][dep]
    wo_d = W["wo"][dep]
    msk, blk2, sel2, cst_ck = (ctx.l0[k] for k in ("msk", "blk2", "sel2", "cst_ck"))

    xb_stride = (T + 4) * 2
    xb_off = ctx.ar.alloc(DT * xb_stride)
    xb = [ctx.v(xb_off + k * xb_stride, T + 1, BF16) for k in range(DT)]
    xb_tok = []
    for k in range(DT):
        z = b.op("pool", lambda E, t=xb[k]: E.memset(t[:, 0:1], 0.0), [src_ck])
        tk = b.dma("pool", xb[k][:, 1:T + 1], cur[k * P:(k + 1) * P, :],
                   sem=f"q_gx{k}", waits=[src_ck, z])
        xb_tok.append(tk)

    def xs(k, c0, n):
        off = 0 if k >= DT // 2 else 1
        return xb[k][:, off + c0: off + c0 + n]

    qs_col = ctx.v(ctx.ar.alloc(64), 1, F32)
    qss_col = ctx.v(ctx.ar.alloc(64), 1, F32)
    ks_col = ctx.v(ctx.ar.alloc(64), 1, F32)
    kss_col = ctx.v(ctx.ar.alloc(64), 1, F32)
    for dst, srcv, sw in ((qs_col, W["qs"][dep], 0), (qss_col, W["qs"][dep], 1),
                          (ks_col, W["ks"][dep], 0), (kss_col, W["ks"][dep], 1)):
        for half in range(2):
            p0 = half * 64
            if sw:
                b.dma("sp", dst[p0:p0 + 32, :], col1(srcv[32:64]), sem="q_sld", exclusive=False)
                b.dma("sp", dst[p0 + 32:p0 + 64, :], col1(srcv[0:32]), sem="q_sld", exclusive=False)
            else:
                b.dma("sp", dst[p0:p0 + 64, :], col1(srcv[0:64]), sem="q_sld", exclusive=False)
    scol_ck = b.ckpt("sp", "q_sld")
    csq = ctx.v(ctx.ar.alloc(T * 2), T, BF16)
    ssq = ctx.v(ctx.ar.alloc(T * 2), T, BF16)
    csk = ctx.v(ctx.ar.alloc(T * 2), T, BF16)
    ssk = ctx.v(ctx.ar.alloc(T * 2), T, BF16)
    mk_rot = ctx.ar.mark()
    rotq_raw = ctx.v(ctx.ar.alloc(2 * T * 4), 2 * T, F32)
    rotk_raw = ctx.v(ctx.ar.alloc(2 * T * 4), 2 * T, F32)
    b.dma("pool", rotq_raw, ctx.dram["c_rotq"], sem="q_gld", exclusive=False,
          waits=[src_ck])
    b.dma("pool", rotk_raw, ctx.dram["c_rotk"], sem="q_gld", exclusive=False,
          waits=[src_ck])
    rot_ck = b.ckpt("pool", "q_gld")
    tab_tok = []
    for dst, raw, c0, scal, s2 in ((csq, rotq_raw, 0, qs_col, 8.0),
                                   (ssq, rotq_raw, T, qss_col, 8.0),
                                   (csk, rotk_raw, 0, ks_col, 1.0),
                                   (ssk, rotk_raw, T, kss_col, 1.0)):
        tab_tok.append(b.op("dve", lambda E, o=dst, i=raw[:, c0:c0 + T], s=scal, s2=s2:
                            E.tensor_scalar(o, i, s, s2, op0=ALU.mult, op1=ALU.mult),
                            [rot_ck, scol_ck]))
    tab_ck = tab_tok[-1]
    ctx.ar.reset(mk_rot)

    v2_off = ctx.ar.alloc(NV * 1024 * 2)
    v2 = ctx.v(v2_off, NV * 1024, BF16)
    vz = b.op("pool", lambda E, o=v2[:, 0:2048]: E.memset(o, 0.0), [src_ck, tab_ck])
    vs_off = ctx.ar.alloc(DT * 512 * 2)
    v_tok = None
    vload_war = None
    for vc in range(2):
        vs = ctx.v(vs_off, DT * 512, BF16)
        wt = b.dma("pool", vs, wslice_ap(qkv_d, 3072, 2048 + vc * 512, 512),
                   waits=[vload_war or src_ck, tab_ck], sem="q_gw0")
        mm = None
        for tt in range(T // 128):
            bi = (tt + vc) % 2
            bk = ctx.bank(bi, 512, F32)
            for k in range(DT):
                w = [wt, xb_tok[k], ctx.bankuse.get(bi)]
                mm = b.op("pe", lambda E, o=bk, l=xs(k, tt * 128, 128),
                          r=vs[:, k * 512:(k + 1) * 512], k=k:
                          E.matmul(o, l, r, start=(k == 0), stop=(k == DT - 1)), w)
            ev = b.op("act" if tt % 2 else "dve",
                      lambda E, o=v2[:, (tt + 2) * 1024 + vc * 512:(tt + 2) * 1024 + vc * 512 + 512],
                      i=bk: (E.copy(o, i) if hasattr(E, "copy") else E.tensor_copy(o, i)),
                      [mm, vz, tab_ck])
            v_tok = ev
            ctx.bankuse[bi] = ev
        vload_war = mm

    oT_off = ctx.ar.alloc(DT * T * 2)
    oT = [ctx.v(oT_off + j * T * 2, T, BF16) for j in range(DT)]
    mark_heads = ctx.ar.mark()
    qraw = ctx.v(ctx.ar.alloc(T * 2), T, BF16)
    kraw = qraw
    swp = ctx.v(ctx.ar.alloc(T * 2), T, BF16)
    t1b = ctx.v(ctx.ar.alloc(T * 2), T, BF16)
    sq_off2 = ctx.ar.alloc(2 * 512 * 2)
    qf_off = ctx.ar.alloc(2 * T * 2)
    kf_off = ctx.ar.alloc(2 * (T + 256) * 2)
    qfin2 = [ctx.v(qf_off + i * T * 2, T, BF16) for i in range(2)]
    kfin2 = [ctx.v(kf_off + i * (T + 256) * 2, T + 256, BF16) for i in range(2)]
    kz0s = [b.op("pool", lambda E, o=kfin2[0][:, 0:256]: E.memset(o, 0.0), [src_ck]),
            b.op("pool", lambda E, o=kfin2[1][:, 0:256]: E.memset(o, 0.0), [src_ck])]
    nrow = ctx.v(ctx.ar.alloc(T * 4), T, F32, pn=2)
    nrow_bf = ctx.v(ctx.ar.alloc(T * 2), T, BF16, pn=2)
    Pf_off = ctx.ar.alloc(2 * 512 * 4)
    P2_off = ctx.ar.alloc(2 * 1024 * 2)
    PTo_off = ctx.ar.alloc(2 * 1024 * 2)
    sm_off = ctx.ar.alloc(2 * 192)
    P_f2 = [ctx.v(Pf_off + i * 2048, 512, F32) for i in range(2)]
    P22 = [ctx.v(P2_off + i * 2048, 1024, BF16) for i in range(2)]
    PT2 = [ctx.v(PTo_off + i * 2048, 1024, BF16) for i in range(2)]
    negmax2 = [ctx.v(sm_off + i * 192, 1, F32) for i in range(2)]
    sums2 = [ctx.v(sm_off + i * 192 + 64, 1, F32) for i in range(2)]
    recip2 = [ctx.v(sm_off + i * 192 + 128, 1, F32) for i in range(2)]
    qg_off = ctx.ar.alloc(DT * 512 * 2)
    kg_off = ctx.ar.alloc(DT * 512 * 2)
    qgv = ctx.v(qg_off, DT * 512, BF16)
    kgv = ctx.v(kg_off, DT * 512, BF16)
    grp_war = {"q": src_ck, "k": src_ck}
    grp_tok = {}

    fin_prev = {}
    sim_last = [None, None]
    prod_prev = [None]
    oev = None
    et_war = [None, None]
    trl = [None, None]
    ptev_war = [None, None]
    hw_c = [0]

    def emit_production(hp):
        sl = hp % 2
        if hp % 4 == 0:
            grp = hp // 4
            grp_tok["q"] = b.dma("pool", qgv, wslice_ap(qkv_d, 3072, grp * 512, 512),
                                 waits=[grp_war["q"]], sem="q_gw0")
            grp_tok["k"] = b.dma("pool", kgv, wslice_ap(qkv_d, 3072, 1024 + grp * 512, 512),
                                 waits=[grp_war["k"]], sem="q_gw1")
        po = (hp % 4) * 128
        cks = {}
        mm = None
        side_prev = [None]
        for side, raw, fin, cst, sst in (
                ("q", qraw, qfin2[sl], csq, ssq), ("k", kraw, kfin2[sl], csk, ssk)):
            gv = qgv if side == "q" else kgv
            wtok = grp_tok[side]
            ev_toks = []
            for c in range(C):
                bi = 2 + c % 2
                bk = ctx.bank(bi, 512, F32)
                mm = None
                for k in range(DT):
                    w = [wtok, xb_tok[k], ctx.bankuse.get(bi)]
                    mm = b.op("pe", lambda E, o=bk, l=gv[:, k * 512 + po:k * 512 + po + 128],
                              r=xs(k, c * 512, 512), k=k:
                              E.matmul(o, l, r, start=(k == 0), stop=(k == DT - 1)), w)
                ev = b.op("dve", lambda E, o=raw[:, c * 512:(c + 1) * 512], i=bk:
                          E.tensor_copy(o, i), [mm, prod_prev[0], side_prev[0]])
                ev_toks.append(ev)
                ctx.bankuse[bi] = ev
            raw_ck = ev_toks[-1]
            nr_toks = []
            sq_war = [None, None]
            for c in range(C):
                sq = ctx.v(sq_off2 + (c % 2) * 1024, 512, BF16)
                sq_t = b.op("act", lambda E, o=sq, i=raw[:, c * 512:(c + 1) * 512]:
                            E.activation(o, i, AF.Square),
                            [ev_toks[c], sq_war[c % 2]])
                bk4 = ctx.bank(4, 512, F32, pn=2)
                mm2 = b.op("pe", lambda E, o=bk4, l=blk2, r=sq:
                           E.matmul(o, l, r, start=True, stop=True),
                           [sq_t, cst_ck, ctx.bankuse.get(4)])
                sq_war[c % 2] = mm2
                ev = b.op("dve", lambda E, o=nrow[:, c * 512:(c + 1) * 512], i=bk4:
                          E.tensor_copy(o, i), [mm2])
                nr_toks.append(ev)
                ctx.bankuse[4] = ev
            t_cl = b.op("dve", lambda E, o=nrow: E.tensor_scalar(
                o, o, 1e-24, None, op0=ALU.max), [nr_toks[-1]])
            t_ln = b.op("act", lambda E, o=nrow: E.activation(o, o, AF.Ln), [t_cl])
            t_ex = b.op("act", lambda E, o=nrow: E.activation(o, o, AF.Exp, scale=-0.5), [t_ln])
            t_bf = b.op("dve", lambda E, o=nrow_bf, i=nrow: E.tensor_copy(o, i), [t_ex])
            sw_t = []
            for g4 in range(4):
                p0 = g4 * 32
                p1 = (g4 * 32 + 32) % 64 + (64 if g4 >= 2 else 0)
                sw_t.append(b.op("act", lambda E, o=swp[p0:p0 + 32, :], i=raw[p1:p1 + 32, 0:T]:
                                 E.copy(o, i), [raw_ck, prod_prev[0]]))
            tt1 = b.op("dve", lambda E, o=t1b, i=raw[:, 0:T], cc=cst: E.tensor_tensor(
                o, i, cc, op=ALU.mult), [raw_ck, tab_ck])
            tt2 = b.op("dve", lambda E, o=swp, i=swp, ss=sst: E.tensor_tensor(
                o, i, ss, op=ALU.mult), sw_t + [tab_ck])
            tt3 = b.op("dve", lambda E, o=t1b, a=t1b, bb=swp: E.tensor_tensor(
                o, a, bb, op=ALU.add), [tt1, tt2])
            fin0 = 0 if side == "q" else 256
            f_toks = []
            for c in range(C):
                bk4b = ctx.bank(4, 512, F32)
                mmb = b.op("pe", lambda E, o=bk4b, l=sel2, r=nrow_bf[:, c * 512:(c + 1) * 512]:
                           E.matmul(o, l, r, start=True, stop=True),
                           [t_bf, cst_ck, ctx.bankuse.get(4)])
                fv = b.op("dve", lambda E, o=fin[:, fin0 + c * 512:fin0 + c * 512 + 512],
                          t=t1b[:, c * 512:(c + 1) * 512], m=bk4b: E.tensor_tensor(
                    o, t, m, op=ALU.mult),
                    [tt3, mmb, kz0s[sl] if side == "k" else None,
                     fin_prev.get((side, sl)), sim_last[sl]])
                f_toks.append(fv)
                ctx.bankuse[4] = fv
            fin_prev[(side, sl)] = f_toks[-1]
            cks[side] = f_toks[-1]
            side_prev[0] = tt2
            if hp % 4 == 3:
                grp_war[side] = mm
        prod_prev[0] = cks["k"]
        return cks

    prod_cks = {0: emit_production(0)}
    for hp in range(DT):
        sl = hp % 2
        cks = prod_cks.pop(sl)
        qfin, kfin = qfin2[sl], kfin2[sl]
        qfin_ck, kfin_ck = cks["q"], cks["k"]
        if hp + 1 < DT:
            prod_cks[(hp + 1) % 2] = emit_production(hp + 1)
        last_sim = None
        for hh in range(2):
            h = hp * 2 + hh
            pb = 64 * hh
            for w_ in range(NW):
                sl2 = hw_c[0] % 2
                hw_c[0] += 1
                P_f, P2, PT = P_f2[sl2], P22[sl2], PT2[sl2]
                negmax, sums, recip = negmax2[sl2], sums2[sl2], recip2[sl2]
                smm = {}
                for qsub in range(2):
                    bi = (w_ * 2 + qsub) % 2
                    bk = ctx.bank(bi, 512, F32)
                    mv = qsub + (2 if w_ == 0 else 0)
                    m1 = b.op("pe", lambda E, o=bk, l=qfin[pb:pb + 64, (w_ * 256 + qsub * 128):(w_ * 256 + qsub * 128 + 128)],
                              r=kfin[pb:pb + 64, w_ * 256:w_ * 256 + 512]:
                              E.matmul(o, l, r, start=True, stop=False),
                              [qfin_ck, kfin_ck, ctx.bankuse.get(bi)])
                    m2 = b.op("pe", lambda E, o=bk, l=ctx.id_bf,
                              r=msk[:, mv * 512:(mv + 1) * 512]:
                              E.matmul(o, l, r, start=False, stop=True),
                              [m1, cst_ck, ctx.id_tok])
                    smm[qsub] = m2
                    last_sim = m2
                for qsub in range(2):
                    bi = (w_ * 2 + qsub) % 2
                    bk = ctx.bank(bi, 512, F32)
                    mx = b.op("dve", lambda E, o=negmax, i=bk: E.reduce_max(
                        o, i, axis=AX.X, negate=True), [smm[qsub]])
                    et = b.op("act", lambda E, o=P_f, i=bk, bb=negmax, s=sums:
                              E.activation(o, i, AF.Exp, bias=bb, accum_out=s),
                              [mx, et_war[sl2]])
                    ctx.bankuse[bi] = et
                    rc = b.op("dve", lambda E, o=recip, i=sums: E.reciprocal(o, i), [et])
                    sc = b.op("act", lambda E, o=P2[:, qsub * 512:(qsub + 1) * 512],
                              i=P_f, r=recip: E.activation(o, i, AF.Copy, scale=r),
                              [rc, trl[sl2]])
                    et_war[sl2] = sc
                    smm[qsub] = sc
                tb = ctx.bank(6 + w_ % 2, 1024, BF16)
                ttok = None
                for jt in range(4):
                    for qsub in range(2):
                        ttok = b.op("pe", lambda E, o=tb[:, (jt * 2 + qsub) * 128:(jt * 2 + qsub + 1) * 128],
                                    i=P2[:, qsub * 512 + jt * 128:qsub * 512 + (jt + 1) * 128]:
                                    E.transpose(o, i, ctx.id_bf),
                                    [smm[0], smm[1], ctx.id_tok, ctx.bankuse.get(6 + w_ % 2)])
                trl[sl2] = ttok
                ptev = b.op("dve", lambda E, o=PT, i=tb: E.tensor_copy(o, i),
                            [ttok, ptev_war[sl2]])
                ctx.bankuse[6 + w_ % 2] = ptev
                cmm = None
                for jt in range(4):
                    bk = ctx.bank(5, 256, F32, pn=64)
                    w = [ptev, v_tok, ctx.bankuse.get(5)]
                    cmm = b.op("pe", lambda E, o=bk,
                               l=v2[:, (2 * w_ + jt) * 1024 + 64 * h:(2 * w_ + jt) * 1024 + 64 * h + 64],
                               r=PT[:, jt * 256:(jt + 1) * 256], st=(jt == 0), sp=(jt == 3):
                               E.matmul(o, l, r, start=st, stop=sp), w)
                oev = b.op("act", lambda E, o=oT[hp][pb:pb + 64, w_ * 256:(w_ + 1) * 256],
                           i=ctx.bank(5, 256, F32, pn=64): E.copy(o, i), [cmm])
                ctx.bankuse[5] = oev
                ptev_war[sl2] = cmm
        sim_last[sl] = last_sim

    o_done = oev
    ctx.ar.reset(mark_heads)
    ws2_off = ctx.ar.alloc(2 * DT * 128 * 2)
    xi_off = ctx.ar.alloc(4 * 512 * 4)
    xo_off = ctx.ar.alloc(4 * 512 * 4)
    xi_war, xo_war = {}, {}
    ws2_war = [None, None]
    for j in range(DT):
        sl = j % 2
        ws = ctx.v(ws2_off + sl * DT * 128 * 2, DT * 128, BF16)
        wt = b.dma("pool", ws, wslice_ap(wo_d, 1024, j * 128, 128),
                   waits=[ws2_war[sl] or o_done], sem=f"q_gw{sl}")
        for c in range(C):
            bi = (j * C + c) % 4
            bk = ctx.bank(bi, 512, F32)
            mm = None
            for k in range(DT):
                w = [wt, o_done, ctx.bankuse.get(bi)]
                mm = b.op("pe", lambda E, o=bk, l=ws[:, k * 128:(k + 1) * 128],
                          r=oT[k][:, c * 512:(c + 1) * 512], k=k:
                          E.matmul(o, l, r, start=(k == 0), stop=(k == DT - 1)), w)
            i4 = (j * C + c) % 4
            xi = ctx.v(xi_off + i4 * 2048, 512, F32)
            xo = ctx.v(xo_off + i4 * 2048, 512, F32)
            ld = b.dma("sp", xi, cur[j * P:(j + 1) * P, c * 512:c * 512 + 512],
                       waits=[xi_war.get(i4) or o_done], sem=f"q_sxi{i4}")
            t = b.op("dve", lambda E, o=xo, i=bk, x=xi: E.tensor_tensor(
                o, i, x, op=ALU.add), [mm, ld, xo_war.get(i4)])
            xi_war[i4] = t
            ctx.bankuse[bi] = t
            d = b.dma("sp", nxt[j * P:(j + 1) * P, c * 512:c * 512 + 512], xo,
                      waits=[t], sem=f"q_sxo{i4}")
            xo_war[i4] = d
        ws2_war[sl] = mm
    ck = b.nopop("sp", [(f"q_sxo{i}", b.cnt.get(f"q_sxo{i}", 0)) for i in range(4)])
    ctx.ar.reset(mark)
    return ck


def _l0_ffn_sub(ctx, W, dep, cur, nxt, src_ck, tag):
    b, T, C = ctx.b, ctx.T, ctx.C
    mark = ctx.ar.mark()
    w1_d, w2_d = W["w1"][dep], W["w2"][dep]
    NP = 22

    Pst_off = ctx.ar.alloc(NP * T * 2)
    Pst = [ctx.v(Pst_off + j * T * 2, T, BF16) for j in range(NP)]

    b1a, b1g, b2c = [], [], []
    for j in range(NP):
        na = 128 if j < NP - 1 else 42
        va = ctx.v(ctx.ar.alloc(64), 1, F32, pn=na)
        vg = ctx.v(ctx.ar.alloc(64), 1, F32, pn=na)
        b.dma("sp", va, col1(W["b1"][dep][j * 128:j * 128 + na]), sem="q_sld", exclusive=False)
        b.dma("sp", vg, col1(W["b1"][dep][2730 + j * 128:2730 + j * 128 + na]), sem="q_sld", exclusive=False)
        b1a.append(va)
        b1g.append(vg)
    for j in range(DT):
        v = ctx.v(ctx.ar.alloc(64), 1, F32)
        b.dma("sp", v, col1(W["b2"][dep][j * 128:(j + 1) * 128]), sem="q_sld", exclusive=False)
        b2c.append(v)
    col_ck = b.ckpt("sp", "q_sld")

    mk_ln = ctx.ar.mark()
    xn_stride = (T + 4) * 2
    xn_off = ctx.ar.alloc(DT * xn_stride)
    ln = emit_ln(ctx, cur, xn_off, xn_stride, g_dram=W["ln_g"][dep],
                 bg_dram=W["ln_b"][dep], shift=True, tag=tag + f"n{dep}",
                 src_tok=[src_ck])
    xn, xn_tok = ln["xn_tiles"], ln["xn_tok"]

    gbuf = ctx.v(ctx.ar.alloc(T * 2), T, BF16)
    wbig_off = ctx.ar.alloc(DT * 2730 * 2)
    wbig = ctx.v(wbig_off, DT * 2730, BF16)
    ln_done = ln["xn_tok"][DT - 1][ctx.C - 1]
    half_war = ln_done
    gw_tok = None
    gel_toks = []
    for half in (0, 1):
        wt = b.dma("pool", wbig, wslice_ap(w1_d, 5460, half * 2730, 2730),
                   waits=[half_war], sem="q_gw0")
        for j in range(NP):
            na = 128 if j < NP - 1 else 42
            bias_c = b1a[j] if half == 0 else b1g[j]
            for c in range(C):
                bi = 2 + (j + c) % 2
                bk = ctx.bank(bi, 512, F32, pn=na)
                mm = None
                for k in range(DT):
                    w = [wt, xn_tok[k][c], ctx.bankuse.get(bi)]
                    mm = b.op("pe", lambda E, o=bk,
                              l=wbig[:, k * 2730 + j * 128:k * 2730 + j * 128 + na],
                              r=xn[k][:, 1 + c * 512:1 + c * 512 + 512], k=k:
                              E.matmul(o, l, r, start=(k == 0), stop=(k == DT - 1)), w)
                if half == 0:
                    ev = b.op("dve", lambda E, o=Pst[j][0:na, c * 512:(c + 1) * 512],
                              i=bk, bc=bias_c: E.tensor_scalar(
                        o, i, bc, None, op0=ALU.add), [mm, col_ck])
                else:
                    ev = b.op("act", lambda E, o=gbuf[0:na, c * 512:(c + 1) * 512],
                              i=bk, bc=bias_c: E.activation(o, i, AF.Gelu, bias=bc),
                              [mm, col_ck, gw_tok])
                ctx.bankuse[bi] = ev
            if half == 1:
                gw_tok = b.op("dve", lambda E, o=Pst[j][0:na, 0:T], a=Pst[j][0:na, 0:T],
                              g=gbuf[0:na, 0:T]: E.tensor_tensor(o, a, g, op=ALU.mult),
                              [ev])
                gel_toks.append(gw_tok)
        half_war = mm
    P_ck = gw_tok

    ctx.ar.reset(mk_ln)
    w2_off = ctx.ar.alloc(NP * 1024 * 2)
    w2t = [ctx.v(w2_off + k * 1024 * 2, 1024, BF16, pn=(128 if k < NP - 1 else 42))
           for k in range(NP)]
    w2_tok = []
    for k in range(NP):
        na = 128 if k < NP - 1 else 42
        w2_tok.append(b.dma("pool", w2t[k], w2_d[k * 128:k * 128 + na, :],
                            waits=[P_ck if k < 2 else w2_tok[k - 2]], sem=f"q_gw{k % 2}"))
    xi_off = ctx.ar.alloc(4 * 512 * 4)
    xo_off = ctx.ar.alloc(4 * 512 * 4)
    xi_war, xo_war = {}, {}
    for j in range(DT):
        for c in range(C):
            bi = (j * C + c) % 2
            bk = ctx.bank(bi, 512, F32)
            mm = None
            for k in range(NP):
                na = 128 if k < NP - 1 else 42
                w = [w2_tok[k], P_ck, ctx.bankuse.get(bi)]
                mm = b.op("pe", lambda E, o=bk, l=w2t[k][:, j * 128:(j + 1) * 128],
                          r=Pst[k][0:na, c * 512:(c + 1) * 512], k=k:
                          E.matmul(o, l, r, start=(k == 0), stop=(k == NP - 1)), w)
            i4 = (j * C + c) % 4
            xi = ctx.v(xi_off + i4 * 2048, 512, F32)
            xo = ctx.v(xo_off + i4 * 2048, 512, F32)
            ld = b.dma("sp", xi, cur[j * P:(j + 1) * P, c * 512:c * 512 + 512],
                       waits=[xi_war.get(i4) or P_ck], sem=f"q_sxi{i4}")
            t = b.op("dve", lambda E, o=xo, i=bk, bc=b2c[j], x=xi:
                     E.scalar_tensor_tensor(o, i, bc, x, op0=ALU.add, op1=ALU.add),
                     [mm, ld, col_ck, xo_war.get(i4)])
            xi_war[i4] = t
            ctx.bankuse[bi] = t
            d = b.dma("sp", nxt[j * P:(j + 1) * P, c * 512:c * 512 + 512], xo,
                      waits=[t], sem=f"q_sxo{i4}")
            xo_war[i4] = d
    ck = b.nopop("sp", [(f"q_sxo{i}", b.cnt.get(f"q_sxo{i}", 0)) for i in range(4)])
    ctx.ar.reset(mark)
    return ck


def phase_barrier(b):
    sp_d = [(s, v) for s, v in b.cnt.items() if s.startswith("q_s") and v > 0]
    pl_d = [(s, v) for s, v in b.cnt.items() if (s.startswith("q_g") or s == "q_pre") and v > 0]
    t1 = b.nopop("sp", sp_d)
    t2 = b.nopop("pool", pl_d)
    base = [t1, t2]
    for e in ENGINES:
        if b.cnt.get(e, 0) > 0:
            base.append((e, b.cnt[e]))
    b.base = []
    bb = []
    for t in base:
        bb.append(t)
    b.base = bb
    return bb


def build_module(T):
    nc = bass.Bass("TRN2", target_bir_lowering=False)
    b = Bld(nc)
    ctx = Ctx(nc, b, T)
    D_ = D

    def di(name, shape):
        ap = nc.dram_tensor(name, list(shape), F32, kind="ExternalInput").ap()
        ctx.dram[name] = ap
        return ap

    x = di("x_fm", (D_, T))
    W0a = {
        "dq": di("L0_attn_W_dq", (128, D_)),
        "uq": di("L0_attn_W_uq", (D_, 128)),
        "dkv": di("L0_attn_W_dkv", (128, D_)),
        "uk": di("L0_attn_W_uk", (D_, 128)),
        "uv": di("L0_attn_W_uv", (D_, 128)),
        "o": di("L0_attn_W_o", (D_, D_)),
    }
    W0f = {
        "qkv": di("L0_ffn_Wqkv", (6, D_, 3072)),
        "qs": di("L0_ffn_q_scale", (6, 64)),
        "ks": di("L0_ffn_k_scale", (6, 64)),
        "wo": di("L0_ffn_Wo", (6, 1024, D_)),
        "ln_g": di("L0_ffn_ln_g", (6, D_)),
        "ln_b": di("L0_ffn_ln_b", (6, D_)),
        "w1": di("L0_ffn_W1", (6, D_, 5460)),
        "b1": di("L0_ffn_b1", (6, 5460)),
        "w2": di("L0_ffn_W2", (6, 2730, D_)),
        "b2": di("L0_ffn_b2", (6, D_)),
    }
    W1a_qkv = di("L1_attn_W_qkv", (D_, 3 * D_))
    W1a_out = di("L1_attn_W_out", (D_, D_))
    W1a_rel = di("L1_attn_rel_emb", (32, 16))
    W1f = {}
    for n, shp in [("w1_w", (4096, D_)), ("w2_w", (4096, D_)), ("h1_w", (4096, 4096)),
                   ("h2_w", (4096, 4096)), ("w3_w", (D_, 4096)),
                   ("w1_s", (4096,)), ("w2_s", (4096,)), ("h1_s", (4096,)),
                   ("h2_s", (4096,)), ("w3_s", (D_,))]:
        W1f[n] = di("L1_ffn_" + n, shp)
    for n, shp in [("c_eye", (P, P)), ("c_l0fm", (P, 4 * 512)), ("c_l0am", (P, 4 * 512)),
                   ("c_rotq", (P, 2 * T)), ("c_rotk", (P, 2 * T)),
                   ("c_blk2", (P, 2)), ("c_sel2", (2, P))]:
        di(n, shp)
    oht = nc.dram_tensor("c_t5oh", [32, 2 * T - 1], F32, kind="ExternalInput").ap()
    ctx.dram["c_t5oh"] = oht
    idxt = nc.dram_tensor("c_t5idx", [P, 16], I32, kind="ExternalInput").ap()
    ctx.dram["c_t5idx"] = idxt

    xo = nc.dram_tensor("xo_fm", [D_, T], F32, kind="ExternalOutput").ap()
    ya = nc.dram_tensor("ya", [D_, T], F32, kind="Internal").ap()
    yb = nc.dram_tensor("yb", [D_, T], F32, kind="Internal").ap()
    xA = nc.dram_tensor("xA", [D_, T], F32, kind="Internal").ap()
    xB = nc.dram_tensor("xB", [D_, T], F32, kind="Internal").ap()
    scr_a = nc.dram_tensor("scr_a", [4096, T], BF16, kind="Internal").ap()
    scr_b = nc.dram_tensor("scr_b", [4096, T], BF16, kind="Internal").ap()
    bias_hbm = nc.dram_tensor("bias_hbm", [16 * (2 * T - 1)], BF16, kind="Internal").ap()

    from contextlib import ExitStack
    es = ExitStack()
    ctx.sb = es.enter_context(nc.sbuf_tensor("SB", [P, SBUF_BYTES], U8))
    ctx.ps = es.enter_context(nc.psum_tensor("PS", [P, 16384], U8))
    emit_prelude(ctx)
    emit_l0ffn(ctx, x, ya, W0f, xA, xB, tag="s0")
    phase_barrier(b)
    emit_l0attn(ctx, ya, yb, W0a, tag="m0")
    phase_barrier(b)
    emit_l0ffn(ctx, yb, ya, W0f, xA, xB, tag="s1")
    phase_barrier(b)
    emit_l1ffn(ctx, ya, yb, W1f, scr_a, scr_b, tag="f0")
    phase_barrier(b)
    emit_l1attn(ctx, yb, ya, W1a_qkv, W1a_out, W1a_rel, bias_hbm, tag="a0")
    phase_barrier(b)
    done = emit_l1ffn(ctx, ya, xo, W1f, scr_a, scr_b, tag="f1")
    b.nopop("sp", done)

    sems = {}
    for name in b.cnt:
        sems[name] = es.enter_context(nc.semaphore(name))
    blk = es.enter_context(nc.Block())
    b.build_into(blk, sems)
    es.close()
    return nc, b


def host_consts(T):
    HS, WIN = 64, 256
    half = np.arange(0, HS, 2, dtype=np.float64)
    invf = 1.0 / 10000.0 ** (half / HS)
    t = np.arange(T, dtype=np.float64)
    fr = np.concatenate([t[:, None] * invf] * 2, -1)
    base = (half + 0.4 * HS) / (1.4 * HS)
    sc = base[None, :] ** ((t - T // 2) / (WIN // 2))[:, None]
    sc = np.concatenate([sc, sc], -1)
    cos, sin = np.cos(fr), np.sin(fr)
    sign = np.where(np.arange(64) < 32, -1.0, 1.0)

    def fm(a):
        m = a.T
        return np.concatenate([m, m], 0).astype(np.float32)

    rotq = np.concatenate([fm(cos * sc), fm(sin * sc * sign[None, :])], 1)
    rotk = np.concatenate([fm(cos / sc), fm(sin / sc * sign[None, :])], 1)

    l0fm = np.zeros((P, 4 * 512), np.float32)
    for v in range(4):
        qsub, w0 = v % 2, v >= 2
        for p in range(P):
            i = p + 128 * qsub
            ok = np.zeros(512, bool)
            ok[i:i + 257] = True
            if w0:
                ok[:256] = False
            l0fm[p, v * 512:(v + 1) * 512] = np.where(ok, 0.0, -3e38)

    l0am = np.zeros((P, 4 * 512), np.float32)
    for v in range(4):
        for p in range(P):
            jmax = v * 128 + p
            l0am[p, v * 512 + jmax + 1:(v + 1) * 512] = -3e38

    RLEN = 2 * T - 1
    NB, MD = 32, 128
    r = np.arange(-(T - 1), T)
    nb = NB // 2
    rb = (r > 0).astype(np.int64) * nb
    ra = np.abs(r)
    mx = nb // 2
    small = ra < mx
    large = mx + (np.log(np.maximum(ra, 1) / mx) / np.log(MD / mx) * (nb - mx)).astype(np.int64)
    large = np.minimum(large, nb - 1)
    buck = rb + np.where(small, ra, large)
    oh = np.zeros((NB, RLEN), np.float32)
    oh[buck, np.arange(RLEN)] = 1.0
    idx = np.zeros((P, 16), np.int32)
    for h in range(16):
        idx[:, h] = h * RLEN + 127 - np.arange(P)

    blk2 = np.zeros((P, 2), np.float32)
    blk2[:64, 0] = 1
    blk2[64:, 1] = 1

    return dict(c_eye=np.eye(P, dtype=np.float32), c_l0fm=l0fm, c_l0am=l0am,
                c_rotq=rotq, c_rotk=rotk, c_blk2=blk2, c_sel2=blk2.T.copy(),
                c_t5oh=oh, c_t5idx=idx)


_CACHED = {}


def _get_program():
    if "nc" not in _CACHED:
        nc, b = build_module(2048)
        _CACHED["nc"] = nc
    return _CACHED["nc"]


def kernel(**inputs):
    T = 2048
    nc = _get_program()
    consts = host_consts(T)
    x = inputs["L0_attn_x"]
    names = [
        "L0_attn_W_dq", "L0_attn_W_uq", "L0_attn_W_dkv", "L0_attn_W_uk",
        "L0_attn_W_uv", "L0_attn_W_o", "L0_ffn_Wqkv", "L0_ffn_q_scale",
        "L0_ffn_k_scale", "L0_ffn_Wo", "L0_ffn_ln_g", "L0_ffn_ln_b",
        "L0_ffn_W1", "L0_ffn_b1", "L0_ffn_W2", "L0_ffn_b2",
        "L1_attn_W_qkv", "L1_attn_W_out", "L1_attn_rel_emb",
        "L1_ffn_w1_w", "L1_ffn_w1_s", "L1_ffn_w2_w", "L1_ffn_w2_s",
        "L1_ffn_h1_w", "L1_ffn_h1_s", "L1_ffn_h2_w", "L1_ffn_h2_s",
        "L1_ffn_w3_w", "L1_ffn_w3_s",
    ]
    shared = {n: np.ascontiguousarray(np.asarray(inputs[n], dtype=np.float32))
              for n in names}
    shared.update(consts)
    in_maps = []
    for i in range(8):
        m = dict(shared)
        m["x_fm"] = np.ascontiguousarray(np.asarray(x[i], dtype=np.float32).T)
        in_maps.append(m)
    res = run_bass_kernel_spmd(nc, in_maps, core_ids=list(range(8)))
    out = np.stack([np.ascontiguousarray(r["xo_fm"].T) for r in res.results])
    return out.astype(np.float32)


# version 10
# speedup vs baseline: 1.0457x; 1.0245x over previous
import numpy as np
import ml_dtypes
import bass_rust
import concourse.bass as bass
import concourse.mybir as mybir
from concourse.bass_utils import run_bass_kernel_spmd

F32 = mybir.dt.float32
BF16 = mybir.dt.bfloat16
I32 = mybir.dt.int32
U8 = mybir.dt.uint8
AF = mybir.ActivationFunctionType
ALU = mybir.AluOpType
AX = mybir.AxisListType

P = 128
D = 1024
DT = D // P
SBUF_BYTES = 203 * 1024
SZ = {F32: 4, BF16: 2, I32: 4, U8: 1}

ENGINES = ("pe", "act", "dve", "pool", "sp")


class Bld:
    def __init__(self, nc):
        self.nc = nc
        self.stream = {e: [] for e in ENGINES}
        self.cnt = {}
        self.known = {e: {} for e in ENGINES}
        self.base = []

    def barrier(self):
        self.base = [(s, v) for s, v in self.cnt.items() if v > 0]
        return list(self.base)

    def sem(self, name):
        if name not in self.cnt:
            self.cnt[name] = 0
        return name

    def op(self, eng, fn, waits=(), inc=None, n=1):
        elide = eng != "pool"
        ws = []
        seen = {}
        for t in list(self.base) + list(waits):
            if t is None:
                continue
            s, v = t
            if v <= 0 or (elide and v <= self.known[eng].get(s, 0)):
                continue
            if v <= seen.get(s, 0):
                continue
            seen[s] = v
            ws.append((s, v))
            if elide:
                self.known[eng][s] = v
        ws = [(s, v) for s, v in ws if v == seen[s]]
        tok = None
        if inc is None:
            inc = eng
        self.sem(inc)
        self.cnt[inc] += n
        tok = (inc, self.cnt[inc])
        self.stream[eng].append((ws, fn, inc, n))
        return tok

    def nopop(self, eng, waits):
        return self.op(eng, lambda E: E.nop(), waits, inc=eng + "_ck")

    def ckpt(self, eng, sem):
        return self.nopop(eng, [(sem, self.cnt.get(sem, 0))])

    def dma(self, eng, out, in_, waits=(), sem=None, exclusive=True):
        sem = self.sem(sem or ("dq_" + eng))
        ws = list(waits)
        if exclusive and self.cnt.get(sem, 0) > 0:
            ws.append((sem, self.cnt[sem]))
        return self.op(eng, lambda E, o=out, i=in_: E.dma_start(o, i), ws,
                       inc=sem, n=16)

    def build_into(self, block, semobjs):
        decos = {"pe": block.tensor, "act": block.scalar, "dve": block.vector,
                 "pool": block.gpsimd, "sp": block.sync}
        for eng in ENGINES:
            ops = self.stream[eng]

            @decos[eng]
            def _(E, ops=ops):
                for ws, fn, inc, n in ops:
                    for s, v in ws:
                        E.wait_ge(semobjs[s], v)
                    ins = fn(E)
                    ins.then_inc(semobjs[inc], n)


class Arena:
    def __init__(self, size):
        self.size = size
        self.off = 0
        self.hi = 0

    def alloc(self, nbytes, align=64):
        self.off = (self.off + align - 1) // align * align
        r = self.off
        self.off += nbytes
        self.hi = max(self.hi, self.off)
        assert self.off <= self.size, f"arena overflow: {self.off} > {self.size}"
        return r

    def mark(self):
        return self.off

    def reset(self, mark):
        self.off = mark


class Ctx:
    def __init__(self, nc, b, T):
        self.nc = nc
        self.b = b
        self.T = T
        self.C = T // 512
        self.ar = Arena(SBUF_BYTES)
        self.sb = None
        self.ps = None
        self.dram = {}
        self.bankuse = {}
        self.stg_war = {}

    def v(self, off, cols, dtype, p0=0, pn=P):
        return self.sb[p0:p0 + pn, off:off + cols * SZ[dtype]].bitcast(dtype)

    def bank(self, i, cols, dtype, p0=0, pn=P):
        o = 2048 * i
        return self.ps[p0:p0 + pn, o:o + cols * SZ[dtype]].bitcast(dtype)


def row1(dram_1d):
    return dram_1d.rearrange("(o x) -> o x", o=1)


def col1(dram_1d):
    return dram_1d.rearrange("(x o) -> x o", o=1)


def emit_ln(ctx, xsrc, xn_off, xn_stride, g_dram=None, bg_dram=None, shift=False,
            tag="ln", want_xb=False, src_tok=(), fout_dram=None):
    b, T, C = ctx.b, ctx.T, ctx.C
    mark0 = ctx.ar.mark()

    def sl(k, c0, n, tiles):
        off = 0 if (shift and k >= DT // 2) else 1
        return tiles[k][:, off + c0: off + c0 + n]

    oc = ctx.v(ctx.ar.alloc(256), 1, BF16)
    oc_tok = b.op("pool", lambda E, o=oc: E.memset(o, 1.0), list(src_tok))

    g_rows, b_cols, g_tok, b_tok = [], [], [], []
    for k in range(DT):
        gv = ctx.v(ctx.ar.alloc(256), 128, BF16, pn=1)
        g_rows.append(gv)
        if g_dram is not None:
            b.dma("pool", gv, row1(g_dram[k * P:(k + 1) * P]), sem="q_gld",
                  exclusive=False, waits=list(src_tok))
            g_tok.append(None)
        else:
            g_tok.append(b.op("pool", lambda E, o=gv: E.memset(o, 1.0), list(src_tok)))
        if bg_dram is not None:
            bv = ctx.v(ctx.ar.alloc(64), 1, F32)
            b_cols.append(bv)
            b.dma("sp", bv, col1(bg_dram[k * P:(k + 1) * P]), sem="q_sld",
                  exclusive=False, waits=list(src_tok))
            b_tok.append(None)
        else:
            b_cols.append(None)
            b_tok.append(None)
    if g_dram is not None:
        gall = b.ckpt("pool", "q_gld")
        g_tok = [gall if t is None else t for t in g_tok]
    if bg_dram is not None:
        ball = b.ckpt("sp", "q_sld")
        b_tok = [ball for _ in b_tok]

    rstd_bf = ctx.v(ctx.ar.alloc(T * 2), T, BF16, pn=1)
    nmr_bf = ctx.v(ctx.ar.alloc(T * 2), T, BF16, pn=1)
    srows = ctx.ar.alloc(4 * 512 * 4)
    rstd_tok = [None] * C
    nmr_tok = [None] * C

    mark1 = ctx.ar.mark()
    xb_stride = (T + 4) * 2
    xb_off = ctx.ar.alloc(DT * xb_stride)
    xb_tiles = [ctx.v(xb_off + k * xb_stride, T + 1, BF16) for k in range(DT)]
    xb_tok = []
    for k in range(DT):
        z = b.op("pool", lambda E, t=xb_tiles[k]: E.memset(t[:, 0:1], 0.0),
                 list(src_tok))
        tk = b.dma("pool", xb_tiles[k][:, 1:T + 1], xsrc[k * P:(k + 1) * P, :],
                   sem=f"q_gx{k}", waits=list(src_tok))
        xb_tok.append(tk)

    sq_off = ctx.ar.alloc(2 * 512 * 2)
    sq_bufs = [ctx.v(sq_off + i * 1024, 512, BF16) for i in range(2)]
    sq_war = [None, None]

    bank_consumed = ctx.bankuse

    for c in range(C):
        bmu_i, bsq_i = (0, 1) if c % 2 == 0 else (2, 3)
        bmu = ctx.bank(bmu_i, 512, F32, pn=1)
        bsq = ctx.bank(bsq_i, 512, F32, pn=1)
        mu_tok = sq_mm_tok = None
        for k in range(DT):
            w = [xb_tok[k], oc_tok, bank_consumed.get(bmu_i)]
            mu_tok = b.op("pe", lambda E, o=bmu, x=sl(k, c * 512, 512, xb_tiles),
                          k=k: E.matmul(o, oc, x, start=(k == 0), stop=(k == DT - 1)), w)
        for k in range(DT):
            i2 = k % 2
            sq = sq_bufs[i2]
            sqt = b.op("dve", lambda E, s=sq, x=sl(k, c * 512, 512, xb_tiles):
                       E.tensor_tensor(s, x, x, op=ALU.mult),
                       [xb_tok[k], sq_war[i2]])
            w = [sqt, oc_tok, bank_consumed.get(bsq_i)]
            sq_mm_tok = b.op("pe", lambda E, o=bsq, s=sq, k=k:
                             E.matmul(o, oc, s, start=(k == 0), stop=(k == DT - 1)), w)
            sq_war[i2] = sq_mm_tok
        mu_row = ctx.v(srows, 512, F32, pn=1)
        var_row = ctx.v(srows + 2048, 512, F32, pn=1)
        musq = ctx.v(srows + 4096, 512, F32, pn=1)
        rstd_f = ctx.v(srows + 6144, 512, F32, pn=1)
        t1 = b.op("dve", lambda E, o=mu_row, i=bmu: E.tensor_scalar(
            o, i, 1.0 / D, None, op0=ALU.mult), [mu_tok])
        bank_consumed[bmu_i] = t1
        t2 = b.op("dve", lambda E, o=var_row, i=bsq: E.tensor_scalar(
            o, i, 1.0 / D, 1e-5, op0=ALU.mult, op1=ALU.add), [sq_mm_tok])
        bank_consumed[bsq_i] = t2
        t3 = b.op("dve", lambda E, o=musq, i=mu_row: E.tensor_tensor(
            o, i, i, op=ALU.mult), [t1])
        t4 = b.op("dve", lambda E, o=var_row, a=var_row, m=musq: E.tensor_tensor(
            o, a, m, op=ALU.subtract), [t2, t3])
        t5 = b.op("act", lambda E, o=rstd_f, i=var_row: E.activation(o, i, AF.Ln), [t4])
        t6 = b.op("act", lambda E, o=rstd_f, i=rstd_f: E.activation(
            o, i, AF.Exp, scale=-0.5), [t5])
        rstd_tok[c] = b.op("dve", lambda E, o=rstd_bf[:, c * 512:c * 512 + 512],
                           i=rstd_f: E.tensor_copy(o, i), [t6])
        t7 = b.op("dve", lambda E, o=nmr_bf[:, c * 512:c * 512 + 512], m=mu_row,
                  r=rstd_f: E.tensor_tensor(o, m, r, op=ALU.mult), [t6, t3])
        nmr_tok[c] = b.op("dve", lambda E, o=nmr_bf[:, c * 512:c * 512 + 512]:
                          E.tensor_scalar(o, o, -1.0, None, op0=ALU.mult), [t7])

    sq_done = sq_mm_tok

    tmp_off = ctx.ar.alloc(2 * 512 * 4)
    xn_tiles = [ctx.v(xn_off + k * xn_stride, T + 1, BF16) for k in range(DT)]
    if fout_dram is not None:
        fo_off = ctx.ar.alloc(4 * 512 * 4)
        fo_war = {}
    xn_tok = [[None] * C for _ in range(DT)]
    xn_z = [None] * DT
    for k in range(DT):
        for c in range(C):
            kc = k * C + c
            bR_i, bM_i = (4, 5) if kc % 2 == 0 else (6, 7)
            bR = ctx.bank(bR_i, 512, F32)
            bM = ctx.bank(bM_i, 512, F32)
            rtok = b.op("pe", lambda E, o=bR, g=g_rows[k],
                        r=rstd_bf[:, c * 512:c * 512 + 512]: E.matmul(
                o, g, r, start=True, stop=True),
                [rstd_tok[c], g_tok[k], bank_consumed.get(bR_i)])
            mtok = b.op("pe", lambda E, o=bM, g=g_rows[k],
                        m=nmr_bf[:, c * 512:c * 512 + 512]: E.matmul(
                o, g, m, start=True, stop=True),
                [nmr_tok[c], bank_consumed.get(bM_i)])
            xsl = sl(k, c * 512, 512, xb_tiles)
            tmp = ctx.v(tmp_off + (kc % 2) * 2048, 512, F32)
            a1 = b.op("dve", lambda E, o=tmp, x=xsl, R=bR: E.tensor_tensor(
                o, x, R, op=ALU.mult), [xb_tok[k], rtok])
            bank_consumed[bR_i] = a1
            if fout_dram is not None:
                i4 = kc % 4
                fo = ctx.v(fo_off + i4 * 2048, 512, F32)
                a2 = b.op("dve", lambda E, o=fo, t=tmp, M=bM: E.tensor_tensor(
                    o, t, M, op=ALU.add), [a1, mtok, fo_war.get(i4)])
                d = b.dma("sp", fout_dram[k * P:(k + 1) * P, c * 512:c * 512 + 512],
                          fo, waits=[a2], sem=f"q_sxo{i4}")
                fo_war[i4] = d
            elif b_cols[k] is not None:
                a2 = b.op("dve", lambda E, o=xn_tiles[k][:, 1 + c * 512:1 + c * 512 + 512],
                          t=tmp, bb=b_cols[k], M=bM: E.scalar_tensor_tensor(
                    o, t, bb, M, op0=ALU.add, op1=ALU.add), [a1, mtok, b_tok[k]])
            else:
                a2 = b.op("dve", lambda E, o=xn_tiles[k][:, 1 + c * 512:1 + c * 512 + 512],
                          t=tmp, M=bM: E.tensor_tensor(o, t, M, op=ALU.add),
                          [a1, mtok])
            bank_consumed[bM_i] = a2
            xn_tok[k][c] = a2

    store_ck = None
    if fout_dram is not None:
        store_ck = b.nopop("sp", [(f"q_sxo{i}", b.cnt.get(f"q_sxo{i}", 0)) for i in range(4)])
    ctx.ar.reset(mark0)
    return dict(xn_tiles=xn_tiles, xn_tok=xn_tok, xn_z=xn_z, store_ck=store_ck)


def emit_prelude(ctx):
    b = ctx.b
    idb = ctx.v(ctx.ar.alloc(128 * 2), 128, BF16)
    t = b.dma("pool", idb, ctx.dram["c_eye"], sem="q_pre")
    ctx.id_bf = idb
    ctx.id_tok = ("q_pre", b.cnt["q_pre"])


def tlin_stage(ctx, w_dram, KT, MT, NIN, rhs_fn, out_fn, rhs_deps, tag,
               tr_banks=(0, 1), mm_banks=(2, 3), war0=(), scratch_off=None):
    b, C = ctx.b, ctx.C
    G = 8
    ngrp = (KT + G - 1) // G
    nwt = 2 if KT <= 8 else 1
    mark = ctx.ar.mark()
    wraw_sz = NIN * 2
    wT_sz = KT * 128 * 2
    if scratch_off is not None:
        wraw_off = scratch_off
        wT_off = scratch_off + 2 * wraw_sz
    else:
        wraw_off = ctx.ar.alloc(2 * wraw_sz)
        wT_off = ctx.ar.alloc(nwt * wT_sz)
    wload_tok = [None, None]
    wT_tok = [[None] * ngrp, [None] * ngrp]
    slot_last = [None, None]
    mm_last = None
    for j in range(MT):
        sl = j % 2
        wraw = ctx.v(wraw_off + sl * wraw_sz, NIN, BF16)
        wT = ctx.v(wT_off + (sl % nwt) * wT_sz, KT * 128, BF16)
        war = [slot_last[sl]] if slot_last[sl] is not None else list(war0)
        wload_tok[sl] = b.dma("pool", wraw, w_dram[j * P:(j + 1) * P, :],
                              waits=war, sem=f"q_gw{sl}")
        for g in range(ngrp):
            tb_i = tr_banks[(j * ngrp + g) % 2]
            tb = ctx.bank(tb_i, 1024, BF16)
            n_in_g = min(G, KT - g * G)
            ttok = None
            for ki in range(n_in_g):
                k = g * G + ki
                w = [wload_tok[sl], ctx.id_tok, ctx.bankuse.get(tb_i)]
                ttok = b.op("pe", lambda E, o=tb[:, ki * 128:(ki + 1) * 128],
                            i=wraw[:, k * 128:(k + 1) * 128]:
                            E.transpose(o, i, ctx.id_bf), w)
            ev = b.op("act", lambda E, o=wT[:, g * G * 128:(g * G + n_in_g) * 128],
                      i=tb[:, 0:n_in_g * 128]: E.copy(o, i), [ttok])
            ctx.bankuse[tb_i] = ev
            wT_tok[sl][g] = ev
        for c in range(C):
            mb_i = mm_banks[(j * C + c) % 2]
            mb = ctx.bank(mb_i, 512, F32)
            mm = None
            for k in range(KT):
                w = [wT_tok[sl][k // G], ctx.bankuse.get(mb_i)] + (rhs_deps if k == 0 and c == 0 else [])
                mm = b.op("pe", lambda E, o=mb, l=wT[:, k * 128:(k + 1) * 128],
                          r=rhs_fn(k, c), k=k:
                          E.matmul(o, l, r, start=(k == 0), stop=(k == KT - 1)), w)
            mm_last = mm
            ctx.bankuse[mb_i] = out_fn(j, c, mb, mm)
        slot_last[sl] = mm_last
    ctx.ar.reset(mark)
    return mm_last


def emit_scale_cols(ctx, s_dram, ntiles, half=False, tag="sc"):
    b = ctx.b
    views = []
    for j in range(ntiles):
        v = ctx.v(ctx.ar.alloc(64), 1, F32)
        views.append(v)
        b.dma("sp", v, col1(s_dram[j * P:(j + 1) * P]), sem="q_sld", exclusive=False)
    tok = b.ckpt("sp", "q_sld")
    toks = [tok] * ntiles
    if half:
        newtok = []
        for j in range(ntiles):
            t = b.op("dve", lambda E, o=views[j]: E.tensor_scalar(
                o, o, 0.5, None, op0=ALU.mult), [tok])
            newtok.append(t)
        toks = newtok
    return views, toks


def emit_l1ffn(ctx, x_in, x_out, W, scr_a, scr_b, tag="f"):
    b, T, C = ctx.b, ctx.T, ctx.C
    HT = 32
    mark0 = ctx.ar.mark()

    s1_v, s1_t = emit_scale_cols(ctx, W["w1_s"], HT)
    s2_v, s2_t = emit_scale_cols(ctx, W["w2_s"], HT)
    h1_v, h1_t = emit_scale_cols(ctx, W["h1_s"], HT)
    h2_v, h2_t = emit_scale_cols(ctx, W["h2_s"], HT)
    w3_v, w3_t = emit_scale_cols(ctx, W["w3_s"], DT, half=True)

    tl_off = ctx.ar.alloc(2 * 4096 * 2 + 32 * 128 * 2)

    xn_stride = (T + 4) * 2
    xn_off = ctx.ar.alloc(DT * xn_stride)
    ln = emit_ln(ctx, x_in, xn_off, xn_stride, tag=tag + "l")
    xn, xn_tok, xn_z = ln["xn_tiles"], ln["xn_tok"], ln["xn_z"]
    xn_all = [xn_tok[k][c] for k in range(DT) for c in range(ctx.C)]
    mark_ln = ctx.ar.mark()

    big_sz = T * 2
    big_off = ctx.ar.alloc(HT * big_sz)
    big = [ctx.v(big_off + j * big_sz, T, BF16) for j in range(HT)]

    def rhs_xn(k, c):
        return xn[k][:, 1 + c * 512: 1 + c * 512 + 512]

    out_last = [None]

    def out_t1(j, c, ps, mm):
        t = b.op("act", lambda E, o=big[j][:, c * 512:c * 512 + 512], i=ps:
                 E.activation(o, i, AF.Silu, scale=s1_v[j]), [mm, s1_t[j]])
        out_last[0] = t
        return t

    st1_mm = tlin_stage(ctx, W["w1_w"], DT, HT, D, rhs_xn, out_t1, xn_all, tag + "1",
                        war0=[xn_all[-1]], scratch_off=tl_off)
    st1_out = out_last[0]

    def out_h(j, c, ps, mm):
        t = b.op("dve", lambda E, o=big[j][:, c * 512:c * 512 + 512], i=ps,
                 s=s2_v[j], t1s=big[j][:, c * 512:c * 512 + 512]:
                 E.scalar_tensor_tensor(o, i, s, t1s, op0=ALU.mult, op1=ALU.mult),
                 [mm, s2_t[j], out_last[0] if out_last[0] is not None else None])
        out_last[0] = t
        return t

    st2_mm = tlin_stage(ctx, W["w2_w"], DT, HT, D, rhs_xn, out_h, xn_all, tag + "2",
                        tr_banks=(4, 5), mm_banks=(6, 7), war0=[st1_mm],
                        scratch_off=tl_off)
    st2_out = out_last[0]


    if DT * xn_stride >= 20480:
        st_off = xn_off
        xi_off = xn_off + 4 * 1024
        xo_off = xn_off + 4 * 1024 + 4 * 2048
    else:
        st_off = ctx.ar.alloc(4 * 1024)
        xi_off = ctx.ar.alloc(4 * 2048)
        xo_off = ctx.ar.alloc(4 * 2048)

    def mk_store_out(scale_v, scale_t, scr, qsem):
        def out(j, c, ps, mm):
            i = (j * C + c) % 4
            stg = ctx.v(st_off + i * 1024, 512, BF16)
            t = b.op("act", lambda E, o=stg, i=ps: E.activation(
                o, i, AF.Gelu, scale=scale_v[j]),
                [mm, scale_t[j], ctx.stg_war.get(i)])
            d = b.dma("sp", scr[j * P:(j + 1) * P, c * 512:c * 512 + 512], stg,
                      waits=[t], sem=f"{qsem}{i}")
            ctx.stg_war[i] = d
            return t
        return out

    def rhs_big(k, c):
        return big[k][:, c * 512:c * 512 + 512]

    last = tlin_stage(ctx, W["h1_w"], HT, HT, 4096, rhs_big,
                      mk_store_out(h1_v, h1_t, scr_a, "q_s3"), [st2_out], tag + "3",
                      war0=[st2_mm], scratch_off=tl_off)

    g1_tok = []
    for j in range(HT):
        g1_tok.append(b.dma("sp", big[j], scr_a[j * P:(j + 1) * P, :],
                            waits=[last] + [ctx.stg_war.get(i) for i in range(4)],
                            sem="q_sg", exclusive=False))
    g1_all = b.ckpt("sp", "q_sg")
    last4 = tlin_stage(ctx, W["h2_w"], HT, HT, 4096, rhs_big,
                       mk_store_out(h2_v, h2_t, scr_b, "q_s4"), [g1_all], tag + "4",
                       tr_banks=(4, 5), mm_banks=(6, 7), war0=[last],
                       scratch_off=tl_off)

    g2_tok = []
    for j in range(HT):
        g2_tok.append(b.dma("sp", big[j], scr_b[j * P:(j + 1) * P, :],
                            waits=[last4] + [ctx.stg_war.get(i) for i in range(4)],
                            sem="q_sg2", exclusive=False))
    g2_all = b.ckpt("sp", "q_sg2")

    xi_war = {}
    xo_war = {}

    def out_res(j, c, ps, mm):
        i = (j * C + c) % 4
        xi = ctx.v(xi_off + i * 2048, 512, F32)
        xo = ctx.v(xo_off + i * 2048, 512, F32)
        ld = b.dma("sp", xi, x_in[j * P:(j + 1) * P, c * 512:c * 512 + 512],
                   waits=[xi_war.get(i) or g2_all], sem=f"q_sxi{i}")
        t = b.op("dve", lambda E, o=xo, i=ps, s=w3_v[j], x=xi:
                 E.scalar_tensor_tensor(o, i, s, x, op0=ALU.mult, op1=ALU.add),
                 [mm, ld, w3_t[j], xo_war.get(i)])
        xi_war[i] = t
        d = b.dma("sp", x_out[j * P:(j + 1) * P, c * 512:c * 512 + 512], xo,
                  waits=[t], sem=f"q_sxo{i}")
        xo_war[i] = d
        return t

    tlin_stage(ctx, W["w3_w"], HT, DT, 4096, rhs_big, out_res, [g2_all], tag + "5",
               war0=[last4], scratch_off=tl_off)
    out_done = [("q_sxo%d" % i, b.cnt.get("q_sxo%d" % i, 0)) for i in range(4)]
    ctx.ar.reset(mark0)
    return out_done


def wslice_ap(w_dram, ncols_total, col0, ncols, ktiles=DT):
    c = w_dram.copy()
    c.ap = bass_rust.VecI64Pair([[ncols_total, P], [ncols_total * P, ktiles], [1, ncols]])
    c.offset = w_dram.offset + col0
    return c


def emit_l1attn(ctx, x_in, x_out, W_qkv, W_out, rel_emb, bias_hbm, tag="a"):
    b, T, C = ctx.b, ctx.T, ctx.C
    TT_ = T // 128
    NG = (TT_ + 7) // 8
    mark0 = ctx.ar.mark()
    RLEN = 2 * T - 1
    WT = 2 * T - 128
    SKB = T - 128
    bias2d = bias_hbm.rearrange("(a b) -> a b", b=RLEN)
    bias_fl = bias_hbm.rearrange("(x o) -> x o", o=1)

    idx = ctx.v(ctx.ar.alloc(64), 16, I32)
    idx_t = b.dma("sp", idx, ctx.dram["c_t5idx"], sem="q_sld")

    mark_b = ctx.ar.mark()
    oh = ctx.v(ctx.ar.alloc(RLEN * 2 + 64), RLEN, BF16, pn=32)
    b.dma("pool", oh, ctx.dram["c_t5oh"], sem="q_gld", exclusive=False)
    rel = ctx.v(ctx.ar.alloc(64), 16, BF16, pn=32)
    b.dma("pool", rel, rel_emb, sem="q_gld", exclusive=False)
    oh_t = rel_t = b.ckpt("pool", "q_gld")
    brow = ctx.v(ctx.ar.alloc(RLEN * 2 + 64), RLEN, BF16, pn=16)
    ev_last = None
    for c in range((RLEN + 511) // 512):
        n = min(512, RLEN - c * 512)
        bk = ctx.bank(c % 2, n, F32, pn=16)
        mm = b.op("pe", lambda E, o=bk, r=rel, h=oh[:, c * 512:c * 512 + n]:
                  E.matmul(o, r, h, start=True, stop=True),
                  [oh_t, rel_t, ctx.bankuse.get(c % 2)])
        ev_last = b.op("dve", lambda E, o=brow[:, c * 512:c * 512 + n], i=bk:
                       E.tensor_copy(o, i), [mm])
        ctx.bankuse[c % 2] = ev_last
    b.dma("sp", bias2d, brow, waits=[ev_last], sem="q_sbias")
    bias_ck = b.ckpt("sp", "q_sbias")
    ctx.ar.reset(mark_b)

    xn_stride = (T + 4) * 2
    xn_off = ctx.ar.alloc(DT * xn_stride)
    ln = emit_ln(ctx, x_in, xn_off, xn_stride, tag=tag + "l", src_tok=[bias_ck])
    xn, xn_tok = ln["xn_tiles"], ln["xn_tok"]

    q_off = ctx.ar.alloc(DT * T * 2)
    k_off = ctx.ar.alloc(DT * T * 2)
    v_off = ctx.ar.alloc(TT_ * 1024 * 2)
    o_off = ctx.ar.alloc(DT * T * 2)
    q_sb = [ctx.v(q_off + j * T * 2, T, BF16) for j in range(DT)]
    k_sb = [ctx.v(k_off + j * T * 2, T, BF16) for j in range(DT)]
    v_sb = [ctx.v(v_off + j * 1024 * 2, 1024, BF16) for j in range(TT_)]
    o_sb = [ctx.v(o_off + j * T * 2, T, BF16) for j in range(DT)]
    mark1 = ctx.ar.mark()

    ws_off = ctx.ar.alloc(2 * DT * 128 * 2)
    vs_off = ctx.ar.alloc(DT * 512 * 2)
    q_tok = [[None] * C for _ in range(DT)]
    k_tok = [[None] * C for _ in range(DT)]
    v_tok = [None] * TT_
    ln_done = xn_tok[DT - 1][C - 1]
    ws_war = [ln_done, ln_done]
    mmb = 0
    for dst, base, toks in ((q_sb, 0, q_tok), (k_sb, 1024, k_tok)):
        for j in range(DT):
            sl = j % 2
            ws = ctx.v(ws_off + sl * DT * 128 * 2, DT * 128, BF16)
            wt = b.dma("pool", ws, wslice_ap(W_qkv, 3072, base + j * 128, 128),
                       waits=[ws_war[sl]], sem=f"q_gw{sl}")
            mm = None
            for c in range(C):
                bi = 2 + (mmb % 2)
                mmb += 1
                bk = ctx.bank(bi, 512, F32)
                for k in range(DT):
                    w = [wt, xn_tok[k][c], ctx.bankuse.get(bi)]
                    mm = b.op("pe", lambda E, o=bk, l=ws[:, k * 128:(k + 1) * 128],
                              r=xn[k][:, 1 + c * 512:1 + c * 512 + 512], k=k:
                              E.matmul(o, l, r, start=(k == 0), stop=(k == DT - 1)), w)
                ev = b.op("act" if (j + c) % 2 else "dve",
                          lambda E, o=dst[j][:, c * 512:c * 512 + 512], i=bk:
                          (E.copy(o, i) if hasattr(E, "copy") else E.tensor_copy(o, i)),
                          [mm])
                toks[j][c] = ev
                ctx.bankuse[bi] = ev
            ws_war[sl] = mm
    vs_war = ln_done
    for vc in range(2):
        vs = ctx.v(vs_off, DT * 512, BF16)
        wt = b.dma("pool", vs, wslice_ap(W_qkv, 3072, 2048 + vc * 512, 512),
                   waits=[vs_war], sem="q_gw0")
        for tt in range(TT_):
            bi = 4 + (tt % 2)
            bk = ctx.bank(bi, 512, F32)
            mm = None
            for k in range(DT):
                w = [wt, xn_tok[k][(tt * 128) // 512], ctx.bankuse.get(bi)]
                mm = b.op("pe", lambda E, o=bk, l=xn[k][:, 1 + tt * 128:1 + (tt + 1) * 128],
                          r=vs[:, k * 512:(k + 1) * 512], k=k:
                          E.matmul(o, l, r, start=(k == 0), stop=(k == DT - 1)), w)
            ev = b.op("act" if tt % 2 else "dve",
                      lambda E, o=v_sb[tt][:, vc * 512:(vc + 1) * 512], i=bk:
                      (E.copy(o, i) if hasattr(E, "copy") else E.tensor_copy(o, i)),
                      [mm])
            v_tok[tt] = ev
            ctx.bankuse[bi] = ev
        vs_war = mm

    ctx.ar.reset(mark1)
    bsk_off = ctx.ar.alloc(2 * WT * 2)
    P_off = ctx.ar.alloc(2 * T * 2)
    Pf_off = ctx.ar.alloc(T * 4)
    PT_off = ctx.ar.alloc(2 * TT_ * 128 * 2)
    sm_off = ctx.ar.alloc(2 * 192)
    P_bf2 = [ctx.v(P_off + i * T * 2, T, BF16) for i in range(2)]
    P_f2 = [ctx.v(Pf_off, T, F32), ctx.v(Pf_off, T, F32)]
    PT_sb2 = [ctx.v(PT_off + i * TT_ * 128 * 2, TT_ * 128, BF16) for i in range(2)]
    negmax2 = [ctx.v(sm_off + i * 192, 1, F32) for i in range(2)]
    sums2 = [ctx.v(sm_off + i * 192 + 64, 1, F32) for i in range(2)]
    recip2 = [ctx.v(sm_off + i * 192 + 128, 1, F32) for i in range(2)]
    bsk_tok = [None, None]
    bsk_war = [None, None]
    tr_tok = [None, None]
    sc_tok_prev = [None, None]
    mx_prev = [None, None]
    qq = 0
    cmm = None
    oev_tok = None
    for h in range(16):
        bsk = ctx.v(bsk_off + (h % 2) * WT * 2, WT, BF16)
        war = [bsk_war[h % 2] or vs_war, bias_ck, idx_t]
        bsk_tok[h % 2] = b.op(
            "pool",
            lambda E, o=bsk, i=bias_fl, ix=idx[:, h:h + 1]: E.indirect_dma_start(
                o, None, i, bass.IndirectOffsetOnAxis(ap=ix, axis=0)),
            war, inc=f"q_gbsk{h % 2}", n=16)
        jt2, pb2 = h // 2, 64 * (h % 2)
        for qt in range(TT_):
            sl2 = qq % 2
            qq += 1
            P_bf, P_f, PT_sb = P_bf2[sl2], P_f2[sl2], PT_sb2[sl2]
            negmax, sums, recip = negmax2[sl2], sums2[sl2], recip2[sl2]
            smm = None
            for c in range(C):
                bk = ctx.bank(c, 512, F32)
                w = [q_tok[jt2][(qt * 128) // 512], k_tok[jt2][c],
                     ctx.bankuse.get(c), bsk_tok[h % 2]]
                smm = b.op("pe", lambda E, o=bk,
                           l=q_sb[jt2][pb2:pb2 + 64, qt * 128:(qt + 1) * 128],
                           r=k_sb[jt2][pb2:pb2 + 64, c * 512:(c + 1) * 512]:
                           E.matmul(o, l, r, start=True, stop=False), w)
                smm = b.op("pe", lambda E, o=bk, l=ctx.id_bf,
                           r=bsk[:, SKB - 128 * qt + c * 512:SKB - 128 * qt + c * 512 + 512]:
                           E.matmul(o, l, r, start=False, stop=True), [smm, ctx.id_tok])
            spsum = ctx.ps[:, 0:C * 2048].bitcast(F32)
            mx = b.op("dve", lambda E, o=negmax, i=spsum: E.reduce_max(
                o, i, axis=AX.X, negate=True), [smm, mx_prev[sl2]])
            et = b.op("act", lambda E, o=P_f, i=spsum, bb=negmax, s=sums:
                      E.activation(o, i, AF.Exp, bias=bb, accum_out=s),
                      [mx, sc_tok_prev[0], sc_tok_prev[1]])
            for c in range(C):
                ctx.bankuse[c] = et
            rc = b.op("dve", lambda E, o=recip, i=sums: E.reciprocal(o, i), [et])
            sc_tok = b.op("dve", lambda E, o=P_bf, i=P_f, r=recip: E.tensor_scalar(
                o, i, r, None, op0=ALU.mult), [rc, tr_tok[sl2]])
            sc_tok_prev[sl2] = sc_tok
            mx_prev[sl2] = rc
            for g in range(NG):
                tb_i = 6 + (g % 2)
                tb = ctx.bank(tb_i, 1024, BF16)
                nin = min(8, TT_ - g * 8)
                ttok = None
                for ki in range(nin):
                    jt = g * 8 + ki
                    w = [sc_tok, ctx.id_tok, ctx.bankuse.get(tb_i)]
                    ttok = b.op("pe", lambda E, o=tb[:, ki * 128:(ki + 1) * 128],
                                i=P_bf[:, jt * 128:(jt + 1) * 128]:
                                E.transpose(o, i, ctx.id_bf), w)
                g_ev = b.op("dve", lambda E, o=PT_sb[:, g * 1024:g * 1024 + nin * 128],
                            i=tb[:, 0:nin * 128]: E.tensor_copy(o, i), [ttok])
                ctx.bankuse[tb_i] = g_ev
                tr_tok[sl2] = ttok
            pt_ck = g_ev
            for jt in range(TT_):
                bk = ctx.bank(4, 128, F32, pn=64)
                w = [pt_ck, v_tok[jt], ctx.bankuse.get(4)]
                cmm = b.op("pe", lambda E, o=bk, l=v_sb[jt][:, 64 * h:64 * h + 64],
                           r=PT_sb[:, jt * 128:(jt + 1) * 128], jt=jt:
                           E.matmul(o, l, r, start=(jt == 0), stop=(jt == TT_ - 1)), w)
            oev_tok = b.op("act", lambda E, o=o_sb[jt2][pb2:pb2 + 64, qt * 128:(qt + 1) * 128],
                           i=ctx.bank(4, 128, F32, pn=64): E.copy(o, i), [cmm])
            ctx.bankuse[4] = oev_tok
        bsk_war[h % 2] = smm

    o_done = oev_tok
    ctx.ar.reset(mark1)

    ws2_off = ctx.ar.alloc(2 * DT * 128 * 2)
    xi_off = ctx.ar.alloc(4 * 512 * 4)
    xo2_off = ctx.ar.alloc(4 * 512 * 4)
    xi_war, xo_war = {}, {}
    ws2_war = [None, None]
    out_done = []
    for j in range(DT):
        sl = j % 2
        ws = ctx.v(ws2_off + sl * DT * 128 * 2, DT * 128, BF16)
        wt = b.dma("pool", ws, wslice_ap(W_out, 1024, j * 128, 128),
                   waits=[ws2_war[sl] or cmm], sem=f"q_gw{sl}")
        for c in range(C):
            bi = (j * C + c) % 4
            bk = ctx.bank(bi, 512, F32)
            mm = None
            for k in range(DT):
                w = [wt, o_done, ctx.bankuse.get(bi)]
                mm = b.op("pe", lambda E, o=bk, l=ws[:, k * 128:(k + 1) * 128],
                          r=o_sb[k][:, c * 512:(c + 1) * 512], k=k:
                          E.matmul(o, l, r, start=(k == 0), stop=(k == DT - 1)), w)
            i4 = (j * C + c) % 4
            xi = ctx.v(xi_off + i4 * 2048, 512, F32)
            xo = ctx.v(xo2_off + i4 * 2048, 512, F32)
            ld = b.dma("sp", xi, x_in[j * P:(j + 1) * P, c * 512:c * 512 + 512],
                       waits=[xi_war.get(i4) or cmm], sem=f"q_sxi{i4}")
            t = b.op("dve", lambda E, o=xo, i=bk, x=xi: E.tensor_tensor(
                o, i, x, op=ALU.add), [mm, ld, xo_war.get(i4)])
            xi_war[i4] = t
            ctx.bankuse[bi] = t
            d = b.dma("sp", x_out[j * P:(j + 1) * P, c * 512:c * 512 + 512], xo,
                      waits=[t], sem=f"q_sxo{i4}")
            xo_war[i4] = d
            out_done.append(d)
        ws2_war[sl] = mm
    ctx.ar.reset(mark0)
    return out_done[-4:]


def emit_l0attn(ctx, x_in, x_out, W, tag="m"):
    b, T, C = ctx.b, ctx.T, ctx.C
    TT_ = T // 128
    mark0 = ctx.ar.mark()

    qup_off = ctx.ar.alloc(DT * T * 2)
    y_off = ctx.ar.alloc(DT * T * 2)
    ckvT_off = ctx.ar.alloc(T * 2)
    ckv_off = ctx.ar.alloc(TT_ * 128 * 2)
    qup_sb = [ctx.v(qup_off + j * T * 2, T, BF16) for j in range(DT)]
    y_sb = [ctx.v(y_off + j * T * 2, T, BF16) for j in range(DT)]
    ckvT = ctx.v(ckvT_off, T, BF16)
    ckv_sb = ctx.v(ckv_off, TT_ * 128, BF16)

    mark1 = ctx.ar.mark()

    keff = ctx.v(ctx.ar.alloc(8 * 128 * 2), 8 * 128, BF16)
    veff = ctx.v(ctx.ar.alloc(1024 * 2), 1024, BF16)
    dqT = ctx.v(ctx.ar.alloc(8 * 128 * 2), 8 * 128, BF16)
    dkvT = ctx.v(ctx.ar.alloc(8 * 128 * 2), 8 * 128, BF16)
    uqT = ctx.v(ctx.ar.alloc(8 * 128 * 2), 8 * 128, BF16)
    msk = ctx.v(ctx.ar.alloc(4 * 512 * 2), 4 * 512, BF16)
    mark_prep = ctx.ar.mark()

    dq_raw = ctx.v(ctx.ar.alloc(1024 * 2), 1024, BF16)
    dkv_raw = ctx.v(ctx.ar.alloc(1024 * 2), 1024, BF16)
    uq_raw = ctx.v(ctx.ar.alloc(8 * 128 * 2), 8 * 128, BF16)
    uk_raw = ctx.v(ctx.ar.alloc(8 * 128 * 2), 8 * 128, BF16)
    uv_raw = ctx.v(ctx.ar.alloc(8 * 128 * 2), 8 * 128, BF16)
    b.dma("pool", dq_raw, W["dq"], sem="q_gld", exclusive=False)
    b.dma("pool", dkv_raw, W["dkv"], sem="q_gld", exclusive=False)
    b.dma("pool", uq_raw, wslice_ap(W["uq"], 128, 0, 128), sem="q_gld", exclusive=False)
    b.dma("pool", uk_raw, wslice_ap(W["uk"], 128, 0, 128), sem="q_gld", exclusive=False)
    b.dma("pool", uv_raw, wslice_ap(W["uv"], 128, 0, 128), sem="q_gld", exclusive=False)
    raw_ck = b.ckpt("pool", "q_gld")

    tr_ev = {}
    for name, srcv, dstv in (("dq", dq_raw, dqT), ("dkv", dkv_raw, dkvT)):
        for g in range(1):
            tb = ctx.bank(6, 1024, BF16)
            tt = None
            for k in range(DT):
                tt = b.op("pe", lambda E, o=tb[:, k * 128:(k + 1) * 128],
                          i=srcv[:, k * 128:(k + 1) * 128]:
                          E.transpose(o, i, ctx.id_bf), [raw_ck, ctx.id_tok, ctx.bankuse.get(6)])
            ev = b.op("dve", lambda E, o=dstv, i=tb: E.tensor_copy(o, i), [tt])
            ctx.bankuse[6] = ev
            tr_ev[name] = ev
    tb = ctx.bank(7, 1024, BF16)
    tt = None
    for k in range(DT):
        tt = b.op("pe", lambda E, o=tb[:, k * 128:(k + 1) * 128],
                  i=uq_raw[:, k * 128:(k + 1) * 128]:
                  E.transpose(o, i, ctx.id_bf), [raw_ck, ctx.id_tok, ctx.bankuse.get(7)])
    uqT_ev = b.op("dve", lambda E, o=uqT, i=tb: E.tensor_copy(o, i), [tt])
    ctx.bankuse[7] = uqT_ev

    t1_sb = ctx.v(ctx.ar.alloc(128 * 2), 128, BF16)
    bk5 = ctx.bank(5, 128, F32)
    mm = None
    for k in range(DT):
        mm = b.op("pe", lambda E, o=bk5, l=uq_raw[:, k * 128:(k + 1) * 128],
                  r=uk_raw[:, k * 128:(k + 1) * 128], k=k:
                  E.matmul(o, l, r, start=(k == 0), stop=(k == DT - 1)),
                  [raw_ck, ctx.bankuse.get(5)])
    t1_ev = b.op("dve", lambda E, o=t1_sb, i=bk5: E.tensor_copy(o, i), [mm])
    ctx.bankuse[5] = t1_ev
    keff_ev = None
    for k in range(DT):
        bk5b = ctx.bank(5, 128, F32)
        mm = b.op("pe", lambda E, o=bk5b, l=dq_raw[:, k * 128:(k + 1) * 128], r=t1_sb:
                  E.matmul(o, l, r, start=True, stop=True),
                  [t1_ev, ctx.bankuse.get(5)])
        keff_ev = b.op("act", lambda E, o=keff[:, k * 128:(k + 1) * 128], i=bk5b:
                       E.activation(o, i, AF.Copy, scale=0.125), [mm])
        ctx.bankuse[5] = keff_ev

    wo_raw = ctx.v(ctx.ar.alloc(1024 * 2), 1024, BF16)
    woT = ctx.v(ctx.ar.alloc(DT * 1024 * 2), DT * 1024, BF16)
    wo_war = None
    for k in range(DT):
        ld = b.dma("pool", wo_raw, W["o"][k * P:(k + 1) * P, :], waits=[wo_war],
                   sem="q_gw0")
        for g in range(2):
            tb = ctx.bank(6 + g % 2, 1024, BF16)
            tt = None
            for ki in range(4):
                cc = g * 4 + ki
                tt = b.op("pe", lambda E, o=tb[:, ki * 128:(ki + 1) * 128],
                          i=wo_raw[:, cc * 128:(cc + 1) * 128]:
                          E.transpose(o, i, ctx.id_bf),
                          [ld, ctx.id_tok, ctx.bankuse.get(6 + g % 2)])
            ev = None
            for ki in range(4):
                cc = g * 4 + ki
                ev = b.op("dve", lambda E, o=woT[:, cc * 1024 + k * 128:cc * 1024 + (k + 1) * 128],
                          i=tb[:, ki * 128:(ki + 1) * 128]: E.tensor_copy(o, i), [tt])
            ctx.bankuse[6 + g % 2] = ev
            wo_war = ev
    woT_ck = wo_war
    for half in range(2):
        bkv = ctx.bank(5, 512, F32)
        mm = None
        for k in range(DT):
            mm = b.op("pe", lambda E, o=bkv, l=uv_raw[:, k * 128:(k + 1) * 128],
                      r=woT[:, k * 1024 + half * 512:k * 1024 + half * 512 + 512], k=k:
                      E.matmul(o, l, r, start=(k == 0), stop=(k == DT - 1)),
                      [raw_ck, woT_ck, ctx.bankuse.get(5)])
        ev = b.op("dve", lambda E, o=veff[:, half * 512:(half + 1) * 512], i=bkv:
                  E.tensor_copy(o, i), [mm])
        ctx.bankuse[5] = ev
        veff_ck = ev

    b.dma("pool", msk, ctx.dram["c_l0am"], sem="q_gld", exclusive=False)
    msk_ck = b.ckpt("pool", "q_gld")
    ctx.ar.reset(mark_prep)

    xn_stride = (T + 4) * 2
    xn_off = ctx.ar.alloc(DT * xn_stride)
    ln = emit_ln(ctx, x_in, xn_off, xn_stride, tag=tag + "l", src_tok=[veff_ck])
    xn, xn_tok = ln["xn_tiles"], ln["xn_tok"]

    qlow = ctx.v(ctx.ar.alloc(T * 2), T, BF16)
    for c in range(C):
        bk = ctx.bank(c % 2, 512, F32)
        mm = None
        for k in range(DT):
            mm = b.op("pe", lambda E, o=bk, l=dkvT[:, k * 128:(k + 1) * 128],
                      r=xn[k][:, 1 + c * 512:1 + c * 512 + 512], k=k:
                      E.matmul(o, l, r, start=(k == 0), stop=(k == DT - 1)),
                      [tr_ev["dkv"], xn_tok[k][c], ctx.bankuse.get(c % 2)])
        ev = b.op("dve", lambda E, o=ckvT[:, c * 512:(c + 1) * 512], i=bk:
                  E.tensor_copy(o, i), [mm])
        ctx.bankuse[c % 2] = ev
        ckvT_tok = ev
    for c in range(C):
        bk = ctx.bank(2 + c % 2, 512, F32)
        mm = None
        for k in range(DT):
            mm = b.op("pe", lambda E, o=bk, l=dqT[:, k * 128:(k + 1) * 128],
                      r=xn[k][:, 1 + c * 512:1 + c * 512 + 512], k=k:
                      E.matmul(o, l, r, start=(k == 0), stop=(k == DT - 1)),
                      [tr_ev["dq"], xn_tok[k][c], ctx.bankuse.get(2 + c % 2)])
        ev = b.op("dve", lambda E, o=qlow[:, c * 512:(c + 1) * 512], i=bk:
                  E.tensor_copy(o, i), [mm])
        ctx.bankuse[2 + c % 2] = ev
        qlow_tok = ev
    ckv_tok = None
    for g in range((TT_ + 7) // 8):
        tb = ctx.bank(6 + g % 2, 1024, BF16)
        nin = min(8, TT_ - g * 8)
        tt = None
        for ki in range(nin):
            jt = g * 8 + ki
            tt = b.op("pe", lambda E, o=tb[:, ki * 128:(ki + 1) * 128],
                      i=ckvT[:, jt * 128:(jt + 1) * 128]:
                      E.transpose(o, i, ctx.id_bf),
                      [ckvT_tok, ctx.id_tok, ctx.bankuse.get(6 + g % 2)])
        ckv_tok = b.op("dve", lambda E, o=ckv_sb[:, g * 1024:g * 1024 + nin * 128],
                       i=tb[:, 0:nin * 128]: E.tensor_copy(o, i), [tt])
        ctx.bankuse[6 + g % 2] = ckv_tok
    qup_tok = [[None] * C for _ in range(DT)]
    for j in range(DT):
        for c in range(C):
            bk = ctx.bank((j * C + c) % 2, 512, F32)
            mm = b.op("pe", lambda E, o=bk, l=uqT[:, j * 128:(j + 1) * 128],
                      r=qlow[:, c * 512:(c + 1) * 512]:
                      E.matmul(o, l, r, start=True, stop=True),
                      [uqT_ev, qlow_tok, ctx.bankuse.get((j * C + c) % 2)])
            ev = b.op("act" if (j + c) % 2 else "dve",
                      lambda E, o=qup_sb[j][:, c * 512:(c + 1) * 512], i=bk:
                      (E.copy(o, i) if hasattr(E, "copy") else E.tensor_copy(o, i)),
                      [mm])
            qup_tok[j][c] = ev
            ctx.bankuse[(j * C + c) % 2] = ev

    ql_off = ctx.ar.alloc(T * 2)
    P_off = ctx.ar.alloc(2 * T * 2)
    Pf_off = ctx.ar.alloc(T * 4)
    PT_off = ctx.ar.alloc(2 * TT_ * 128 * 2)
    cx_off = ctx.ar.alloc(2 * 128 * 2)
    sm_off = ctx.ar.alloc(2 * 192)
    ql_sb = ctx.v(ql_off, T, BF16)
    P_bf2 = [ctx.v(P_off + i * T * 2, T, BF16) for i in range(2)]
    P_f2 = [ctx.v(Pf_off, T, F32), ctx.v(Pf_off, T, F32)]
    PT_sb2 = [ctx.v(PT_off + i * TT_ * 128 * 2, TT_ * 128, BF16) for i in range(2)]
    cx_sb2 = [ctx.v(cx_off + i * 128 * 2, 128, BF16) for i in range(2)]
    negmax2 = [ctx.v(sm_off + i * 192, 1, F32) for i in range(2)]
    sums2 = [ctx.v(sm_off + i * 192 + 64, 1, F32) for i in range(2)]
    recip2 = [ctx.v(sm_off + i * 192 + 128, 1, F32) for i in range(2)]
    tr_tok = [None, None]
    sc_prev = [None, None]
    mx_prev = [None, None]
    cx_war = [None, None]
    qq = 0
    ql_war = None
    yev = None
    for h in range(16):
        jt2, pb2 = h // 2, 64 * (h % 2)
        ql_toks = []
        for c in range(C):
            bki = c % 2
            bk = ctx.bank(bki, 512, F32)
            mm = b.op("pe", lambda E, o=bk, l=keff[pb2:pb2 + 64, jt2 * 128:(jt2 + 1) * 128],
                      r=qup_sb[jt2][pb2:pb2 + 64, c * 512:(c + 1) * 512]:
                      E.matmul(o, l, r, start=True, stop=True),
                      [keff_ev, qup_tok[jt2][c], ctx.bankuse.get(bki), ql_war])
            ev = b.op("dve", lambda E, o=ql_sb[:, c * 512:(c + 1) * 512], i=bk:
                      E.tensor_copy(o, i), [mm])
            ctx.bankuse[bki] = ev
            ql_toks.append(ev)
        for qt in range(TT_):
            sl2 = qq % 2
            qq += 1
            P_bf, P_f, PT_sb, cx_sb = P_bf2[sl2], P_f2[sl2], PT_sb2[sl2], cx_sb2[sl2]
            negmax, sums, recip = negmax2[sl2], sums2[sl2], recip2[sl2]
            nch = qt // 4 + 1
            ncols = nch * 512
            nt = qt + 1
            smm = None
            for c in range(nch):
                bk = ctx.bank(c, 512, F32)
                w = [ql_toks[(qt * 128) // 512], ckvT_tok, ctx.bankuse.get(c)]
                smm = b.op("pe", lambda E, o=bk, l=ql_sb[:, qt * 128:(qt + 1) * 128],
                           r=ckvT[:, c * 512:(c + 1) * 512], sp=(c != nch - 1):
                           E.matmul(o, l, r, start=True, stop=sp), w)
                if c == nch - 1:
                    smm = b.op("pe", lambda E, o=bk, l=ctx.id_bf,
                               r=msk[:, (qt % 4) * 512:(qt % 4 + 1) * 512]:
                               E.matmul(o, l, r, start=False, stop=True),
                               [smm, msk_ck, ctx.id_tok])
            spsum = ctx.ps[:, 0:ncols * 4].bitcast(F32)
            mx = b.op("dve", lambda E, o=negmax, i=spsum: E.reduce_max(
                o, i, axis=AX.X, negate=True), [smm, mx_prev[sl2]])
            et = b.op("act", lambda E, o=P_f[:, 0:ncols], i=spsum, bb=negmax, s=sums:
                      E.activation(o, i, AF.Exp, bias=bb, accum_out=s),
                      [mx, sc_prev[0], sc_prev[1]])
            for c in range(nch):
                ctx.bankuse[c] = et
            rc = b.op("dve", lambda E, o=recip, i=sums: E.reciprocal(o, i), [et])
            sc_tok = b.op("dve", lambda E, o=P_bf[:, 0:ncols], i=P_f[:, 0:ncols],
                          r=recip: E.tensor_scalar(o, i, r, None, op0=ALU.mult),
                          [rc, tr_tok[sl2]])
            sc_prev[sl2] = sc_tok
            mx_prev[sl2] = rc
            g_ev = None
            for g in range((nt + 7) // 8):
                tb_i = 6 + (g % 2)
                tb = ctx.bank(tb_i, 1024, BF16)
                nin = min(8, nt - g * 8)
                ttok = None
                for ki in range(nin):
                    jt = g * 8 + ki
                    ttok = b.op("pe", lambda E, o=tb[:, ki * 128:(ki + 1) * 128],
                                i=P_bf[:, jt * 128:(jt + 1) * 128]:
                                E.transpose(o, i, ctx.id_bf),
                                [sc_tok, ctx.id_tok, ctx.bankuse.get(tb_i)])
                g_ev = b.op("dve", lambda E, o=PT_sb[:, g * 1024:g * 1024 + nin * 128],
                            i=tb[:, 0:nin * 128]: E.tensor_copy(o, i), [ttok])
                ctx.bankuse[tb_i] = g_ev
                tr_tok[sl2] = ttok
            cmm = None
            for jt in range(nt):
                bk = ctx.bank(4, 128, F32)
                w = [g_ev, ckv_tok, ctx.bankuse.get(4)]
                cmm = b.op("pe", lambda E, o=bk, l=ckv_sb[:, jt * 128:(jt + 1) * 128],
                           r=PT_sb[:, jt * 128:(jt + 1) * 128], st=(jt == 0),
                           sp=(jt == nt - 1): E.matmul(o, l, r, start=st, stop=sp), w)
            cx_ev = b.op("dve", lambda E, o=cx_sb, i=ctx.bank(4, 128, F32):
                         E.tensor_copy(o, i), [cmm, cx_war[sl2]])
            ctx.bankuse[4] = cx_ev
            ymm = b.op("pe", lambda E, o=ctx.bank(5, 128, F32, pn=64),
                       l=veff[:, 64 * h:64 * h + 64], r=cx_sb:
                       E.matmul(o, l, r, start=True, stop=True),
                       [cx_ev, veff_ck, ctx.bankuse.get(5)])
            yev = b.op("act", lambda E, o=y_sb[jt2][pb2:pb2 + 64, qt * 128:(qt + 1) * 128],
                       i=ctx.bank(5, 128, F32, pn=64): E.copy(o, i), [ymm])
            ctx.bankuse[5] = yev
            cx_war[sl2] = ymm
        ql_war = smm

    y_done = yev
    ctx.ar.reset(mark1)

    xi_off = ctx.ar.alloc(4 * 512 * 4)
    xo2_off = ctx.ar.alloc(4 * 512 * 4)
    xi_war, xo_war = {}, {}
    out_done = []
    for j in range(DT):
        for c in range(C):
            i4 = (j * C + c) % 4
            xi = ctx.v(xi_off + i4 * 2048, 512, F32)
            xo = ctx.v(xo2_off + i4 * 2048, 512, F32)
            ld = b.dma("sp", xi, x_in[j * P:(j + 1) * P, c * 512:c * 512 + 512],
                       waits=[xi_war.get(i4) or y_done], sem=f"q_sxi{i4}")
            t = b.op("dve", lambda E, o=xo, y=y_sb[j][:, c * 512:(c + 1) * 512], x=xi:
                     E.tensor_tensor(o, y, x, op=ALU.add),
                     [y_done, ld, xo_war.get(i4)])
            xi_war[i4] = t
            d = b.dma("sp", x_out[j * P:(j + 1) * P, c * 512:c * 512 + 512], xo,
                      waits=[t], sem=f"q_sxo{i4}")
            xo_war[i4] = d
            out_done.append(d)
    ctx.ar.reset(mark0)
    return out_done[-4:]


def emit_l0ffn(ctx, x_in, x_out, W, xA, xB, tag="s"):
    b, T, C = ctx.b, ctx.T, ctx.C
    NW = T // 256
    NV = T // 128 + 2
    mark_call = ctx.ar.mark()

    msk = ctx.v(ctx.ar.alloc(4 * 512 * 2), 4 * 512, BF16)
    b.dma("pool", msk, ctx.dram["c_l0fm"], sem="q_gld", exclusive=False)
    blk2 = ctx.v(ctx.ar.alloc(2 * 2), 2, BF16)
    b.dma("pool", blk2, ctx.dram["c_blk2"], sem="q_gld", exclusive=False)
    sel2 = ctx.v(ctx.ar.alloc(256), 128, BF16, pn=2)
    b.dma("pool", sel2, ctx.dram["c_sel2"], sem="q_gld", exclusive=False)
    cst_ck = b.ckpt("pool", "q_gld")
    ctx.l0 = dict(msk=msk, blk2=blk2, sel2=sel2, cst_ck=cst_ck)

    xn_stride = (T + 4) * 2
    mk = ctx.ar.mark()
    xn_off0 = ctx.ar.alloc(DT * xn_stride)
    r0 = emit_ln(ctx, x_in, xn_off0, xn_stride, tag=tag + "0", fout_dram=xA)
    src_ck = r0["store_ck"]
    ctx.ar.reset(mk)

    cur, nxt = xA, xB
    for dep in range(6):
        src_ck = _l0_attn_sub(ctx, W, dep, cur, nxt, src_ck, tag)
        cur, nxt = nxt, cur
        src_ck = _l0_ffn_sub(ctx, W, dep, cur, nxt, src_ck, tag)
        cur, nxt = nxt, cur

    xi_off = ctx.ar.alloc(4 * 512 * 4)
    xf_off = ctx.ar.alloc(4 * 512 * 4)
    xo_off = ctx.ar.alloc(4 * 512 * 4)
    xi_war, xf_war, xo_war = {}, {}, {}
    for j in range(DT):
        for c in range(C):
            i4 = (j * C + c) % 4
            xi = ctx.v(xi_off + i4 * 2048, 512, F32)
            xf = ctx.v(xf_off + i4 * 2048, 512, F32)
            xo = ctx.v(xo_off + i4 * 2048, 512, F32)
            l1 = b.dma("sp", xi, x_in[j * P:(j + 1) * P, c * 512:c * 512 + 512],
                       waits=[xi_war.get(i4) or src_ck], sem=f"q_sxi{i4}")
            l2 = b.dma("sp", xf, cur[j * P:(j + 1) * P, c * 512:c * 512 + 512],
                       waits=[xf_war.get(i4) or src_ck], sem=f"q_sxf{i4 % 2}")
            t = b.op("dve", lambda E, o=xo, a=xf, x=xi: E.scalar_tensor_tensor(
                o, a, 0.5, x, op0=ALU.mult, op1=ALU.add),
                [l1, l2, xo_war.get(i4)])
            xi_war[i4] = t
            xf_war[i4] = t
            d = b.dma("sp", x_out[j * P:(j + 1) * P, c * 512:c * 512 + 512], xo,
                      waits=[t], sem=f"q_sxo{i4}")
            xo_war[i4] = d
    done = b.nopop("sp", [(f"q_sxo{i}", b.cnt.get(f"q_sxo{i}", 0)) for i in range(4)])
    ctx.ar.reset(mark_call)
    return done


def _l0_attn_sub(ctx, W, dep, cur, nxt, src_ck, tag):
    b, T, C = ctx.b, ctx.T, ctx.C
    NW = T // 256
    NV = T // 128 + 2
    mark = ctx.ar.mark()
    qkv_d = W["qkv"][dep]
    wo_d = W["wo"][dep]
    msk, blk2, sel2, cst_ck = (ctx.l0[k] for k in ("msk", "blk2", "sel2", "cst_ck"))

    xb_stride = (T + 4) * 2
    xb_off = ctx.ar.alloc(DT * xb_stride)
    xb = [ctx.v(xb_off + k * xb_stride, T + 1, BF16) for k in range(DT)]
    xb_tok = []
    for k in range(DT):
        z = b.op("pool", lambda E, t=xb[k]: E.memset(t[:, 0:1], 0.0), [src_ck])
        tk = b.dma("pool", xb[k][:, 1:T + 1], cur[k * P:(k + 1) * P, :],
                   sem=f"q_gx{k}", waits=[src_ck, z])
        xb_tok.append(tk)

    def xs(k, c0, n):
        off = 0 if k >= DT // 2 else 1
        return xb[k][:, off + c0: off + c0 + n]

    qs_col = ctx.v(ctx.ar.alloc(64), 1, F32)
    qss_col = ctx.v(ctx.ar.alloc(64), 1, F32)
    ks_col = ctx.v(ctx.ar.alloc(64), 1, F32)
    kss_col = ctx.v(ctx.ar.alloc(64), 1, F32)
    for dst, srcv, sw in ((qs_col, W["qs"][dep], 0), (qss_col, W["qs"][dep], 1),
                          (ks_col, W["ks"][dep], 0), (kss_col, W["ks"][dep], 1)):
        for half in range(2):
            p0 = half * 64
            if sw:
                b.dma("sp", dst[p0:p0 + 32, :], col1(srcv[32:64]), sem="q_sld", exclusive=False)
                b.dma("sp", dst[p0 + 32:p0 + 64, :], col1(srcv[0:32]), sem="q_sld", exclusive=False)
            else:
                b.dma("sp", dst[p0:p0 + 64, :], col1(srcv[0:64]), sem="q_sld", exclusive=False)
    scol_ck = b.ckpt("sp", "q_sld")
    csq = ctx.v(ctx.ar.alloc(T * 2), T, BF16)
    ssq = ctx.v(ctx.ar.alloc(T * 2), T, BF16)
    csk = ctx.v(ctx.ar.alloc(T * 2), T, BF16)
    ssk = ctx.v(ctx.ar.alloc(T * 2), T, BF16)
    mk_rot = ctx.ar.mark()
    rotq_raw = ctx.v(ctx.ar.alloc(2 * T * 4), 2 * T, F32)
    rotk_raw = ctx.v(ctx.ar.alloc(2 * T * 4), 2 * T, F32)
    b.dma("pool", rotq_raw, ctx.dram["c_rotq"], sem="q_gld", exclusive=False,
          waits=[src_ck])
    b.dma("pool", rotk_raw, ctx.dram["c_rotk"], sem="q_gld", exclusive=False,
          waits=[src_ck])
    rot_ck = b.ckpt("pool", "q_gld")
    tab_tok = []
    for dst, raw, c0, scal, s2 in ((csq, rotq_raw, 0, qs_col, 8.0),
                                   (ssq, rotq_raw, T, qss_col, 8.0),
                                   (csk, rotk_raw, 0, ks_col, 1.0),
                                   (ssk, rotk_raw, T, kss_col, 1.0)):
        tab_tok.append(b.op("dve", lambda E, o=dst, i=raw[:, c0:c0 + T], s=scal, s2=s2:
                            E.tensor_scalar(o, i, s, s2, op0=ALU.mult, op1=ALU.mult),
                            [rot_ck, scol_ck]))
    tab_ck = tab_tok[-1]
    ctx.ar.reset(mk_rot)

    v2_off = ctx.ar.alloc(NV * 1024 * 2)
    v2 = ctx.v(v2_off, NV * 1024, BF16)
    vz = b.op("pool", lambda E, o=v2[:, 0:2048]: E.memset(o, 0.0), [src_ck, tab_ck])
    vs_off = ctx.ar.alloc(DT * 512 * 2)
    v_tok = None
    vload_war = None
    for vc in range(2):
        vs = ctx.v(vs_off, DT * 512, BF16)
        wt = b.dma("pool", vs, wslice_ap(qkv_d, 3072, 2048 + vc * 512, 512),
                   waits=[vload_war or src_ck, tab_ck], sem="q_gw0")
        mm = None
        for tt in range(T // 128):
            bi = (tt + vc) % 2
            bk = ctx.bank(bi, 512, F32)
            for k in range(DT):
                w = [wt, xb_tok[k], ctx.bankuse.get(bi)]
                mm = b.op("pe", lambda E, o=bk, l=xs(k, tt * 128, 128),
                          r=vs[:, k * 512:(k + 1) * 512], k=k:
                          E.matmul(o, l, r, start=(k == 0), stop=(k == DT - 1)), w)
            ev = b.op("act" if tt % 2 else "dve",
                      lambda E, o=v2[:, (tt + 2) * 1024 + vc * 512:(tt + 2) * 1024 + vc * 512 + 512],
                      i=bk: (E.copy(o, i) if hasattr(E, "copy") else E.tensor_copy(o, i)),
                      [mm, vz, tab_ck])
            v_tok = ev
            ctx.bankuse[bi] = ev
        vload_war = mm

    oT_off = ctx.ar.alloc(DT * T * 2)
    oT = [ctx.v(oT_off + j * T * 2, T, BF16) for j in range(DT)]
    mark_heads = ctx.ar.mark()
    qraw = ctx.v(ctx.ar.alloc(T * 2), T, BF16)
    kraw = qraw
    swp = ctx.v(ctx.ar.alloc(T * 2), T, BF16)
    t1b = ctx.v(ctx.ar.alloc(T * 2), T, BF16)
    sq_off2 = ctx.ar.alloc(2 * 512 * 2)
    qf_off = ctx.ar.alloc(2 * T * 2)
    kf_off = ctx.ar.alloc(2 * (T + 256) * 2)
    qfin2 = [ctx.v(qf_off + i * T * 2, T, BF16) for i in range(2)]
    kfin2 = [ctx.v(kf_off + i * (T + 256) * 2, T + 256, BF16) for i in range(2)]
    kz0s = [b.op("pool", lambda E, o=kfin2[0][:, 0:256]: E.memset(o, 0.0), [src_ck]),
            b.op("pool", lambda E, o=kfin2[1][:, 0:256]: E.memset(o, 0.0), [src_ck])]
    nrow = ctx.v(ctx.ar.alloc(T * 4), T, F32, pn=2)
    nrow_bf = ctx.v(ctx.ar.alloc(T * 2), T, BF16, pn=2)
    NSL = 3
    P2_off = ctx.ar.alloc(NSL * 1024 * 2)
    PTo_off = ctx.ar.alloc(NSL * 1024 * 2)
    sm_off = ctx.ar.alloc(NSL * 128)
    P22 = [ctx.v(P2_off + i * 2048, 1024, BF16) for i in range(NSL)]
    PT2 = [ctx.v(PTo_off + i * 2048, 1024, BF16) for i in range(NSL)]
    negmax4 = [ctx.v(sm_off + i * 128, 2, F32) for i in range(NSL)]
    NU = 4 * NW
    sa_off = ctx.ar.alloc(2 * NU * 4)
    rr_off = ctx.ar.alloc(2 * NU * 4)
    rrb_off = ctx.ar.alloc(2 * NU * 2)
    recT_off = ctx.ar.alloc(2 * 128 * 2)
    Rsb_off = ctx.ar.alloc(T * 2)
    sums_all = [ctx.v(sa_off + i * NU * 4, NU, F32) for i in range(2)]
    rr2 = [ctx.v(rr_off + i * NU * 4, NU, F32) for i in range(2)]
    rrb2 = [ctx.v(rrb_off + i * NU * 2, NU, BF16) for i in range(2)]
    recT2 = [ctx.v(recT_off + i * 256, 128, BF16, pn=NU) for i in range(2)]
    Rsb = ctx.v(Rsb_off, T, BF16)
    bcoh = ctx.v(ctx.ar.alloc(NU * 64 * 2), NU * 64, BF16, pn=NU)
    ones_tok = b.dma("pool", bcoh, ctx.dram["c_bcoh"], sem="q_gld",
                     exclusive=False, waits=[src_ck])
    ones_tok = b.ckpt("pool", "q_gld")
    qg_off = ctx.ar.alloc(DT * 256 * 2)
    kg_off = ctx.ar.alloc(DT * 256 * 2)
    qgv = ctx.v(qg_off, DT * 256, BF16)
    kgv = ctx.v(kg_off, DT * 256, BF16)
    grp_war = {"q": src_ck, "k": src_ck}
    grp_tok = {}

    fin_prev = {}
    sim_last = [None, None]
    prod_prev = [None]
    oev = None
    et4 = [[None, None] for _ in range(NSL)]
    trl4 = [None] * NSL
    ptev_war4 = [None] * NSL
    hw_c = [0]
    rc_tok = [None, None]
    ca_tok = [None, None]
    ttr_tok = [None, None]
    bc_tok = [None, None]
    r_mult_prev = [None]

    def emit_production(hp):
        sl = hp % 2
        if hp % 2 == 0:
            grp = hp // 2
            grp_tok["q"] = b.dma("pool", qgv, wslice_ap(qkv_d, 3072, grp * 256, 256),
                                 waits=[grp_war["q"]], sem="q_gw0")
            grp_tok["k"] = b.dma("pool", kgv, wslice_ap(qkv_d, 3072, 1024 + grp * 256, 256),
                                 waits=[grp_war["k"]], sem="q_gw1")
        po = (hp % 2) * 128
        cks = {}
        mm = None
        side_prev = [None]
        for side, raw, fin, cst, sst in (
                ("q", qraw, qfin2[sl], csq, ssq), ("k", kraw, kfin2[sl], csk, ssk)):
            gv = qgv if side == "q" else kgv
            wtok = grp_tok[side]
            ev_toks = []
            for c in range(C):
                bi = 2 + c % 2
                bk = ctx.bank(bi, 512, F32)
                mm = None
                for k in range(DT):
                    w = [wtok, xb_tok[k], ctx.bankuse.get(bi)]
                    mm = b.op("pe", lambda E, o=bk, l=gv[:, k * 256 + po:k * 256 + po + 128],
                              r=xs(k, c * 512, 512), k=k:
                              E.matmul(o, l, r, start=(k == 0), stop=(k == DT - 1)), w)
                ev = b.op("dve", lambda E, o=raw[:, c * 512:(c + 1) * 512], i=bk:
                          E.tensor_copy(o, i), [mm, prod_prev[0], side_prev[0]])
                ev_toks.append(ev)
                ctx.bankuse[bi] = ev
            raw_ck = ev_toks[-1]
            nr_toks = []
            sq_war = [None, None]
            for c in range(C):
                sq = ctx.v(sq_off2 + (c % 2) * 1024, 512, BF16)
                sq_t = b.op("dve", lambda E, o=sq, i=raw[:, c * 512:(c + 1) * 512]:
                            E.tensor_tensor(o, i, i, op=ALU.mult),
                            [ev_toks[c], sq_war[c % 2]])
                bk4 = ctx.bank(4, 512, F32, pn=2)
                mm2 = b.op("pe", lambda E, o=bk4, l=blk2, r=sq:
                           E.matmul(o, l, r, start=True, stop=True),
                           [sq_t, cst_ck, ctx.bankuse.get(4)])
                sq_war[c % 2] = mm2
                ev = b.op("dve", lambda E, o=nrow[:, c * 512:(c + 1) * 512], i=bk4:
                          E.tensor_copy(o, i), [mm2])
                nr_toks.append(ev)
                ctx.bankuse[4] = ev
            t_cl = b.op("dve", lambda E, o=nrow: E.tensor_scalar(
                o, o, 1e-24, None, op0=ALU.max), [nr_toks[-1]])
            t_ln = b.op("act", lambda E, o=nrow: E.activation(o, o, AF.Ln), [t_cl])
            t_ex = b.op("act", lambda E, o=nrow: E.activation(o, o, AF.Exp, scale=-0.5), [t_ln])
            t_bf = b.op("dve", lambda E, o=nrow_bf, i=nrow: E.tensor_copy(o, i), [t_ex])
            sw_t = []
            for g4 in range(4):
                p0 = g4 * 32
                p1 = (g4 * 32 + 32) % 64 + (64 if g4 >= 2 else 0)
                sw_t.append(b.op("act", lambda E, o=swp[p0:p0 + 32, :], i=raw[p1:p1 + 32, 0:T]:
                                 E.copy(o, i), [raw_ck, prod_prev[0]]))
            tt1 = b.op("dve", lambda E, o=t1b, i=raw[:, 0:T], cc=cst: E.tensor_tensor(
                o, i, cc, op=ALU.mult), [raw_ck, tab_ck])
            tt2 = b.op("dve", lambda E, o=swp, i=swp, ss=sst: E.tensor_tensor(
                o, i, ss, op=ALU.mult), sw_t + [tab_ck])
            tt3 = b.op("dve", lambda E, o=t1b, a=t1b, bb=swp: E.tensor_tensor(
                o, a, bb, op=ALU.add), [tt1, tt2])
            fin0 = 0 if side == "q" else 256
            f_toks = []
            for c in range(C):
                bk4b = ctx.bank(4, 512, F32)
                mmb = b.op("pe", lambda E, o=bk4b, l=sel2, r=nrow_bf[:, c * 512:(c + 1) * 512]:
                           E.matmul(o, l, r, start=True, stop=True),
                           [t_bf, cst_ck, ctx.bankuse.get(4)])
                fv = b.op("dve", lambda E, o=fin[:, fin0 + c * 512:fin0 + c * 512 + 512],
                          t=t1b[:, c * 512:(c + 1) * 512], m=bk4b: E.tensor_tensor(
                    o, t, m, op=ALU.mult),
                    [tt3, mmb, kz0s[sl] if side == "k" else None,
                     fin_prev.get((side, sl)), sim_last[sl]])
                f_toks.append(fv)
                ctx.bankuse[4] = fv
            fin_prev[(side, sl)] = f_toks[-1]
            cks[side] = f_toks[-1]
            side_prev[0] = tt2
            if hp % 2 == 1:
                grp_war[side] = mm
        prod_prev[0] = cks["k"]
        return cks

    prod_cks = {0: emit_production(0)}
    for hp in range(DT):
        sl = hp % 2
        cks = prod_cks.pop(sl)
        qfin, kfin = qfin2[sl], kfin2[sl]
        qfin_ck, kfin_ck = cks["q"], cks["k"]
        if hp + 1 < DT:
            prod_cks[(hp + 1) % 2] = emit_production(hp + 1)
        last_sim = None
        last_et = None
        for hh in range(2):
            h = hp * 2 + hh
            pb = 64 * hh
            for w_ in range(NW):
                sl2 = hw_c[0] % NSL
                hw_c[0] += 1
                P2, PT = P22[sl2], PT2[sl2]
                negmax = negmax4[sl2]
                smm = {}
                for qsub in range(2):
                    bi = (w_ * 2 + qsub) % 2
                    bk = ctx.bank(bi, 512, F32)
                    mv = qsub + (2 if w_ == 0 else 0)
                    m1 = b.op("pe", lambda E, o=bk, l=qfin[pb:pb + 64, (w_ * 256 + qsub * 128):(w_ * 256 + qsub * 128 + 128)],
                              r=kfin[pb:pb + 64, w_ * 256:w_ * 256 + 512]:
                              E.matmul(o, l, r, start=True, stop=False),
                              [qfin_ck, kfin_ck, ctx.bankuse.get(bi)])
                    m2 = b.op("pe", lambda E, o=bk, l=ctx.id_bf,
                              r=msk[:, mv * 512:(mv + 1) * 512]:
                              E.matmul(o, l, r, start=False, stop=True),
                              [m1, cst_ck, ctx.id_tok])
                    smm[qsub] = m2
                    last_sim = m2
                for qsub in range(2):
                    bi = (w_ * 2 + qsub) % 2
                    bk = ctx.bank(bi, 512, F32)
                    colf = hh * 2 * NW + w_ * 2 + qsub
                    mx = b.op("dve", lambda E, o=negmax[:, qsub:qsub + 1], i=bk:
                              E.reduce_max(o, i, axis=AX.X, negate=True),
                              [smm[qsub], et4[sl2][qsub]])
                    et = b.op("act", lambda E, o=P2[:, qsub * 512:(qsub + 1) * 512],
                              i=bk, bb=negmax[:, qsub:qsub + 1],
                              s=sums_all[sl][:, colf:colf + 1]:
                              E.activation(o, i, AF.Exp, bias=bb, accum_out=s),
                              [mx, trl4[sl2], rc_tok[sl]])
                    et4[sl2][qsub] = et
                    ctx.bankuse[bi] = et
                    smm[qsub] = et
                    last_et = et
                tb = ctx.bank(6 + w_ % 2, 1024, BF16)
                ttok = None
                for jt in range(4):
                    for qsub in range(2):
                        ttok = b.op("pe", lambda E, o=tb[:, (jt * 2 + qsub) * 128:(jt * 2 + qsub + 1) * 128],
                                    i=P2[:, qsub * 512 + jt * 128:qsub * 512 + (jt + 1) * 128]:
                                    E.transpose(o, i, ctx.id_bf),
                                    [smm[0], smm[1], ctx.id_tok, ctx.bankuse.get(6 + w_ % 2)])
                trl4[sl2] = ttok
                ptev = b.op("dve", lambda E, o=PT, i=tb: E.tensor_copy(o, i),
                            [ttok, ptev_war4[sl2]])
                ctx.bankuse[6 + w_ % 2] = ptev
                cmm = None
                for jt in range(4):
                    bk = ctx.bank(5, 256, F32, pn=64)
                    w = [ptev, v_tok, ctx.bankuse.get(5)]
                    cmm = b.op("pe", lambda E, o=bk,
                               l=v2[:, (2 * w_ + jt) * 1024 + 64 * h:(2 * w_ + jt) * 1024 + 64 * h + 64],
                               r=PT[:, jt * 256:(jt + 1) * 256], st=(jt == 0), sp=(jt == 3):
                               E.matmul(o, l, r, start=st, stop=sp), w)
                oev = b.op("act", lambda E, o=oT[hp][pb:pb + 64, w_ * 256:(w_ + 1) * 256],
                           i=ctx.bank(5, 256, F32, pn=64): E.copy(o, i), [cmm])
                ctx.bankuse[5] = oev
                ptev_war4[sl2] = cmm
        sim_last[sl] = last_sim
        rc = b.op("dve", lambda E, o=rr2[sl], i=sums_all[sl]: E.reciprocal(o, i),
                  [last_et, ca_tok[sl]])
        rc_tok[sl] = rc
        ca = b.op("dve", lambda E, o=rrb2[sl], i=rr2[sl]: E.tensor_copy(o, i),
                  [rc, ttr_tok[sl]])
        ca_tok[sl] = ca
        tbk = ctx.bank(4, 1024, BF16)
        ttr = b.op("pe", lambda E, o=tbk[0:NU, 0:128], i=rrb2[sl]:
                   E.transpose(o, i, ctx.id_bf),
                   [ca, ctx.id_tok, ctx.bankuse.get(4)])
        ttr_tok[sl] = ttr
        evT = b.op("dve", lambda E, o=recT2[sl], i=tbk[0:NU, 0:128]:
                   E.tensor_copy(o, i), [ttr, bc_tok[sl]])
        ctx.bankuse[4] = evT
        recT = recT2[sl]
        ev_last = None
        for hh in range(2):
            for g in range(C):
                bkR = ctx.bank(4, 512, F32, pn=64)
                mmlast = None
                for q in range(4):
                    r_idx = hh * 2 * NW + (g * 2 + q // 2) * 2 + (q % 2)
                    mmlast = b.op("pe", lambda E, o=bkR[:, q * 128:(q + 1) * 128],
                                  l=bcoh[:, r_idx * 64:(r_idx + 1) * 64],
                                  r=recT[0:NU, 0:128]:
                                  E.matmul(o, l, r, start=True, stop=True),
                                  [evT, ones_tok, ctx.bankuse.get(4)])
                ev = b.op("dve", lambda E, o=Rsb[64 * hh:64 * hh + 64, g * 512:(g + 1) * 512],
                          i=bkR: E.tensor_copy(o, i), [mmlast, r_mult_prev[0]])
                ctx.bankuse[4] = ev
                ev_last = ev
        bc_tok[sl] = mmlast
        mult = b.op("dve", lambda E, o=oT[hp], a=oT[hp], r=Rsb: E.tensor_tensor(
            o, a, r, op=ALU.mult), [oev, ev_last])
        r_mult_prev[0] = mult

    o_done = r_mult_prev[0]
    ctx.ar.reset(mark_heads)
    ws2_off = ctx.ar.alloc(2 * DT * 128 * 2)
    xi_off = ctx.ar.alloc(4 * 512 * 4)
    xo_off = ctx.ar.alloc(4 * 512 * 4)
    xi_war, xo_war = {}, {}
    ws2_war = [None, None]
    for j in range(DT):
        sl = j % 2
        ws = ctx.v(ws2_off + sl * DT * 128 * 2, DT * 128, BF16)
        wt = b.dma("pool", ws, wslice_ap(wo_d, 1024, j * 128, 128),
                   waits=[ws2_war[sl] or o_done], sem=f"q_gw{sl}")
        for c in range(C):
            bi = (j * C + c) % 4
            bk = ctx.bank(bi, 512, F32)
            mm = None
            for k in range(DT):
                w = [wt, o_done, ctx.bankuse.get(bi)]
                mm = b.op("pe", lambda E, o=bk, l=ws[:, k * 128:(k + 1) * 128],
                          r=oT[k][:, c * 512:(c + 1) * 512], k=k:
                          E.matmul(o, l, r, start=(k == 0), stop=(k == DT - 1)), w)
            i4 = (j * C + c) % 4
            xi = ctx.v(xi_off + i4 * 2048, 512, F32)
            xo = ctx.v(xo_off + i4 * 2048, 512, F32)
            ld = b.dma("sp", xi, cur[j * P:(j + 1) * P, c * 512:c * 512 + 512],
                       waits=[xi_war.get(i4) or o_done], sem=f"q_sxi{i4}")
            t = b.op("dve", lambda E, o=xo, i=bk, x=xi: E.tensor_tensor(
                o, i, x, op=ALU.add), [mm, ld, xo_war.get(i4)])
            xi_war[i4] = t
            ctx.bankuse[bi] = t
            d = b.dma("sp", nxt[j * P:(j + 1) * P, c * 512:c * 512 + 512], xo,
                      waits=[t], sem=f"q_sxo{i4}")
            xo_war[i4] = d
        ws2_war[sl] = mm
    ck = b.nopop("sp", [(f"q_sxo{i}", b.cnt.get(f"q_sxo{i}", 0)) for i in range(4)])
    ctx.ar.reset(mark)
    return ck


def _l0_ffn_sub(ctx, W, dep, cur, nxt, src_ck, tag):
    b, T, C = ctx.b, ctx.T, ctx.C
    mark = ctx.ar.mark()
    w1_d, w2_d = W["w1"][dep], W["w2"][dep]
    NP = 22

    Pst_off = ctx.ar.alloc(NP * T * 2)
    Pst = [ctx.v(Pst_off + j * T * 2, T, BF16) for j in range(NP)]

    b1a, b1g, b2c = [], [], []
    for j in range(NP):
        na = 128 if j < NP - 1 else 42
        va = ctx.v(ctx.ar.alloc(64), 1, F32, pn=na)
        vg = ctx.v(ctx.ar.alloc(64), 1, F32, pn=na)
        b.dma("sp", va, col1(W["b1"][dep][j * 128:j * 128 + na]), sem="q_sld", exclusive=False)
        b.dma("sp", vg, col1(W["b1"][dep][2730 + j * 128:2730 + j * 128 + na]), sem="q_sld", exclusive=False)
        b1a.append(va)
        b1g.append(vg)
    for j in range(DT):
        v = ctx.v(ctx.ar.alloc(64), 1, F32)
        b.dma("sp", v, col1(W["b2"][dep][j * 128:(j + 1) * 128]), sem="q_sld", exclusive=False)
        b2c.append(v)
    col_ck = b.ckpt("sp", "q_sld")

    mk_ln = ctx.ar.mark()
    xn_stride = (T + 4) * 2
    xn_off = ctx.ar.alloc(DT * xn_stride)
    ln = emit_ln(ctx, cur, xn_off, xn_stride, g_dram=W["ln_g"][dep],
                 bg_dram=W["ln_b"][dep], shift=True, tag=tag + f"n{dep}",
                 src_tok=[src_ck])
    xn, xn_tok = ln["xn_tiles"], ln["xn_tok"]

    gbuf = ctx.v(ctx.ar.alloc(T * 2), T, BF16)
    wbig_off = ctx.ar.alloc(DT * 2730 * 2)
    wbig = ctx.v(wbig_off, DT * 2730, BF16)
    ln_done = ln["xn_tok"][DT - 1][ctx.C - 1]
    half_war = ln_done
    gw_tok = None
    gel_toks = []
    for half in (0, 1):
        wt = b.dma("pool", wbig, wslice_ap(w1_d, 5460, half * 2730, 2730),
                   waits=[half_war], sem="q_gw0")
        for j in range(NP):
            na = 128 if j < NP - 1 else 42
            bias_c = b1a[j] if half == 0 else b1g[j]
            for c in range(C):
                bi = 2 + (j + c) % 2
                bk = ctx.bank(bi, 512, F32, pn=na)
                mm = None
                for k in range(DT):
                    w = [wt, xn_tok[k][c], ctx.bankuse.get(bi)]
                    mm = b.op("pe", lambda E, o=bk,
                              l=wbig[:, k * 2730 + j * 128:k * 2730 + j * 128 + na],
                              r=xn[k][:, 1 + c * 512:1 + c * 512 + 512], k=k:
                              E.matmul(o, l, r, start=(k == 0), stop=(k == DT - 1)), w)
                if half == 0:
                    ev = b.op("dve", lambda E, o=Pst[j][0:na, c * 512:(c + 1) * 512],
                              i=bk, bc=bias_c: E.tensor_scalar(
                        o, i, bc, None, op0=ALU.add), [mm, col_ck])
                else:
                    ev = b.op("act", lambda E, o=gbuf[0:na, c * 512:(c + 1) * 512],
                              i=bk, bc=bias_c: E.activation(o, i, AF.Gelu, bias=bc),
                              [mm, col_ck, gw_tok])
                ctx.bankuse[bi] = ev
            if half == 1:
                gw_tok = b.op("dve", lambda E, o=Pst[j][0:na, 0:T], a=Pst[j][0:na, 0:T],
                              g=gbuf[0:na, 0:T]: E.tensor_tensor(o, a, g, op=ALU.mult),
                              [ev])
                gel_toks.append(gw_tok)
        half_war = mm
    P_ck = gw_tok

    ctx.ar.reset(mk_ln)
    w2_off = ctx.ar.alloc(NP * 1024 * 2)
    w2t = [ctx.v(w2_off + k * 1024 * 2, 1024, BF16, pn=(128 if k < NP - 1 else 42))
           for k in range(NP)]
    w2_tok = []
    for k in range(NP):
        na = 128 if k < NP - 1 else 42
        w2_tok.append(b.dma("pool", w2t[k], w2_d[k * 128:k * 128 + na, :],
                            waits=[P_ck if k < 2 else w2_tok[k - 2]], sem=f"q_gw{k % 2}"))
    xi_off = ctx.ar.alloc(4 * 512 * 4)
    xo_off = ctx.ar.alloc(4 * 512 * 4)
    xi_war, xo_war = {}, {}
    for j in range(DT):
        for c in range(C):
            bi = (j * C + c) % 2
            bk = ctx.bank(bi, 512, F32)
            mm = None
            for k in range(NP):
                na = 128 if k < NP - 1 else 42
                w = [w2_tok[k], P_ck, ctx.bankuse.get(bi)]
                mm = b.op("pe", lambda E, o=bk, l=w2t[k][:, j * 128:(j + 1) * 128],
                          r=Pst[k][0:na, c * 512:(c + 1) * 512], k=k:
                          E.matmul(o, l, r, start=(k == 0), stop=(k == NP - 1)), w)
            i4 = (j * C + c) % 4
            xi = ctx.v(xi_off + i4 * 2048, 512, F32)
            xo = ctx.v(xo_off + i4 * 2048, 512, F32)
            ld = b.dma("sp", xi, cur[j * P:(j + 1) * P, c * 512:c * 512 + 512],
                       waits=[xi_war.get(i4) or P_ck], sem=f"q_sxi{i4}")
            t = b.op("dve", lambda E, o=xo, i=bk, bc=b2c[j], x=xi:
                     E.scalar_tensor_tensor(o, i, bc, x, op0=ALU.add, op1=ALU.add),
                     [mm, ld, col_ck, xo_war.get(i4)])
            xi_war[i4] = t
            ctx.bankuse[bi] = t
            d = b.dma("sp", nxt[j * P:(j + 1) * P, c * 512:c * 512 + 512], xo,
                      waits=[t], sem=f"q_sxo{i4}")
            xo_war[i4] = d
    ck = b.nopop("sp", [(f"q_sxo{i}", b.cnt.get(f"q_sxo{i}", 0)) for i in range(4)])
    ctx.ar.reset(mark)
    return ck


def phase_barrier(b):
    sp_d = [(s, v) for s, v in b.cnt.items() if s.startswith("q_s") and v > 0]
    pl_d = [(s, v) for s, v in b.cnt.items() if (s.startswith("q_g") or s == "q_pre") and v > 0]
    t1 = b.nopop("sp", sp_d)
    t2 = b.nopop("pool", pl_d)
    base = [t1, t2]
    for e in ENGINES:
        if b.cnt.get(e, 0) > 0:
            base.append((e, b.cnt[e]))
    b.base = []
    bb = []
    for t in base:
        bb.append(t)
    b.base = bb
    return bb


def build_module(T):
    nc = bass.Bass("TRN2", target_bir_lowering=False)
    b = Bld(nc)
    ctx = Ctx(nc, b, T)
    D_ = D

    def di(name, shape):
        ap = nc.dram_tensor(name, list(shape), F32, kind="ExternalInput").ap()
        ctx.dram[name] = ap
        return ap

    x = di("x_fm", (D_, T))
    W0a = {
        "dq": di("L0_attn_W_dq", (128, D_)),
        "uq": di("L0_attn_W_uq", (D_, 128)),
        "dkv": di("L0_attn_W_dkv", (128, D_)),
        "uk": di("L0_attn_W_uk", (D_, 128)),
        "uv": di("L0_attn_W_uv", (D_, 128)),
        "o": di("L0_attn_W_o", (D_, D_)),
    }
    W0f = {
        "qkv": di("L0_ffn_Wqkv", (6, D_, 3072)),
        "qs": di("L0_ffn_q_scale", (6, 64)),
        "ks": di("L0_ffn_k_scale", (6, 64)),
        "wo": di("L0_ffn_Wo", (6, 1024, D_)),
        "ln_g": di("L0_ffn_ln_g", (6, D_)),
        "ln_b": di("L0_ffn_ln_b", (6, D_)),
        "w1": di("L0_ffn_W1", (6, D_, 5460)),
        "b1": di("L0_ffn_b1", (6, 5460)),
        "w2": di("L0_ffn_W2", (6, 2730, D_)),
        "b2": di("L0_ffn_b2", (6, D_)),
    }
    W1a_qkv = di("L1_attn_W_qkv", (D_, 3 * D_))
    W1a_out = di("L1_attn_W_out", (D_, D_))
    W1a_rel = di("L1_attn_rel_emb", (32, 16))
    W1f = {}
    for n, shp in [("w1_w", (4096, D_)), ("w2_w", (4096, D_)), ("h1_w", (4096, 4096)),
                   ("h2_w", (4096, 4096)), ("w3_w", (D_, 4096)),
                   ("w1_s", (4096,)), ("w2_s", (4096,)), ("h1_s", (4096,)),
                   ("h2_s", (4096,)), ("w3_s", (D_,))]:
        W1f[n] = di("L1_ffn_" + n, shp)
    for n, shp in [("c_eye", (P, P)), ("c_l0fm", (P, 4 * 512)), ("c_l0am", (P, 4 * 512)),
                   ("c_rotq", (P, 2 * T)), ("c_rotk", (P, 2 * T)),
                   ("c_blk2", (P, 2)), ("c_sel2", (2, P)),
                   ("c_bcoh", (T // 64, (T // 64) * 64))]:
        di(n, shp)
    oht = nc.dram_tensor("c_t5oh", [32, 2 * T - 1], F32, kind="ExternalInput").ap()
    ctx.dram["c_t5oh"] = oht
    idxt = nc.dram_tensor("c_t5idx", [P, 16], I32, kind="ExternalInput").ap()
    ctx.dram["c_t5idx"] = idxt

    xo = nc.dram_tensor("xo_fm", [D_, T], F32, kind="ExternalOutput").ap()
    ya = nc.dram_tensor("ya", [D_, T], F32, kind="Internal").ap()
    yb = nc.dram_tensor("yb", [D_, T], F32, kind="Internal").ap()
    xA = nc.dram_tensor("xA", [D_, T], F32, kind="Internal").ap()
    xB = nc.dram_tensor("xB", [D_, T], F32, kind="Internal").ap()
    scr_a = nc.dram_tensor("scr_a", [4096, T], BF16, kind="Internal").ap()
    scr_b = nc.dram_tensor("scr_b", [4096, T], BF16, kind="Internal").ap()
    bias_hbm = nc.dram_tensor("bias_hbm", [16 * (2 * T - 1)], BF16, kind="Internal").ap()

    from contextlib import ExitStack
    es = ExitStack()
    ctx.sb = es.enter_context(nc.sbuf_tensor("SB", [P, SBUF_BYTES], U8))
    ctx.ps = es.enter_context(nc.psum_tensor("PS", [P, 16384], U8))
    emit_prelude(ctx)
    emit_l0ffn(ctx, x, ya, W0f, xA, xB, tag="s0")
    phase_barrier(b)
    emit_l0attn(ctx, ya, yb, W0a, tag="m0")
    phase_barrier(b)
    emit_l0ffn(ctx, yb, ya, W0f, xA, xB, tag="s1")
    phase_barrier(b)
    emit_l1ffn(ctx, ya, yb, W1f, scr_a, scr_b, tag="f0")
    phase_barrier(b)
    emit_l1attn(ctx, yb, ya, W1a_qkv, W1a_out, W1a_rel, bias_hbm, tag="a0")
    phase_barrier(b)
    done = emit_l1ffn(ctx, ya, xo, W1f, scr_a, scr_b, tag="f1")
    b.nopop("sp", done)

    sems = {}
    for name in b.cnt:
        sems[name] = es.enter_context(nc.semaphore(name))
    blk = es.enter_context(nc.Block())
    b.build_into(blk, sems)
    es.close()
    return nc, b


def host_consts(T):
    HS, WIN = 64, 256
    half = np.arange(0, HS, 2, dtype=np.float64)
    invf = 1.0 / 10000.0 ** (half / HS)
    t = np.arange(T, dtype=np.float64)
    fr = np.concatenate([t[:, None] * invf] * 2, -1)
    base = (half + 0.4 * HS) / (1.4 * HS)
    sc = base[None, :] ** ((t - T // 2) / (WIN // 2))[:, None]
    sc = np.concatenate([sc, sc], -1)
    cos, sin = np.cos(fr), np.sin(fr)
    sign = np.where(np.arange(64) < 32, -1.0, 1.0)

    def fm(a):
        m = a.T
        return np.concatenate([m, m], 0).astype(np.float32)

    rotq = np.concatenate([fm(cos * sc), fm(sin * sc * sign[None, :])], 1)
    rotk = np.concatenate([fm(cos / sc), fm(sin / sc * sign[None, :])], 1)

    l0fm = np.zeros((P, 4 * 512), np.float32)
    for v in range(4):
        qsub, w0 = v % 2, v >= 2
        for p in range(P):
            i = p + 128 * qsub
            ok = np.zeros(512, bool)
            ok[i:i + 257] = True
            if w0:
                ok[:256] = False
            l0fm[p, v * 512:(v + 1) * 512] = np.where(ok, 0.0, -3e38)

    l0am = np.zeros((P, 4 * 512), np.float32)
    for v in range(4):
        for p in range(P):
            jmax = v * 128 + p
            l0am[p, v * 512 + jmax + 1:(v + 1) * 512] = -3e38

    RLEN = 2 * T - 1
    NB, MD = 32, 128
    r = np.arange(-(T - 1), T)
    nb = NB // 2
    rb = (r > 0).astype(np.int64) * nb
    ra = np.abs(r)
    mx = nb // 2
    small = ra < mx
    large = mx + (np.log(np.maximum(ra, 1) / mx) / np.log(MD / mx) * (nb - mx)).astype(np.int64)
    large = np.minimum(large, nb - 1)
    buck = rb + np.where(small, ra, large)
    oh = np.zeros((NB, RLEN), np.float32)
    oh[buck, np.arange(RLEN)] = 1.0
    idx = np.zeros((P, 16), np.int32)
    for h in range(16):
        idx[:, h] = h * RLEN + 127 - np.arange(P)

    blk2 = np.zeros((P, 2), np.float32)
    blk2[:64, 0] = 1
    blk2[64:, 1] = 1

    NU = T // 64
    bcoh = np.zeros((NU, NU * 64), np.float32)
    for rr_ in range(NU):
        bcoh[rr_, rr_ * 64:(rr_ + 1) * 64] = 1.0

    return dict(c_eye=np.eye(P, dtype=np.float32), c_l0fm=l0fm, c_l0am=l0am,
                c_rotq=rotq, c_rotk=rotk, c_blk2=blk2, c_sel2=blk2.T.copy(),
                c_t5oh=oh, c_t5idx=idx, c_bcoh=bcoh)


_CACHED = {}


def _get_program():
    if "nc" not in _CACHED:
        nc, b = build_module(2048)
        _CACHED["nc"] = nc
    return _CACHED["nc"]


def kernel(**inputs):
    T = 2048
    nc = _get_program()
    consts = host_consts(T)
    x = inputs["L0_attn_x"]
    names = [
        "L0_attn_W_dq", "L0_attn_W_uq", "L0_attn_W_dkv", "L0_attn_W_uk",
        "L0_attn_W_uv", "L0_attn_W_o", "L0_ffn_Wqkv", "L0_ffn_q_scale",
        "L0_ffn_k_scale", "L0_ffn_Wo", "L0_ffn_ln_g", "L0_ffn_ln_b",
        "L0_ffn_W1", "L0_ffn_b1", "L0_ffn_W2", "L0_ffn_b2",
        "L1_attn_W_qkv", "L1_attn_W_out", "L1_attn_rel_emb",
        "L1_ffn_w1_w", "L1_ffn_w1_s", "L1_ffn_w2_w", "L1_ffn_w2_s",
        "L1_ffn_h1_w", "L1_ffn_h1_s", "L1_ffn_h2_w", "L1_ffn_h2_s",
        "L1_ffn_w3_w", "L1_ffn_w3_s",
    ]
    shared = {n: np.ascontiguousarray(np.asarray(inputs[n], dtype=np.float32))
              for n in names}
    shared.update(consts)
    in_maps = []
    for i in range(8):
        m = dict(shared)
        m["x_fm"] = np.ascontiguousarray(np.asarray(x[i], dtype=np.float32).T)
        in_maps.append(m)
    res = run_bass_kernel_spmd(nc, in_maps, core_ids=list(range(8)))
    out = np.stack([np.ascontiguousarray(r["xo_fm"].T) for r in res.results])
    return out.astype(np.float32)


# version 11
# speedup vs baseline: 1.0464x; 1.0007x over previous
import numpy as np
import ml_dtypes
import bass_rust
import concourse.bass as bass
import concourse.mybir as mybir
from concourse.bass_utils import run_bass_kernel_spmd

F32 = mybir.dt.float32
BF16 = mybir.dt.bfloat16
I32 = mybir.dt.int32
U8 = mybir.dt.uint8
AF = mybir.ActivationFunctionType
ALU = mybir.AluOpType
AX = mybir.AxisListType

P = 128
D = 1024
DT = D // P
SBUF_BYTES = 203 * 1024
SZ = {F32: 4, BF16: 2, I32: 4, U8: 1}

ENGINES = ("pe", "act", "dve", "pool", "sp")


class Bld:
    def __init__(self, nc):
        self.nc = nc
        self.stream = {e: [] for e in ENGINES}
        self.cnt = {}
        self.known = {e: {} for e in ENGINES}
        self.base = []

    def barrier(self):
        self.base = [(s, v) for s, v in self.cnt.items() if v > 0]
        return list(self.base)

    def sem(self, name):
        if name not in self.cnt:
            self.cnt[name] = 0
        return name

    def op(self, eng, fn, waits=(), inc=None, n=1):
        elide = eng != "pool"
        ws = []
        seen = {}
        for t in list(self.base) + list(waits):
            if t is None:
                continue
            s, v = t
            if v <= 0 or (elide and v <= self.known[eng].get(s, 0)):
                continue
            if v <= seen.get(s, 0):
                continue
            seen[s] = v
            ws.append((s, v))
            if elide:
                self.known[eng][s] = v
        ws = [(s, v) for s, v in ws if v == seen[s]]
        tok = None
        if inc is None:
            inc = eng
        self.sem(inc)
        self.cnt[inc] += n
        tok = (inc, self.cnt[inc])
        self.stream[eng].append((ws, fn, inc, n))
        return tok

    def nopop(self, eng, waits):
        return self.op(eng, lambda E: E.nop(), waits, inc=eng + "_ck")

    def ckpt(self, eng, sem):
        return self.nopop(eng, [(sem, self.cnt.get(sem, 0))])

    def dma(self, eng, out, in_, waits=(), sem=None, exclusive=True):
        sem = self.sem(sem or ("dq_" + eng))
        ws = list(waits)
        if exclusive and self.cnt.get(sem, 0) > 0:
            ws.append((sem, self.cnt[sem]))
        return self.op(eng, lambda E, o=out, i=in_: E.dma_start(o, i), ws,
                       inc=sem, n=16)

    def build_into(self, block, semobjs):
        decos = {"pe": block.tensor, "act": block.scalar, "dve": block.vector,
                 "pool": block.gpsimd, "sp": block.sync}
        for eng in ENGINES:
            ops = self.stream[eng]

            @decos[eng]
            def _(E, ops=ops):
                for ws, fn, inc, n in ops:
                    for s, v in ws:
                        E.wait_ge(semobjs[s], v)
                    ins = fn(E)
                    ins.then_inc(semobjs[inc], n)


class Arena:
    def __init__(self, size):
        self.size = size
        self.off = 0
        self.hi = 0

    def alloc(self, nbytes, align=64):
        self.off = (self.off + align - 1) // align * align
        r = self.off
        self.off += nbytes
        self.hi = max(self.hi, self.off)
        assert self.off <= self.size, f"arena overflow: {self.off} > {self.size}"
        return r

    def mark(self):
        return self.off

    def reset(self, mark):
        self.off = mark


class Ctx:
    def __init__(self, nc, b, T):
        self.nc = nc
        self.b = b
        self.T = T
        self.C = T // 512
        self.ar = Arena(SBUF_BYTES)
        self.sb = None
        self.ps = None
        self.dram = {}
        self.bankuse = {}
        self.stg_war = {}

    def v(self, off, cols, dtype, p0=0, pn=P):
        return self.sb[p0:p0 + pn, off:off + cols * SZ[dtype]].bitcast(dtype)

    def bank(self, i, cols, dtype, p0=0, pn=P):
        o = 2048 * i
        return self.ps[p0:p0 + pn, o:o + cols * SZ[dtype]].bitcast(dtype)


def row1(dram_1d):
    return dram_1d.rearrange("(o x) -> o x", o=1)


def col1(dram_1d):
    return dram_1d.rearrange("(x o) -> x o", o=1)


def emit_ln(ctx, xsrc, xn_off, xn_stride, g_dram=None, bg_dram=None, shift=False,
            tag="ln", want_xb=False, src_tok=(), fout_dram=None):
    b, T, C = ctx.b, ctx.T, ctx.C
    mark0 = ctx.ar.mark()

    def sl(k, c0, n, tiles):
        off = 0 if (shift and k >= DT // 2) else 1
        return tiles[k][:, off + c0: off + c0 + n]

    oc = ctx.v(ctx.ar.alloc(256), 1, BF16)
    oc_tok = b.op("pool", lambda E, o=oc: E.memset(o, 1.0), list(src_tok))

    g_rows, b_cols, g_tok, b_tok = [], [], [], []
    for k in range(DT):
        gv = ctx.v(ctx.ar.alloc(256), 128, BF16, pn=1)
        g_rows.append(gv)
        if g_dram is not None:
            b.dma("pool", gv, row1(g_dram[k * P:(k + 1) * P]), sem="q_gld",
                  exclusive=False, waits=list(src_tok))
            g_tok.append(None)
        else:
            g_tok.append(b.op("pool", lambda E, o=gv: E.memset(o, 1.0), list(src_tok)))
        if bg_dram is not None:
            bv = ctx.v(ctx.ar.alloc(64), 1, F32)
            b_cols.append(bv)
            b.dma("sp", bv, col1(bg_dram[k * P:(k + 1) * P]), sem="q_sld",
                  exclusive=False, waits=list(src_tok))
            b_tok.append(None)
        else:
            b_cols.append(None)
            b_tok.append(None)
    if g_dram is not None:
        gall = b.ckpt("pool", "q_gld")
        g_tok = [gall if t is None else t for t in g_tok]
    if bg_dram is not None:
        ball = b.ckpt("sp", "q_sld")
        b_tok = [ball for _ in b_tok]

    rstd_bf = ctx.v(ctx.ar.alloc(T * 2), T, BF16, pn=1)
    nmr_bf = ctx.v(ctx.ar.alloc(T * 2), T, BF16, pn=1)
    srows = ctx.ar.alloc(4 * 512 * 4)
    rstd_tok = [None] * C
    nmr_tok = [None] * C

    mark1 = ctx.ar.mark()
    xb_stride = (T + 4) * 2
    xb_off = ctx.ar.alloc(DT * xb_stride)
    xb_tiles = [ctx.v(xb_off + k * xb_stride, T + 1, BF16) for k in range(DT)]
    xb_tok = []
    for k in range(DT):
        z = b.op("pool", lambda E, t=xb_tiles[k]: E.memset(t[:, 0:1], 0.0),
                 list(src_tok))
        tk = b.dma("pool", xb_tiles[k][:, 1:T + 1], xsrc[k * P:(k + 1) * P, :],
                   sem=f"q_gx{k}", waits=list(src_tok))
        xb_tok.append(tk)

    sq_off = ctx.ar.alloc(2 * 512 * 2)
    sq_bufs = [ctx.v(sq_off + i * 1024, 512, BF16) for i in range(2)]
    sq_war = [None, None]

    bank_consumed = ctx.bankuse

    for c in range(C):
        bmu_i, bsq_i = (0, 1) if c % 2 == 0 else (2, 3)
        bmu = ctx.bank(bmu_i, 512, F32, pn=1)
        bsq = ctx.bank(bsq_i, 512, F32, pn=1)
        mu_tok = sq_mm_tok = None
        for k in range(DT):
            w = [xb_tok[k], oc_tok, bank_consumed.get(bmu_i)]
            mu_tok = b.op("pe", lambda E, o=bmu, x=sl(k, c * 512, 512, xb_tiles),
                          k=k: E.matmul(o, oc, x, start=(k == 0), stop=(k == DT - 1)), w)
        for k in range(DT):
            i2 = k % 2
            sq = sq_bufs[i2]
            sqt = b.op("dve", lambda E, s=sq, x=sl(k, c * 512, 512, xb_tiles):
                       E.tensor_tensor(s, x, x, op=ALU.mult),
                       [xb_tok[k], sq_war[i2]])
            w = [sqt, oc_tok, bank_consumed.get(bsq_i)]
            sq_mm_tok = b.op("pe", lambda E, o=bsq, s=sq, k=k:
                             E.matmul(o, oc, s, start=(k == 0), stop=(k == DT - 1)), w)
            sq_war[i2] = sq_mm_tok
        mu_row = ctx.v(srows, 512, F32, pn=1)
        var_row = ctx.v(srows + 2048, 512, F32, pn=1)
        musq = ctx.v(srows + 4096, 512, F32, pn=1)
        rstd_f = ctx.v(srows + 6144, 512, F32, pn=1)
        t1 = b.op("dve", lambda E, o=mu_row, i=bmu: E.tensor_scalar(
            o, i, 1.0 / D, None, op0=ALU.mult), [mu_tok])
        bank_consumed[bmu_i] = t1
        t2 = b.op("dve", lambda E, o=var_row, i=bsq: E.tensor_scalar(
            o, i, 1.0 / D, 1e-5, op0=ALU.mult, op1=ALU.add), [sq_mm_tok])
        bank_consumed[bsq_i] = t2
        t3 = b.op("dve", lambda E, o=musq, i=mu_row: E.tensor_tensor(
            o, i, i, op=ALU.mult), [t1])
        t4 = b.op("dve", lambda E, o=var_row, a=var_row, m=musq: E.tensor_tensor(
            o, a, m, op=ALU.subtract), [t2, t3])
        t5 = b.op("act", lambda E, o=rstd_f, i=var_row: E.activation(o, i, AF.Ln), [t4])
        t6 = b.op("act", lambda E, o=rstd_f, i=rstd_f: E.activation(
            o, i, AF.Exp, scale=-0.5), [t5])
        rstd_tok[c] = b.op("dve", lambda E, o=rstd_bf[:, c * 512:c * 512 + 512],
                           i=rstd_f: E.tensor_copy(o, i), [t6])
        t7 = b.op("dve", lambda E, o=nmr_bf[:, c * 512:c * 512 + 512], m=mu_row,
                  r=rstd_f: E.tensor_tensor(o, m, r, op=ALU.mult), [t6, t3])
        nmr_tok[c] = b.op("dve", lambda E, o=nmr_bf[:, c * 512:c * 512 + 512]:
                          E.tensor_scalar(o, o, -1.0, None, op0=ALU.mult), [t7])

    sq_done = sq_mm_tok

    tmp_off = ctx.ar.alloc(2 * 512 * 4)
    xn_tiles = [ctx.v(xn_off + k * xn_stride, T + 1, BF16) for k in range(DT)]
    if fout_dram is not None:
        fo_off = ctx.ar.alloc(4 * 512 * 4)
        fo_war = {}
    xn_tok = [[None] * C for _ in range(DT)]
    xn_z = [None] * DT
    for k in range(DT):
        for c in range(C):
            kc = k * C + c
            bR_i, bM_i = (4, 5) if kc % 2 == 0 else (6, 7)
            bR = ctx.bank(bR_i, 512, F32)
            bM = ctx.bank(bM_i, 512, F32)
            rtok = b.op("pe", lambda E, o=bR, g=g_rows[k],
                        r=rstd_bf[:, c * 512:c * 512 + 512]: E.matmul(
                o, g, r, start=True, stop=True),
                [rstd_tok[c], g_tok[k], bank_consumed.get(bR_i)])
            mtok = b.op("pe", lambda E, o=bM, g=g_rows[k],
                        m=nmr_bf[:, c * 512:c * 512 + 512]: E.matmul(
                o, g, m, start=True, stop=True),
                [nmr_tok[c], bank_consumed.get(bM_i)])
            xsl = sl(k, c * 512, 512, xb_tiles)
            tmp = ctx.v(tmp_off + (kc % 2) * 2048, 512, F32)
            a1 = b.op("dve", lambda E, o=tmp, x=xsl, R=bR: E.tensor_tensor(
                o, x, R, op=ALU.mult), [xb_tok[k], rtok])
            bank_consumed[bR_i] = a1
            if fout_dram is not None:
                i4 = kc % 4
                fo = ctx.v(fo_off + i4 * 2048, 512, F32)
                a2 = b.op("dve", lambda E, o=fo, t=tmp, M=bM: E.tensor_tensor(
                    o, t, M, op=ALU.add), [a1, mtok, fo_war.get(i4)])
                d = b.dma("sp", fout_dram[k * P:(k + 1) * P, c * 512:c * 512 + 512],
                          fo, waits=[a2], sem=f"q_sxo{i4}")
                fo_war[i4] = d
            elif b_cols[k] is not None:
                a2 = b.op("dve", lambda E, o=xn_tiles[k][:, 1 + c * 512:1 + c * 512 + 512],
                          t=tmp, bb=b_cols[k], M=bM: E.scalar_tensor_tensor(
                    o, t, bb, M, op0=ALU.add, op1=ALU.add), [a1, mtok, b_tok[k]])
            else:
                a2 = b.op("dve", lambda E, o=xn_tiles[k][:, 1 + c * 512:1 + c * 512 + 512],
                          t=tmp, M=bM: E.tensor_tensor(o, t, M, op=ALU.add),
                          [a1, mtok])
            bank_consumed[bM_i] = a2
            xn_tok[k][c] = a2

    store_ck = None
    if fout_dram is not None:
        store_ck = b.nopop("sp", [(f"q_sxo{i}", b.cnt.get(f"q_sxo{i}", 0)) for i in range(4)])
    ctx.ar.reset(mark0)
    return dict(xn_tiles=xn_tiles, xn_tok=xn_tok, xn_z=xn_z, store_ck=store_ck)


def emit_prelude(ctx):
    b = ctx.b
    idb = ctx.v(ctx.ar.alloc(128 * 2), 128, BF16)
    t = b.dma("pool", idb, ctx.dram["c_eye"], sem="q_pre")
    ctx.id_bf = idb
    ctx.id_tok = ("q_pre", b.cnt["q_pre"])


def tlin_stage(ctx, w_dram, KT, MT, NIN, rhs_fn, out_fn, rhs_deps, tag,
               tr_banks=(0, 1), mm_banks=(2, 3), war0=(), scratch_off=None):
    b, C = ctx.b, ctx.C
    G = 8
    ngrp = (KT + G - 1) // G
    nwt = 2 if KT <= 8 else 1
    mark = ctx.ar.mark()
    wraw_sz = NIN * 2
    wT_sz = KT * 128 * 2
    if scratch_off is not None:
        wraw_off = scratch_off
        wT_off = scratch_off + 2 * wraw_sz
    else:
        wraw_off = ctx.ar.alloc(2 * wraw_sz)
        wT_off = ctx.ar.alloc(nwt * wT_sz)
    wload_tok = [None, None]
    wT_tok = [[None] * ngrp, [None] * ngrp]
    slot_last = [None, None]
    mm_last = None
    for j in range(MT):
        sl = j % 2
        wraw = ctx.v(wraw_off + sl * wraw_sz, NIN, BF16)
        wT = ctx.v(wT_off + (sl % nwt) * wT_sz, KT * 128, BF16)
        war = [slot_last[sl]] if slot_last[sl] is not None else list(war0)
        wload_tok[sl] = b.dma("pool", wraw, w_dram[j * P:(j + 1) * P, :],
                              waits=war, sem=f"q_gw{sl}")
        for g in range(ngrp):
            tb_i = tr_banks[(j * ngrp + g) % 2]
            tb = ctx.bank(tb_i, 1024, BF16)
            n_in_g = min(G, KT - g * G)
            ttok = None
            for ki in range(n_in_g):
                k = g * G + ki
                w = [wload_tok[sl], ctx.id_tok, ctx.bankuse.get(tb_i)]
                ttok = b.op("pe", lambda E, o=tb[:, ki * 128:(ki + 1) * 128],
                            i=wraw[:, k * 128:(k + 1) * 128]:
                            E.transpose(o, i, ctx.id_bf), w)
            ev = b.op("act", lambda E, o=wT[:, g * G * 128:(g * G + n_in_g) * 128],
                      i=tb[:, 0:n_in_g * 128]: E.copy(o, i), [ttok])
            ctx.bankuse[tb_i] = ev
            wT_tok[sl][g] = ev
        for c in range(C):
            mb_i = mm_banks[(j * C + c) % 2]
            mb = ctx.bank(mb_i, 512, F32)
            mm = None
            for k in range(KT):
                w = [wT_tok[sl][k // G], ctx.bankuse.get(mb_i)] + (rhs_deps if k == 0 and c == 0 else [])
                mm = b.op("pe", lambda E, o=mb, l=wT[:, k * 128:(k + 1) * 128],
                          r=rhs_fn(k, c), k=k:
                          E.matmul(o, l, r, start=(k == 0), stop=(k == KT - 1)), w)
            mm_last = mm
            ctx.bankuse[mb_i] = out_fn(j, c, mb, mm)
        slot_last[sl] = mm_last
    ctx.ar.reset(mark)
    return mm_last


def emit_scale_cols(ctx, s_dram, ntiles, half=False, tag="sc"):
    b = ctx.b
    views = []
    for j in range(ntiles):
        v = ctx.v(ctx.ar.alloc(64), 1, F32)
        views.append(v)
        b.dma("sp", v, col1(s_dram[j * P:(j + 1) * P]), sem="q_sld", exclusive=False)
    tok = b.ckpt("sp", "q_sld")
    toks = [tok] * ntiles
    if half:
        newtok = []
        for j in range(ntiles):
            t = b.op("dve", lambda E, o=views[j]: E.tensor_scalar(
                o, o, 0.5, None, op0=ALU.mult), [tok])
            newtok.append(t)
        toks = newtok
    return views, toks


def emit_l1ffn(ctx, x_in, x_out, W, scr_a, scr_b, tag="f"):
    b, T, C = ctx.b, ctx.T, ctx.C
    HT = 32
    mark0 = ctx.ar.mark()

    s1_v, s1_t = emit_scale_cols(ctx, W["w1_s"], HT)
    s2_v, s2_t = emit_scale_cols(ctx, W["w2_s"], HT)
    h1_v, h1_t = emit_scale_cols(ctx, W["h1_s"], HT)
    h2_v, h2_t = emit_scale_cols(ctx, W["h2_s"], HT)
    w3_v, w3_t = emit_scale_cols(ctx, W["w3_s"], DT, half=True)

    tl_off = ctx.ar.alloc(2 * 4096 * 2 + 32 * 128 * 2)

    xn_stride = (T + 4) * 2
    xn_off = ctx.ar.alloc(DT * xn_stride)
    ln = emit_ln(ctx, x_in, xn_off, xn_stride, tag=tag + "l")
    xn, xn_tok, xn_z = ln["xn_tiles"], ln["xn_tok"], ln["xn_z"]
    xn_all = [xn_tok[k][c] for k in range(DT) for c in range(ctx.C)]
    mark_ln = ctx.ar.mark()

    big_sz = T * 2
    big_off = ctx.ar.alloc(HT * big_sz)
    big = [ctx.v(big_off + j * big_sz, T, BF16) for j in range(HT)]

    def rhs_xn(k, c):
        return xn[k][:, 1 + c * 512: 1 + c * 512 + 512]

    out_last = [None]

    def out_t1(j, c, ps, mm):
        t = b.op("act", lambda E, o=big[j][:, c * 512:c * 512 + 512], i=ps:
                 E.activation(o, i, AF.Silu, scale=s1_v[j]), [mm, s1_t[j]])
        out_last[0] = t
        return t

    st1_mm = tlin_stage(ctx, W["w1_w"], DT, HT, D, rhs_xn, out_t1, xn_all, tag + "1",
                        war0=[xn_all[-1]], scratch_off=tl_off)
    st1_out = out_last[0]

    def out_h(j, c, ps, mm):
        t = b.op("dve", lambda E, o=big[j][:, c * 512:c * 512 + 512], i=ps,
                 s=s2_v[j], t1s=big[j][:, c * 512:c * 512 + 512]:
                 E.scalar_tensor_tensor(o, i, s, t1s, op0=ALU.mult, op1=ALU.mult),
                 [mm, s2_t[j], out_last[0] if out_last[0] is not None else None])
        out_last[0] = t
        return t

    st2_mm = tlin_stage(ctx, W["w2_w"], DT, HT, D, rhs_xn, out_h, xn_all, tag + "2",
                        tr_banks=(4, 5), mm_banks=(6, 7), war0=[st1_mm],
                        scratch_off=tl_off)
    st2_out = out_last[0]


    if DT * xn_stride >= 20480:
        st_off = xn_off
        xi_off = xn_off + 4 * 1024
        xo_off = xn_off + 4 * 1024 + 4 * 2048
    else:
        st_off = ctx.ar.alloc(4 * 1024)
        xi_off = ctx.ar.alloc(4 * 2048)
        xo_off = ctx.ar.alloc(4 * 2048)

    def mk_store_out(scale_v, scale_t, scr, qsem):
        def out(j, c, ps, mm):
            i = (j * C + c) % 4
            stg = ctx.v(st_off + i * 1024, 512, BF16)
            t = b.op("act", lambda E, o=stg, i=ps: E.activation(
                o, i, AF.Gelu, scale=scale_v[j]),
                [mm, scale_t[j], ctx.stg_war.get(i)])
            d = b.dma("sp", scr[j * P:(j + 1) * P, c * 512:c * 512 + 512], stg,
                      waits=[t], sem=f"{qsem}{i}")
            ctx.stg_war[i] = d
            return t
        return out

    def rhs_big(k, c):
        return big[k][:, c * 512:c * 512 + 512]

    last = tlin_stage(ctx, W["h1_w"], HT, HT, 4096, rhs_big,
                      mk_store_out(h1_v, h1_t, scr_a, "q_s3"), [st2_out], tag + "3",
                      war0=[st2_mm], scratch_off=tl_off)

    g1_tok = []
    for j in range(HT):
        g1_tok.append(b.dma("sp", big[j], scr_a[j * P:(j + 1) * P, :],
                            waits=[last] + [ctx.stg_war.get(i) for i in range(4)],
                            sem="q_sg", exclusive=False))
    g1_all = b.ckpt("sp", "q_sg")
    last4 = tlin_stage(ctx, W["h2_w"], HT, HT, 4096, rhs_big,
                       mk_store_out(h2_v, h2_t, scr_b, "q_s4"), [g1_all], tag + "4",
                       tr_banks=(4, 5), mm_banks=(6, 7), war0=[last],
                       scratch_off=tl_off)

    g2_tok = []
    for j in range(HT):
        g2_tok.append(b.dma("sp", big[j], scr_b[j * P:(j + 1) * P, :],
                            waits=[last4] + [ctx.stg_war.get(i) for i in range(4)],
                            sem="q_sg2", exclusive=False))
    g2_all = b.ckpt("sp", "q_sg2")

    xi_war = {}
    xo_war = {}

    def out_res(j, c, ps, mm):
        i = (j * C + c) % 4
        xi = ctx.v(xi_off + i * 2048, 512, F32)
        xo = ctx.v(xo_off + i * 2048, 512, F32)
        ld = b.dma("sp", xi, x_in[j * P:(j + 1) * P, c * 512:c * 512 + 512],
                   waits=[xi_war.get(i) or g2_all], sem=f"q_sxi{i}")
        t = b.op("dve", lambda E, o=xo, i=ps, s=w3_v[j], x=xi:
                 E.scalar_tensor_tensor(o, i, s, x, op0=ALU.mult, op1=ALU.add),
                 [mm, ld, w3_t[j], xo_war.get(i)])
        xi_war[i] = t
        d = b.dma("sp", x_out[j * P:(j + 1) * P, c * 512:c * 512 + 512], xo,
                  waits=[t], sem=f"q_sxo{i}")
        xo_war[i] = d
        return t

    tlin_stage(ctx, W["w3_w"], HT, DT, 4096, rhs_big, out_res, [g2_all], tag + "5",
               war0=[last4], scratch_off=tl_off)
    out_done = [("q_sxo%d" % i, b.cnt.get("q_sxo%d" % i, 0)) for i in range(4)]
    ctx.ar.reset(mark0)
    return out_done


def wslice_ap(w_dram, ncols_total, col0, ncols, ktiles=DT):
    c = w_dram.copy()
    c.ap = bass_rust.VecI64Pair([[ncols_total, P], [ncols_total * P, ktiles], [1, ncols]])
    c.offset = w_dram.offset + col0
    return c


def emit_l1attn(ctx, x_in, x_out, W_qkv, W_out, rel_emb, bias_hbm, tag="a"):
    b, T, C = ctx.b, ctx.T, ctx.C
    TT_ = T // 128
    NG = (TT_ + 7) // 8
    mark0 = ctx.ar.mark()
    RLEN = 2 * T - 1
    WT = 2 * T - 128
    SKB = T - 128
    bias2d = bias_hbm.rearrange("(a b) -> a b", b=RLEN)
    bias_fl = bias_hbm.rearrange("(x o) -> x o", o=1)

    idx = ctx.v(ctx.ar.alloc(64), 16, I32)
    idx_t = b.dma("sp", idx, ctx.dram["c_t5idx"], sem="q_sld")

    mark_b = ctx.ar.mark()
    oh = ctx.v(ctx.ar.alloc(RLEN * 2 + 64), RLEN, BF16, pn=32)
    b.dma("pool", oh, ctx.dram["c_t5oh"], sem="q_gld", exclusive=False)
    rel = ctx.v(ctx.ar.alloc(64), 16, BF16, pn=32)
    b.dma("pool", rel, rel_emb, sem="q_gld", exclusive=False)
    oh_t = rel_t = b.ckpt("pool", "q_gld")
    brow = ctx.v(ctx.ar.alloc(RLEN * 2 + 64), RLEN, BF16, pn=16)
    ev_last = None
    for c in range((RLEN + 511) // 512):
        n = min(512, RLEN - c * 512)
        bk = ctx.bank(c % 2, n, F32, pn=16)
        mm = b.op("pe", lambda E, o=bk, r=rel, h=oh[:, c * 512:c * 512 + n]:
                  E.matmul(o, r, h, start=True, stop=True),
                  [oh_t, rel_t, ctx.bankuse.get(c % 2)])
        ev_last = b.op("dve", lambda E, o=brow[:, c * 512:c * 512 + n], i=bk:
                       E.tensor_copy(o, i), [mm])
        ctx.bankuse[c % 2] = ev_last
    b.dma("sp", bias2d, brow, waits=[ev_last], sem="q_sbias")
    bias_ck = b.ckpt("sp", "q_sbias")
    ctx.ar.reset(mark_b)

    xn_stride = (T + 4) * 2
    xn_off = ctx.ar.alloc(DT * xn_stride)
    ln = emit_ln(ctx, x_in, xn_off, xn_stride, tag=tag + "l", src_tok=[bias_ck])
    xn, xn_tok = ln["xn_tiles"], ln["xn_tok"]

    q_off = ctx.ar.alloc(DT * T * 2)
    k_off = ctx.ar.alloc(DT * T * 2)
    v_off = ctx.ar.alloc(TT_ * 1024 * 2)
    o_off = ctx.ar.alloc(DT * T * 2)
    q_sb = [ctx.v(q_off + j * T * 2, T, BF16) for j in range(DT)]
    k_sb = [ctx.v(k_off + j * T * 2, T, BF16) for j in range(DT)]
    v_sb = [ctx.v(v_off + j * 1024 * 2, 1024, BF16) for j in range(TT_)]
    o_sb = [ctx.v(o_off + j * T * 2, T, BF16) for j in range(DT)]
    mark1 = ctx.ar.mark()

    ws_off = ctx.ar.alloc(2 * DT * 128 * 2)
    vs_off = ctx.ar.alloc(DT * 512 * 2)
    q_tok = [[None] * C for _ in range(DT)]
    k_tok = [[None] * C for _ in range(DT)]
    v_tok = [None] * TT_
    ln_done = xn_tok[DT - 1][C - 1]
    ws_war = [ln_done, ln_done]
    mmb = 0
    for dst, base, toks in ((q_sb, 0, q_tok), (k_sb, 1024, k_tok)):
        for j in range(DT):
            sl = j % 2
            ws = ctx.v(ws_off + sl * DT * 128 * 2, DT * 128, BF16)
            wt = b.dma("pool", ws, wslice_ap(W_qkv, 3072, base + j * 128, 128),
                       waits=[ws_war[sl]], sem=f"q_gw{sl}")
            mm = None
            for c in range(C):
                bi = 2 + (mmb % 2)
                mmb += 1
                bk = ctx.bank(bi, 512, F32)
                for k in range(DT):
                    w = [wt, xn_tok[k][c], ctx.bankuse.get(bi)]
                    mm = b.op("pe", lambda E, o=bk, l=ws[:, k * 128:(k + 1) * 128],
                              r=xn[k][:, 1 + c * 512:1 + c * 512 + 512], k=k:
                              E.matmul(o, l, r, start=(k == 0), stop=(k == DT - 1)), w)
                ev = b.op("act" if (j + c) % 2 else "dve",
                          lambda E, o=dst[j][:, c * 512:c * 512 + 512], i=bk:
                          (E.copy(o, i) if hasattr(E, "copy") else E.tensor_copy(o, i)),
                          [mm])
                toks[j][c] = ev
                ctx.bankuse[bi] = ev
            ws_war[sl] = mm
    vs_war = ln_done
    for vc in range(2):
        vs = ctx.v(vs_off, DT * 512, BF16)
        wt = b.dma("pool", vs, wslice_ap(W_qkv, 3072, 2048 + vc * 512, 512),
                   waits=[vs_war], sem="q_gw0")
        for tt in range(TT_):
            bi = 4 + (tt % 2)
            bk = ctx.bank(bi, 512, F32)
            mm = None
            for k in range(DT):
                w = [wt, xn_tok[k][(tt * 128) // 512], ctx.bankuse.get(bi)]
                mm = b.op("pe", lambda E, o=bk, l=xn[k][:, 1 + tt * 128:1 + (tt + 1) * 128],
                          r=vs[:, k * 512:(k + 1) * 512], k=k:
                          E.matmul(o, l, r, start=(k == 0), stop=(k == DT - 1)), w)
            ev = b.op("act" if tt % 2 else "dve",
                      lambda E, o=v_sb[tt][:, vc * 512:(vc + 1) * 512], i=bk:
                      (E.copy(o, i) if hasattr(E, "copy") else E.tensor_copy(o, i)),
                      [mm])
            v_tok[tt] = ev
            ctx.bankuse[bi] = ev
        vs_war = mm

    ctx.ar.reset(mark1)
    bsk_off = ctx.ar.alloc(2 * WT * 2)
    P_off = ctx.ar.alloc(2 * T * 2)
    Pf_off = ctx.ar.alloc(T * 4)
    PT_off = ctx.ar.alloc(2 * TT_ * 128 * 2)
    sm_off = ctx.ar.alloc(2 * 192)
    P_bf2 = [ctx.v(P_off + i * T * 2, T, BF16) for i in range(2)]
    P_f2 = [ctx.v(Pf_off, T, F32), ctx.v(Pf_off, T, F32)]
    PT_sb2 = [ctx.v(PT_off + i * TT_ * 128 * 2, TT_ * 128, BF16) for i in range(2)]
    negmax2 = [ctx.v(sm_off + i * 192, 1, F32) for i in range(2)]
    sums2 = [ctx.v(sm_off + i * 192 + 64, 1, F32) for i in range(2)]
    recip2 = [ctx.v(sm_off + i * 192 + 128, 1, F32) for i in range(2)]
    bsk_tok = [None, None]
    bsk_war = [None, None]
    tr_tok = [None, None]
    sc_tok_prev = [None, None]
    mx_prev = [None, None]
    qq = 0
    cmm = None
    oev_tok = None
    for h in range(16):
        bsk = ctx.v(bsk_off + (h % 2) * WT * 2, WT, BF16)
        war = [bsk_war[h % 2] or vs_war, bias_ck, idx_t]
        bsk_tok[h % 2] = b.op(
            "pool",
            lambda E, o=bsk, i=bias_fl, ix=idx[:, h:h + 1]: E.indirect_dma_start(
                o, None, i, bass.IndirectOffsetOnAxis(ap=ix, axis=0)),
            war, inc=f"q_gbsk{h % 2}", n=16)
        jt2, pb2 = h // 2, 64 * (h % 2)
        for qt in range(TT_):
            sl2 = qq % 2
            qq += 1
            P_bf, P_f, PT_sb = P_bf2[sl2], P_f2[sl2], PT_sb2[sl2]
            negmax, sums, recip = negmax2[sl2], sums2[sl2], recip2[sl2]
            smm = None
            for c in range(C):
                bk = ctx.bank(c, 512, F32)
                w = [q_tok[jt2][(qt * 128) // 512], k_tok[jt2][c],
                     ctx.bankuse.get(c), bsk_tok[h % 2]]
                smm = b.op("pe", lambda E, o=bk,
                           l=q_sb[jt2][pb2:pb2 + 64, qt * 128:(qt + 1) * 128],
                           r=k_sb[jt2][pb2:pb2 + 64, c * 512:(c + 1) * 512]:
                           E.matmul(o, l, r, start=True, stop=False), w)
                smm = b.op("pe", lambda E, o=bk, l=ctx.id_bf,
                           r=bsk[:, SKB - 128 * qt + c * 512:SKB - 128 * qt + c * 512 + 512]:
                           E.matmul(o, l, r, start=False, stop=True), [smm, ctx.id_tok])
            spsum = ctx.ps[:, 0:C * 2048].bitcast(F32)
            mx = b.op("dve", lambda E, o=negmax, i=spsum: E.reduce_max(
                o, i, axis=AX.X, negate=True), [smm, mx_prev[sl2]])
            et = b.op("act", lambda E, o=P_f, i=spsum, bb=negmax, s=sums:
                      E.activation(o, i, AF.Exp, bias=bb, accum_out=s),
                      [mx, sc_tok_prev[0], sc_tok_prev[1]])
            for c in range(C):
                ctx.bankuse[c] = et
            rc = b.op("dve", lambda E, o=recip, i=sums: E.reciprocal(o, i), [et])
            sc_tok = b.op("dve", lambda E, o=P_bf, i=P_f, r=recip: E.tensor_scalar(
                o, i, r, None, op0=ALU.mult), [rc, tr_tok[sl2]])
            sc_tok_prev[sl2] = sc_tok
            mx_prev[sl2] = rc
            for g in range(NG):
                tb_i = 6 + (g % 2)
                tb = ctx.bank(tb_i, 1024, BF16)
                nin = min(8, TT_ - g * 8)
                ttok = None
                for ki in range(nin):
                    jt = g * 8 + ki
                    w = [sc_tok, ctx.id_tok, ctx.bankuse.get(tb_i)]
                    ttok = b.op("pe", lambda E, o=tb[:, ki * 128:(ki + 1) * 128],
                                i=P_bf[:, jt * 128:(jt + 1) * 128]:
                                E.transpose(o, i, ctx.id_bf), w)
                g_ev = b.op("dve", lambda E, o=PT_sb[:, g * 1024:g * 1024 + nin * 128],
                            i=tb[:, 0:nin * 128]: E.tensor_copy(o, i), [ttok])
                ctx.bankuse[tb_i] = g_ev
                tr_tok[sl2] = ttok
            pt_ck = g_ev
            for jt in range(TT_):
                bk = ctx.bank(4, 128, F32, pn=64)
                w = [pt_ck, v_tok[jt], ctx.bankuse.get(4)]
                cmm = b.op("pe", lambda E, o=bk, l=v_sb[jt][:, 64 * h:64 * h + 64],
                           r=PT_sb[:, jt * 128:(jt + 1) * 128], jt=jt:
                           E.matmul(o, l, r, start=(jt == 0), stop=(jt == TT_ - 1)), w)
            oev_tok = b.op("act", lambda E, o=o_sb[jt2][pb2:pb2 + 64, qt * 128:(qt + 1) * 128],
                           i=ctx.bank(4, 128, F32, pn=64): E.copy(o, i), [cmm])
            ctx.bankuse[4] = oev_tok
        bsk_war[h % 2] = smm

    o_done = oev_tok
    ctx.ar.reset(mark1)

    ws2_off = ctx.ar.alloc(2 * DT * 128 * 2)
    xi_off = ctx.ar.alloc(4 * 512 * 4)
    xo2_off = ctx.ar.alloc(4 * 512 * 4)
    xi_war, xo_war = {}, {}
    ws2_war = [None, None]
    out_done = []
    for j in range(DT):
        sl = j % 2
        ws = ctx.v(ws2_off + sl * DT * 128 * 2, DT * 128, BF16)
        wt = b.dma("pool", ws, wslice_ap(W_out, 1024, j * 128, 128),
                   waits=[ws2_war[sl] or cmm], sem=f"q_gw{sl}")
        for c in range(C):
            bi = (j * C + c) % 4
            bk = ctx.bank(bi, 512, F32)
            mm = None
            for k in range(DT):
                w = [wt, o_done, ctx.bankuse.get(bi)]
                mm = b.op("pe", lambda E, o=bk, l=ws[:, k * 128:(k + 1) * 128],
                          r=o_sb[k][:, c * 512:(c + 1) * 512], k=k:
                          E.matmul(o, l, r, start=(k == 0), stop=(k == DT - 1)), w)
            i4 = (j * C + c) % 4
            xi = ctx.v(xi_off + i4 * 2048, 512, F32)
            xo = ctx.v(xo2_off + i4 * 2048, 512, F32)
            ld = b.dma("sp", xi, x_in[j * P:(j + 1) * P, c * 512:c * 512 + 512],
                       waits=[xi_war.get(i4) or cmm], sem=f"q_sxi{i4}")
            t = b.op("dve", lambda E, o=xo, i=bk, x=xi: E.tensor_tensor(
                o, i, x, op=ALU.add), [mm, ld, xo_war.get(i4)])
            xi_war[i4] = t
            ctx.bankuse[bi] = t
            d = b.dma("sp", x_out[j * P:(j + 1) * P, c * 512:c * 512 + 512], xo,
                      waits=[t], sem=f"q_sxo{i4}")
            xo_war[i4] = d
            out_done.append(d)
        ws2_war[sl] = mm
    ctx.ar.reset(mark0)
    return out_done[-4:]


def emit_l0attn(ctx, x_in, x_out, W, tag="m"):
    b, T, C = ctx.b, ctx.T, ctx.C
    TT_ = T // 128
    mark0 = ctx.ar.mark()

    qup_off = ctx.ar.alloc(DT * T * 2)
    y_off = ctx.ar.alloc(DT * T * 2)
    ckvT_off = ctx.ar.alloc(T * 2)
    ckv_off = ctx.ar.alloc(TT_ * 128 * 2)
    qup_sb = [ctx.v(qup_off + j * T * 2, T, BF16) for j in range(DT)]
    y_sb = [ctx.v(y_off + j * T * 2, T, BF16) for j in range(DT)]
    ckvT = ctx.v(ckvT_off, T, BF16)
    ckv_sb = ctx.v(ckv_off, TT_ * 128, BF16)

    mark1 = ctx.ar.mark()

    keff = ctx.v(ctx.ar.alloc(8 * 128 * 2), 8 * 128, BF16)
    veff = ctx.v(ctx.ar.alloc(1024 * 2), 1024, BF16)
    dqT = ctx.v(ctx.ar.alloc(8 * 128 * 2), 8 * 128, BF16)
    dkvT = ctx.v(ctx.ar.alloc(8 * 128 * 2), 8 * 128, BF16)
    uqT = ctx.v(ctx.ar.alloc(8 * 128 * 2), 8 * 128, BF16)
    msk = ctx.v(ctx.ar.alloc(4 * 512 * 2), 4 * 512, BF16)
    mark_prep = ctx.ar.mark()

    dq_raw = ctx.v(ctx.ar.alloc(1024 * 2), 1024, BF16)
    dkv_raw = ctx.v(ctx.ar.alloc(1024 * 2), 1024, BF16)
    uq_raw = ctx.v(ctx.ar.alloc(8 * 128 * 2), 8 * 128, BF16)
    uk_raw = ctx.v(ctx.ar.alloc(8 * 128 * 2), 8 * 128, BF16)
    uv_raw = ctx.v(ctx.ar.alloc(8 * 128 * 2), 8 * 128, BF16)
    b.dma("pool", dq_raw, W["dq"], sem="q_gld", exclusive=False)
    b.dma("pool", dkv_raw, W["dkv"], sem="q_gld", exclusive=False)
    b.dma("pool", uq_raw, wslice_ap(W["uq"], 128, 0, 128), sem="q_gld", exclusive=False)
    b.dma("pool", uk_raw, wslice_ap(W["uk"], 128, 0, 128), sem="q_gld", exclusive=False)
    b.dma("pool", uv_raw, wslice_ap(W["uv"], 128, 0, 128), sem="q_gld", exclusive=False)
    raw_ck = b.ckpt("pool", "q_gld")

    tr_ev = {}
    for name, srcv, dstv in (("dq", dq_raw, dqT), ("dkv", dkv_raw, dkvT)):
        for g in range(1):
            tb = ctx.bank(6, 1024, BF16)
            tt = None
            for k in range(DT):
                tt = b.op("pe", lambda E, o=tb[:, k * 128:(k + 1) * 128],
                          i=srcv[:, k * 128:(k + 1) * 128]:
                          E.transpose(o, i, ctx.id_bf), [raw_ck, ctx.id_tok, ctx.bankuse.get(6)])
            ev = b.op("dve", lambda E, o=dstv, i=tb: E.tensor_copy(o, i), [tt])
            ctx.bankuse[6] = ev
            tr_ev[name] = ev
    tb = ctx.bank(7, 1024, BF16)
    tt = None
    for k in range(DT):
        tt = b.op("pe", lambda E, o=tb[:, k * 128:(k + 1) * 128],
                  i=uq_raw[:, k * 128:(k + 1) * 128]:
                  E.transpose(o, i, ctx.id_bf), [raw_ck, ctx.id_tok, ctx.bankuse.get(7)])
    uqT_ev = b.op("dve", lambda E, o=uqT, i=tb: E.tensor_copy(o, i), [tt])
    ctx.bankuse[7] = uqT_ev

    t1_sb = ctx.v(ctx.ar.alloc(128 * 2), 128, BF16)
    bk5 = ctx.bank(5, 128, F32)
    mm = None
    for k in range(DT):
        mm = b.op("pe", lambda E, o=bk5, l=uq_raw[:, k * 128:(k + 1) * 128],
                  r=uk_raw[:, k * 128:(k + 1) * 128], k=k:
                  E.matmul(o, l, r, start=(k == 0), stop=(k == DT - 1)),
                  [raw_ck, ctx.bankuse.get(5)])
    t1_ev = b.op("dve", lambda E, o=t1_sb, i=bk5: E.tensor_copy(o, i), [mm])
    ctx.bankuse[5] = t1_ev
    keff_ev = None
    for k in range(DT):
        bk5b = ctx.bank(5, 128, F32)
        mm = b.op("pe", lambda E, o=bk5b, l=dq_raw[:, k * 128:(k + 1) * 128], r=t1_sb:
                  E.matmul(o, l, r, start=True, stop=True),
                  [t1_ev, ctx.bankuse.get(5)])
        keff_ev = b.op("act", lambda E, o=keff[:, k * 128:(k + 1) * 128], i=bk5b:
                       E.activation(o, i, AF.Copy, scale=0.125), [mm])
        ctx.bankuse[5] = keff_ev

    wo_raw = ctx.v(ctx.ar.alloc(1024 * 2), 1024, BF16)
    woT = ctx.v(ctx.ar.alloc(DT * 1024 * 2), DT * 1024, BF16)
    wo_war = None
    for k in range(DT):
        ld = b.dma("pool", wo_raw, W["o"][k * P:(k + 1) * P, :], waits=[wo_war],
                   sem="q_gw0")
        for g in range(2):
            tb = ctx.bank(6 + g % 2, 1024, BF16)
            tt = None
            for ki in range(4):
                cc = g * 4 + ki
                tt = b.op("pe", lambda E, o=tb[:, ki * 128:(ki + 1) * 128],
                          i=wo_raw[:, cc * 128:(cc + 1) * 128]:
                          E.transpose(o, i, ctx.id_bf),
                          [ld, ctx.id_tok, ctx.bankuse.get(6 + g % 2)])
            ev = None
            for ki in range(4):
                cc = g * 4 + ki
                ev = b.op("dve", lambda E, o=woT[:, cc * 1024 + k * 128:cc * 1024 + (k + 1) * 128],
                          i=tb[:, ki * 128:(ki + 1) * 128]: E.tensor_copy(o, i), [tt])
            ctx.bankuse[6 + g % 2] = ev
            wo_war = ev
    woT_ck = wo_war
    for half in range(2):
        bkv = ctx.bank(5, 512, F32)
        mm = None
        for k in range(DT):
            mm = b.op("pe", lambda E, o=bkv, l=uv_raw[:, k * 128:(k + 1) * 128],
                      r=woT[:, k * 1024 + half * 512:k * 1024 + half * 512 + 512], k=k:
                      E.matmul(o, l, r, start=(k == 0), stop=(k == DT - 1)),
                      [raw_ck, woT_ck, ctx.bankuse.get(5)])
        ev = b.op("dve", lambda E, o=veff[:, half * 512:(half + 1) * 512], i=bkv:
                  E.tensor_copy(o, i), [mm])
        ctx.bankuse[5] = ev
        veff_ck = ev

    b.dma("pool", msk, ctx.dram["c_l0am"], sem="q_gld", exclusive=False)
    msk_ck = b.ckpt("pool", "q_gld")
    ctx.ar.reset(mark_prep)

    xn_stride = (T + 4) * 2
    xn_off = ctx.ar.alloc(DT * xn_stride)
    ln = emit_ln(ctx, x_in, xn_off, xn_stride, tag=tag + "l", src_tok=[veff_ck])
    xn, xn_tok = ln["xn_tiles"], ln["xn_tok"]

    qlow = ctx.v(ctx.ar.alloc(T * 2), T, BF16)
    for c in range(C):
        bk = ctx.bank(c % 2, 512, F32)
        mm = None
        for k in range(DT):
            mm = b.op("pe", lambda E, o=bk, l=dkvT[:, k * 128:(k + 1) * 128],
                      r=xn[k][:, 1 + c * 512:1 + c * 512 + 512], k=k:
                      E.matmul(o, l, r, start=(k == 0), stop=(k == DT - 1)),
                      [tr_ev["dkv"], xn_tok[k][c], ctx.bankuse.get(c % 2)])
        ev = b.op("dve", lambda E, o=ckvT[:, c * 512:(c + 1) * 512], i=bk:
                  E.tensor_copy(o, i), [mm])
        ctx.bankuse[c % 2] = ev
        ckvT_tok = ev
    for c in range(C):
        bk = ctx.bank(2 + c % 2, 512, F32)
        mm = None
        for k in range(DT):
            mm = b.op("pe", lambda E, o=bk, l=dqT[:, k * 128:(k + 1) * 128],
                      r=xn[k][:, 1 + c * 512:1 + c * 512 + 512], k=k:
                      E.matmul(o, l, r, start=(k == 0), stop=(k == DT - 1)),
                      [tr_ev["dq"], xn_tok[k][c], ctx.bankuse.get(2 + c % 2)])
        ev = b.op("dve", lambda E, o=qlow[:, c * 512:(c + 1) * 512], i=bk:
                  E.tensor_copy(o, i), [mm])
        ctx.bankuse[2 + c % 2] = ev
        qlow_tok = ev
    ckv_tok = None
    for g in range((TT_ + 7) // 8):
        tb = ctx.bank(6 + g % 2, 1024, BF16)
        nin = min(8, TT_ - g * 8)
        tt = None
        for ki in range(nin):
            jt = g * 8 + ki
            tt = b.op("pe", lambda E, o=tb[:, ki * 128:(ki + 1) * 128],
                      i=ckvT[:, jt * 128:(jt + 1) * 128]:
                      E.transpose(o, i, ctx.id_bf),
                      [ckvT_tok, ctx.id_tok, ctx.bankuse.get(6 + g % 2)])
        ckv_tok = b.op("dve", lambda E, o=ckv_sb[:, g * 1024:g * 1024 + nin * 128],
                       i=tb[:, 0:nin * 128]: E.tensor_copy(o, i), [tt])
        ctx.bankuse[6 + g % 2] = ckv_tok
    qup_tok = [[None] * C for _ in range(DT)]
    for j in range(DT):
        for c in range(C):
            bk = ctx.bank((j * C + c) % 2, 512, F32)
            mm = b.op("pe", lambda E, o=bk, l=uqT[:, j * 128:(j + 1) * 128],
                      r=qlow[:, c * 512:(c + 1) * 512]:
                      E.matmul(o, l, r, start=True, stop=True),
                      [uqT_ev, qlow_tok, ctx.bankuse.get((j * C + c) % 2)])
            ev = b.op("act" if (j + c) % 2 else "dve",
                      lambda E, o=qup_sb[j][:, c * 512:(c + 1) * 512], i=bk:
                      (E.copy(o, i) if hasattr(E, "copy") else E.tensor_copy(o, i)),
                      [mm])
            qup_tok[j][c] = ev
            ctx.bankuse[(j * C + c) % 2] = ev

    ql_off = ctx.ar.alloc(T * 2)
    P_off = ctx.ar.alloc(2 * T * 2)
    Pf_off = ctx.ar.alloc(T * 4)
    PT_off = ctx.ar.alloc(2 * TT_ * 128 * 2)
    cx_off = ctx.ar.alloc(2 * 128 * 2)
    sm_off = ctx.ar.alloc(2 * 192)
    ql_sb = ctx.v(ql_off, T, BF16)
    P_bf2 = [ctx.v(P_off + i * T * 2, T, BF16) for i in range(2)]
    P_f2 = [ctx.v(Pf_off, T, F32), ctx.v(Pf_off, T, F32)]
    PT_sb2 = [ctx.v(PT_off + i * TT_ * 128 * 2, TT_ * 128, BF16) for i in range(2)]
    cx_sb2 = [ctx.v(cx_off + i * 128 * 2, 128, BF16) for i in range(2)]
    negmax2 = [ctx.v(sm_off + i * 192, 1, F32) for i in range(2)]
    sums2 = [ctx.v(sm_off + i * 192 + 64, 1, F32) for i in range(2)]
    recip2 = [ctx.v(sm_off + i * 192 + 128, 1, F32) for i in range(2)]
    tr_tok = [None, None]
    sc_prev = [None, None]
    mx_prev = [None, None]
    cx_war = [None, None]
    qq = 0
    ql_war = None
    yev = None
    for h in range(16):
        jt2, pb2 = h // 2, 64 * (h % 2)
        ql_toks = []
        for c in range(C):
            bki = c % 2
            bk = ctx.bank(bki, 512, F32)
            mm = b.op("pe", lambda E, o=bk, l=keff[pb2:pb2 + 64, jt2 * 128:(jt2 + 1) * 128],
                      r=qup_sb[jt2][pb2:pb2 + 64, c * 512:(c + 1) * 512]:
                      E.matmul(o, l, r, start=True, stop=True),
                      [keff_ev, qup_tok[jt2][c], ctx.bankuse.get(bki), ql_war])
            ev = b.op("dve", lambda E, o=ql_sb[:, c * 512:(c + 1) * 512], i=bk:
                      E.tensor_copy(o, i), [mm])
            ctx.bankuse[bki] = ev
            ql_toks.append(ev)
        for qt in range(TT_):
            sl2 = qq % 2
            qq += 1
            P_bf, P_f, PT_sb, cx_sb = P_bf2[sl2], P_f2[sl2], PT_sb2[sl2], cx_sb2[sl2]
            negmax, sums, recip = negmax2[sl2], sums2[sl2], recip2[sl2]
            nch = qt // 4 + 1
            ncols = nch * 512
            nt = qt + 1
            smm = None
            for c in range(nch):
                bk = ctx.bank(c, 512, F32)
                w = [ql_toks[(qt * 128) // 512], ckvT_tok, ctx.bankuse.get(c)]
                smm = b.op("pe", lambda E, o=bk, l=ql_sb[:, qt * 128:(qt + 1) * 128],
                           r=ckvT[:, c * 512:(c + 1) * 512], sp=(c != nch - 1):
                           E.matmul(o, l, r, start=True, stop=sp), w)
                if c == nch - 1:
                    smm = b.op("pe", lambda E, o=bk, l=ctx.id_bf,
                               r=msk[:, (qt % 4) * 512:(qt % 4 + 1) * 512]:
                               E.matmul(o, l, r, start=False, stop=True),
                               [smm, msk_ck, ctx.id_tok])
            spsum = ctx.ps[:, 0:ncols * 4].bitcast(F32)
            mx = b.op("dve", lambda E, o=negmax, i=spsum: E.reduce_max(
                o, i, axis=AX.X, negate=True), [smm, mx_prev[sl2]])
            et = b.op("act", lambda E, o=P_f[:, 0:ncols], i=spsum, bb=negmax, s=sums:
                      E.activation(o, i, AF.Exp, bias=bb, accum_out=s),
                      [mx, sc_prev[0], sc_prev[1]])
            for c in range(nch):
                ctx.bankuse[c] = et
            rc = b.op("dve", lambda E, o=recip, i=sums: E.reciprocal(o, i), [et])
            sc_tok = b.op("dve", lambda E, o=P_bf[:, 0:ncols], i=P_f[:, 0:ncols],
                          r=recip: E.tensor_scalar(o, i, r, None, op0=ALU.mult),
                          [rc, tr_tok[sl2]])
            sc_prev[sl2] = sc_tok
            mx_prev[sl2] = rc
            g_ev = None
            for g in range((nt + 7) // 8):
                tb_i = 6 + (g % 2)
                tb = ctx.bank(tb_i, 1024, BF16)
                nin = min(8, nt - g * 8)
                ttok = None
                for ki in range(nin):
                    jt = g * 8 + ki
                    ttok = b.op("pe", lambda E, o=tb[:, ki * 128:(ki + 1) * 128],
                                i=P_bf[:, jt * 128:(jt + 1) * 128]:
                                E.transpose(o, i, ctx.id_bf),
                                [sc_tok, ctx.id_tok, ctx.bankuse.get(tb_i)])
                g_ev = b.op("dve", lambda E, o=PT_sb[:, g * 1024:g * 1024 + nin * 128],
                            i=tb[:, 0:nin * 128]: E.tensor_copy(o, i), [ttok])
                ctx.bankuse[tb_i] = g_ev
                tr_tok[sl2] = ttok
            cmm = None
            for jt in range(nt):
                bk = ctx.bank(4, 128, F32)
                w = [g_ev, ckv_tok, ctx.bankuse.get(4)]
                cmm = b.op("pe", lambda E, o=bk, l=ckv_sb[:, jt * 128:(jt + 1) * 128],
                           r=PT_sb[:, jt * 128:(jt + 1) * 128], st=(jt == 0),
                           sp=(jt == nt - 1): E.matmul(o, l, r, start=st, stop=sp), w)
            cx_ev = b.op("dve", lambda E, o=cx_sb, i=ctx.bank(4, 128, F32):
                         E.tensor_copy(o, i), [cmm, cx_war[sl2]])
            ctx.bankuse[4] = cx_ev
            ymm = b.op("pe", lambda E, o=ctx.bank(5, 128, F32, pn=64),
                       l=veff[:, 64 * h:64 * h + 64], r=cx_sb:
                       E.matmul(o, l, r, start=True, stop=True),
                       [cx_ev, veff_ck, ctx.bankuse.get(5)])
            yev = b.op("act", lambda E, o=y_sb[jt2][pb2:pb2 + 64, qt * 128:(qt + 1) * 128],
                       i=ctx.bank(5, 128, F32, pn=64): E.copy(o, i), [ymm])
            ctx.bankuse[5] = yev
            cx_war[sl2] = ymm
        ql_war = smm

    y_done = yev
    ctx.ar.reset(mark1)

    xi_off = ctx.ar.alloc(4 * 512 * 4)
    xo2_off = ctx.ar.alloc(4 * 512 * 4)
    xi_war, xo_war = {}, {}
    out_done = []
    for j in range(DT):
        for c in range(C):
            i4 = (j * C + c) % 4
            xi = ctx.v(xi_off + i4 * 2048, 512, F32)
            xo = ctx.v(xo2_off + i4 * 2048, 512, F32)
            ld = b.dma("sp", xi, x_in[j * P:(j + 1) * P, c * 512:c * 512 + 512],
                       waits=[xi_war.get(i4) or y_done], sem=f"q_sxi{i4}")
            t = b.op("dve", lambda E, o=xo, y=y_sb[j][:, c * 512:(c + 1) * 512], x=xi:
                     E.tensor_tensor(o, y, x, op=ALU.add),
                     [y_done, ld, xo_war.get(i4)])
            xi_war[i4] = t
            d = b.dma("sp", x_out[j * P:(j + 1) * P, c * 512:c * 512 + 512], xo,
                      waits=[t], sem=f"q_sxo{i4}")
            xo_war[i4] = d
            out_done.append(d)
    ctx.ar.reset(mark0)
    return out_done[-4:]


def emit_l0ffn(ctx, x_in, x_out, W, xA, xB, tag="s"):
    b, T, C = ctx.b, ctx.T, ctx.C
    NW = T // 256
    NV = T // 128 + 2
    mark_call = ctx.ar.mark()

    msk = ctx.v(ctx.ar.alloc(4 * 512 * 2), 4 * 512, BF16)
    b.dma("pool", msk, ctx.dram["c_l0fm"], sem="q_gld", exclusive=False)
    blk2 = ctx.v(ctx.ar.alloc(2 * 2), 2, BF16)
    b.dma("pool", blk2, ctx.dram["c_blk2"], sem="q_gld", exclusive=False)
    sel2 = ctx.v(ctx.ar.alloc(256), 128, BF16, pn=2)
    b.dma("pool", sel2, ctx.dram["c_sel2"], sem="q_gld", exclusive=False)
    cst_ck = b.ckpt("pool", "q_gld")
    ctx.l0 = dict(msk=msk, blk2=blk2, sel2=sel2, cst_ck=cst_ck)

    xn_stride = (T + 4) * 2
    mk = ctx.ar.mark()
    xn_off0 = ctx.ar.alloc(DT * xn_stride)
    r0 = emit_ln(ctx, x_in, xn_off0, xn_stride, tag=tag + "0", fout_dram=xA)
    src_ck = r0["store_ck"]
    ctx.ar.reset(mk)

    cur, nxt = xA, xB
    for dep in range(6):
        src_ck = _l0_attn_sub(ctx, W, dep, cur, nxt, src_ck, tag)
        cur, nxt = nxt, cur
        src_ck = _l0_ffn_sub(ctx, W, dep, cur, nxt, src_ck, tag)
        cur, nxt = nxt, cur

    xi_off = ctx.ar.alloc(4 * 512 * 4)
    xf_off = ctx.ar.alloc(4 * 512 * 4)
    xo_off = ctx.ar.alloc(4 * 512 * 4)
    xi_war, xf_war, xo_war = {}, {}, {}
    for j in range(DT):
        for c in range(C):
            i4 = (j * C + c) % 4
            xi = ctx.v(xi_off + i4 * 2048, 512, F32)
            xf = ctx.v(xf_off + i4 * 2048, 512, F32)
            xo = ctx.v(xo_off + i4 * 2048, 512, F32)
            l1 = b.dma("sp", xi, x_in[j * P:(j + 1) * P, c * 512:c * 512 + 512],
                       waits=[xi_war.get(i4) or src_ck], sem=f"q_sxi{i4}")
            l2 = b.dma("sp", xf, cur[j * P:(j + 1) * P, c * 512:c * 512 + 512],
                       waits=[xf_war.get(i4) or src_ck], sem=f"q_sxf{i4 % 2}")
            t = b.op("dve", lambda E, o=xo, a=xf, x=xi: E.scalar_tensor_tensor(
                o, a, 0.5, x, op0=ALU.mult, op1=ALU.add),
                [l1, l2, xo_war.get(i4)])
            xi_war[i4] = t
            xf_war[i4] = t
            d = b.dma("sp", x_out[j * P:(j + 1) * P, c * 512:c * 512 + 512], xo,
                      waits=[t], sem=f"q_sxo{i4}")
            xo_war[i4] = d
    done = b.nopop("sp", [(f"q_sxo{i}", b.cnt.get(f"q_sxo{i}", 0)) for i in range(4)])
    ctx.ar.reset(mark_call)
    return done


def _l0_attn_sub(ctx, W, dep, cur, nxt, src_ck, tag):
    b, T, C = ctx.b, ctx.T, ctx.C
    NW = T // 256
    NV = T // 128 + 2
    mark = ctx.ar.mark()
    qkv_d = W["qkv"][dep]
    wo_d = W["wo"][dep]
    msk, blk2, sel2, cst_ck = (ctx.l0[k] for k in ("msk", "blk2", "sel2", "cst_ck"))

    xb_stride = (T + 4) * 2
    xb_off = ctx.ar.alloc(DT * xb_stride)
    xb = [ctx.v(xb_off + k * xb_stride, T + 1, BF16) for k in range(DT)]
    xb_tok = []
    for k in range(DT):
        z = b.op("pool", lambda E, t=xb[k]: E.memset(t[:, 0:1], 0.0), [src_ck])
        tk = b.dma("pool", xb[k][:, 1:T + 1], cur[k * P:(k + 1) * P, :],
                   sem=f"q_gx{k}", waits=[src_ck, z])
        xb_tok.append(tk)

    def xs(k, c0, n):
        off = 0 if k >= DT // 2 else 1
        return xb[k][:, off + c0: off + c0 + n]

    qs_col = ctx.v(ctx.ar.alloc(64), 1, F32)
    qss_col = ctx.v(ctx.ar.alloc(64), 1, F32)
    ks_col = ctx.v(ctx.ar.alloc(64), 1, F32)
    kss_col = ctx.v(ctx.ar.alloc(64), 1, F32)
    for dst, srcv, sw in ((qs_col, W["qs"][dep], 0), (qss_col, W["qs"][dep], 1),
                          (ks_col, W["ks"][dep], 0), (kss_col, W["ks"][dep], 1)):
        for half in range(2):
            p0 = half * 64
            if sw:
                b.dma("sp", dst[p0:p0 + 32, :], col1(srcv[32:64]), sem="q_sld", exclusive=False)
                b.dma("sp", dst[p0 + 32:p0 + 64, :], col1(srcv[0:32]), sem="q_sld", exclusive=False)
            else:
                b.dma("sp", dst[p0:p0 + 64, :], col1(srcv[0:64]), sem="q_sld", exclusive=False)
    scol_ck = b.ckpt("sp", "q_sld")
    csq = ctx.v(ctx.ar.alloc(T * 2), T, BF16)
    ssq = ctx.v(ctx.ar.alloc(T * 2), T, BF16)
    csk = ctx.v(ctx.ar.alloc(T * 2), T, BF16)
    ssk = ctx.v(ctx.ar.alloc(T * 2), T, BF16)
    mk_rot = ctx.ar.mark()
    rotq_raw = ctx.v(ctx.ar.alloc(2 * T * 4), 2 * T, F32)
    rotk_raw = ctx.v(ctx.ar.alloc(2 * T * 4), 2 * T, F32)
    b.dma("pool", rotq_raw, ctx.dram["c_rotq"], sem="q_gld", exclusive=False,
          waits=[src_ck])
    b.dma("pool", rotk_raw, ctx.dram["c_rotk"], sem="q_gld", exclusive=False,
          waits=[src_ck])
    rot_ck = b.ckpt("pool", "q_gld")
    tab_tok = []
    for dst, raw, c0, scal, s2 in ((csq, rotq_raw, 0, qs_col, 8.0),
                                   (ssq, rotq_raw, T, qss_col, 8.0),
                                   (csk, rotk_raw, 0, ks_col, 1.0),
                                   (ssk, rotk_raw, T, kss_col, 1.0)):
        tab_tok.append(b.op("dve", lambda E, o=dst, i=raw[:, c0:c0 + T], s=scal, s2=s2:
                            E.tensor_scalar(o, i, s, s2, op0=ALU.mult, op1=ALU.mult),
                            [rot_ck, scol_ck]))
    tab_ck = tab_tok[-1]
    ctx.ar.reset(mk_rot)

    v2_off = ctx.ar.alloc(NV * 1024 * 2)
    v2 = ctx.v(v2_off, NV * 1024, BF16)
    vz = b.op("pool", lambda E, o=v2[:, 0:2048]: E.memset(o, 0.0), [src_ck, tab_ck])
    vs_off = ctx.ar.alloc(DT * 512 * 2)
    v_tok = None
    vload_war = None
    for vc in range(2):
        vs = ctx.v(vs_off, DT * 512, BF16)
        wt = b.dma("pool", vs, wslice_ap(qkv_d, 3072, 2048 + vc * 512, 512),
                   waits=[vload_war or src_ck, tab_ck], sem="q_gw0")
        mm = None
        for tt in range(T // 128):
            bi = (tt + vc) % 2
            bk = ctx.bank(bi, 512, F32)
            for k in range(DT):
                w = [wt, xb_tok[k], ctx.bankuse.get(bi)]
                mm = b.op("pe", lambda E, o=bk, l=xs(k, tt * 128, 128),
                          r=vs[:, k * 512:(k + 1) * 512], k=k:
                          E.matmul(o, l, r, start=(k == 0), stop=(k == DT - 1)), w)
            ev = b.op("act" if tt % 2 else "dve",
                      lambda E, o=v2[:, (tt + 2) * 1024 + vc * 512:(tt + 2) * 1024 + vc * 512 + 512],
                      i=bk: (E.copy(o, i) if hasattr(E, "copy") else E.tensor_copy(o, i)),
                      [mm, vz, tab_ck])
            v_tok = ev
            ctx.bankuse[bi] = ev
        vload_war = mm

    oT_off = ctx.ar.alloc(DT * T * 2)
    oT = [ctx.v(oT_off + j * T * 2, T, BF16) for j in range(DT)]
    mark_heads = ctx.ar.mark()
    qraw = ctx.v(ctx.ar.alloc(T * 2), T, BF16)
    kraw = qraw
    swp = ctx.v(ctx.ar.alloc(T * 2), T, BF16)
    t1b = ctx.v(ctx.ar.alloc(T * 2), T, BF16)
    sq_off2 = ctx.ar.alloc(2 * 512 * 2)
    qf_off = ctx.ar.alloc(2 * T * 2)
    kf_off = ctx.ar.alloc(2 * (T + 256) * 2)
    qfin2 = [ctx.v(qf_off + i * T * 2, T, BF16) for i in range(2)]
    kfin2 = [ctx.v(kf_off + i * (T + 256) * 2, T + 256, BF16) for i in range(2)]
    kz0s = [b.op("pool", lambda E, o=kfin2[0][:, 0:256]: E.memset(o, 0.0), [src_ck]),
            b.op("pool", lambda E, o=kfin2[1][:, 0:256]: E.memset(o, 0.0), [src_ck])]
    nrow = ctx.v(ctx.ar.alloc(T * 4), T, F32, pn=2)
    nrow_bf = ctx.v(ctx.ar.alloc(T * 2), T, BF16, pn=2)
    NSL = 3
    P2_off = ctx.ar.alloc(NSL * 1024 * 2)
    PTo_off = ctx.ar.alloc(NSL * 1024 * 2)
    sm_off = ctx.ar.alloc(NSL * 128)
    P22 = [ctx.v(P2_off + i * 2048, 1024, BF16) for i in range(NSL)]
    PT2 = [ctx.v(PTo_off + i * 2048, 1024, BF16) for i in range(NSL)]
    negmax4 = [ctx.v(sm_off + i * 128, 2, F32) for i in range(NSL)]
    NU = 4 * NW
    sa_off = ctx.ar.alloc(2 * NU * 4)
    rr_off = ctx.ar.alloc(2 * NU * 4)
    rrb_off = ctx.ar.alloc(2 * NU * 2)
    recT_off = ctx.ar.alloc(2 * 128 * 2)
    Rsb_off = ctx.ar.alloc(T * 2)
    sums_all = [ctx.v(sa_off + i * NU * 4, NU, F32) for i in range(2)]
    rr2 = [ctx.v(rr_off + i * NU * 4, NU, F32) for i in range(2)]
    rrb2 = [ctx.v(rrb_off + i * NU * 2, NU, BF16) for i in range(2)]
    recT2 = [ctx.v(recT_off + i * 256, 128, BF16, pn=NU) for i in range(2)]
    Rsb = ctx.v(Rsb_off, T, BF16)
    bcoh = ctx.v(ctx.ar.alloc(NU * 64 * 2), NU * 64, BF16, pn=NU)
    ones_tok = b.dma("pool", bcoh, ctx.dram["c_bcoh"], sem="q_gld",
                     exclusive=False, waits=[src_ck])
    ones_tok = b.ckpt("pool", "q_gld")
    qg_off = ctx.ar.alloc(DT * 256 * 2)
    kg_off = ctx.ar.alloc(DT * 256 * 2)
    qgv = ctx.v(qg_off, DT * 256, BF16)
    kgv = ctx.v(kg_off, DT * 256, BF16)
    grp_war = {"q": src_ck, "k": src_ck}
    grp_tok = {}

    fin_prev = {}
    sim_last = [None, None]
    prod_prev = [None]
    oev = None
    et4 = [[None, None] for _ in range(NSL)]
    trl4 = [None] * NSL
    ptev_war4 = [None] * NSL
    hw_c = [0]
    rc_tok = [None, None]
    ca_tok = [None, None]
    ttr_tok = [None, None]
    bc_tok = [None, None]
    r_mult_prev = [None]
    prod_cks = {}

    def gen_production(hp):
        sl = hp % 2
        if hp % 2 == 0:
            grp = hp // 2
            grp_tok["q"] = b.dma("pool", qgv, wslice_ap(qkv_d, 3072, grp * 256, 256),
                                 waits=[grp_war["q"]], sem="q_gw0")
            grp_tok["k"] = b.dma("pool", kgv, wslice_ap(qkv_d, 3072, 1024 + grp * 256, 256),
                                 waits=[grp_war["k"]], sem="q_gw1")
        po = (hp % 2) * 128
        cks = {}
        mm = None
        side_prev = [None]
        for side, raw, fin, cst, sst in (
                ("q", qraw, qfin2[sl], csq, ssq), ("k", kraw, kfin2[sl], csk, ssk)):
            gv = qgv if side == "q" else kgv
            wtok = grp_tok[side]
            ev_toks = []
            for c in range(C):
                bi = 2 + c % 2
                bk = ctx.bank(bi, 512, F32)
                mm = None
                for k in range(DT):
                    w = [wtok, xb_tok[k], ctx.bankuse.get(bi)]
                    mm = b.op("pe", lambda E, o=bk, l=gv[:, k * 256 + po:k * 256 + po + 128],
                              r=xs(k, c * 512, 512), k=k:
                              E.matmul(o, l, r, start=(k == 0), stop=(k == DT - 1)), w)
                ev = b.op("dve", lambda E, o=raw[:, c * 512:(c + 1) * 512], i=bk:
                          E.tensor_copy(o, i), [mm, prod_prev[0], side_prev[0]])
                ev_toks.append(ev)
                ctx.bankuse[bi] = ev
                yield
            raw_ck = ev_toks[-1]
            nr_toks = []
            sq_war = [None, None]
            for c in range(C):
                sq = ctx.v(sq_off2 + (c % 2) * 1024, 512, BF16)
                sq_t = b.op("dve", lambda E, o=sq, i=raw[:, c * 512:(c + 1) * 512]:
                            E.tensor_tensor(o, i, i, op=ALU.mult),
                            [ev_toks[c], sq_war[c % 2]])
                bk4 = ctx.bank(4, 512, F32, pn=2)
                mm2 = b.op("pe", lambda E, o=bk4, l=blk2, r=sq:
                           E.matmul(o, l, r, start=True, stop=True),
                           [sq_t, cst_ck, ctx.bankuse.get(4)])
                sq_war[c % 2] = mm2
                ev = b.op("dve", lambda E, o=nrow[:, c * 512:(c + 1) * 512], i=bk4:
                          E.tensor_copy(o, i), [mm2])
                nr_toks.append(ev)
                ctx.bankuse[4] = ev
                if c % 2 == 1:
                    yield
            t_cl = b.op("dve", lambda E, o=nrow: E.tensor_scalar(
                o, o, 1e-24, None, op0=ALU.max), [nr_toks[-1]])
            t_ln = b.op("act", lambda E, o=nrow: E.activation(o, o, AF.Ln), [t_cl])
            t_ex = b.op("act", lambda E, o=nrow: E.activation(o, o, AF.Exp, scale=-0.5), [t_ln])
            t_bf = b.op("dve", lambda E, o=nrow_bf, i=nrow: E.tensor_copy(o, i), [t_ex])
            yield
            sw_t = []
            for g4 in range(4):
                p0 = g4 * 32
                p1 = (g4 * 32 + 32) % 64 + (64 if g4 >= 2 else 0)
                sw_t.append(b.op("act", lambda E, o=swp[p0:p0 + 32, :], i=raw[p1:p1 + 32, 0:T]:
                                 E.copy(o, i), [raw_ck, prod_prev[0]]))
            tt1 = b.op("dve", lambda E, o=t1b, i=raw[:, 0:T], cc=cst: E.tensor_tensor(
                o, i, cc, op=ALU.mult), [raw_ck, tab_ck])
            tt2 = b.op("dve", lambda E, o=swp, i=swp, ss=sst: E.tensor_tensor(
                o, i, ss, op=ALU.mult), sw_t + [tab_ck])
            yield
            tt3 = b.op("dve", lambda E, o=t1b, a=t1b, bb=swp: E.tensor_tensor(
                o, a, bb, op=ALU.add), [tt1, tt2])
            fin0 = 0 if side == "q" else 256
            f_toks = []
            for c in range(C):
                bk4b = ctx.bank(4, 512, F32)
                mmb = b.op("pe", lambda E, o=bk4b, l=sel2, r=nrow_bf[:, c * 512:(c + 1) * 512]:
                           E.matmul(o, l, r, start=True, stop=True),
                           [t_bf, cst_ck, ctx.bankuse.get(4)])
                fv = b.op("dve", lambda E, o=fin[:, fin0 + c * 512:fin0 + c * 512 + 512],
                          t=t1b[:, c * 512:(c + 1) * 512], m=bk4b: E.tensor_tensor(
                    o, t, m, op=ALU.mult),
                    [tt3, mmb, kz0s[sl] if side == "k" else None,
                     fin_prev.get((side, sl)), sim_last[sl]])
                f_toks.append(fv)
                ctx.bankuse[4] = fv
                if c % 2 == 1:
                    yield
            fin_prev[(side, sl)] = f_toks[-1]
            cks[side] = f_toks[-1]
            side_prev[0] = tt2
            if hp % 2 == 1:
                grp_war[side] = mm
        prod_prev[0] = cks["k"]
        prod_cks[sl] = cks

    for _ in gen_production(0):
        pass
    for hp in range(DT):
        sl = hp % 2
        cks = prod_cks.pop(sl)
        qfin, kfin = qfin2[sl], kfin2[sl]
        qfin_ck, kfin_ck = cks["q"], cks["k"]
        gen = gen_production(hp + 1) if hp + 1 < DT else None
        last_sim = None
        last_et = None
        for hh in range(2):
            h = hp * 2 + hh
            pb = 64 * hh
            for w_ in range(NW):
                sl2 = hw_c[0] % NSL
                hw_c[0] += 1
                P2, PT = P22[sl2], PT2[sl2]
                negmax = negmax4[sl2]
                smm = {}
                for qsub in range(2):
                    bi = (w_ * 2 + qsub) % 2
                    bk = ctx.bank(bi, 512, F32)
                    mv = qsub + (2 if w_ == 0 else 0)
                    m1 = b.op("pe", lambda E, o=bk, l=qfin[pb:pb + 64, (w_ * 256 + qsub * 128):(w_ * 256 + qsub * 128 + 128)],
                              r=kfin[pb:pb + 64, w_ * 256:w_ * 256 + 512]:
                              E.matmul(o, l, r, start=True, stop=False),
                              [qfin_ck, kfin_ck, ctx.bankuse.get(bi)])
                    m2 = b.op("pe", lambda E, o=bk, l=ctx.id_bf,
                              r=msk[:, mv * 512:(mv + 1) * 512]:
                              E.matmul(o, l, r, start=False, stop=True),
                              [m1, cst_ck, ctx.id_tok])
                    smm[qsub] = m2
                    last_sim = m2
                for qsub in range(2):
                    bi = (w_ * 2 + qsub) % 2
                    bk = ctx.bank(bi, 512, F32)
                    colf = hh * 2 * NW + w_ * 2 + qsub
                    mx = b.op("dve", lambda E, o=negmax[:, qsub:qsub + 1], i=bk:
                              E.reduce_max(o, i, axis=AX.X, negate=True),
                              [smm[qsub], et4[sl2][qsub]])
                    et = b.op("act", lambda E, o=P2[:, qsub * 512:(qsub + 1) * 512],
                              i=bk, bb=negmax[:, qsub:qsub + 1],
                              s=sums_all[sl][:, colf:colf + 1]:
                              E.activation(o, i, AF.Exp, bias=bb, accum_out=s),
                              [mx, trl4[sl2], rc_tok[sl]])
                    et4[sl2][qsub] = et
                    ctx.bankuse[bi] = et
                    smm[qsub] = et
                    last_et = et
                tb = ctx.bank(6 + w_ % 2, 1024, BF16)
                ttok = None
                for jt in range(4):
                    for qsub in range(2):
                        ttok = b.op("pe", lambda E, o=tb[:, (jt * 2 + qsub) * 128:(jt * 2 + qsub + 1) * 128],
                                    i=P2[:, qsub * 512 + jt * 128:qsub * 512 + (jt + 1) * 128]:
                                    E.transpose(o, i, ctx.id_bf),
                                    [smm[0], smm[1], ctx.id_tok, ctx.bankuse.get(6 + w_ % 2)])
                trl4[sl2] = ttok
                ptev = b.op("dve", lambda E, o=PT, i=tb: E.tensor_copy(o, i),
                            [ttok, ptev_war4[sl2]])
                ctx.bankuse[6 + w_ % 2] = ptev
                cmm = None
                for jt in range(4):
                    bk = ctx.bank(5, 256, F32, pn=64)
                    w = [ptev, v_tok, ctx.bankuse.get(5)]
                    cmm = b.op("pe", lambda E, o=bk,
                               l=v2[:, (2 * w_ + jt) * 1024 + 64 * h:(2 * w_ + jt) * 1024 + 64 * h + 64],
                               r=PT[:, jt * 256:(jt + 1) * 256], st=(jt == 0), sp=(jt == 3):
                               E.matmul(o, l, r, start=st, stop=sp), w)
                oev = b.op("act", lambda E, o=oT[hp][pb:pb + 64, w_ * 256:(w_ + 1) * 256],
                           i=ctx.bank(5, 256, F32, pn=64): E.copy(o, i), [cmm])
                ctx.bankuse[5] = oev
                ptev_war4[sl2] = cmm
                if gen is not None:
                    next(gen, None)
        sim_last[sl] = last_sim
        rc = b.op("dve", lambda E, o=rr2[sl], i=sums_all[sl]: E.reciprocal(o, i),
                  [last_et, ca_tok[sl]])
        rc_tok[sl] = rc
        ca = b.op("dve", lambda E, o=rrb2[sl], i=rr2[sl]: E.tensor_copy(o, i),
                  [rc, ttr_tok[sl]])
        ca_tok[sl] = ca
        tbk = ctx.bank(4, 1024, BF16)
        ttr = b.op("pe", lambda E, o=tbk[0:NU, 0:128], i=rrb2[sl]:
                   E.transpose(o, i, ctx.id_bf),
                   [ca, ctx.id_tok, ctx.bankuse.get(4)])
        ttr_tok[sl] = ttr
        evT = b.op("dve", lambda E, o=recT2[sl], i=tbk[0:NU, 0:128]:
                   E.tensor_copy(o, i), [ttr, bc_tok[sl]])
        ctx.bankuse[4] = evT
        recT = recT2[sl]
        ev_last = None
        for hh in range(2):
            for g in range(C):
                bkR = ctx.bank(4, 512, F32, pn=64)
                mmlast = None
                for q in range(4):
                    r_idx = hh * 2 * NW + (g * 2 + q // 2) * 2 + (q % 2)
                    mmlast = b.op("pe", lambda E, o=bkR[:, q * 128:(q + 1) * 128],
                                  l=bcoh[:, r_idx * 64:(r_idx + 1) * 64],
                                  r=recT[0:NU, 0:128]:
                                  E.matmul(o, l, r, start=True, stop=True),
                                  [evT, ones_tok, ctx.bankuse.get(4)])
                ev = b.op("dve", lambda E, o=Rsb[64 * hh:64 * hh + 64, g * 512:(g + 1) * 512],
                          i=bkR: E.tensor_copy(o, i), [mmlast, r_mult_prev[0]])
                ctx.bankuse[4] = ev
                ev_last = ev
        bc_tok[sl] = mmlast
        mult = b.op("dve", lambda E, o=oT[hp], a=oT[hp], r=Rsb: E.tensor_tensor(
            o, a, r, op=ALU.mult), [oev, ev_last])
        r_mult_prev[0] = mult
        if gen is not None:
            for _ in gen:
                pass

    o_done = r_mult_prev[0]
    ctx.ar.reset(mark_heads)
    ws2_off = ctx.ar.alloc(2 * DT * 128 * 2)
    xi_off = ctx.ar.alloc(4 * 512 * 4)
    xo_off = ctx.ar.alloc(4 * 512 * 4)
    xi_war, xo_war = {}, {}
    ws2_war = [None, None]
    for j in range(DT):
        sl = j % 2
        ws = ctx.v(ws2_off + sl * DT * 128 * 2, DT * 128, BF16)
        wt = b.dma("pool", ws, wslice_ap(wo_d, 1024, j * 128, 128),
                   waits=[ws2_war[sl] or o_done], sem=f"q_gw{sl}")
        for c in range(C):
            bi = (j * C + c) % 4
            bk = ctx.bank(bi, 512, F32)
            mm = None
            for k in range(DT):
                w = [wt, o_done, ctx.bankuse.get(bi)]
                mm = b.op("pe", lambda E, o=bk, l=ws[:, k * 128:(k + 1) * 128],
                          r=oT[k][:, c * 512:(c + 1) * 512], k=k:
                          E.matmul(o, l, r, start=(k == 0), stop=(k == DT - 1)), w)
            i4 = (j * C + c) % 4
            xi = ctx.v(xi_off + i4 * 2048, 512, F32)
            xo = ctx.v(xo_off + i4 * 2048, 512, F32)
            ld = b.dma("sp", xi, cur[j * P:(j + 1) * P, c * 512:c * 512 + 512],
                       waits=[xi_war.get(i4) or o_done], sem=f"q_sxi{i4}")
            t = b.op("dve", lambda E, o=xo, i=bk, x=xi: E.tensor_tensor(
                o, i, x, op=ALU.add), [mm, ld, xo_war.get(i4)])
            xi_war[i4] = t
            ctx.bankuse[bi] = t
            d = b.dma("sp", nxt[j * P:(j + 1) * P, c * 512:c * 512 + 512], xo,
                      waits=[t], sem=f"q_sxo{i4}")
            xo_war[i4] = d
        ws2_war[sl] = mm
    ck = b.nopop("sp", [(f"q_sxo{i}", b.cnt.get(f"q_sxo{i}", 0)) for i in range(4)])
    ctx.ar.reset(mark)
    return ck


def _l0_ffn_sub(ctx, W, dep, cur, nxt, src_ck, tag):
    b, T, C = ctx.b, ctx.T, ctx.C
    mark = ctx.ar.mark()
    w1_d, w2_d = W["w1"][dep], W["w2"][dep]
    NP = 22

    Pst_off = ctx.ar.alloc(NP * T * 2)
    Pst = [ctx.v(Pst_off + j * T * 2, T, BF16) for j in range(NP)]

    b1a, b1g, b2c = [], [], []
    for j in range(NP):
        na = 128 if j < NP - 1 else 42
        va = ctx.v(ctx.ar.alloc(64), 1, F32, pn=na)
        vg = ctx.v(ctx.ar.alloc(64), 1, F32, pn=na)
        b.dma("sp", va, col1(W["b1"][dep][j * 128:j * 128 + na]), sem="q_sld", exclusive=False)
        b.dma("sp", vg, col1(W["b1"][dep][2730 + j * 128:2730 + j * 128 + na]), sem="q_sld", exclusive=False)
        b1a.append(va)
        b1g.append(vg)
    for j in range(DT):
        v = ctx.v(ctx.ar.alloc(64), 1, F32)
        b.dma("sp", v, col1(W["b2"][dep][j * 128:(j + 1) * 128]), sem="q_sld", exclusive=False)
        b2c.append(v)
    col_ck = b.ckpt("sp", "q_sld")

    mk_ln = ctx.ar.mark()
    xn_stride = (T + 4) * 2
    xn_off = ctx.ar.alloc(DT * xn_stride)
    ln = emit_ln(ctx, cur, xn_off, xn_stride, g_dram=W["ln_g"][dep],
                 bg_dram=W["ln_b"][dep], shift=True, tag=tag + f"n{dep}",
                 src_tok=[src_ck])
    xn, xn_tok = ln["xn_tiles"], ln["xn_tok"]

    gbuf = ctx.v(ctx.ar.alloc(T * 2), T, BF16)
    wbig_off = ctx.ar.alloc(DT * 2730 * 2)
    wbig = ctx.v(wbig_off, DT * 2730, BF16)
    ln_done = ln["xn_tok"][DT - 1][ctx.C - 1]
    half_war = ln_done
    gw_tok = None
    gel_toks = []
    for half in (0, 1):
        wt = b.dma("pool", wbig, wslice_ap(w1_d, 5460, half * 2730, 2730),
                   waits=[half_war], sem="q_gw0")
        for j in range(NP):
            na = 128 if j < NP - 1 else 42
            bias_c = b1a[j] if half == 0 else b1g[j]
            for c in range(C):
                bi = 2 + (j + c) % 2
                bk = ctx.bank(bi, 512, F32, pn=na)
                mm = None
                for k in range(DT):
                    w = [wt, xn_tok[k][c], ctx.bankuse.get(bi)]
                    mm = b.op("pe", lambda E, o=bk,
                              l=wbig[:, k * 2730 + j * 128:k * 2730 + j * 128 + na],
                              r=xn[k][:, 1 + c * 512:1 + c * 512 + 512], k=k:
                              E.matmul(o, l, r, start=(k == 0), stop=(k == DT - 1)), w)
                if half == 0:
                    ev = b.op("dve", lambda E, o=Pst[j][0:na, c * 512:(c + 1) * 512],
                              i=bk, bc=bias_c: E.tensor_scalar(
                        o, i, bc, None, op0=ALU.add), [mm, col_ck])
                else:
                    ev = b.op("act", lambda E, o=gbuf[0:na, c * 512:(c + 1) * 512],
                              i=bk, bc=bias_c: E.activation(o, i, AF.Gelu, bias=bc),
                              [mm, col_ck, gw_tok])
                ctx.bankuse[bi] = ev
            if half == 1:
                gw_tok = b.op("dve", lambda E, o=Pst[j][0:na, 0:T], a=Pst[j][0:na, 0:T],
                              g=gbuf[0:na, 0:T]: E.tensor_tensor(o, a, g, op=ALU.mult),
                              [ev])
                gel_toks.append(gw_tok)
        half_war = mm
    P_ck = gw_tok

    ctx.ar.reset(mk_ln)
    w2_off = ctx.ar.alloc(NP * 1024 * 2)
    w2t = [ctx.v(w2_off + k * 1024 * 2, 1024, BF16, pn=(128 if k < NP - 1 else 42))
           for k in range(NP)]
    w2_tok = []
    for k in range(NP):
        na = 128 if k < NP - 1 else 42
        w2_tok.append(b.dma("pool", w2t[k], w2_d[k * 128:k * 128 + na, :],
                            waits=[P_ck if k < 2 else w2_tok[k - 2]], sem=f"q_gw{k % 2}"))
    xi_off = ctx.ar.alloc(4 * 512 * 4)
    xo_off = ctx.ar.alloc(4 * 512 * 4)
    xi_war, xo_war = {}, {}
    for j in range(DT):
        for c in range(C):
            bi = (j * C + c) % 2
            bk = ctx.bank(bi, 512, F32)
            mm = None
            for k in range(NP):
                na = 128 if k < NP - 1 else 42
                w = [w2_tok[k], P_ck, ctx.bankuse.get(bi)]
                mm = b.op("pe", lambda E, o=bk, l=w2t[k][:, j * 128:(j + 1) * 128],
                          r=Pst[k][0:na, c * 512:(c + 1) * 512], k=k:
                          E.matmul(o, l, r, start=(k == 0), stop=(k == NP - 1)), w)
            i4 = (j * C + c) % 4
            xi = ctx.v(xi_off + i4 * 2048, 512, F32)
            xo = ctx.v(xo_off + i4 * 2048, 512, F32)
            ld = b.dma("sp", xi, cur[j * P:(j + 1) * P, c * 512:c * 512 + 512],
                       waits=[xi_war.get(i4) or P_ck], sem=f"q_sxi{i4}")
            t = b.op("dve", lambda E, o=xo, i=bk, bc=b2c[j], x=xi:
                     E.scalar_tensor_tensor(o, i, bc, x, op0=ALU.add, op1=ALU.add),
                     [mm, ld, col_ck, xo_war.get(i4)])
            xi_war[i4] = t
            ctx.bankuse[bi] = t
            d = b.dma("sp", nxt[j * P:(j + 1) * P, c * 512:c * 512 + 512], xo,
                      waits=[t], sem=f"q_sxo{i4}")
            xo_war[i4] = d
    ck = b.nopop("sp", [(f"q_sxo{i}", b.cnt.get(f"q_sxo{i}", 0)) for i in range(4)])
    ctx.ar.reset(mark)
    return ck


def phase_barrier(b):
    sp_d = [(s, v) for s, v in b.cnt.items() if s.startswith("q_s") and v > 0]
    pl_d = [(s, v) for s, v in b.cnt.items() if (s.startswith("q_g") or s == "q_pre") and v > 0]
    t1 = b.nopop("sp", sp_d)
    t2 = b.nopop("pool", pl_d)
    base = [t1, t2]
    for e in ENGINES:
        if b.cnt.get(e, 0) > 0:
            base.append((e, b.cnt[e]))
    b.base = []
    bb = []
    for t in base:
        bb.append(t)
    b.base = bb
    return bb


def build_module(T):
    nc = bass.Bass("TRN2", target_bir_lowering=False)
    b = Bld(nc)
    ctx = Ctx(nc, b, T)
    D_ = D

    def di(name, shape):
        ap = nc.dram_tensor(name, list(shape), F32, kind="ExternalInput").ap()
        ctx.dram[name] = ap
        return ap

    x = di("x_fm", (D_, T))
    W0a = {
        "dq": di("L0_attn_W_dq", (128, D_)),
        "uq": di("L0_attn_W_uq", (D_, 128)),
        "dkv": di("L0_attn_W_dkv", (128, D_)),
        "uk": di("L0_attn_W_uk", (D_, 128)),
        "uv": di("L0_attn_W_uv", (D_, 128)),
        "o": di("L0_attn_W_o", (D_, D_)),
    }
    W0f = {
        "qkv": di("L0_ffn_Wqkv", (6, D_, 3072)),
        "qs": di("L0_ffn_q_scale", (6, 64)),
        "ks": di("L0_ffn_k_scale", (6, 64)),
        "wo": di("L0_ffn_Wo", (6, 1024, D_)),
        "ln_g": di("L0_ffn_ln_g", (6, D_)),
        "ln_b": di("L0_ffn_ln_b", (6, D_)),
        "w1": di("L0_ffn_W1", (6, D_, 5460)),
        "b1": di("L0_ffn_b1", (6, 5460)),
        "w2": di("L0_ffn_W2", (6, 2730, D_)),
        "b2": di("L0_ffn_b2", (6, D_)),
    }
    W1a_qkv = di("L1_attn_W_qkv", (D_, 3 * D_))
    W1a_out = di("L1_attn_W_out", (D_, D_))
    W1a_rel = di("L1_attn_rel_emb", (32, 16))
    W1f = {}
    for n, shp in [("w1_w", (4096, D_)), ("w2_w", (4096, D_)), ("h1_w", (4096, 4096)),
                   ("h2_w", (4096, 4096)), ("w3_w", (D_, 4096)),
                   ("w1_s", (4096,)), ("w2_s", (4096,)), ("h1_s", (4096,)),
                   ("h2_s", (4096,)), ("w3_s", (D_,))]:
        W1f[n] = di("L1_ffn_" + n, shp)
    for n, shp in [("c_eye", (P, P)), ("c_l0fm", (P, 4 * 512)), ("c_l0am", (P, 4 * 512)),
                   ("c_rotq", (P, 2 * T)), ("c_rotk", (P, 2 * T)),
                   ("c_blk2", (P, 2)), ("c_sel2", (2, P)),
                   ("c_bcoh", (T // 64, (T // 64) * 64))]:
        di(n, shp)
    oht = nc.dram_tensor("c_t5oh", [32, 2 * T - 1], F32, kind="ExternalInput").ap()
    ctx.dram["c_t5oh"] = oht
    idxt = nc.dram_tensor("c_t5idx", [P, 16], I32, kind="ExternalInput").ap()
    ctx.dram["c_t5idx"] = idxt

    xo = nc.dram_tensor("xo_fm", [D_, T], F32, kind="ExternalOutput").ap()
    ya = nc.dram_tensor("ya", [D_, T], F32, kind="Internal").ap()
    yb = nc.dram_tensor("yb", [D_, T], F32, kind="Internal").ap()
    xA = nc.dram_tensor("xA", [D_, T], F32, kind="Internal").ap()
    xB = nc.dram_tensor("xB", [D_, T], F32, kind="Internal").ap()
    scr_a = nc.dram_tensor("scr_a", [4096, T], BF16, kind="Internal").ap()
    scr_b = nc.dram_tensor("scr_b", [4096, T], BF16, kind="Internal").ap()
    bias_hbm = nc.dram_tensor("bias_hbm", [16 * (2 * T - 1)], BF16, kind="Internal").ap()

    from contextlib import ExitStack
    es = ExitStack()
    ctx.sb = es.enter_context(nc.sbuf_tensor("SB", [P, SBUF_BYTES], U8))
    ctx.ps = es.enter_context(nc.psum_tensor("PS", [P, 16384], U8))
    emit_prelude(ctx)
    emit_l0ffn(ctx, x, ya, W0f, xA, xB, tag="s0")
    phase_barrier(b)
    emit_l0attn(ctx, ya, yb, W0a, tag="m0")
    phase_barrier(b)
    emit_l0ffn(ctx, yb, ya, W0f, xA, xB, tag="s1")
    phase_barrier(b)
    emit_l1ffn(ctx, ya, yb, W1f, scr_a, scr_b, tag="f0")
    phase_barrier(b)
    emit_l1attn(ctx, yb, ya, W1a_qkv, W1a_out, W1a_rel, bias_hbm, tag="a0")
    phase_barrier(b)
    done = emit_l1ffn(ctx, ya, xo, W1f, scr_a, scr_b, tag="f1")
    b.nopop("sp", done)

    sems = {}
    for name in b.cnt:
        sems[name] = es.enter_context(nc.semaphore(name))
    blk = es.enter_context(nc.Block())
    b.build_into(blk, sems)
    es.close()
    return nc, b


def host_consts(T):
    HS, WIN = 64, 256
    half = np.arange(0, HS, 2, dtype=np.float64)
    invf = 1.0 / 10000.0 ** (half / HS)
    t = np.arange(T, dtype=np.float64)
    fr = np.concatenate([t[:, None] * invf] * 2, -1)
    base = (half + 0.4 * HS) / (1.4 * HS)
    sc = base[None, :] ** ((t - T // 2) / (WIN // 2))[:, None]
    sc = np.concatenate([sc, sc], -1)
    cos, sin = np.cos(fr), np.sin(fr)
    sign = np.where(np.arange(64) < 32, -1.0, 1.0)

    def fm(a):
        m = a.T
        return np.concatenate([m, m], 0).astype(np.float32)

    rotq = np.concatenate([fm(cos * sc), fm(sin * sc * sign[None, :])], 1)
    rotk = np.concatenate([fm(cos / sc), fm(sin / sc * sign[None, :])], 1)

    l0fm = np.zeros((P, 4 * 512), np.float32)
    for v in range(4):
        qsub, w0 = v % 2, v >= 2
        for p in range(P):
            i = p + 128 * qsub
            ok = np.zeros(512, bool)
            ok[i:i + 257] = True
            if w0:
                ok[:256] = False
            l0fm[p, v * 512:(v + 1) * 512] = np.where(ok, 0.0, -3e38)

    l0am = np.zeros((P, 4 * 512), np.float32)
    for v in range(4):
        for p in range(P):
            jmax = v * 128 + p
            l0am[p, v * 512 + jmax + 1:(v + 1) * 512] = -3e38

    RLEN = 2 * T - 1
    NB, MD = 32, 128
    r = np.arange(-(T - 1), T)
    nb = NB // 2
    rb = (r > 0).astype(np.int64) * nb
    ra = np.abs(r)
    mx = nb // 2
    small = ra < mx
    large = mx + (np.log(np.maximum(ra, 1) / mx) / np.log(MD / mx) * (nb - mx)).astype(np.int64)
    large = np.minimum(large, nb - 1)
    buck = rb + np.where(small, ra, large)
    oh = np.zeros((NB, RLEN), np.float32)
    oh[buck, np.arange(RLEN)] = 1.0
    idx = np.zeros((P, 16), np.int32)
    for h in range(16):
        idx[:, h] = h * RLEN + 127 - np.arange(P)

    blk2 = np.zeros((P, 2), np.float32)
    blk2[:64, 0] = 1
    blk2[64:, 1] = 1

    NU = T // 64
    bcoh = np.zeros((NU, NU * 64), np.float32)
    for rr_ in range(NU):
        bcoh[rr_, rr_ * 64:(rr_ + 1) * 64] = 1.0

    return dict(c_eye=np.eye(P, dtype=np.float32), c_l0fm=l0fm, c_l0am=l0am,
                c_rotq=rotq, c_rotk=rotk, c_blk2=blk2, c_sel2=blk2.T.copy(),
                c_t5oh=oh, c_t5idx=idx, c_bcoh=bcoh)


_CACHED = {}


def _get_program():
    if "nc" not in _CACHED:
        nc, b = build_module(2048)
        _CACHED["nc"] = nc
    return _CACHED["nc"]


def kernel(**inputs):
    T = 2048
    nc = _get_program()
    consts = host_consts(T)
    x = inputs["L0_attn_x"]
    names = [
        "L0_attn_W_dq", "L0_attn_W_uq", "L0_attn_W_dkv", "L0_attn_W_uk",
        "L0_attn_W_uv", "L0_attn_W_o", "L0_ffn_Wqkv", "L0_ffn_q_scale",
        "L0_ffn_k_scale", "L0_ffn_Wo", "L0_ffn_ln_g", "L0_ffn_ln_b",
        "L0_ffn_W1", "L0_ffn_b1", "L0_ffn_W2", "L0_ffn_b2",
        "L1_attn_W_qkv", "L1_attn_W_out", "L1_attn_rel_emb",
        "L1_ffn_w1_w", "L1_ffn_w1_s", "L1_ffn_w2_w", "L1_ffn_w2_s",
        "L1_ffn_h1_w", "L1_ffn_h1_s", "L1_ffn_h2_w", "L1_ffn_h2_s",
        "L1_ffn_w3_w", "L1_ffn_w3_s",
    ]
    shared = {n: np.ascontiguousarray(np.asarray(inputs[n], dtype=np.float32))
              for n in names}
    shared.update(consts)
    in_maps = []
    for i in range(8):
        m = dict(shared)
        m["x_fm"] = np.ascontiguousarray(np.asarray(x[i], dtype=np.float32).T)
        in_maps.append(m)
    res = run_bass_kernel_spmd(nc, in_maps, core_ids=list(range(8)))
    out = np.stack([np.ascontiguousarray(r["xo_fm"].T) for r in res.results])
    return out.astype(np.float32)
